# version 02
# baseline (speedup 1.0000x reference)
import numpy as np
import concourse.bass as bass
import concourse.mybir as mybir

F32 = mybir.dt.float32
BF16 = mybir.dt.bfloat16
AF = mybir.ActivationFunctionType
ALU = mybir.AluOpType
AX = mybir.AxisListType

ENGINES = ("tensor", "vector", "scalar", "gpsimd", "sync")
P = 128


class KB:
    SEG_LIMIT = 1500

    def __init__(self, nc):
        self.nc = nc
        self.ops = {e: [] for e in ENGINES}
        self.dma_count = {e: 0 for e in ENGINES}
        self.buffers = {}
        self.watermark = {e: {} for e in ENGINES}
        self.n_pruned = 0
        self.segment = 0
        self._seg_ops = 0

    def seg(self):
        if self._seg_ops > self.SEG_LIMIT:
            self.segment += 1
            self._seg_ops = 0

    def emit(self, eng, fn, reads=(), writes=(), dma=False, tag=""):
        assert eng in ENGINES
        idx = len(self.ops[eng])
        if dma:
            tok = ("d", eng, self.dma_count[eng])
            self.dma_count[eng] += 1
        else:
            tok = ("c", eng, idx)
        waits = []

        def add_wait(t):
            kind, peng, pidx = t
            if kind == "c" and peng == eng and peng == "tensor":
                return
            if kind == "c" and peng == eng and pidx == idx:
                return
            waits.append(t)

        for b in reads:
            ent = self.buffers.get(b)
            if ent is not None and ent["writer"] is not None:
                add_wait(ent["writer"])
        for b in writes:
            ent = self.buffers.get(b)
            if ent is not None:
                if ent["writer"] is not None:
                    add_wait(ent["writer"])
                for rt in ent["readers"]:
                    kind, peng, pidx = rt
                    if kind == "c" and peng == eng:
                        continue
                    waits.append(rt)
        for b in reads:
            ent = self.buffers.setdefault(b, {"writer": None, "readers": []})
            ent["readers"].append(tok)
        for b in writes:
            ent = self.buffers.setdefault(b, {"writer": None, "readers": []})
            ent["writer"] = tok
            ent["readers"] = []
        wm = self.watermark[eng]
        final_waits = []
        seen = {}
        for (kind, peng, pidx) in waits:
            key = (kind, peng)
            if key in seen:
                seen[key] = max(seen[key], pidx)
            else:
                seen[key] = pidx
        for key, pidx in seen.items():
            if wm.get(key, -1) >= pidx:
                self.n_pruned += 1
                continue
            wm[key] = pidx
            final_waits.append((key[0], key[1], pidx))
        self.ops[eng].append(dict(fn=fn, waits=final_waits, dma=dma, tag=tag, tok=tok,
                                  seg=self.segment))
        self._seg_ops += 1
        return tok

    def dma(self, q, out, in_, reads=(), writes=(), tag="", **kw):
        rd = list(reads)
        return self.emit(q, lambda e: e.dma_start(out, in_, **kw),
                         reads=rd, writes=writes, dma=True, tag=tag)

    def finalize(self, es):
        nc = self.nc
        for e in ENGINES:
            for op in self.ops[e]:
                for (kind, peng, pidx) in op["waits"]:
                    if kind == "c":
                        self.ops[peng][pidx]["inc"] = True
        self.cval = {}
        for e in ENGINES:
            c = 0
            vals = []
            for op in self.ops[e]:
                if op.get("inc") and not op["dma"]:
                    c += 1
                vals.append(c)
            self.cval[e] = vals
        self.csem = {}
        self.dsem = {}
        for e in ENGINES:
            if any((not op["dma"]) and op.get("inc") for op in self.ops[e]):
                self.csem[e] = es.enter_context(nc.semaphore("sem_c_" + e))
            if self.dma_count[e]:
                self.dsem[e] = es.enter_context(nc.semaphore("sem_d_" + e))

        stats = {e: (len(self.ops[e]), sum(1 for o in self.ops[e] if o.get("inc")),
                     sum(len(o["waits"]) for o in self.ops[e]))
                 for e in ENGINES}
        self.stats = stats
        allsems = list(self.csem.values()) + list(self.dsem.values())
        for s in allsems:
            nc.gpsimd.sem_clear(s)
        nc.all_engine_barrier()
        wu = nc.alloc_sbuf_tensor_at("warmup_tile", [128, 512], mybir.dt.float32,
                                     offset=Mem.BASE + Mem.LIMIT - 2048)
        wua = wu.ap()
        nc.vector.memset(wua[:], 1.0)
        for i in range(24):
            nc.scalar.activation(wua[:, 0:256], wua[:, 0:256], AF.Exp, scale=0.0)
            nc.vector.tensor_copy(wua[:, 256:512], wua[:, 256:512])
            nc.gpsimd.memset(wua[:, 0:64], 1.0)
        nc.all_engine_barrier()

        kb = self
        nseg = self.segment + 1
        seg_ops = {e: [[] for _ in range(nseg)] for e in ENGINES}
        for e in ENGINES:
            for op in self.ops[e]:
                seg_ops[e][op["seg"]].append(op)

        def make_section(e, ops):
            def section(engobj):
                for op in ops:
                    for (kind, peng, pidx) in op["waits"]:
                        if kind == "c":
                            engobj.wait_ge(kb.csem[peng], kb.cval[peng][pidx])
                        else:
                            engobj.wait_ge(kb.dsem[peng], 16 * (pidx + 1))
                    inst = op["fn"](engobj)
                    if op["dma"]:
                        inst.then_inc(kb.dsem[e], 16)
                    elif op.get("inc"):
                        if e == "tensor":
                            inst.then_inc(kb.csem[e], 1)
                        else:
                            engobj.drain(fusable=True).then_inc(kb.csem[e], 1)
            return section

        for si in range(nseg):
            with nc.Block() as block:
                for e in ENGINES:
                    if seg_ops[e][si]:
                        getattr(block, e)(make_section(e, seg_ops[e][si]))
        return None


class Mem:
    BASE = 16512
    LIMIT = 229216 - 16512

    def __init__(self, nc):
        self.nc = nc
        self.tensors = {}

    def at(self, name, offset, shape, dtype):
        assert shape[0] <= 128
        nbytes = int(np.prod(shape[1:])) * mybir.dt.size(dtype)
        assert offset + nbytes <= self.LIMIT, (name, offset, nbytes)
        h = self.nc.alloc_sbuf_tensor_at(name, list(shape), dtype,
                                         offset=self.BASE + offset)
        ap = h.ap()
        self.tensors[name] = (offset, nbytes, ap)
        return ap


"""builder for mk_macaron_d4 (per-core: T=4096 tokens, D=1024).

Layout conventions:
  - feature-major (fm): [feature chunk c (128 partitions), T free]
  - token-major (tm): [token (128 partitions), features free]
  - residual x lives in DRAM scratch (fp32, fm, [1024, 4096]), ping-ponged
    between d_xA / d_xB per sublayer; final sublayer writes ExternalOutput.
  - 8 token windows of N=512 per core.

Host prep (layout only): x transposed to fm; several weights transposed so
their contraction dim is first (see prep_inputs).
"""
import contextlib
import numpy as np

import concourse.bass as bass
import concourse.mybir as mybir

T = 4096
D = 1024
NW = 8
N = T // NW
NC = D // P

LN2 = 0.6931471805599453
PI = 3.141592653589793


def ts(i, n=128):
    return slice(i * n, (i + 1) * n)


class MK:
    def __init__(self, upto=99):
        self.nc = bass.Bass("TRN2", target_bir_lowering=False)
        self.es = contextlib.ExitStack()
        self.kb = KB(self.nc)
        self.mem = Mem(self.nc)
        self.upto = upto
        self.sub_idx = 0

    def sb(self, name, offset, shape, dtype):
        key = name if self.sub_idx == 0 else "%s_u%d" % (name, self.sub_idx)
        ap = self.mem.at(key, offset, shape, dtype)
        nbytes = int(np.prod(shape[1:])) * mybir.dt.size(dtype)
        self._pagemap[key] = (offset, nbytes)
        return ap

    def pages(self, name, lo=None, hi=None):
        key = "%s_u%d" % (name, self.sub_idx)
        if key not in self._pagemap:
            key = name
        off, nb = self._pagemap[key]
        a = off + (lo if lo is not None else 0)
        b = off + (hi if hi is not None else nb)
        return ["sb%d" % p for p in range(a // 1024, (b + 1023) // 1024)]

    def pg(self, *names):
        out = []
        for n in names:
            if isinstance(n, tuple):
                out += self.pages(*n)
            elif ("%s_u%d" % (n, self.sub_idx)) in self._pagemap or n in self._pagemap:
                out += self.pages(n)
            else:
                out.append(n)
        return out

    def setup(self, input_specs):
        nc = self.nc
        self._pagemap = {}
        self.din = {}
        for name, shape, dt in input_specs:
            self.din[name] = nc.dram_tensor(name, list(shape), dt, kind="ExternalInput").ap()
        self.xout = nc.dram_tensor("xout", [D, T], F32, kind="ExternalOutput").ap()
        self.d_xA = nc.dram_tensor("d_xA", [D, T], F32, kind="Internal").ap()
        self.d_xB = nc.dram_tensor("d_xB", [D, T], F32, kind="Internal").ap()
        self.d_q = nc.dram_tensor("d_q", [D, T], BF16, kind="Internal").ap()
        self.d_z = nc.dram_tensor("d_z", [D, T], BF16, kind="Internal").ap()
        self.d_v = nc.dram_tensor("d_v", [T, D], BF16, kind="Internal").ap()
        self.d_cw = nc.dram_tensor("d_cw", [8, T], F32, kind="Internal").ap()

        self.ps = []
        for i in range(8):
            h = nc.alloc_psum_tensor("psb%d" % i, [P, 512], F32)
            self.ps.append(h.ap())

        self.ones_bf = self.sb("ones_bf", 0, [P, 512], BF16)
        self.consts = self.sb("consts", 1024, [P, 16], F32)
        self.C_EPS5 = self.consts[:, 0:1]
        self.C_EPS6 = self.consts[:, 1:2]
        self.C_EPS3 = self.consts[:, 2:3]
        self.C_HALFPI = self.consts[:, 3:4]
        self.C_LNHALF = self.consts[:, 4:5]
        self.C_EPS8 = self.consts[:, 5:6]
        self.C_ZERO = self.consts[:, 6:7]
        kb = self.kb
        kb.emit("gpsimd", lambda g: g.memset(self.ones_bf[:], 1.0), writes=self.pg("ones_bf"))
        for col, val in ((0, 1e-5), (1, 1e-6), (2, 1e-3), (3, PI / 2), (4, -LN2), (5, 1e-8), (6, 0.0)):
            kb.emit("gpsimd", lambda g, c=col, v=val: g.memset(self.consts[:, c:c + 1], v),
                    writes=self.pg("consts"))

        self.ones_f32 = self.sb("ones_f32", 1088, [P, P], F32)
        kb.emit("gpsimd", lambda g: g.memset(self.ones_f32[:], 1.0), writes=self.pg("ones_f32"))
        self.ident_f32 = self.sb("ident_f32", 1600, [P, P], F32)
        kb.dma("sync", self.ident_f32[:], self.din["c_ident"][:], writes=self.pg("ident_f32"))
        self.ident_bf = self.sb("ident_bf", 4928, [P, P], BF16)
        kb.dma("gpsimd", self.ident_bf[:], self.din["c_ident"][:], writes=self.pg("ident_bf"))
        self.stsq = [self.sb("stsq0", 2112, [P, N], BF16), self.sb("stsq1", 3136, [P, N], BF16)]
        self.pptmp = self.sb("pptmp", 4160, [32, P], F32)
        self.PHASE = 5248

    def ln0_window(self, w, xsrc_dram, xw, zb, st, key_src):
        kb, pg = self.kb, self.pg
        wsl = slice(w * N, (w + 1) * N)
        xwk, zbk = st["xw_key"], st["zb_key"]
        kb.dma("sync", xw[:], xsrc_dram.rearrange("(c p) t -> p c t", p=P)[:, :, wsl],
               reads=[key_src + ":%d" % w], writes=pg(xwk), tag="ln_ldx")
        bS, bQ = st["bS"], st["bQ"]
        psS, psQ = self.ps[bS], self.ps[bQ]
        xb, rb = st["xb"], st["rb"]
        xbk, rbk = st["xb_key"], st["rb_key"]
        kb.emit("vector", lambda v: v.tensor_copy(xb.rearrange("p c t -> p (c t)"),
                                                  xw.rearrange("p c t -> p (c t)")),
                reads=pg(xwk), writes=pg(xbk), tag="ln_cast")
        kb.emit("vector", lambda v: v.tensor_sub(rb.rearrange("p c t -> p (c t)"),
                                                 xw.rearrange("p c t -> p (c t)"),
                                                 xb.rearrange("p c t -> p (c t)")),
                reads=pg(xwk, xbk), writes=pg(rbk), tag="ln_resid")
        sq = st["sq"]
        sqk = st["sq_key"]
        ones = self.ones_bf[:, 0:128]
        for c in range(NC):
            p2 = c % 2
            kb.emit("scalar", lambda s, c=c, p2=p2: s.activation(sq[p2][:], xb[:, c, :], AF.Square),
                    reads=pg(xbk), writes=pg(sqk[p2]), tag="ln_sq")
            kb.emit("tensor", lambda t, c=c: t.matmul(psS[:], ones, xb[:, c, :],
                                                      start=(c == 0), stop=False),
                    reads=pg(xbk, "ones_bf"), writes=["PS%d" % bS], tag="ln_mmS")
            kb.emit("tensor", lambda t, c=c: t.matmul(psS[:], ones, rb[:, c, :],
                                                      start=False, stop=(c == NC - 1)),
                    reads=pg(rbk, "ones_bf"), writes=["PS%d" % bS], tag="ln_mmSr")
            kb.emit("tensor", lambda t, c=c, p2=p2: t.matmul(psQ[:], ones, sq[p2][:],
                                                             start=(c == 0), stop=(c == NC - 1)),
                    reads=pg(sqk[p2], "ones_bf"), writes=["PS%d" % bQ], tag="ln_mmQ")
        mean, A, v1 = st["mean"], st["A"], st["v1"]
        meank, Ak, v1k = st["mean_key"], st["A_key"], st["v1_key"]
        kb.emit("vector", lambda v: v.tensor_scalar_mul(mean[:], psS[:], 1.0 / D),
                reads=["PS%d" % bS], writes=pg(meank), tag="ln_mean")
        kb.emit("vector", lambda v: v.tensor_mul(v1[:], mean[:], mean[:]),
                reads=pg(meank), writes=pg(v1k), tag="ln_msq")
        kb.emit("vector", lambda v: v.scalar_tensor_tensor(v1[:], psQ[:], 1.0 / D, v1[:],
                                                           ALU.mult, ALU.subtract),
                reads=["PS%d" % bQ] + pg(v1k), writes=pg(v1k), tag="ln_var")
        kb.emit("scalar", lambda s: s.activation(v1[:], v1[:], AF.Ln, bias=self.C_EPS5),
                reads=pg(v1k, "consts"), writes=pg(v1k), tag="ln_ln")
        kb.emit("scalar", lambda s: s.activation(A[:], v1[:], AF.Exp, scale=-0.5),
                reads=pg(v1k), writes=pg(Ak), tag="ln_exp")
        for c in range(NC):
            kb.emit("vector", lambda v, c=c: v.tensor_sub(st["d"][c % 2][:], xw[:, c, :], mean[:]),
                    reads=pg(xwk, meank), writes=pg(st["d_key"][c % 2]), tag="ln_sub")
            kb.emit("vector", lambda v, c=c: v.tensor_mul(zb[:, c, :], st["d"][c % 2][:], A[:]),
                    reads=pg(st["d_key"][c % 2], Ak), writes=pg(zbk, c * N * 2, (c + 1) * N * 2),
                    tag="ln_mul")

    def x_writeback(self, w, xw, xdst_dram, key_dst, xw_key):
        kb, pg = self.kb, self.pg
        wsl = slice(w * N, (w + 1) * N)
        kb.dma("sync", xdst_dram.rearrange("(c p) t -> p c t", p=P)[:, :, wsl], xw[:],
               reads=pg(xw_key), writes=[key_dst + ":%d" % w], tag="x_wb")

    def next_x(self):
        src = self.x_cur
        self.sub_idx += 1
        last = (self.sub_idx == min(self.upto, 12))
        if last:
            dst = ("xout", self.xout)
        else:
            dst = ("d_xB", self.d_xB) if src[0] == "d_xA" else ("d_xA", self.d_xA)
        self.x_cur = dst
        return src[1], src[0], dst[1], dst[0]

    def prep_trig(self, off):
        kb, pg = self.kb, self.pg
        C_sb = self.sb("trigC", off, [P, NC, D], BF16)
        S_sb = self.sb("trigS", off + 16 * 1024, [P, NC, D], BF16)
        t0 = self.sb("trig_t0", off + 32 * 1024, [P, D], F32)
        t1 = self.sb("trig_t1", off + 36 * 1024, [P, D], F32)
        th = self.din["L0_ffn_theta"].rearrange("(c p) o -> p c o", p=P)
        thn = self.din["L0_ffn_theta_noise"].rearrange("(c p) o -> p c o", p=P)
        ph = self.din["L0_ffn_phi"].rearrange("(c p) o -> p c o", p=P)
        phn = self.din["L0_ffn_phi_noise"].rearrange("(c p) o -> p c o", p=P)
        for c in range(NC):
            kb.dma("sync", t0[:], th[:, c, :], writes=pg("trig_t0"), tag="trig")
            kb.dma("gpsimd", t1[:], thn[:, c, :], writes=pg("trig_t1"), tag="trig")
            kb.emit("vector", lambda v: v.tensor_add(t0[:], t0[:], t1[:]),
                    reads=pg("trig_t0", "trig_t1"), writes=pg("trig_t0"))
            kb.emit("scalar", lambda s, c=c: s.activation(C_sb[:, c, :], t0[:], AF.Sin,
                                                          bias=self.C_HALFPI),
                    reads=pg("trig_t0", "consts"), writes=pg("trigC", c * 2 * D, (c + 1) * 2 * D))
            kb.dma("sync", t0[:], ph[:, c, :], writes=pg("trig_t0"), tag="trig")
            kb.dma("gpsimd", t1[:], phn[:, c, :], writes=pg("trig_t1"), tag="trig")
            kb.emit("vector", lambda v: v.tensor_add(t0[:], t0[:], t1[:]),
                    reads=pg("trig_t0", "trig_t1"), writes=pg("trig_t0"))
            kb.emit("scalar", lambda s, c=c: s.activation(S_sb[:, c, :], t0[:], AF.Sin),
                    reads=pg("trig_t0"), writes=pg("trigS", c * 2 * D, (c + 1) * 2 * D))
        return C_sb, S_sb

    def make_ln_state(self, base, par):
        s = {}
        name = lambda n: "%s_p%d_s%d" % (n, par, base)
        s["xw"] = self.sb(name("xw"), base, [P, NC, N], F32); s["xw_key"] = name("xw")
        s["zb"] = self.sb(name("zb"), base + 16384, [P, NC, N], BF16); s["zb_key"] = name("zb")
        s["xb"] = self.sb(name("xb"), base + 24576, [P, NC, N], BF16); s["xb_key"] = name("xb")
        s["rb"] = self.sb(name("rb"), base + 32768, [P, NC, N], BF16); s["rb_key"] = name("rb")
        sq0 = self.sb(name("sq0"), base + 40960, [P, N], BF16)
        sq1 = self.sb(name("sq1"), base + 41984, [P, N], BF16)
        s["sq"] = [sq0, sq1]; s["sq_key"] = [name("sq0"), name("sq1")]
        s["mean"] = self.sb(name("mean"), base + 43008, [P, N], F32); s["mean_key"] = name("mean")
        s["A"] = self.sb(name("A"), base + 45056, [P, N], F32); s["A_key"] = name("A")
        s["v1"] = self.sb(name("v1"), base + 47104, [P, N], F32); s["v1_key"] = name("v1")
        d0 = self.sb(name("d0"), base + 49152, [P, N], F32)
        d1 = self.sb(name("d1"), base + 51200, [P, N], F32)
        s["d"] = [d0, d1]; s["d_key"] = [name("d0"), name("d1")]
        return s, 53248

    def sub_l0ffn(self, C_sb, S_sb):
        kb, pg = self.kb, self.pg
        xsrc, ksrc, xdst, kdst = self.next_x()
        base = self.PHASE + 48 * 1024
        sts = []
        for par in range(2):
            st, sz = self.make_ln_state(base, par)
            st["bS"], st["bQ"] = 6, 7
            sts.append(st)
            base += sz
        ffn = []
        for par in range(2):
            nm = lambda n: "ff_%s_p%d" % (n, par)
            r2 = self.sb(nm("r2"), base, [P, N], F32)
            i2 = self.sb(nm("i2"), base + 2048, [P, N], F32)
            ffn.append(dict(r2=r2, i2=i2, r2k=nm("r2"), i2k=nm("i2")))
            base += 4096
        for w in range(NW):
            self.kb.seg()
            st = sts[w % 2]
            f = ffn[w % 2]
            xw, zb = st["xw"], st["zb"]
            self.ln0_window(w, xsrc, xw, zb, st, ksrc)
            for oc in range(NC):
                bR = (oc % 2) * 2
                bI = bR + 1
                psR, psI = self.ps[bR], self.ps[bI]
                for kc in range(NC):
                    kb.emit("tensor", lambda t, oc=oc, kc=kc, psR=psR, zb=zb:
                            t.matmul(psR[:], C_sb[:, kc, ts(oc)], zb[:, kc, :],
                                     start=(kc == 0), stop=(kc == NC - 1)),
                            reads=pg("trigC", st["zb_key"]),
                            writes=["PS%d" % bR], tag="f0_mmR")
                    kb.emit("tensor", lambda t, oc=oc, kc=kc, psI=psI, zb=zb:
                            t.matmul(psI[:], S_sb[:, kc, ts(oc)], zb[:, kc, :],
                                     start=(kc == 0), stop=(kc == NC - 1)),
                            reads=pg("trigS", st["zb_key"]),
                            writes=["PS%d" % bI], tag="f0_mmI")
                kb.emit("scalar", lambda s, psR=psR, f=f: s.activation(f["r2"][:], psR[:], AF.Square),
                        reads=["PS%d" % bR], writes=pg(f["r2k"]), tag="f0_sqR")
                kb.emit("scalar", lambda s, psI=psI, f=f: s.activation(f["i2"][:], psI[:], AF.Square),
                        reads=["PS%d" % bI], writes=pg(f["i2k"]), tag="f0_sqI")
                kb.emit("vector", lambda v, f=f: v.tensor_add(f["r2"][:], f["r2"][:], f["i2"][:]),
                        reads=pg(f["r2k"], f["i2k"]), writes=pg(f["r2k"]), tag="f0_add")
                kb.emit("scalar", lambda s, f=f: s.activation(f["i2"][:], f["r2"][:], AF.Ln),
                        reads=pg(f["r2k"]), writes=pg(f["i2k"]), tag="f0_ln")
                kb.emit("scalar", lambda s, f=f: s.activation(f["r2"][:], f["i2"][:], AF.Exp,
                                                              scale=0.5, bias=self.C_LNHALF),
                        reads=pg(f["i2k"], "consts"), writes=pg(f["r2k"]), tag="f0_exp")
                kb.emit("vector", lambda v, oc=oc, xw=xw, f=f:
                        v.tensor_add(xw[:, oc, :], xw[:, oc, :], f["r2"][:]),
                        reads=pg(f["r2k"]) + pg(st["xw_key"]), writes=pg(st["xw_key"]), tag="f0_res")
            self.x_writeback(w, xw, xdst, kdst, st["xw_key"])

    def ln_all_to_dram(self, xsrc, ksrc, dst_dram, kdst, also_resid=None):
        base = self.PHASE
        sts = []
        for par in range(2):
            st, sz = self.make_ln_state(base, par)
            st["bS"], st["bQ"] = 6, 7
            sts.append(st)
            base += sz
        kb, pg = self.kb, self.pg
        for w in range(NW):
            kb.seg()
            st = sts[w % 2]
            self.ln0_window(w, xsrc, st["xw"], st["zb"], st, ksrc)
            wsl = slice(w * N, (w + 1) * N)
            kb.dma("sync", dst_dram.rearrange("(c p) t -> p c t", p=P)[:, :, wsl],
                   st["zb"][:], reads=pg(st["zb_key"]), writes=[kdst + ":%d" % w],
                   tag="z_spill")
        return sts

    def load_weight_bf16(self, name_sb, off, src_ap, shape, q="gpsimd"):
        sb = self.sb(name_sb, off, [P, shape[0] // P, shape[1]], BF16)
        self.kb.dma(q, sb[:], src_ap.rearrange("(c p) m -> p c m", p=P),
                    writes=self.pg(name_sb), tag="wload")
        return sb

    def proj_fm(self, zwin, zkey, w_sb, wkey, out_sb, outkey, bank2, kcs=NC,
                ocs=NC, bias_row=None, bias_key=None, tag="proj"):
        kb, pg = self.kb, self.pg
        for oc in range(ocs):
            b = bank2[oc % 2]
            ps = self.ps[b]
            for kc in range(kcs):
                kb.emit("tensor", lambda t, oc=oc, kc=kc, ps=ps, w_sb=w_sb, zwin=zwin:
                        t.matmul(ps[:], w_sb[:, kc, ts(oc)], zwin[:, kc, :],
                                 start=(kc == 0), stop=(kc == kcs - 1 and bias_row is None)),
                        reads=pg(wkey, zkey), writes=["PS%d" % b], tag=tag)
            if bias_row is not None:
                kb.emit("tensor", lambda t, oc=oc, ps=ps, bias_row=bias_row:
                        t.matmul(ps[:], bias_row[:, ts(oc)], self.ones_bf[0:1, 0:N],
                                 start=False, stop=True),
                        reads=pg(bias_key, "ones_bf"), writes=["PS%d" % b], tag=tag + "_b")
            kb.emit("scalar", lambda s, oc=oc, ps=ps, out_sb=out_sb:
                    s.activation(out_sb[:, oc, :], ps[:], AF.Copy),
                    reads=["PS%d" % b], writes=pg((outkey, oc * N * 2, (oc + 1) * N * 2)),
                    tag=tag + "_ev")

    def proj_tm(self, zwin, zkey, w_sb, wkey, w, out_sb, outkey, bank2, kcs=NC,
                mfree=D, bias_row=None, bias_key=None, tag="projtm"):
        kb, pg = self.kb, self.pg
        nh = mfree // 512
        for tt in range(4):
            gt = w * 4 + tt
            for mh in range(nh):
                b = bank2[(tt * nh + mh) % 2]
                ps = self.ps[b]
                for kc in range(kcs):
                    kb.emit("tensor", lambda t, tt=tt, mh=mh, kc=kc, ps=ps, zwin=zwin, w_sb=w_sb:
                            t.matmul(ps[:], zwin[:, kc, ts(tt)], w_sb[:, kc, ts(mh, 512)],
                                     start=(kc == 0), stop=(kc == kcs - 1 and bias_row is None)),
                            reads=pg(wkey, zkey), writes=["PS%d" % b], tag=tag)
                if bias_row is not None:
                    kb.emit("tensor", lambda t, mh=mh, ps=ps, bias_row=bias_row:
                            t.matmul(ps[:], self.ones_bf[0:1, 0:128], bias_row[:, ts(mh, 512)],
                                     start=False, stop=True),
                            reads=pg(bias_key, "ones_bf"), writes=["PS%d" % b], tag=tag + "_b")
                kb.emit("scalar", lambda s, gt=gt, mh=mh, ps=ps, out_sb=out_sb:
                        s.activation(out_sb[:, gt, ts(mh, 512)], ps[:], AF.Copy),
                        reads=["PS%d" % b],
                        writes=pg((outkey, gt * mfree * 2, (gt + 1) * mfree * 2)),
                        tag=tag + "_ev")

    def load_row_bf16(self, name, off, src_ap, length, q="gpsimd"):
        sb = self.sb(name, off, [1, length], BF16)
        if src_ap.ndim == 1:
            src_ap = src_ap.unsqueeze(0)
        self.kb.dma(q, sb[:], src_ap, writes=self.pg(name), tag="rowload")
        return sb

    def load_pp_vec(self, name, off, src_ap, length, bank=5):
        ncs = length // P
        tmp = self.pptmp[0:ncs, :]
        sb = self.sb(name, off, [P, ncs], F32)
        kb, pg = self.kb, self.pg
        kb.dma("sync", tmp[:], src_ap.rearrange("(c p) -> c p", p=P),
               writes=pg("pptmp"), tag="ppload")
        ps = self.ps[bank]
        kb.emit("tensor", lambda t, tmp=tmp, ncs=ncs: t.transpose(ps[0:P, 0:ncs], tmp[:], self.ident_f32[0:ncs, 0:ncs]),
                reads=pg("pptmp", "ident_f32"), writes=["PS%d" % bank], tag="pptr")
        kb.emit("vector", lambda v, sb=sb: v.tensor_copy(sb[:], ps[0:P, 0:ncs]),
                reads=["PS%d" % bank], writes=pg(name), tag="ppev")
        return sb

    def attn_core(self, w, Qdram, K_sb, Kkey, negB, negBkey, kmax2,
                  kmax2key, scale, sgroups, Ow_sb, Owkey, a):
        kb, pg = self.kb, self.pg
        wsl = slice(w * N, (w + 1) * N)
        qw, qwkey = a["qw"], a["qwk"]
        kb.dma("sync", qw[:], Qdram.rearrange("(c p) t -> p c t", p=P)[:, :, wsl],
               reads=["d_q:%d" % w], writes=pg(qwkey), tag="qload")
        sqq, sqqk = a["sqq"], a["sqqk"]
        bq = a["bank_q"]
        psq = self.ps[bq]
        bgroups = a["bgroups"]
        for gi, chunks in enumerate(bgroups):
            for j, c in enumerate(chunks):
                p2 = j % 2
                kb.emit("scalar", lambda s, c=c, p2=p2, qw=qw: s.activation(sqq[p2][:], qw[:, c, :], AF.Square),
                        reads=pg(qwkey), writes=pg(sqqk[p2]), tag="qsq")
                kb.emit("tensor", lambda t, j=j, p2=p2, nchunks=len(chunks):
                        t.matmul(psq[:], self.ones_bf[:, 0:128], sqq[p2][:],
                                 start=(j == 0), stop=(j == nchunks - 1)),
                        reads=pg(sqqk[p2], "ones_bf"), writes=["PS%d" % bq], tag="qsqmm")
            kb.emit("vector", lambda v, gi=gi: v.tensor_scalar(
                        negB[0:1, gi, :], psq[0:1, :], kmax2[0:1, gi:gi + 1], -0.5,
                        ALU.add, ALU.mult),
                    reads=["PS%d" % bq] + pg(kmax2key),
                    writes=pg((negBkey, gi * N * 2, (gi + 1) * N * 2)), tag="negB")
        PT, PTkey = a["PT"], a["PTk"]
        rl, rlkey = a["rl"], a["rlk"]
        vsl, vslk = a["vsl"], a["vslk"]
        bS0, bS1, bL, bO0, bO1 = a["bank_S0"], a["bank_S1"], a["bank_l"], a["bank_O0"], a["bank_O1"]
        vseq = 0
        ptn = a["ptn"]
        nhalf = 32 // ptn
        def load_v(og):
            nonlocal vseq
            vlo, vhi = og["vcols"]
            m = vhi - vlo
            vb = vseq % 2
            vt = vsl[vb]
            kb.dma("gpsimd", vt[:, :, 0:m],
                   self.d_v.rearrange("(k p) d -> p k d", p=P)[:, :, vlo:vhi],
                   reads=["d_v:%d" % ww for ww in range(NW)],
                   writes=pg(vslk[vb]), tag="vload")
            vseq += 1
            return (vt, vslk[vb], m)

        for sgi, sg in enumerate(sgroups):
            vts = [None] * len(sg["outs"])
            if len(sg["outs"]) == 1:
                vts[0] = load_v(sg["outs"][0])
            psL = self.ps[bL]
            for h in range(nhalf):
                for kt16 in range(ptn):
                    kt = h * ptn + kt16
                    bS = (bS0, bS1)[kt16 % 2]
                    psS = self.ps[bS]
                    if sg.get("kcs"):
                        for j, kc in enumerate(sg["kcs"]):
                            kb.emit("tensor", lambda t, kt=kt, kc=kc, j=j, psS=psS, K_sb=K_sb:
                                    t.matmul(psS[:], K_sb[:, kc, ts(kt)], qw[:, kc, :],
                                             start=(j == 0), stop=False),
                                    reads=pg(Kkey, qwkey), writes=["PS%d" % bS], tag="smm")
                    else:
                        lo, hi = sg["kpart"]
                        kc = sg["kc"]
                        kb.emit("tensor", lambda t, kt=kt, kc=kc, lo=lo, hi=hi, psS=psS, K_sb=K_sb, qw=qw:
                                t.matmul(psS[:], K_sb[lo:hi, kc, ts(kt)], qw[lo:hi, kc, :],
                                         start=True, stop=False),
                                reads=pg(Kkey, qwkey), writes=["PS%d" % bS], tag="smm")
                    gi = sg["bound_idx"]
                    kb.emit("tensor", lambda t, gi=gi, psS=psS:
                            t.matmul(psS[:], self.ones_bf[0:1, 0:128], negB[0:1, gi, :],
                                     start=False, stop=True),
                            reads=pg(negBkey, "ones_bf"), writes=["PS%d" % bS], tag="srk1")
                    kb.emit("scalar", lambda s, kt16=kt16, psS=psS, scale=scale:
                            s.activation(PT[:, kt16, :], psS[:], AF.Exp, scale=scale),
                            reads=["PS%d" % bS],
                            writes=pg((PTkey, kt16 * N * 2, (kt16 + 1) * N * 2)),
                            tag="sexp")
                for kt16 in range(ptn):
                    kt = h * ptn + kt16
                    kb.emit("tensor", lambda t, kt16=kt16, kt=kt, psL=psL:
                            t.matmul(psL[:], self.ones_bf[:, 0:128], PT[:, kt16, :],
                                     start=(kt == 0), stop=(kt == 31)),
                            reads=pg(PTkey, "ones_bf"), writes=["PS%d" % bL], tag="lmm")
                last = (h == nhalf - 1)
                if last:
                    kb.emit("scalar", lambda s: s.activation(rl[:], psL[:], AF.Ln),
                            reads=["PS%d" % bL], writes=pg(rlkey), tag="lln")
                    kb.emit("scalar", lambda s: s.activation(rl[:], rl[:], AF.Exp, scale=-1.0),
                            reads=pg(rlkey), writes=pg(rlkey), tag="lexp")
                for ogi, og in enumerate(sg["outs"]):
                    if vts[ogi] is None:
                        vts[ogi] = load_v(og)
                    vt, vtk, m = vts[ogi]
                    bO = (bO0, bO1)[ogi % 2]
                    psO = self.ps[bO]
                    for kt16 in range(ptn):
                        kt = h * ptn + kt16
                        kb.emit("tensor", lambda t, kt16=kt16, kt=kt, psO=psO, vt=vt, m=m, vtk=vtk:
                                t.matmul(psO[0:m, :], vt[:, kt, 0:m], PT[:, kt16, :],
                                         start=(kt == 0), stop=(kt == 31)),
                                reads=pg(vtk, PTkey), writes=["PS%d" % bO], tag="pvmm")
                    if last:
                        dst_plo, dst_c = og["dst"]
                        kb.emit("vector", lambda v, m=m, dst_plo=dst_plo, dst_c=dst_c, psO=psO, Ow_sb=Ow_sb:
                                v.tensor_mul(Ow_sb[dst_plo:dst_plo + m, dst_c, :], psO[0:m, :],
                                             rl[0:m, :]),
                                reads=["PS%d" % bO] + pg(rlkey),
                                writes=pg((Owkey, dst_c * N * 2, (dst_c + 1) * N * 2)), tag="pvev")

    def knorm_window(self, w, src_sb, srckey, groups, kmax2, kmax2key, a, tag="kn"):
        kb, pg = self.kb, self.pg
        wsl = slice(w * N, (w + 1) * N)
        sqq, sqqk = a["sqq"], a["sqqk"]
        bq = a["bank_q"]
        psq = self.ps[bq]
        for gi, chunks in enumerate(groups):
            for j, c in enumerate(chunks):
                p2 = j % 2
                kb.emit("scalar", lambda s, c=c, p2=p2: s.activation(sqq[p2][:], src_sb[:, c, wsl], AF.Square),
                        reads=pg((srckey,)), writes=pg(sqqk[p2]), tag=tag + "sq")
                kb.emit("tensor", lambda t, j=j, p2=p2, nch=len(chunks):
                        t.matmul(psq[:], self.ones_bf[:, 0:128], sqq[p2][:],
                                 start=(j == 0), stop=(j == nch - 1)),
                        reads=pg(sqqk[p2], "ones_bf"), writes=["PS%d" % bq], tag=tag + "mm")
            kb.emit("vector", lambda v: v.reduce_max(a["kred"][:], psq[0:1, :], AX.X),
                    reads=["PS%d" % bq], writes=pg(a["kredk"]), tag=tag + "red")
            kb.emit("vector", lambda v, gi=gi: v.tensor_max(kmax2[0:1, gi:gi + 1],
                                                            kmax2[0:1, gi:gi + 1], a["kred"][:]),
                    reads=pg(a["kredk"], kmax2key), writes=pg(kmax2key), tag=tag + "max")

    def attn_scratch(self, base, ptn):
        nm = lambda n: "at_%s" % n
        a = {}
        a["qw"] = self.sb(nm("qw"), base, [P, NC, N], BF16); a["qwk"] = nm("qw")
        s0 = self.sb(nm("sqq0"), base + 8192, [P, N], BF16)
        s1 = self.sb(nm("sqq1"), base + 9216, [P, N], BF16)
        a["sqq"] = [s0, s1]; a["sqqk"] = [nm("sqq0"), nm("sqq1")]
        a["PT"] = self.sb(nm("PT"), base + 10240, [P, ptn, N], BF16); a["PTk"] = nm("PT")
        a["ptn"] = ptn
        o = base + 10240 + ptn * 1024
        a["rl"] = self.sb(nm("rl"), o, [P, N], F32); a["rlk"] = nm("rl")
        a["kred"] = self.sb(nm("kred"), o + 2048, [1, 1], F32); a["kredk"] = nm("kred")
        a["vsl"] = [self.sb(nm("vsl0"), o + 3072, [P, 32, P], BF16),
                    self.sb(nm("vsl1"), o + 11264, [P, 32, P], BF16)]
        a["vslk"] = [nm("vsl0"), nm("vsl1")]
        a.update(bank_q=6, bank_S0=7, bank_S1=4, bank_l=5, bank_O0=0, bank_O1=1)
        return a, o + 19456

    def sub_attn_generic(self, kind):
        kb, pg = self.kb, self.pg
        xsrc, ksrc, xdst, kdst = self.next_x()
        self.ln_all_to_dram(xsrc, ksrc, self.d_z, "d_z")
        kb.seg()
        B0 = self.PHASE
        WQ_OFF, WK_OFF, WV_OFF = B0, B0 + 16384, B0 + 32768
        K_OFF = B0 + 49152
        ZS_OFF = K_OFF + 65536
        QS_OFF = ZS_OFF + 16384
        VS_OFF = QS_OFF + 8192
        NRM_OFF = VS_OFF + 8192
        if kind == 'A0':
            wq = self.load_weight_bf16("wqT", WQ_OFF, self.din["h_L0_WqT"], [D, D])
            wk = self.load_weight_bf16("wkT", WK_OFF, self.din["h_L0_WkT"], [D, D])
            wv = self.load_weight_bf16("wvT", WV_OFF, self.din["h_L0_WvT"], [D, D])
            qb = kbias = vbias = None
            scale = 1.0 / 32.0
        elif kind == 'A1':
            wq = self.load_weight_bf16("wqT", WQ_OFF, self.din["L1_attn_Wq"].rearrange("d h k -> d (h k)"), [D, D])
            wk = self.load_weight_bf16("wkT", WK_OFF, self.din["L1_attn_Wk"].rearrange("d h k -> d (h k)"), [D, D])
            wv = self.load_weight_bf16("wvT", WV_OFF, self.din["L1_attn_Wv"].rearrange("d h k -> d (h k)"), [D, D])
            qb = self.load_row_bf16("qb_row", NRM_OFF, self.din["L1_attn_bq"].rearrange("h k -> (h k)"), D)
            kbias = self.load_row_bf16("kb_row", NRM_OFF + 2048, self.din["L1_attn_bk"].rearrange("h k -> (h k)"), D)
            vbias = self.load_row_bf16("vb_row", NRM_OFF + 4096, self.din["L1_attn_bv"].rearrange("h k -> (h k)"), D)
            scale = 1.0 / 8.0
        else:
            wq = self.load_weight_bf16("wqT", WQ_OFF, self.din["h_L1f_qwT"], [D, D])
            wk = self.load_weight_bf16("wkT", WK_OFF, self.din["h_L1f_kwT"], [D, D])
            wv = self.load_weight_bf16("wvT", WV_OFF, self.din["h_L1f_vwT"], [D, D])
            qb = self.load_row_bf16("qb_row", NRM_OFF, self.din["L1_ffn_q_b"], D)
            kbias = self.load_row_bf16("kb_row", NRM_OFF + 2048, self.din["L1_ffn_k_b"], D)
            vbias = self.load_row_bf16("vb_row", NRM_OFF + 4096, self.din["L1_ffn_v_b"], D)
            scale = 1.0 / 32.0
        K_sb = self.sb("K_fm", K_OFF, [P, NC, T], BF16)
        zs = [self.sb("zs0", ZS_OFF, [P, NC, N], BF16), self.sb("zs1", ZS_OFF + 8192, [P, NC, N], BF16)]
        zsk = ["zs0", "zs1"]
        qstage = self.sb("qstage", QS_OFF, [P, NC, N], BF16)
        vstage = self.sb("vstage", VS_OFF, [P, 4, D], BF16)
        kmax2 = self.sb("kmax2", 209632, [1, 8], F32)
        kb.emit("gpsimd", lambda g: g.memset(kmax2[:], 0.0), writes=pg("kmax2"))
        if kind == 'F1':
            f1m = self.sb("f1m", NRM_OFF + 8192, [P, N], F32)
            f1a = self.sb("f1a", NRM_OFF + 10240, [P, N], F32)
            f1d = self.sb("f1d", NRM_OFF + 12288, [P, N], F32)
            f1nw = self.load_pp_vec("f1nw", NRM_OFF + 14336, self.din["L1_ffn_norm_w"], D)
            f1nb = self.load_pp_vec("f1nb", NRM_OFF + 14464, self.din["L1_ffn_norm_b"], D)
        sqq_a = {"sqq": self.stsq, "sqqk": ["stsq0", "stsq1"], "bank_q": 6,
                 "kred": self.sb("p1kred", NRM_OFF + 14720, [1, 1], F32), "kredk": "p1kred"}
        for w in range(NW):
            kb.seg()
            wsl = slice(w * N, (w + 1) * N)
            par = w % 2
            zw, zwk = zs[par], zsk[par]
            kb.dma("sync", zw[:], self.d_z.rearrange("(c p) t -> p c t", p=P)[:, :, wsl],
                   reads=["d_z:%d" % w], writes=pg(zwk), tag="zload")
            kcs = NC
            if kind == 'F1':
                self.stats_of(zw, zwk, f1m, "f1m", f1a, "f1a", bS=7, bQ=5)
                for c in range(NC):
                    kb.emit("vector", lambda v, c=c, zw=zw: v.tensor_sub(f1d[:], zw[:, c, :], f1m[:]),
                            reads=pg(zwk, "f1m"), writes=pg("f1d"), tag="f1sub")
                    kb.emit("vector", lambda v: v.tensor_mul(f1d[:], f1d[:], f1a[:]),
                            reads=pg("f1d", "f1a"), writes=pg("f1d"), tag="f1mul")
                    kb.emit("vector", lambda v, c=c, zw=zw: v.tensor_scalar(
                                zw[:, c, :], f1d[:], f1nw[:, c:c + 1], f1nb[:, c:c + 1],
                                ALU.mult, ALU.add),
                            reads=pg("f1d", "f1nw", "f1nb"), writes=pg(zwk), tag="f1ts")
                kcs = max(1, NC >> (w // 2))
            self.proj_fm(zw, zwk, wq, "wqT", qstage, "qstage", (0, 1), kcs=kcs,
                         bias_row=qb, bias_key=("qb_row" if qb is not None else None), tag="qp")
            kb.dma("sync", self.d_q.rearrange("(c p) t -> p c t", p=P)[:, :, wsl],
                   qstage[:], reads=pg("qstage"), writes=["d_q:%d" % w], tag="qspill")
            for oc in range(NC):
                b = (2, 3)[oc % 2]
                ps = self.ps[b]
                for kc in range(kcs):
                    kb.emit("tensor", lambda t, oc=oc, kc=kc, ps=ps, zw=zw, wk=wk, kcs=kcs:
                            t.matmul(ps[:], wk[:, kc, ts(oc)], zw[:, kc, :],
                                     start=(kc == 0), stop=(kbias is None and kc == kcs - 1)),
                            reads=pg("wkT", zwk), writes=["PS%d" % b], tag="kp")
                if kbias is not None:
                    kb.emit("tensor", lambda t, oc=oc, ps=ps, kbias=kbias:
                            t.matmul(ps[:], kbias[:, ts(oc)], self.ones_bf[0:1, 0:N],
                                     start=False, stop=True),
                            reads=pg("kb_row", "ones_bf"), writes=["PS%d" % b], tag="kpb")
                kb.emit("scalar", lambda s, oc=oc, ps=ps, wsl=wsl, K_sb=K_sb:
                        s.activation(K_sb[:, oc, wsl], ps[:], AF.Copy),
                        reads=["PS%d" % b], writes=pg(("K_fm", oc * T * 2 + w * N * 2,
                                                       oc * T * 2 + (w + 1) * N * 2)), tag="kev")
            self.proj_tm(zw, zwk, wv, "wvT", 0, vstage, "vstage", (2, 3), kcs=kcs,
                         bias_row=vbias, bias_key=("vb_row" if vbias is not None else None), tag="vp")
            kb.dma("sync", self.d_v.rearrange("(g p) m -> p g m", p=P)[:, ts(w, 4), :],
                   vstage[:], reads=pg("vstage"), writes=["d_v:%d" % w], tag="vspill")
            bgroups = [list(range(NC))] if kind == 'A0' else [[c] for c in range(NC)]
            self.knorm_window(w, K_sb, "K_fm", bgroups, kmax2, "kmax2", sqq_a)
        kb.seg()
        ptn = 32 if kind == 'A0' else 16
        a, aend = self.attn_scratch(K_OFF + 65536, ptn)
        a["bgroups"] = [list(range(NC))] if kind == 'A0' else [[c] for c in range(NC)]
        negB = self.sb("negB", 200416, [1, NC, N], BF16)
        assert aend <= 200416, aend
        WO_OFF = B0
        if kind == 'A0':
            wo = self.load_weight_bf16("woT", WO_OFF, self.din["h_L0_WoT"], [D, D])
            ob = None
        elif kind == 'A1':
            wo = self.load_weight_bf16("woT", WO_OFF, self.din["L1_attn_Wo"].rearrange("h k d -> (h k) d"), [D, D])
            ob = self.load_row_bf16("ob_row", WO_OFF + 16384, self.din["L1_attn_bo"], D)
        else:
            wo = self.load_weight_bf16("woT", WO_OFF, self.din["h_L1f_owT"], [D, D])
            ob = self.load_row_bf16("ob_row", WO_OFF + 16384, self.din["L1_ffn_o_b"], D)
        OW_OFF = WO_OFF + 18432
        Ow = self.sb("Ow", OW_OFF, [P, NC, N], BF16)
        xw3 = self.sb("xw3", OW_OFF + 8192, [P, NC, N], F32)
        assert OW_OFF + 24576 <= K_OFF, OW_OFF
        if kind != 'A0':
            E3 = aend
            z3 = self.sb("z3", E3, [P, NC, N], BF16)
            u3 = self.sb("u3", E3 + 8192, [P, NC, N], BF16)
            t3 = self.sb("t3", E3 + 16384, [P, N], F32)
            m3 = self.sb("m3", E3 + 18432, [P, N], F32)
            a3 = self.sb("a3", E3 + 20480, [P, N], F32)
            assert E3 + 22528 + 1024 <= 200416, E3
        if kind == 'A1':
            gam = self.load_pp_vec("gam", E3 + 22528, self.din["L1_attn_gamma"], D)
            bet = self.load_pp_vec("bet", E3 + 22528 + 128, self.din["L1_attn_beta"], D)
        if kind == 'A0':
            sgroups = [dict(kcs=list(range(NC)), bound_idx=0,
                            outs=[dict(vcols=(dc * P, (dc + 1) * P), dst=(0, dc)) for dc in range(NC)])]
        elif kind == 'A1':
            sgroups = []
            for c in range(NC):
                for j in range(2):
                    sgroups.append(dict(kpart=(j * 64, (j + 1) * 64), kc=c, bound_idx=c,
                                        outs=[dict(vcols=(c * P + j * 64, c * P + j * 64 + 64),
                                                   dst=(j * 64, c))]))
        else:
            sgroups = [dict(kpart=(0, P), kc=c, bound_idx=c,
                            outs=[dict(vcols=(c * P, (c + 1) * P), dst=(0, c))]) for c in range(NC)]
        for w in range(NW):
            kb.seg()
            wsl = slice(w * N, (w + 1) * N)
            self.attn_core(w, self.d_q, K_sb, "K_fm", negB, "negB",
                           kmax2, "kmax2", scale, sgroups, Ow, "Ow", a)
            if getattr(self, "dbg_ow", None) is not None:
                kb.dma("sync", self.dbg_ow.rearrange("(c p) t -> p c t", p=P)[:, :, wsl],
                       Ow[:], reads=pg("Ow"), writes=["dbgow:%d" % w], tag="dbgow")
            kb.seg()
            kb.dma("scalar", xw3[:], xsrc.rearrange("(c p) t -> p c t", p=P)[:, :, wsl],
                   reads=[ksrc + ":%d" % w], writes=pg("xw3"), tag="p3x")
            if kind != 'A0':
                kb.dma("scalar", z3[:], self.d_z.rearrange("(c p) t -> p c t", p=P)[:, :, wsl],
                       reads=["d_z:%d" % w], writes=pg("z3"), tag="p3z")
            mcs = NC if kind != 'F1' else max(1, NC >> (w // 2))
            for oc in range(NC):
                if kind == 'F1' and oc >= mcs:
                    kb.emit("vector", lambda v, oc=oc: v.scalar_tensor_tensor(
                                xw3[:, oc, :], z3[:, oc, :], 0.5, xw3[:, oc, :],
                                ALU.mult, ALU.add),
                            reads=pg("z3", "xw3"), writes=pg("xw3"), tag="f1pad")
                    continue
                b = (2, 3)[oc % 2]
                ps = self.ps[b]
                okcs = NC if kind != 'F1' else mcs
                for kc in range(okcs):
                    kb.emit("tensor", lambda t, oc=oc, kc=kc, ps=ps, wo=wo, okcs=okcs:
                            t.matmul(ps[:], wo[:, kc, ts(oc)], Ow[:, kc, :],
                                     start=(kc == 0), stop=(ob is None and kc == okcs - 1)),
                            reads=pg("woT", "Ow"), writes=["PS%d" % b], tag="op")
                if ob is not None:
                    kb.emit("tensor", lambda t, oc=oc, ps=ps, ob=ob:
                            t.matmul(ps[:], ob[:, ts(oc)], self.ones_bf[0:1, 0:N],
                                     start=False, stop=True),
                            reads=pg("ob_row", "ones_bf"), writes=["PS%d" % b], tag="opb")
                if kind == 'A0':
                    kb.emit("vector", lambda v, oc=oc, ps=ps: v.tensor_add(
                                xw3[:, oc, :], xw3[:, oc, :], ps[:]),
                            reads=["PS%d" % b] + pg("xw3"), writes=pg("xw3"), tag="a0res")
                elif kind == 'A1':
                    kb.emit("vector", lambda v, oc=oc, ps=ps: v.tensor_add(
                                u3[:, oc, :], z3[:, oc, :], ps[:]),
                            reads=["PS%d" % b] + pg("z3"),
                            writes=pg(("u3", oc * N * 2, (oc + 1) * N * 2)), tag="a1w")
                else:
                    kb.emit("vector", lambda v, oc=oc, ps=ps: v.scalar_tensor_tensor(
                                xw3[:, oc, :], ps[:], 0.5, xw3[:, oc, :], ALU.mult, ALU.add),
                            reads=["PS%d" % b] + pg("xw3"), writes=pg("xw3"), tag="f1r1")
                    kb.emit("vector", lambda v, oc=oc: v.scalar_tensor_tensor(
                                xw3[:, oc, :], z3[:, oc, :], 0.5, xw3[:, oc, :],
                                ALU.mult, ALU.add),
                            reads=pg("z3", "xw3"), writes=pg("xw3"), tag="f1r2")
            if kind == 'A1':
                self.stats_of(u3, "u3", m3, "m3", a3, "a3", bS=6, bQ=5, eps=1e-3)
                for oc in range(NC):
                    kb.emit("vector", lambda v, oc=oc: v.tensor_sub(t3[:], u3[:, oc, :], m3[:]),
                            reads=pg("u3", "m3"), writes=pg("t3"), tag="a1s")
                    kb.emit("vector", lambda v: v.tensor_mul(t3[:], t3[:], a3[:]),
                            reads=pg("t3", "a3"), writes=pg("t3"), tag="a1m")
                    kb.emit("vector", lambda v, oc=oc: v.tensor_scalar(
                                t3[:], t3[:], gam[:, oc:oc + 1], bet[:, oc:oc + 1],
                                ALU.mult, ALU.add),
                            reads=pg("t3", "gam", "bet"), writes=pg("t3"), tag="a1g")
                    kb.emit("vector", lambda v, oc=oc: v.tensor_add(
                                xw3[:, oc, :], xw3[:, oc, :], t3[:]),
                            reads=pg("t3", "xw3"), writes=pg("xw3"), tag="a1r")
            self.x_writeback(w, xw3, xdst, kdst, "xw3")

    def sub_l2attn(self):
        kb, pg = self.kb, self.pg
        xsrc, ksrc, xdst, kdst = self.next_x()
        self.ln_all_to_dram(xsrc, ksrc, self.d_z, "d_z")
        kb.seg()
        B0 = self.PHASE
        Z_OFF = B0
        W1_OFF = B0 + 65536
        W2_OFF = W1_OFF + 16384
        VC_OFF = W2_OFF + 16384
        ST_OFF = VC_OFF + 16384
        RET_OFF = ST_OFF + 16384
        SM_OFF = 201856
        assert RET_OFF + 65536 <= SM_OFF, RET_OFF
        z_sb = self.sb("z2r", Z_OFF, [P, NC, T], BF16)
        for w in range(NW):
            wsl = slice(w * N, (w + 1) * N)
            kb.dma("sync", z_sb[:, :, wsl], self.d_z.rearrange("(c p) t -> p c t", p=P)[:, :, wsl],
                   reads=["d_z:%d" % w], writes=pg(("z2r",)), tag="z2l")
        wqT = self.load_weight_bf16("r_wqT", W1_OFF, self.din["h_L2_WqT"], [D, D])
        wvT = self.load_weight_bf16("r_wvT", W2_OFF, self.din["h_L2_WvT"], [D, D])
        Vc = self.sb("r_Vc", VC_OFF, [P, T], F32)
        stt = self.sb("r_st", ST_OFF, [P, T], F32)
        ret = self.sb("r_ret", RET_OFF, [P, NC, T], BF16)
        lamc = self.sb("r_lamc", SM_OFF, [16, 8], F32)
        lam16 = self.sb("r_lam16", SM_OFF + 64, [16, 1], F32)
        lam_pp = self.sb("r_lampp", SM_OFF + 96, [P, 8], F32)
        hpar = self.sb("r_hpar", SM_OFF + 128, [16, P], F32)
        hmask = self.sb("r_hmask", SM_OFF + 640, [16, 8], F32)
        kb.dma("sync", lam16[:], self.din["L2_attn_beta"].unsqueeze(1), writes=pg("r_lam16"))
        kb.dma("sync", hpar[:], self.din["c_hpar"][:], writes=pg("r_hpar"))
        kb.dma("sync", hmask[:], self.din["c_hmask"][:], writes=pg("r_hmask"))
        kb.emit("scalar", lambda s: s.activation(lam16[:], lam16[:], AF.Exp, scale=-LN2),
                reads=pg("r_lam16"), writes=pg("r_lam16"), tag="lam")
        kb.emit("vector", lambda v: v.tensor_scalar(lam16[:], lam16[:], -1.0, 1.0,
                                                    ALU.mult, ALU.add),
                reads=pg("r_lam16"), writes=pg("r_lam16"), tag="lam")
        kb.emit("vector", lambda v: v.tensor_scalar_max(lam16[:], lam16[:], 1.1754944e-38),
                reads=pg("r_lam16"), writes=pg("r_lam16"), tag="lam")
        kb.emit("vector", lambda v: v.tensor_scalar_min(lam16[:], lam16[:], 1.0 - 1e-9),
                reads=pg("r_lam16"), writes=pg("r_lam16"), tag="lam")
        kb.emit("vector", lambda v: v.tensor_mul(lamc[:], lam16[:, 0:1].broadcast_to([16, 8]), hmask[:]),
                reads=pg("r_lam16", "r_hmask"), writes=pg("r_lamc"), tag="lam")
        kb.emit("tensor", lambda t: t.matmul(self.ps[5][0:P, 0:8], hpar[:], lamc[:],
                                             start=True, stop=True),
                reads=pg("r_hpar", "r_lamc"), writes=["PS5"], tag="lam")
        kb.emit("vector", lambda v: v.tensor_copy(lam_pp[:], self.ps[5][0:P, 0:8]),
                reads=["PS5"], writes=pg("r_lampp"), tag="lam")
        for c in range(NC):
            kb.seg()
            for w in range(NW):
                wsl = slice(w * N, (w + 1) * N)
                b = (0, 1)[w % 2]
                ps = self.ps[b]
                for kc in range(NC):
                    kb.emit("tensor", lambda t, c=c, kc=kc, ps=ps, wsl=wsl:
                            t.matmul(ps[:], wvT[:, kc, ts(c)], z_sb[:, kc, wsl],
                                     start=(kc == 0), stop=(kc == NC - 1)),
                            reads=pg("r_wvT", "z2r"), writes=["PS%d" % b], tag="rv")
                kb.emit("vector", lambda v, ps=ps, wsl=wsl: v.tensor_copy(Vc[:, wsl], ps[:]),
                        reads=["PS%d" % b], writes=pg(("r_Vc", wsl.start * 4, wsl.stop * 4)), tag="rvev")
            kb.emit("vector", lambda v, c=c: v.tensor_tensor_scan(
                        stt[:], lam_pp[:, c:c + 1].broadcast_to([P, T]), Vc[:],
                        0.0, ALU.mult, ALU.add),
                    reads=pg("r_Vc", "r_lampp"), writes=pg("r_st"), tag="rscan")
            for w in range(NW):
                wsl = slice(w * N, (w + 1) * N)
                b = (2, 3)[w % 2]
                ps = self.ps[b]
                for kc in range(NC):
                    kb.emit("tensor", lambda t, c=c, kc=kc, ps=ps, wsl=wsl:
                            t.matmul(ps[:], wqT[:, kc, ts(c)], z_sb[:, kc, wsl],
                                     start=(kc == 0), stop=(kc == NC - 1)),
                            reads=pg("r_wqT", "z2r"), writes=["PS%d" % b], tag="rq")
                kb.emit("vector", lambda v, c=c, ps=ps, wsl=wsl:
                        v.tensor_mul(ret[:, c, wsl], ps[:], stt[:, wsl]),
                        reads=["PS%d" % b] + pg("r_st"),
                        writes=pg(("r_ret", c * T * 2 + wsl.start * 2, c * T * 2 + wsl.stop * 2)),
                        tag="rmul")
        kb.seg()
        wgT = self.load_weight_bf16("r_wgT", W1_OFF, self.din["h_L2_WgT"], [D, D])
        woT = self.load_weight_bf16("r_woT", W2_OFF, self.din["h_L2_WoT"], [D, D])
        gw = self.sb("r_gw", VC_OFF, [P, NC, N], BF16)
        ym = self.sb("r_ym", VC_OFF + 8192, [P, NC, N], BF16)
        xw2 = self.sb("r_xw", ST_OFF, [P, NC, N], F32)
        m2 = self.sb("r_m2", SM_OFF + 704, [P, N], F32)
        a2 = self.sb("r_a2", SM_OFF + 2752, [P, N], F32)
        t2 = self.sb("r_t2", SM_OFF + 4800, [P, N], F32)
        lng = self.load_pp_vec("r_lng", SM_OFF + 6848, self.din["L2_attn_ln_gamma"], D)
        lnb = self.load_pp_vec("r_lnb", SM_OFF + 6976, self.din["L2_attn_ln_beta"], D)
        for w in range(NW):
            kb.seg()
            wsl = slice(w * N, (w + 1) * N)
            kb.dma("scalar", xw2[:], xsrc.rearrange("(c p) t -> p c t", p=P)[:, :, wsl],
                   reads=[ksrc + ":%d" % w], writes=pg("r_xw"), tag="rx")
            for oc in range(NC):
                b = (0, 1)[oc % 2]
                ps = self.ps[b]
                for kc in range(NC):
                    kb.emit("tensor", lambda t, oc=oc, kc=kc, ps=ps, wsl=wsl:
                            t.matmul(ps[:], wgT[:, kc, ts(oc)], z_sb[:, kc, wsl],
                                     start=(kc == 0), stop=(kc == NC - 1)),
                            reads=pg("r_wgT", "z2r"), writes=["PS%d" % b], tag="rg")
                kb.emit("scalar", lambda s, oc=oc, ps=ps: s.activation(gw[:, oc, :], ps[:], AF.Silu),
                        reads=["PS%d" % b], writes=pg(("r_gw", oc * N * 2, (oc + 1) * N * 2)),
                        tag="rgev")
            retw = ret[:, :, wsl]
            self.stats_of(retw, "r_ret", m2, "r_m2", a2, "r_a2", bS=6, bQ=5)
            for oc in range(NC):
                kb.emit("vector", lambda v, oc=oc, retw=retw: v.tensor_sub(t2[:], retw[:, oc, :], m2[:]),
                        reads=pg("r_ret", "r_m2"), writes=pg("r_t2"), tag="ry1")
                kb.emit("vector", lambda v: v.tensor_mul(t2[:], t2[:], a2[:]),
                        reads=pg("r_t2", "r_a2"), writes=pg("r_t2"), tag="ry2")
                kb.emit("vector", lambda v, oc=oc: v.tensor_scalar(
                            t2[:], t2[:], lng[:, oc:oc + 1], lnb[:, oc:oc + 1],
                            ALU.mult, ALU.add),
                        reads=pg("r_t2", "r_lng", "r_lnb"), writes=pg("r_t2"), tag="ry3")
                kb.emit("vector", lambda v, oc=oc: v.tensor_mul(ym[:, oc, :], t2[:], gw[:, oc, :]),
                        reads=pg("r_t2", "r_gw"),
                        writes=pg(("r_ym", oc * N * 2, (oc + 1) * N * 2)), tag="ry4")
            for oc in range(NC):
                b = (2, 3)[oc % 2]
                ps = self.ps[b]
                for kc in range(NC):
                    kb.emit("tensor", lambda t, oc=oc, kc=kc, ps=ps:
                            t.matmul(ps[:], woT[:, kc, ts(oc)], ym[:, kc, :],
                                     start=(kc == 0), stop=(kc == NC - 1)),
                            reads=pg("r_woT", "r_ym"), writes=["PS%d" % b], tag="ro")
                kb.emit("vector", lambda v, oc=oc, ps=ps: v.tensor_add(
                            xw2[:, oc, :], xw2[:, oc, :], ps[:]),
                        reads=["PS%d" % b] + pg("r_xw"), writes=pg("r_xw"), tag="rres")
            self.x_writeback(w, xw2, xdst, kdst, "r_xw")

    def sub_l3attn(self):
        kb, pg = self.kb, self.pg
        xsrc, ksrc, xdst, kdst = self.next_x()
        self.ln_all_to_dram(xsrc, ksrc, self.d_z, "d_z")
        kb.seg()
        B0 = self.PHASE
        M_OFF = B0
        A_OFF = B0 + 65536
        ZC_OFF = A_OFF + 16384
        YC_OFF = ZC_OFF + 16384
        CV_OFF = YC_OFF + 8192
        BY_OFF = CV_OFF + 16384
        SS_OFF = BY_OFF + 16384
        SM_OFF = SS_OFF + 16384
        m_sb = self.sb("l3_m", M_OFF, [P, NC, T], BF16)
        Abc = self.sb("l3_A", A_OFF, [P, T], F32)
        zc = [self.sb("l3_zc0", ZC_OFF, [P, T], BF16), self.sb("l3_zc1", ZC_OFF + 8192, [P, T], BF16)]
        zck = ["l3_zc0", "l3_zc1"]
        yc = self.sb("l3_yc", YC_OFF, [P, T], BF16)
        cv = self.sb("l3_cv", CV_OFF, [P, T], F32)
        by = self.sb("l3_by", BY_OFF, [P, T], F32)
        ssm = self.sb("l3_ss", SS_OFF, [P, T], F32)
        nw1 = self.load_pp_vec("l3_nw1", SM_OFF, self.din["L3_attn_norm_w"], D)
        cwk = [self.load_pp_vec("l3_cwk%d" % k, SM_OFF + 256 + k * 128, self.din["h_L3_convw"][k], D)
               for k in range(3)]
        cb_pp = self.load_pp_vec("l3_cb", SM_OFF + 1024, self.din["L3_attn_conv_b"], D)
        a_pp = self.load_pp_vec("l3_a", SM_OFF + 1280, self.din["L3_attn_a"], D)
        b_pp = self.load_pp_vec("l3_b", SM_OFF + 1536, self.din["L3_attn_b"], D)
        s_pp = self.sb("l3_s", SM_OFF + 1792, [P, NC], F32)
        v1 = self.sb("l3_v1", SM_OFF + 2048, [P, N], F32)
        kb.emit("scalar", lambda s: s.activation(s_pp[:], a_pp[:], AF.Tanh),
                reads=pg("l3_a"), writes=pg("l3_s"), tag="l3tanh")
        psQ = self.ps[7]
        for w in range(NW):
            wsl = slice(w * N, (w + 1) * N)
            zw = zc[w % 2]
            kb.dma("sync", zw[:, 0:NC * N].rearrange("p (c t) -> p c t", c=NC),
                   self.d_z.rearrange("(c p) t -> p c t", p=P)[:, :, wsl],
                   reads=["d_z:%d" % w], writes=pg(zck[w % 2]), tag="l3zl")
            for c in range(NC):
                p2 = c % 2
                kb.emit("scalar", lambda s, c=c, p2=p2, zw=zw: s.activation(
                            self.stsq[p2][:], zw[:, c * N:(c + 1) * N], AF.Square),
                        reads=pg(zck[w % 2]), writes=pg("stsq%d" % p2), tag="l3sq"),
                kb.emit("tensor", lambda t, c=c, p2=p2: t.matmul(
                            psQ[:], self.ones_bf[:, 0:128], self.stsq[p2][:],
                            start=(c == 0), stop=(c == NC - 1)),
                        reads=pg("stsq%d" % p2, "ones_bf"), writes=["PS7"], tag="l3mm")
            kb.emit("scalar", lambda s: s.activation(v1[:], psQ[:], AF.Ln, bias=self.C_EPS6,
                                                     scale=1.0 / D),
                    reads=["PS7", "consts"], writes=pg("l3_v1"), tag="l3ln")
            kb.emit("scalar", lambda s, wsl=wsl: s.activation(Abc[:, wsl], v1[:], AF.Exp, scale=-0.5),
                    reads=pg("l3_v1"), writes=pg(("l3_A", wsl.start * 4, wsl.stop * 4)), tag="l3ex")
        kb.seg()
        for c in range(NC):
            kb.seg()
            zcc = self.sb("l3_zcc%d" % c, ZC_OFF, [P, T], BF16) if False else zc[0]
            kb.dma("sync", zcc[:], self.d_z.rearrange("(c p) t -> p c t", p=P)[:, c, :],
                   reads=["d_z:%d" % w2 for w2 in range(NW)], writes=pg(zck[0]), tag="l3zc")
            kb.emit("vector", lambda v, c=c, zcc=zcc: v.scalar_tensor_tensor(
                        yc[:], zcc[:], nw1[:, c:c + 1], Abc[:], ALU.mult, ALU.mult),
                    reads=pg(zck[0], "l3_nw1", "l3_A"), writes=pg("l3_yc"), tag="l3y")
            kb.emit("vector", lambda v, c=c: v.tensor_scalar(
                        cv[:], yc[:], cwk[1][:, c:c + 1], cb_pp[:, c:c + 1],
                        ALU.mult, ALU.add),
                    reads=pg("l3_yc", "l3_cwk1", "l3_cb"), writes=pg("l3_cv"), tag="l3c1")
            kb.emit("vector", lambda v, c=c: v.scalar_tensor_tensor(
                        cv[:, 1:T], yc[:, 0:T - 1], cwk[0][:, c:c + 1], cv[:, 1:T],
                        ALU.mult, ALU.add),
                    reads=pg("l3_yc", "l3_cwk0", "l3_cv"), writes=pg("l3_cv"), tag="l3c0")
            kb.emit("vector", lambda v, c=c: v.scalar_tensor_tensor(
                        cv[:, 0:T - 1], yc[:, 1:T], cwk[2][:, c:c + 1], cv[:, 0:T - 1],
                        ALU.mult, ALU.add),
                    reads=pg("l3_yc", "l3_cwk2", "l3_cv"), writes=pg("l3_cv"), tag="l3c2")
            kb.emit("vector", lambda v, c=c: v.tensor_scalar_mul(by[:], yc[:], b_pp[:, c:c + 1]),
                    reads=pg("l3_yc", "l3_b"), writes=pg("l3_by"), tag="l3by")
            kb.emit("vector", lambda v, c=c: v.tensor_tensor_scan(
                        ssm[:], s_pp[:, c:c + 1].broadcast_to([P, T]), by[:],
                        0.0, ALU.mult, ALU.add),
                    reads=pg("l3_by", "l3_s"), writes=pg("l3_ss"), tag="l3scan")
            kb.emit("vector", lambda v, c=c, zcc=zcc: v.tensor_add(cv[:], cv[:], zcc[:]),
                    reads=pg("l3_cv", zck[0]), writes=pg("l3_cv"), tag="l3m1")
            kb.emit("vector", lambda v, c=c: v.tensor_add(
                        m_sb[:, c, :], cv[:], ssm[:]),
                    reads=pg("l3_cv", "l3_ss"),
                    writes=pg(("l3_m", c * T * 2, (c + 1) * T * 2)), tag="l3m2")
        kb.seg()
        nw2 = self.load_pp_vec("l3_nw2", SM_OFF + 2048 + 2048, self.din["L3_attn_mlp_norm_w"], D)
        bg_pp = self.load_pp_vec("l3_bg", SM_OFF + 4352, self.din["L3_attn_b_gate"], 4096)
        self.bu_row = self.load_row_bf16("bu_row", SM_OFF + 4608, self.din["L3_attn_b_up"], 4096)
        bd_pp = self.load_pp_vec("l3_bd", SM_OFF + 13056, self.din["L3_attn_b_down"], D)
        u2 = self.sb("l3_u2", ZC_OFF, [P, NC, N], BF16)
        hwin = self.sb("l3_hw", CV_OFF, [P, 32, N], BF16)
        wgt = [self.sb("l3_wg0", A_OFF, [P, NC, P], BF16),
               self.sb("l3_wg1", A_OFF + 2048, [P, NC, P], BF16)]
        wgtk = ["l3_wg0", "l3_wg1"]
        wut = [self.sb("l3_wu0", A_OFF + 4096, [P, NC, P], BF16),
               self.sb("l3_wu1", A_OFF + 6144, [P, NC, P], BF16)]
        wutk = ["l3_wu0", "l3_wu1"]
        wdt = [self.sb("l3_wd0", A_OFF + 8192, [P, D], BF16),
               self.sb("l3_wd1", A_OFF + 10240, [P, D], BF16)]
        wdtk = ["l3_wd0", "l3_wd1"]
        gt = self.sb("l3_gt", A_OFF + 12288, [P, N], BF16)
        xw3 = self.sb("l3_xw3", YC_OFF, [P, NC, N], F32) if False else self.sb("l3_xw3", SS_OFF, [P, NC, N], F32)
        m3 = self.sb("l3_m3", SM_OFF + 13568, [P, N], F32)
        a3 = self.sb("l3_a3", SM_OFF + 15616, [P, N], F32)
        for w in range(NW):
            kb.seg()
            wsl = slice(w * N, (w + 1) * N)
            mw = m_sb[:, :, wsl]
            for c in range(NC):
                p2 = c % 2
                kb.emit("scalar", lambda s, c=c, p2=p2, mw=mw: s.activation(
                            self.stsq[p2][:], mw[:, c, :], AF.Square),
                        reads=pg("l3_m"), writes=pg("stsq%d" % p2), tag="l3q2")
                kb.emit("tensor", lambda t, c=c, p2=p2: t.matmul(
                            psQ[:], self.ones_bf[:, 0:128], self.stsq[p2][:],
                            start=(c == 0), stop=(c == NC - 1)),
                        reads=pg("stsq%d" % p2, "ones_bf"), writes=["PS7"], tag="l3mm2")
            kb.emit("scalar", lambda s: s.activation(m3[:], psQ[:], AF.Ln, bias=self.C_EPS6,
                                                     scale=1.0 / D),
                    reads=["PS7", "consts"], writes=pg("l3_m3"), tag="l3l2")
            kb.emit("scalar", lambda s: s.activation(a3[:], m3[:], AF.Exp, scale=-0.5),
                    reads=pg("l3_m3"), writes=pg("l3_a3"), tag="l3e2")
            for c in range(NC):
                kb.emit("vector", lambda v, c=c, mw=mw: v.scalar_tensor_tensor(
                            u2[:, c, :], mw[:, c, :], nw2[:, c:c + 1], a3[:],
                            ALU.mult, ALU.mult),
                        reads=pg("l3_m", "l3_nw2", "l3_a3"),
                        writes=pg(("l3_u2", c * N * 2, (c + 1) * N * 2)), tag="l3u2")
            for fc in range(32):
                p2 = fc % 2
                kb.dma("gpsimd", wgt[p2][:], self.din["L3_attn_w_gate"].rearrange(
                            "(c p) h -> p c h", p=P)[:, :, ts(fc)],
                       writes=pg(wgtk[p2]), tag="l3wg")
                kb.dma("gpsimd", wut[p2][:], self.din["L3_attn_w_up"].rearrange(
                            "(c p) h -> p c h", p=P)[:, :, ts(fc)],
                       writes=pg(wutk[p2]), tag="l3wu")
                bA = (0, 1)[fc % 2]
                psA = self.ps[bA]
                for kc in range(NC):
                    kb.emit("tensor", lambda t, fc=fc, kc=kc, p2=p2, psA=psA:
                            t.matmul(psA[:], wgt[p2][:, kc, :], u2[:, kc, :],
                                     start=(kc == 0), stop=(kc == NC - 1)),
                            reads=pg(wgtk[p2], "l3_u2"), writes=["PS%d" % bA], tag="l3g")
                kb.emit("scalar", lambda s, fc=fc, psA=psA: s.activation(
                            gt[:], psA[:], AF.Silu, bias=bg_pp[:, fc:fc + 1]),
                        reads=["PS%d" % bA] + pg("l3_bg"), writes=pg("l3_gt"), tag="l3gs")
                bB = (2, 3)[fc % 2]
                psB = self.ps[bB]
                for kc in range(NC):
                    kb.emit("tensor", lambda t, fc=fc, kc=kc, p2=p2, psB=psB:
                            t.matmul(psB[:], wut[p2][:, kc, :], u2[:, kc, :],
                                     start=(kc == 0), stop=False),
                            reads=pg(wutk[p2], "l3_u2"), writes=["PS%d" % bB], tag="l3up")
                kb.emit("tensor", lambda t, fc=fc, psB=psB: t.matmul(
                            psB[:], self.bu_row[0:1, ts(fc)], self.ones_bf[0:1, 0:N],
                            start=False, stop=True),
                        reads=pg("bu_row", "ones_bf"), writes=["PS%d" % bB], tag="l3ub")
                kb.emit("vector", lambda v, fc=fc, psB=psB: v.tensor_mul(
                            hwin[:, fc, :], gt[:], psB[:]),
                        reads=pg("l3_gt") + ["PS%d" % bB],
                        writes=pg(("l3_hw", fc * N * 2, (fc + 1) * N * 2)), tag="l3h")
            for fc in range(32):
                p2 = fc % 2
                kb.dma("gpsimd", wdt[p2][:], self.din["L3_attn_w_down"].rearrange(
                            "(c p) d -> p c d", p=P)[:, fc, :],
                       writes=pg(wdtk[p2]), tag="l3wd")
                for dc in range(NC):
                    kb.emit("tensor", lambda t, fc=fc, dc=dc, p2=p2:
                            t.matmul(self.ps[dc][:], wdt[p2][:, ts(dc)], hwin[:, fc, :],
                                     start=(fc == 0), stop=(fc == 31)),
                            reads=pg(wdtk[p2], "l3_hw"), writes=["PS%d" % dc], tag="l3d")
            kb.dma("scalar", xw3[:], xsrc.rearrange("(c p) t -> p c t", p=P)[:, :, wsl],
                   reads=[ksrc + ":%d" % w], writes=pg("l3_xw3"), tag="l3x")
            for dc in range(NC):
                kb.emit("vector", lambda v, dc=dc, mw=mw: v.tensor_add(
                            xw3[:, dc, :], xw3[:, dc, :], mw[:, dc, :]),
                        reads=pg("l3_m", "l3_xw3"), writes=pg("l3_xw3"), tag="l3r1")
                kb.emit("vector", lambda v, dc=dc: v.scalar_tensor_tensor(
                            xw3[:, dc, :], self.ps[dc][:], bd_pp[:, dc:dc + 1], xw3[:, dc, :],
                            ALU.add, ALU.add),
                        reads=["PS%d" % dc] + pg("l3_bd", "l3_xw3"), writes=pg("l3_xw3"), tag="l3r2")
            self.x_writeback(w, xw3, xdst, kdst, "l3_xw3")

    def sub_l3ffn(self):
        kb, pg = self.kb, self.pg
        xsrc, ksrc, xdst, kdst = self.next_x()
        self.ln_all_to_dram(xsrc, ksrc, self.d_z, "d_z")
        kb.seg()
        B0 = self.PHASE
        ZS_OFF = B0
        GW_OFF = B0 + 16384
        HP_OFF = B0 + 20480
        WS_OFF = HP_OFF + 36864
        CW_OFF = WS_OFF + 12288
        XW_OFF = CW_OFF + 8192
        SM_OFF = XW_OFF + 16384
        zs = [self.sb("f3_z0", ZS_OFF, [P, NC, N], BF16), self.sb("f3_z1", ZS_OFF + 8192, [P, NC, N], BF16)]
        zsk = ["f3_z0", "f3_z1"]
        gwT = self.sb("f3_gwT", GW_OFF, [P, NC, 8], BF16)
        kb.dma("gpsimd", gwT[:], self.din["h_L3f_gwT"].rearrange("(c p) e -> p c e", p=P),
               writes=pg("f3_gwT"), tag="f3gw")
        hall = self.sb("f3_hall", HP_OFF, [P, 36, N], BF16)
        wg2 = [self.sb("f3_wg0", WS_OFF, [P, NC, P], BF16), self.sb("f3_wg1", WS_OFF + 2048, [P, NC, P], BF16)]
        wg2k = ["f3_wg0", "f3_wg1"]
        wu2 = [self.sb("f3_wu0", WS_OFF + 4096, [P, NC, P], BF16), self.sb("f3_wu1", WS_OFF + 6144, [P, NC, P], BF16)]
        wu2k = ["f3_wu0", "f3_wu1"]
        wd2 = [self.sb("f3_wd0", WS_OFF + 8192, [P, D], BF16), self.sb("f3_wd1", WS_OFF + 10240, [P, D], BF16)]
        wd2k = ["f3_wd0", "f3_wd1"]
        cwb = self.sb("f3_cwb", CW_OFF, [P, N], F32)
        cwrow = self.sb("f3_cwrow", CW_OFF + 2048, [1, N], BF16)
        xw = self.sb("f3_xw", XW_OFF, [P, NC, N], F32)
        gt8 = self.sb("f3_gt8", SM_OFF, [P, 8], F32)
        e8 = self.sb("f3_e8", SM_OFF + 64, [P, 8], F32)
        mx1 = self.sb("f3_mx", SM_OFF + 96, [P, 1], F32)
        s8 = self.sb("f3_s8", SM_OFF + 128, [P, 1], F32)
        t8 = self.sb("f3_t8", SM_OFF + 160, [P, 8], F32)
        cwt = self.sb("f3_cwt", SM_OFF + 192, [P, 8], F32)
        cwfm = self.sb("f3_cwfm", SM_OFF + 256, [8, T], F32)
        gsc = self.sb("f3_gsc", SM_OFF + 16640, [P, N], BF16)
        for w in range(NW):
            kb.seg()
            wsl = slice(w * N, (w + 1) * N)
            par = w % 2
            zw, zwk = zs[par], zsk[par]
            kb.dma("sync", zw[:], self.d_z.rearrange("(c p) t -> p c t", p=P)[:, :, wsl],
                   reads=["d_z:%d" % w], writes=pg(zwk), tag="f3zl")
            psg = self.ps[6]
            for kc in range(NC):
                kb.emit("tensor", lambda t, kc=kc, zw=zw: t.matmul(
                            psg[0:8, :], gwT[:, kc, :], zw[:, kc, :],
                            start=(kc == 0), stop=(kc == NC - 1)),
                        reads=pg("f3_gwT", zwk), writes=["PS6"], tag="f3g")
            kb.emit("scalar", lambda s: s.activation(gsc[0:8, :], psg[0:8, :], AF.Copy),
                    reads=["PS6"], writes=pg("f3_gsc"), tag="f3gev")
            for tt in range(4):
                pst = self.ps[7]
                pstb = pst.bitcast(BF16)
                kb.emit("tensor", lambda t, tt=tt, pstb=pstb: t.transpose(
                            pstb[0:P, 0:8], gsc[0:8, ts(tt)], self.ident_bf[0:8, 0:8]),
                        reads=pg("f3_gsc", "ident_bf"), writes=["PS7"], tag="f3tr")
                kb.emit("vector", lambda v, pstb=pstb: v.tensor_copy(gt8[:], pstb[0:P, 0:8]),
                        reads=["PS7"], writes=pg("f3_gt8"), tag="f3cp")
                kb.emit("vector", lambda v: v.reduce_max(mx1[:], gt8[:], AX.X),
                        reads=pg("f3_gt8"), writes=pg("f3_mx"), tag="f3mx")
                kb.emit("vector", lambda v: v.tensor_scalar_mul(mx1[:], mx1[:], -1.0),
                        reads=pg("f3_mx"), writes=pg("f3_mx"), tag="f3ng")
                kb.emit("scalar", lambda s: s.activation(e8[:], gt8[:], AF.Exp, bias=mx1[:]),
                        reads=pg("f3_gt8", "f3_mx"), writes=pg("f3_e8"), tag="f3ex")
                kb.emit("vector", lambda v: v.reduce_sum(s8[:], e8[:], AX.X),
                        reads=pg("f3_e8"), writes=pg("f3_s8"), tag="f3sm")
                kb.emit("vector", lambda v: v.reciprocal(s8[:], s8[:]),
                        reads=pg("f3_s8"), writes=pg("f3_s8"), tag="f3rc")
                kb.emit("vector", lambda v: v.tensor_scalar_mul(e8[:], e8[:], s8[:]),
                        reads=pg("f3_e8", "f3_s8"), writes=pg("f3_e8"), tag="f3pb")
                kb.emit("vector", lambda v: v.max(t8[:], e8[:]),
                        reads=pg("f3_e8"), writes=pg("f3_t8"), tag="f3t8")
                kb.emit("vector", lambda v: v.tensor_scalar(cwt[:], e8[:], t8[:, 1:2], None,
                                                            ALU.is_ge),
                        reads=pg("f3_e8", "f3_t8"), writes=pg("f3_cwt"), tag="f3ge")
                kb.emit("vector", lambda v: v.tensor_mul(cwt[:], cwt[:], e8[:]),
                        reads=pg("f3_cwt", "f3_e8"), writes=pg("f3_cwt"), tag="f3cw")
                kb.emit("tensor", lambda t: t.transpose(pst[0:8, 0:P], cwt[:],
                                                        self.ident_f32[:]),
                        reads=pg("f3_cwt", "ident_f32"), writes=["PS7"], tag="f3t2")
                kb.emit("vector", lambda v, w=w, tt=tt: v.tensor_copy(
                            cwfm[0:8, w * N + tt * P:w * N + (tt + 1) * P], pst[0:8, 0:P]),
                        reads=["PS7"], writes=pg("f3_cwfm"), tag="f3cf")
            kb.dma("sync", self.d_cw[:, wsl], cwfm[0:8, wsl], reads=pg("f3_cwfm"),
                   writes=["d_cw:%d" % w], tag="f3cs")
            for e in range(9):
                shared = (e == 8)
                p2 = e % 2
                if shared:
                    wsrc_g, wsrc_u = self.din["h_L3f_sgT"], self.din["h_L3f_suT"]
                else:
                    wsrc_g = self.din["h_L3f_wgT"][e]
                    wsrc_u = self.din["h_L3f_wuT"][e]
                for fc in range(4):
                    kb.dma("gpsimd", wg2[p2][:], wsrc_g.rearrange("(c p) f -> p c f", p=P)[:, :, ts(fc)],
                           writes=pg(wg2k[p2]), tag="f3wg")
                    kb.dma("gpsimd", wu2[p2][:], wsrc_u.rearrange("(c p) f -> p c f", p=P)[:, :, ts(fc)],
                           writes=pg(wu2k[p2]), tag="f3wu")
                    bA = (0, 1)[fc % 2]
                    psA = self.ps[bA]
                    for kc in range(NC):
                        kb.emit("tensor", lambda t, kc=kc, p2=p2, psA=psA, zw=zw:
                                t.matmul(psA[:], wg2[p2][:, kc, :], zw[:, kc, :],
                                         start=(kc == 0), stop=(kc == NC - 1)),
                                reads=pg(wg2k[p2], zwk), writes=["PS%d" % bA], tag="f3gm")
                    kb.emit("scalar", lambda s, psA=psA: s.activation(gsc[:], psA[:], AF.Silu),
                            reads=["PS%d" % bA], writes=pg("f3_gsc"), tag="f3si")
                    bB = (2, 3)[fc % 2]
                    psB = self.ps[bB]
                    for kc in range(NC):
                        kb.emit("tensor", lambda t, kc=kc, p2=p2, psB=psB, zw=zw:
                                t.matmul(psB[:], wu2[p2][:, kc, :], zw[:, kc, :],
                                         start=(kc == 0), stop=(kc == NC - 1)),
                                reads=pg(wu2k[p2], zwk), writes=["PS%d" % bB], tag="f3um")
                    hidx = e * 4 + fc
                    if shared:
                        kb.emit("vector", lambda v, hidx=hidx, psB=psB: v.tensor_mul(
                                    hall[:, hidx, :], gsc[:], psB[:]),
                                reads=pg("f3_gsc") + ["PS%d" % bB],
                                writes=pg(("f3_hall", hidx * N * 2, (hidx + 1) * N * 2)), tag="f3h")
                    else:
                        if fc == 0:
                            kb.dma("gpsimd", cwrow[:], self.d_cw[e:e + 1, wsl],
                                   reads=["d_cw:%d" % w], writes=pg("f3_cwrow"), tag="f3cr")
                            psC = self.ps[5]
                            kb.emit("tensor", lambda t: t.matmul(psC[:], self.ones_bf[0:1, 0:128],
                                                                 cwrow[:], start=True, stop=True),
                                    reads=pg("f3_cwrow", "ones_bf"), writes=["PS5"], tag="f3cb")
                            kb.emit("vector", lambda v: v.tensor_copy(cwb[:], psC[:]),
                                    reads=["PS5"], writes=pg("f3_cwb"), tag="f3ce")
                        kb.emit("vector", lambda v, psB=psB: v.tensor_mul(gsc[:], gsc[:], psB[:]),
                                reads=pg("f3_gsc") + ["PS%d" % bB], writes=pg("f3_gsc"), tag="f3h1")
                        kb.emit("vector", lambda v, hidx=hidx: v.tensor_mul(
                                    hall[:, hidx, :], gsc[:], cwb[:]),
                                reads=pg("f3_gsc", "f3_cwb"),
                                writes=pg(("f3_hall", hidx * N * 2, (hidx + 1) * N * 2)), tag="f3h2")
            for e in range(9):
                p2 = e % 2
                wsrc_d = self.din["h_L3f_sdT"] if e == 8 else self.din["h_L3f_wdT"][e]
                for fc in range(4):
                    kb.dma("gpsimd", wd2[p2][:], wsrc_d.rearrange("(c p) d -> p c d", p=P)[:, fc, :],
                           writes=pg(wd2k[p2]), tag="f3wd")
                    hidx = e * 4 + fc
                    for dc in range(NC):
                        kb.emit("tensor", lambda t, fc=fc, dc=dc, p2=p2, hidx=hidx, e=e:
                                t.matmul(self.ps[dc][:], wd2[p2][:, ts(dc)], hall[:, hidx, :],
                                         start=(e == 0 and fc == 0), stop=(e == 8 and fc == 3)),
                                reads=pg(wd2k[p2], "f3_hall"), writes=["PS%d" % dc], tag="f3d")
            kb.dma("scalar", xw[:], xsrc.rearrange("(c p) t -> p c t", p=P)[:, :, wsl],
                   reads=[ksrc + ":%d" % w], writes=pg("f3_xw"), tag="f3x")
            for dc in range(NC):
                kb.emit("vector", lambda v, dc=dc: v.scalar_tensor_tensor(
                            xw[:, dc, :], self.ps[dc][:], 0.5, xw[:, dc, :],
                            ALU.mult, ALU.add),
                        reads=["PS%d" % dc] + pg("f3_xw"), writes=pg("f3_xw"), tag="f3r")
            self.x_writeback(w, xw, xdst, kdst, "f3_xw")

    def sub_l2ffn(self):
        kb, pg = self.kb, self.pg
        xsrc, ksrc, xdst, kdst = self.next_x()
        self.ln_all_to_dram(xsrc, ksrc, self.d_z, "d_z")
        kb.seg()
        B0 = self.PHASE
        Z_OFF = B0
        W1_OFF = Z_OFF + 65536
        W2_OFF = W1_OFF + 16384
        S0F_OFF = W2_OFF + 512
        S0S_OFF = S0F_OFF + 1024
        H1_OFF = S0S_OFF + 2048
        ZT_OFF = H1_OFF + 4096
        AFM_OFF = 200416
        SM_OFF = ZT_OFF + 4096
        WIH_OFF = 139264
        z_sb = self.sb("f2_z", Z_OFF, [P, NC, T], BF16)
        for w in range(NW):
            wsl = slice(w * N, (w + 1) * N)
            kb.dma("sync", z_sb[:, :, wsl], self.d_z.rearrange("(c p) t -> p c t", p=P)[:, :, wsl],
                   reads=["d_z:%d" % w], writes=pg(("f2_z",)), tag="f2zl")
        w1 = self.sb("f2_w1", W1_OFF, [P, 16, 512], BF16)
        kb.dma("gpsimd", w1[:], self.din["L2_ffn_w1"].rearrange("(c p) f -> p c f", p=P),
               writes=pg("f2_w1"), tag="f2w1")
        w2 = self.sb("f2_w2", W2_OFF, [P, 4, 64], BF16)
        kb.dma("gpsimd", w2[:], self.din["L2_ffn_w2"].rearrange("(c p) s -> p c s", p=P),
               writes=pg("f2_w2"), tag="f2w2")
        S0f = self.sb("f2_S0f", S0F_OFF, [P, NC, 64], BF16)
        kb.dma("gpsimd", S0f[:], self.din["h_L2f_S0T"].rearrange("(c p) s -> p c s", p=P),
               writes=pg("f2_S0f"), tag="f2s0f")
        h1w = self.sb("f2_h1", H1_OFF, [P, 4, N], BF16)
        zt = [self.sb("f2_zt0", ZT_OFF, [P, D], BF16), self.sb("f2_zt1", ZT_OFF + 2048, [P, D], BF16)]
        ztk = ["f2_zt0", "f2_zt1"]
        afm = self.sb("f2_afm", AFM_OFF, [64, T], BF16)
        smb = self.sb("f2_smb", SM_OFF, [P, NC], BF16)
        smf = self.sb("f2_smf", SM_OFF + 32, [P, NC], F32)
        cfb = self.sb("f2_cfb", SM_OFF + 64, [P, 4], F32)
        b1pp = self.load_pp_vec("f2_b1", SM_OFF + 96, self.din["L2_ffn_b1"], 512)
        b2row = self.load_row_bf16("f2_b2", SM_OFF + 128, self.din["L2_ffn_b2"], 64)
        rtau = self.sb("f2_rtau", SM_OFF + 288, [1, 1], F32)
        rtaub = self.sb("f2_rtaub", SM_OFF + 320, [P, 1], F32)
        rlf = self.sb("f2_rlf", SM_OFF + 352, [64, N], F32)
        rltm = self.sb("f2_rltm", SM_OFF + 2400, [P, 64], F32)
        e64 = self.sb("f2_e64", SM_OFF + 2656, [P, 64], F32)
        mx1 = self.sb("f2_mx", SM_OFF + 2912, [P, 1], F32)
        t8 = self.sb("f2_t8", SM_OFF + 2944, [P, 8], F32)
        rd1 = self.sb("f2_rd", SM_OFF + 2976, [P, 1], F32)
        atm = self.sb("f2_atm", SM_OFF + 3008, [P, 64], BF16)
        for c in range(NC):
            kb.emit("vector", lambda v, c=c: v.reduce_sum(smf[:, c:c + 1], S0f[:, c, :], AX.X),
                    reads=pg("f2_S0f"), writes=pg("f2_smf"), tag="f2sm")
        kb.emit("vector", lambda v: v.tensor_scalar_mul(smb[:], smf[:], 1.0 / 64),
                reads=pg("f2_smf"), writes=pg("f2_smb"), tag="f2sm2")
        for fc in range(4):
            ps = self.ps[0]
            for kc in range(NC):
                kb.emit("tensor", lambda t, fc=fc, kc=kc, ps=ps: t.matmul(
                            ps[0:P, 0:1], w1[:, 8 + kc, ts(fc)], smb[:, kc:kc + 1],
                            start=(kc == 0), stop=(kc == NC - 1)),
                        reads=pg("f2_w1", "f2_smb"), writes=["PS0"], tag="f2cf")
            kb.emit("vector", lambda v, fc=fc: v.tensor_add(cfb[:, fc:fc + 1], ps[0:P, 0:1],
                                                            b1pp[:, fc:fc + 1]),
                    reads=["PS0"] + pg("f2_b1"), writes=pg("f2_cfb"), tag="f2cfe")
        kb.dma("sync", rtau[:], self.din["L2_ffn_tau"].unsqueeze(0), writes=pg("f2_rtau"))
        kb.emit("scalar", lambda s: s.activation(rtau[:], rtau[:], AF.Abs),
                reads=pg("f2_rtau"), writes=pg("f2_rtau"), tag="f2tau")
        kb.emit("vector", lambda v: v.tensor_scalar_add(rtau[:], rtau[:], 0.1),
                reads=pg("f2_rtau"), writes=pg("f2_rtau"), tag="f2tau")
        kb.emit("vector", lambda v: v.reciprocal(rtau[:], rtau[:]),
                reads=pg("f2_rtau"), writes=pg("f2_rtau"), tag="f2tau")
        kb.emit("vector", lambda v: v.tensor_copy(smb[0:1, 0:1].bitcast(BF16)[0:1, 0:1] if False else atm[0:1, 0:1], rtau[:]),
                reads=pg("f2_rtau"), writes=pg("f2_atm"), tag="f2tc")
        kb.emit("tensor", lambda t: t.matmul(self.ps[1][0:P, 0:1], self.ones_bf[0:1, 0:128],
                                             atm[0:1, 0:1], start=True, stop=True),
                reads=pg("f2_atm", "ones_bf"), writes=["PS1"], tag="f2tb")
        kb.emit("vector", lambda v: v.tensor_copy(rtaub[:], self.ps[1][0:P, 0:1]),
                reads=["PS1"], writes=pg("f2_rtaub"), tag="f2tb2")
        psSI = [self.ps[5], self.ps[6]]
        psDN = self.ps[7]
        for w in range(NW):
            kb.seg()
            wsl = slice(w * N, (w + 1) * N)
            for fc in range(4):
                b = (0, 1)[fc % 2]
                ps = self.ps[b]
                for kc in range(NC):
                    kb.emit("tensor", lambda t, fc=fc, kc=kc, ps=ps, wsl=wsl: t.matmul(
                                ps[:], w1[:, kc, ts(fc)], z_sb[:, kc, wsl],
                                start=(kc == 0), stop=(kc == NC - 1)),
                            reads=pg("f2_w1", "f2_z"), writes=["PS%d" % b], tag="f2h1")
                kb.emit("scalar", lambda s, fc=fc, ps=ps: s.activation(
                            h1w[:, fc, :], ps[:], AF.Gelu, bias=cfb[:, fc:fc + 1]),
                        reads=["PS%d" % b] + pg("f2_cfb"),
                        writes=pg(("f2_h1", fc * N * 2, (fc + 1) * N * 2)), tag="f2ge")
            b = 2
            ps = self.ps[b]
            for fc in range(4):
                kb.emit("tensor", lambda t, fc=fc, ps=ps: t.matmul(
                            ps[0:64, :], w2[:, fc, :], h1w[:, fc, :],
                            start=(fc == 0), stop=False),
                        reads=pg("f2_w2", "f2_h1"), writes=["PS2"], tag="f2rl")
            kb.emit("tensor", lambda t, ps=ps: t.matmul(
                        ps[0:64, :], b2row[:], self.ones_bf[0:1, 0:N],
                        start=False, stop=True),
                    reads=pg("f2_b2", "ones_bf"), writes=["PS2"], tag="f2rlb")
            kb.emit("scalar", lambda s, ps=ps: s.activation(rlf[:], ps[0:64, :], AF.Copy),
                    reads=["PS2"], writes=pg("f2_rlf"), tag="f2rle")
            for tt in range(4):
                gt = w * 4 + tt
                pst = self.ps[3]
                pstb = pst.bitcast(BF16)
                kb.emit("tensor", lambda t, tt=tt, pst=pst: t.transpose(
                            pst[0:P, 0:64], rlf[0:64, ts(tt)], self.ident_f32[0:64, 0:64]),
                        reads=pg("f2_rlf", "ident_f32"), writes=["PS3"], tag="f2tr")
                kb.emit("vector", lambda v, pst=pst: v.tensor_copy(rltm[:], pst[0:P, 0:64]),
                        reads=["PS3"], writes=pg("f2_rltm"), tag="f2te")
                kb.emit("vector", lambda v: v.tensor_scalar_mul(rltm[:], rltm[:], rtaub[:]),
                        reads=pg("f2_rltm", "f2_rtaub"), writes=pg("f2_rltm"), tag="f2sc")
                kb.emit("vector", lambda v: v.reduce_max(mx1[:], rltm[:], AX.X),
                        reads=pg("f2_rltm"), writes=pg("f2_mx"), tag="f2mx")
                kb.emit("vector", lambda v: v.tensor_scalar_mul(mx1[:], mx1[:], -1.0),
                        reads=pg("f2_mx"), writes=pg("f2_mx"), tag="f2ng")
                kb.emit("scalar", lambda s: s.activation(e64[:], rltm[:], AF.Exp, bias=mx1[:]),
                        reads=pg("f2_rltm", "f2_mx"), writes=pg("f2_e64"), tag="f2ex")
                kb.emit("vector", lambda v: v.max(t8[:], e64[:]),
                        reads=pg("f2_e64"), writes=pg("f2_t8"), tag="f2t8")
                kb.emit("vector", lambda v: v.reduce_sum(rd1[:], t8[:], AX.X),
                        reads=pg("f2_t8"), writes=pg("f2_rd"), tag="f2dn")
                kb.emit("vector", lambda v: v.reciprocal(rd1[:], rd1[:]),
                        reads=pg("f2_rd"), writes=pg("f2_rd"), tag="f2rc")
                kb.emit("vector", lambda v: v.tensor_scalar(rltm[:], e64[:], t8[:, 7:8], None,
                                                            ALU.is_ge),
                        reads=pg("f2_e64", "f2_t8"), writes=pg("f2_rltm"), tag="f2msk")
                kb.emit("vector", lambda v: v.tensor_mul(rltm[:], rltm[:], e64[:]),
                        reads=pg("f2_rltm", "f2_e64"), writes=pg("f2_rltm"), tag="f2am")
                kb.emit("vector", lambda v: v.tensor_scalar_mul(atm[:], rltm[:], rd1[:]),
                        reads=pg("f2_rltm", "f2_rd"), writes=pg("f2_atm"), tag="f2al")
                kb.emit("tensor", lambda t, pstb=pstb: t.transpose(
                            pstb[0:64, 0:P], atm[:], self.ident_bf[:]),
                        reads=pg("f2_atm", "ident_bf"), writes=["PS3"], tag="f2at")
                kb.emit("vector", lambda v, gt=gt, pstb=pstb: v.tensor_copy(
                            afm[0:64, gt * P:(gt + 1) * P], pstb[0:64, 0:P]),
                        reads=["PS3"], writes=pg("f2_afm"), tag="f2ae")
                p2 = tt % 2
                for c in range(NC):
                    kb.emit("tensor", lambda t, c=c, gt=gt, pstb=pstb: t.transpose(
                                pstb[0:P, 0:P], z_sb[:, c, gt * P:(gt + 1) * P],
                                self.ident_bf[:]),
                            reads=pg("f2_z", "ident_bf"), writes=["PS3"], tag="f2zt")
                    kb.emit("vector", lambda v, c=c, p2=p2, pstb=pstb: v.tensor_copy(
                                zt[p2][:, ts(c)], pstb[0:P, 0:P]),
                            reads=["PS3"], writes=pg(ztk[p2]), tag="f2ze")
                for half in range(2):
                    kb.emit("tensor", lambda t, half=half, p2=p2, gt=gt: t.matmul(
                                psSI[half][0:64, :], atm[:], zt[p2][:, ts(half, 512)],
                                start=(gt == 0), stop=(gt == 31)),
                            reads=pg("f2_atm", ztk[p2]), writes=["PS%d" % (5 + half)], tag="f2si")
                kb.emit("tensor", lambda t, gt=gt: t.matmul(
                            psDN[0:64, 0:1], atm[:], self.ones_bf[:, 0:1],
                            start=(gt == 0), stop=(gt == 31)),
                        reads=pg("f2_atm", "ones_bf"), writes=["PS7"], tag="f2dn2")
        kb.seg()
        SL = B0
        wih = self.sb("f2_wih", WIH_OFF, [P, NC, 3072], BF16)
        kb.dma("gpsimd", wih[:], self.din["h_L2f_WihT"].rearrange("(c p) f -> p c f", p=P),
               writes=pg("f2_wih"), tag="f2wih")
        sin_sm = self.sb("f2_sin", SL, [64, D], F32)
        den = self.sb("f2_den", SL + 4096, [64, 1], F32)
        sif = self.sb("f2_sif", SL + 4128, [P, NC, 64], BF16)
        gi = self.sb("f2_gi", SL + 5152, [P, 24, 64], F32)
        gh = self.sb("f2_gh", SL + 11296, [P, 24, 64], F32)
        rr = self.sb("f2_rr", SL + 17440, [P, NC, 64], F32)
        zzt = self.sb("f2_zz", SL + 19488, [P, NC, 64], F32)
        nn = self.sb("f2_nn", SL + 21536, [P, NC, 64], F32)
        snb = self.sb("f2_snb", SL + 23584, [P, NC, 64], BF16)
        qf = self.sb("f2_qf", SL + 24608, [P, 2, 64], BF16)
        kf = self.sb("f2_kf", SL + 24864, [P, 2, 64], BF16)
        vf = self.sb("f2_vf", SL + 25120, [P, NC, 64], BF16)
        vsm = self.sb("f2_vsm", SL + 26144, [64, D], BF16)
        snsm = self.sb("f2_snsm", SL + 28192, [64, D], BF16)
        ysm = self.sb("f2_ysm", SL + 30240, [64, D], F32)
        p2s = self.sb("f2_p2s", SL + 34336, [64, 64], BF16)
        p2t = self.sb("f2_p2t", SL + 34464, [64, 64], BF16)
        mu2 = self.sb("f2_mu2", SL + 34592, [64, 1], F32)
        l2t = self.sb("f2_l2", SL + 34624, [64, 1], F32)
        lngb = self.sb("f2_lngb", SL + 34656, [64, D], F32)
        lnbb = self.sb("f2_lnbb", SL + 38752, [64, D], F32)
        svs = self.sb("f2_svs", SL + 42848, [64, D], BF16)
        y2f = self.sb("f2_y2f", SL + 44896, [P, NC, 64], BF16)
        bihpp = self.load_pp_vec("f2_bih", SL + 45920, self.din["L2_ffn_bih"], 3072)
        bhhpp = self.load_pp_vec("f2_bhh", SL + 46016, self.din["L2_ffn_bhh"], 3072)
        bqpp = self.load_pp_vec("f2_bq", SL + 46112, self.din["L2_ffn_bq"], 256)
        bkpp = self.load_pp_vec("f2_bk", SL + 46144, self.din["L2_ffn_bk"], 256)
        bvpp = self.load_pp_vec("f2_bv", SL + 46176, self.din["L2_ffn_bv"], D)
        lngr = self.load_row_bf16("f2_lngr", SL + 46208, self.din["L2_ffn_ln_g"], D)
        lnbr = self.load_row_bf16("f2_lnbr", SL + 48256, self.din["L2_ffn_ln_b"], D)
        bvpr = self.load_row_bf16("f2_bvpr", SL + 50304, self.din["L2_ffn_bvp"], D)
        bor = self.load_row_bf16("f2_bor", SL + 52352, self.din["L2_ffn_bo"], D)
        wq2 = self.sb("f2_wq2", SL + 54400, [P, NC, 256], BF16)
        kb.dma("gpsimd", wq2[:], self.din["L2_ffn_wq"].rearrange("(c p) f -> p c f", p=P),
               writes=pg("f2_wq2"), tag="f2wq")
        wk2 = self.sb("f2_wk2", SL + 58496, [P, NC, 256], BF16)
        kb.dma("gpsimd", wk2[:], self.din["L2_ffn_wk"].rearrange("(c p) f -> p c f", p=P),
               writes=pg("f2_wk2"), tag="f2wk")
        wv2 = self.sb("f2_wv2", SL + 62592, [P, NC, D], BF16)
        kb.dma("gpsimd", wv2[:], self.din["L2_ffn_wv"].rearrange("(c p) f -> p c f", p=P),
               writes=pg("f2_wv2"), tag="f2wv")
        f2varscr = self.sb("f2_varscr", SL + 78976, [64, D], F32)
        f2ycast = self.sb("f2_ycast", SL + 83072, [64, D], BF16)
        f2wvp = self.sb("f2_wvp", SL + 85120, [P, NC, D], BF16)
        kb.dma("gpsimd", f2wvp[:], self.din["L2_ffn_wvp"].rearrange("(c p) f -> p c f", p=P),
               writes=pg("f2_wvp"), tag="f2wvp")
        for half in range(2):
            kb.emit("tensor", lambda t, half=half: t.matmul(
                        self.ps[0][0:64, :], self.ones_bf[0:1, 0:64], lngr[:, ts(half, 512)],
                        start=True, stop=True),
                    reads=pg("f2_lngr", "ones_bf"), writes=["PS0"], tag="f2lg")
            kb.emit("vector", lambda v, half=half: v.tensor_copy(lngb[:, ts(half, 512)],
                                                                 self.ps[0][0:64, :]),
                    reads=["PS0"], writes=pg("f2_lngb"), tag="f2lg2")
            kb.emit("tensor", lambda t, half=half: t.matmul(
                        self.ps[1][0:64, :], self.ones_bf[0:1, 0:64], lnbr[:, ts(half, 512)],
                        start=True, stop=True),
                    reads=pg("f2_lnbr", "ones_bf"), writes=["PS1"], tag="f2lb")
            kb.emit("vector", lambda v, half=half: v.tensor_copy(lnbb[:, ts(half, 512)],
                                                                 self.ps[1][0:64, :]),
                    reads=["PS1"], writes=pg("f2_lnbb"), tag="f2lb2")
        kb.emit("vector", lambda v: v.tensor_copy(den[:], psDN[0:64, 0:1]),
                reads=["PS7"], writes=pg("f2_den"), tag="f2de")
        kb.emit("vector", lambda v: v.tensor_scalar_add(den[:], den[:], 1e-8),
                reads=pg("f2_den"), writes=pg("f2_den"), tag="f2de2")
        kb.emit("vector", lambda v: v.reciprocal(den[:], den[:]),
                reads=pg("f2_den"), writes=pg("f2_den"), tag="f2de3")
        for half in range(2):
            kb.emit("vector", lambda v, half=half: v.tensor_scalar_mul(
                        sin_sm[:, ts(half, 512)], psSI[half][0:64, :], den[:]),
                    reads=["PS%d" % (5 + half)] + pg("f2_den"),
                    writes=pg("f2_sin"), tag="f2si2")
        for c in range(NC):
            pst = self.ps[2]
            kb.emit("tensor", lambda t, c=c, pst=pst: t.transpose(
                        pst[0:P, 0:64], sin_sm[0:64, ts(c)], self.ident_f32[0:64, 0:64]),
                    reads=pg("f2_sin", "ident_f32"), writes=["PS2"], tag="f2st")
            kb.emit("vector", lambda v, c=c, pst=pst: v.tensor_copy(sif[:, c, :], pst[0:P, 0:64]),
                    reads=["PS2"], writes=pg("f2_sif"), tag="f2se")
        for oc in range(24):
            b = (0, 1)[oc % 2]
            ps = self.ps[b]
            for kc in range(NC):
                kb.emit("tensor", lambda t, oc=oc, kc=kc, ps=ps: t.matmul(
                            ps[0:P, 0:64], wih[:, kc, ts(oc)], sif[:, kc, :],
                            start=(kc == 0), stop=(kc == NC - 1)),
                        reads=pg("f2_wih", "f2_sif"), writes=["PS%d" % b], tag="f2gi")
            kb.emit("vector", lambda v, oc=oc, ps=ps: v.tensor_scalar_add(
                        gi[:, oc, :], ps[0:P, 0:64], bihpp[:, oc:oc + 1]),
                    reads=["PS%d" % b] + pg("f2_bih"), writes=pg("f2_gi"), tag="f2gie")
        kb.dma("gpsimd", wih[:], self.din["h_L2f_WhhT"].rearrange("(c p) f -> p c f", p=P),
               reads=pg("f2_wih"), writes=pg("f2_wih"), tag="f2whh")
        for oc in range(24):
            b = (0, 1)[oc % 2]
            ps = self.ps[b]
            for kc in range(NC):
                kb.emit("tensor", lambda t, oc=oc, kc=kc, ps=ps: t.matmul(
                            ps[0:P, 0:64], wih[:, kc, ts(oc)], S0f[:, kc, :],
                            start=(kc == 0), stop=(kc == NC - 1)),
                        reads=pg("f2_wih", "f2_S0f"), writes=["PS%d" % b], tag="f2gh")
            kb.emit("vector", lambda v, oc=oc, ps=ps: v.tensor_scalar_add(
                        gh[:, oc, :], ps[0:P, 0:64], bhhpp[:, oc:oc + 1]),
                    reads=["PS%d" % b] + pg("f2_bhh"), writes=pg("f2_gh"), tag="f2ghe")
        for c in range(NC):
            kb.emit("vector", lambda v, c=c: v.tensor_add(rr[:, c, :], gi[:, c, :], gh[:, c, :]),
                    reads=pg("f2_gi", "f2_gh"), writes=pg("f2_rr"), tag="f2r1")
            kb.emit("scalar", lambda s, c=c: s.activation(rr[:, c, :], rr[:, c, :], AF.Sigmoid),
                    reads=pg("f2_rr"), writes=pg("f2_rr"), tag="f2r2")
            kb.emit("vector", lambda v, c=c: v.tensor_add(zzt[:, c, :], gi[:, 8 + c, :], gh[:, 8 + c, :]),
                    reads=pg("f2_gi", "f2_gh"), writes=pg("f2_zz"), tag="f2z1")
            kb.emit("scalar", lambda s, c=c: s.activation(zzt[:, c, :], zzt[:, c, :], AF.Sigmoid),
                    reads=pg("f2_zz"), writes=pg("f2_zz"), tag="f2z2")
            kb.emit("vector", lambda v, c=c: v.tensor_mul(nn[:, c, :], rr[:, c, :], gh[:, 16 + c, :]),
                    reads=pg("f2_rr", "f2_gh"), writes=pg("f2_nn"), tag="f2n1")
            kb.emit("vector", lambda v, c=c: v.tensor_add(nn[:, c, :], nn[:, c, :], gi[:, 16 + c, :]),
                    reads=pg("f2_nn", "f2_gi"), writes=pg("f2_nn"), tag="f2n2")
            kb.emit("scalar", lambda s, c=c: s.activation(nn[:, c, :], nn[:, c, :], AF.Tanh),
                    reads=pg("f2_nn"), writes=pg("f2_nn"), tag="f2n3")
            kb.emit("vector", lambda v, c=c: v.tensor_mul(rr[:, c, :], zzt[:, c, :], nn[:, c, :]),
                    reads=pg("f2_zz", "f2_nn"), writes=pg("f2_rr"), tag="f2s1")
            kb.emit("vector", lambda v, c=c: v.tensor_sub(nn[:, c, :], nn[:, c, :], rr[:, c, :]),
                    reads=pg("f2_nn", "f2_rr"), writes=pg("f2_nn"), tag="f2s2")
            kb.emit("vector", lambda v, c=c: v.tensor_mul(rr[:, c, :], zzt[:, c, :], S0f[:, c, :]),
                    reads=pg("f2_zz", "f2_S0f"), writes=pg("f2_rr"), tag="f2s3")
            kb.emit("vector", lambda v, c=c: v.tensor_add(snb[:, c, :], nn[:, c, :], rr[:, c, :]),
                    reads=pg("f2_nn", "f2_rr"), writes=pg("f2_snb"), tag="f2s4")
        for oc in range(2):
            ps = self.ps[2]
            for kc in range(NC):
                kb.emit("tensor", lambda t, oc=oc, kc=kc, ps=ps: t.matmul(
                            ps[0:P, 0:64], wq2[:, kc, ts(oc)], snb[:, kc, :],
                            start=(kc == 0), stop=(kc == NC - 1)),
                        reads=pg("f2_wq2", "f2_snb"), writes=["PS2"], tag="f2q")
            kb.emit("vector", lambda v, oc=oc, ps=ps: v.tensor_scalar_add(
                        qf[:, oc, :], ps[0:P, 0:64], bqpp[:, oc:oc + 1]),
                    reads=["PS2"] + pg("f2_bq"), writes=pg("f2_qf"), tag="f2qe")
            for kc in range(NC):
                kb.emit("tensor", lambda t, oc=oc, kc=kc, ps=ps: t.matmul(
                            ps[0:P, 0:64], wk2[:, kc, ts(oc)], snb[:, kc, :],
                            start=(kc == 0), stop=(kc == NC - 1)),
                        reads=pg("f2_wk2", "f2_snb"), writes=["PS2"], tag="f2k")
            kb.emit("vector", lambda v, oc=oc, ps=ps: v.tensor_scalar_add(
                        kf[:, oc, :], ps[0:P, 0:64], bkpp[:, oc:oc + 1]),
                    reads=["PS2"] + pg("f2_bk"), writes=pg("f2_kf"), tag="f2ke")
        for oc in range(NC):
            ps = self.ps[2]
            for kc in range(NC):
                kb.emit("tensor", lambda t, oc=oc, kc=kc, ps=ps: t.matmul(
                            ps[0:P, 0:64], wv2[:, kc, ts(oc)], snb[:, kc, :],
                            start=(kc == 0), stop=(kc == NC - 1)),
                        reads=pg("f2_wv2", "f2_snb"), writes=["PS2"], tag="f2v")
            kb.emit("vector", lambda v, oc=oc, ps=ps: v.tensor_scalar_add(
                        vf[:, oc, :], ps[0:P, 0:64], bvpp[:, oc:oc + 1]),
                    reads=["PS2"] + pg("f2_bv"), writes=pg("f2_vf"), tag="f2ve")
        ps = self.ps[3]
        for kc in range(2):
            kb.emit("tensor", lambda t, kc=kc, ps=ps: t.matmul(
                        ps[0:64, 0:64], qf[:, kc, :], kf[:, kc, :],
                        start=(kc == 0), stop=(kc == 1)),
                    reads=pg("f2_qf", "f2_kf"), writes=["PS3"], tag="f2s2m")
        kb.emit("vector", lambda v, ps=ps: v.reduce_max(mu2[:], ps[0:64, 0:64], AX.X),
                reads=["PS3"], writes=pg("f2_mu2"), tag="f2s2x")
        kb.emit("vector", lambda v: v.tensor_scalar_mul(mu2[:], mu2[:], -1.0 / 16.0),
                reads=pg("f2_mu2"), writes=pg("f2_mu2"), tag="f2s2n")
        kb.emit("scalar", lambda s, ps=ps: s.activation(p2s[0:64, :], ps[0:64, 0:64], AF.Exp,
                                                        bias=mu2[:], scale=1.0 / 16.0,
                                                        accum_out=l2t[:]),
                reads=["PS3"] + pg("f2_mu2"), writes=pg("f2_p2s", "f2_l2"), tag="f2s2e")
        kb.emit("vector", lambda v: v.reciprocal(l2t[:], l2t[:]),
                reads=pg("f2_l2"), writes=pg("f2_l2"), tag="f2s2r")
        kb.emit("vector", lambda v: v.tensor_scalar_mul(p2s[0:64, :], p2s[0:64, :], l2t[:]),
                reads=pg("f2_p2s", "f2_l2"), writes=pg("f2_p2s"), tag="f2s2d")
        pstb = self.ps[3].bitcast(BF16)
        kb.emit("tensor", lambda t, pstb=pstb: t.transpose(
                    pstb[0:64, 0:64], p2s[0:64, 0:64], self.ident_bf[0:64, 0:64]),
                reads=pg("f2_p2s", "ident_bf"), writes=["PS3"], tag="f2s2t")
        kb.emit("vector", lambda v, pstb=pstb: v.tensor_copy(p2t[0:64, :], pstb[0:64, 0:64]),
                reads=["PS3"], writes=pg("f2_p2t"), tag="f2s2te")
        for c in range(NC):
            kb.emit("tensor", lambda t, c=c, pstb=pstb: t.transpose(
                        pstb[0:64, 0:P], vf[:, c, :], self.ident_bf[:]),
                    reads=pg("f2_vf", "ident_bf"), writes=["PS3"], tag="f2vt")
            kb.emit("vector", lambda v, c=c, pstb=pstb: v.tensor_copy(vsm[:, ts(c)], pstb[0:64, 0:P]),
                    reads=["PS3"], writes=pg("f2_vsm"), tag="f2vte")
            kb.emit("tensor", lambda t, c=c, pstb=pstb: t.transpose(
                        pstb[0:64, 0:P], snb[:, c, :], self.ident_bf[:]),
                    reads=pg("f2_snb", "ident_bf"), writes=["PS3"], tag="f2nt")
            kb.emit("vector", lambda v, c=c, pstb=pstb: v.tensor_copy(snsm[:, ts(c)], pstb[0:64, 0:P]),
                    reads=["PS3"], writes=pg("f2_snsm"), tag="f2nte")
        for half in range(2):
            ps = self.ps[4]
            kb.emit("tensor", lambda t, half=half, ps=ps: t.matmul(
                        ps[0:64, :], p2t[0:64, :], vsm[:, ts(half, 512)],
                        start=True, stop=True),
                    reads=pg("f2_p2t", "f2_vsm"), writes=["PS4"], tag="f2cm")
            kb.emit("vector", lambda v, half=half, ps=ps: v.tensor_add(
                        ysm[:, ts(half, 512)], ps[0:64, :], snsm[0:64, ts(half, 512)]),
                    reads=["PS4"] + pg("f2_snsm"), writes=pg("f2_ysm"), tag="f2y")
        kb.emit("vector", lambda v: v.reduce_sum(mu2[:], ysm[0:64, :], AX.X),
                reads=pg("f2_ysm"), writes=pg("f2_mu2"), tag="f2l1")
        kb.emit("vector", lambda v: v.tensor_scalar_mul(mu2[:], mu2[:], 1.0 / D),
                reads=pg("f2_mu2"), writes=pg("f2_mu2"), tag="f2l2b")
        kb.emit("vector", lambda v: v.tensor_scalar(ysm[0:64, :], ysm[0:64, :], mu2[:], None,
                                                    ALU.subtract),
                reads=pg("f2_ysm", "f2_mu2"), writes=pg("f2_ysm"), tag="f2l3")
        kb.emit("vector", lambda v: v.tensor_mul(f2varscr[0:64, :], ysm[0:64, :], ysm[0:64, :]),
                reads=pg("f2_ysm"), writes=pg("f2_varscr"), tag="f2l4a")
        kb.emit("vector", lambda v: v.reduce_sum(l2t[:], f2varscr[0:64, :], AX.X),
                reads=pg("f2_varscr"), writes=pg("f2_l2"), tag="f2l4b")
        kb.emit("scalar", lambda s: s.activation(l2t[:], l2t[:], AF.Ln, bias=self.C_EPS5[0:64, :],
                                                 scale=1.0 / D),
                reads=pg("f2_l2", "consts"), writes=pg("f2_l2"), tag="f2l5")
        kb.emit("scalar", lambda s: s.activation(l2t[:], l2t[:], AF.Exp, scale=-0.5),
                reads=pg("f2_l2"), writes=pg("f2_l2"), tag="f2l6")
        kb.emit("vector", lambda v: v.tensor_scalar_mul(ysm[0:64, :], ysm[0:64, :], l2t[:]),
                reads=pg("f2_ysm", "f2_l2"), writes=pg("f2_ysm"), tag="f2l7")
        kb.emit("vector", lambda v: v.tensor_mul(ysm[0:64, :], ysm[0:64, :], lngb[0:64, :]),
                reads=pg("f2_ysm", "f2_lngb"), writes=pg("f2_ysm"), tag="f2l8")
        kb.emit("vector", lambda v: v.tensor_add(ysm[0:64, :], ysm[0:64, :], lnbb[0:64, :]),
                reads=pg("f2_ysm", "f2_lnbb"), writes=pg("f2_ysm"), tag="f2l9")
        kb.emit("vector", lambda v: v.tensor_copy(f2ycast[0:64, :], ysm[0:64, :]),
                reads=pg("f2_ysm"), writes=pg("f2_ycast"), tag="f2yc")
        for c in range(NC):
            pst = self.ps[2]
            kb.emit("tensor", lambda t, c=c, pst=pst: t.transpose(
                        pst.bitcast(BF16)[0:P, 0:64], f2ycast[0:64, ts(c)],
                        self.ident_bf[0:64, 0:64]),
                    reads=pg("f2_ycast", "ident_bf"), writes=["PS2"], tag="f2yt")
            kb.emit("vector", lambda v, c=c, pst=pst: v.tensor_copy(y2f[:, c, :],
                                                                    pst.bitcast(BF16)[0:P, 0:64]),
                    reads=["PS2"], writes=pg("f2_y2f"), tag="f2ye")
        for half in range(2):
            ps = self.ps[4]
            for kc in range(NC):
                kb.emit("tensor", lambda t, half=half, kc=kc, ps=ps: t.matmul(
                            ps[0:64, :], y2f[:, kc, :], f2wvp[:, kc, ts(half, 512)],
                            start=(kc == 0), stop=False),
                        reads=pg("f2_wvp", "f2_y2f"), writes=["PS4"], tag="f2sv")
            kb.emit("tensor", lambda t, half=half, ps=ps: t.matmul(
                        ps[0:64, :], self.ones_bf[0:1, 0:64], bvpr[:, ts(half, 512)],
                        start=False, stop=True),
                    reads=pg("f2_bvpr", "ones_bf"), writes=["PS4"], tag="f2svb")
            kb.emit("vector", lambda v, half=half, ps=ps: v.tensor_copy(
                        svs[:, ts(half, 512)], ps[0:64, :]),
                    reads=["PS4"], writes=pg("f2_svs"), tag="f2sve")
        kb.seg()
        OW2 = WIH_OFF
        wo2 = self.sb("f2_wo2", OW2, [P, NC, D], BF16)
        kb.dma("gpsimd", wo2[:], self.din["L2_ffn_wo"].rearrange("(c p) f -> p c f", p=P),
               reads=pg("f2_wih"), writes=pg("f2_wo2"), tag="f2wo")
        ow2 = self.sb("f2_ow", OW2 + 16384, [P, NC, N], BF16)
        xw2 = self.sb("f2_xw", OW2 + 24576, [P, NC, N], F32)
        for w in range(NW):
            kb.seg()
            wsl = slice(w * N, (w + 1) * N)
            for oc in range(NC):
                b = (0, 1)[oc % 2]
                ps = self.ps[b]
                kb.emit("tensor", lambda t, oc=oc, ps=ps, wsl=wsl: t.matmul(
                            ps[:], svs[0:64, ts(oc)], afm[0:64, wsl],
                            start=True, stop=True),
                        reads=pg("f2_svs", "f2_afm"), writes=["PS%d" % b], tag="f2o")
                kb.emit("scalar", lambda s, oc=oc, ps=ps: s.activation(ow2[:, oc, :], ps[:], AF.Copy),
                        reads=["PS%d" % b], writes=pg(("f2_ow", oc * N * 2, (oc + 1) * N * 2)),
                        tag="f2oe")
            kb.dma("scalar", xw2[:], xsrc.rearrange("(c p) t -> p c t", p=P)[:, :, wsl],
                   reads=[ksrc + ":%d" % w], writes=pg("f2_xw"), tag="f2x")
            for oc in range(NC):
                b = (2, 3)[oc % 2]
                ps = self.ps[b]
                for kc in range(NC):
                    kb.emit("tensor", lambda t, oc=oc, kc=kc, ps=ps: t.matmul(
                                ps[:], wo2[:, kc, ts(oc)], ow2[:, kc, :],
                                start=(kc == 0), stop=False),
                            reads=pg("f2_wo2", "f2_ow"), writes=["PS%d" % b], tag="f2f")
                kb.emit("tensor", lambda t, oc=oc, ps=ps: t.matmul(
                            ps[:], bor[:, ts(oc)], self.ones_bf[0:1, 0:N],
                            start=False, stop=True),
                        reads=pg("f2_bor", "ones_bf"), writes=["PS%d" % b], tag="f2fb")
                kb.emit("vector", lambda v, oc=oc, ps=ps: v.scalar_tensor_tensor(
                            xw2[:, oc, :], ps[:], 0.5, xw2[:, oc, :], ALU.mult, ALU.add),
                        reads=["PS%d" % b] + pg("f2_xw"), writes=pg("f2_xw"), tag="f2res")
            self.x_writeback(w, xw2, xdst, kdst, "f2_xw")

    def stats_of(self, src, srckey, mean, meank, A, Ak, bS, bQ, eps=1e-5, ncs=NC):
        kb, pg = self.kb, self.pg
        psS, psQ = self.ps[bS], self.ps[bQ]
        stsq = self.stsq
        for c in range(ncs):
            p2 = c % 2
            kb.emit("scalar", lambda s, c=c, p2=p2, src=src: s.activation(stsq[p2][:], src[:, c, :], AF.Square),
                    reads=pg(srckey), writes=pg("stsq%d" % p2), tag="st_sq")
            kb.emit("tensor", lambda t, c=c, src=src: t.matmul(psS[:], self.ones_bf[:, 0:128], src[:, c, :],
                                                               start=(c == 0), stop=(c == ncs - 1)),
                    reads=pg(srckey, "ones_bf"), writes=["PS%d" % bS], tag="st_mmS")
            kb.emit("tensor", lambda t, c=c, p2=p2: t.matmul(psQ[:], self.ones_bf[:, 0:128], stsq[p2][:],
                                                             start=(c == 0), stop=(c == ncs - 1)),
                    reads=pg("stsq%d" % p2, "ones_bf"), writes=["PS%d" % bQ], tag="st_mmQ")
        kb.emit("vector", lambda v: v.tensor_scalar_mul(mean[:], psS[:], 1.0 / (ncs * P)),
                reads=["PS%d" % bS], writes=pg(meank), tag="st_mean")
        kb.emit("vector", lambda v: v.tensor_mul(A[:], mean[:], mean[:]),
                reads=pg(meank), writes=pg(Ak), tag="st_msq")
        kb.emit("vector", lambda v: v.scalar_tensor_tensor(A[:], psQ[:], 1.0 / (ncs * P), A[:],
                                                           ALU.mult, ALU.subtract),
                reads=["PS%d" % bQ] + pg(Ak), writes=pg(Ak), tag="st_var")
        epst = {1e-5: self.C_EPS5, 1e-6: self.C_EPS6, 1e-3: self.C_EPS3}[eps]
        kb.emit("scalar", lambda s: s.activation(A[:], A[:], AF.Ln, bias=epst),
                reads=pg(Ak, "consts"), writes=pg(Ak), tag="st_ln")
        kb.emit("scalar", lambda s: s.activation(A[:], A[:], AF.Exp, scale=-0.5),
                reads=pg(Ak), writes=pg(Ak), tag="st_exp")

    def finalize(self):
        kb = self.kb
        if kb.dma_count["sync"]:
            nd = kb.dma_count["sync"]
            kb.emit("sync", lambda e: e.nop(), reads=[], writes=[], tag="fin")
            kb.ops["sync"][-1]["waits"].append(("d", "sync", nd - 1))
        if kb.dma_count["gpsimd"]:
            nd = kb.dma_count["gpsimd"]
            kb.emit("gpsimd", lambda e: e.nop(), tag="fin")
            kb.ops["gpsimd"][-1]["waits"].append(("d", "gpsimd", nd - 1))
        if kb.dma_count["scalar"]:
            nd = kb.dma_count["scalar"]
            kb.emit("scalar", lambda e: e.nop(), tag="fin")
            kb.ops["scalar"][-1]["waits"].append(("d", "scalar", nd - 1))
        return kb.finalize(self.es)


ALL_SPECS = [
    ("xin", [D, T], F32), ("c_ident", [128, 128], F32),
    ("c_hpar", [16, 128], F32), ("c_hmask", [16, 8], F32),
    ("L0_ffn_theta", [D, D], F32), ("L0_ffn_phi", [D, D], F32),
    ("L0_ffn_theta_noise", [D, D], F32), ("L0_ffn_phi_noise", [D, D], F32),
    ("h_L0_WqT", [D, D], F32), ("h_L0_WkT", [D, D], F32),
    ("h_L0_WvT", [D, D], F32), ("h_L0_WoT", [D, D], F32),
    ("h_L1f_qwT", [D, D], F32), ("L1_ffn_q_b", [D], F32),
    ("h_L1f_kwT", [D, D], F32), ("L1_ffn_k_b", [D], F32),
    ("h_L1f_vwT", [D, D], F32), ("L1_ffn_v_b", [D], F32),
    ("h_L1f_owT", [D, D], F32), ("L1_ffn_o_b", [D], F32),
    ("L1_ffn_norm_w", [D], F32), ("L1_ffn_norm_b", [D], F32),
    ("L1_attn_Wq", [D, 16, 64], F32), ("L1_attn_bq", [16, 64], F32),
    ("L1_attn_Wk", [D, 16, 64], F32), ("L1_attn_bk", [16, 64], F32),
    ("L1_attn_Wv", [D, 16, 64], F32), ("L1_attn_bv", [16, 64], F32),
    ("L1_attn_Wo", [16, 64, D], F32), ("L1_attn_bo", [D], F32),
    ("L1_attn_gamma", [D], F32), ("L1_attn_beta", [D], F32),
    ("L2_ffn_slot_init", [64, D], F32), ("h_L2f_S0T", [D, 64], F32),
    ("L2_ffn_w1", [2 * D, 512], F32), ("L2_ffn_b1", [512], F32),
    ("L2_ffn_w2", [512, 64], F32), ("L2_ffn_b2", [64], F32),
    ("L2_ffn_tau", [1], F32),
    ("h_L2f_WihT", [D, 3 * D], F32), ("h_L2f_WhhT", [D, 3 * D], F32),
    ("L2_ffn_bih", [3 * D], F32), ("L2_ffn_bhh", [3 * D], F32),
    ("L2_ffn_wq", [D, 256], F32), ("L2_ffn_bq", [256], F32),
    ("L2_ffn_wk", [D, 256], F32), ("L2_ffn_bk", [256], F32),
    ("L2_ffn_wv", [D, D], F32), ("L2_ffn_bv", [D], F32),
    ("L2_ffn_ln_g", [D], F32), ("L2_ffn_ln_b", [D], F32),
    ("L2_ffn_wvp", [D, D], F32), ("L2_ffn_bvp", [D], F32),
    ("L2_ffn_wo", [D, D], F32), ("L2_ffn_bo", [D], F32),
    ("h_L2_WqT", [D, D], F32), ("h_L2_WvT", [D, D], F32),
    ("h_L2_WgT", [D, D], F32), ("h_L2_WoT", [D, D], F32),
    ("L2_attn_beta", [16], F32),
    ("L2_attn_ln_gamma", [D], F32), ("L2_attn_ln_beta", [D], F32),
    ("h_L3f_gwT", [D, 8], F32),
    ("h_L3f_wgT", [8, D, 512], F32), ("h_L3f_wuT", [8, D, 512], F32),
    ("h_L3f_wdT", [8, 512, D], F32),
    ("h_L3f_sgT", [D, 512], F32), ("h_L3f_suT", [D, 512], F32),
    ("h_L3f_sdT", [512, D], F32),
    ("L3_attn_norm_w", [D], F32), ("h_L3_convw", [3, D], F32),
    ("L3_attn_conv_b", [D], F32), ("L3_attn_a", [D], F32),
    ("L3_attn_b", [D], F32), ("L3_attn_mlp_norm_w", [D], F32),
    ("L3_attn_w_gate", [D, 4096], F32), ("L3_attn_b_gate", [4096], F32),
    ("L3_attn_w_up", [D, 4096], F32), ("L3_attn_b_up", [4096], F32),
    ("L3_attn_w_down", [4096, D], F32), ("L3_attn_b_down", [D], F32),
]


def build_full(upto=12):
    m = MK(upto=upto)
    m.setup(ALL_SPECS)
    m.x_cur = ("xin", m.din["xin"])
    steps = []
    def f0():
        C_sb, S_sb = m.prep_trig(m.PHASE)
        m.sub_l0ffn(C_sb, S_sb)
    steps.append(f0)
    steps.append(lambda: m.sub_attn_generic('A0'))
    steps.append(f0)
    steps.append(lambda: m.sub_attn_generic('F1'))
    steps.append(lambda: m.sub_attn_generic('A1'))
    steps.append(lambda: m.sub_attn_generic('F1'))
    steps.append(lambda: m.sub_l2ffn())
    steps.append(lambda: m.sub_l2attn())
    steps.append(lambda: m.sub_l2ffn())
    steps.append(lambda: m.sub_l3ffn())
    steps.append(lambda: m.sub_l3attn())
    steps.append(lambda: m.sub_l3ffn())
    for i in range(min(upto, 12)):
        steps[i]()
        m.kb.seg()
    m.finalize()
    return m


def prep_inputs(inputs):
    f = np.ascontiguousarray
    d = dict(inputs)
    w = {}
    w["c_ident"] = np.eye(128, dtype=np.float32)
    hpar = np.zeros((16, 128), np.float32)
    for h in range(16):
        hpar[h, :] = [1.0 if h % 2 == p // 64 else 0.0 for p in range(128)]
    w["c_hpar"] = hpar
    hmask = np.zeros((16, 8), np.float32)
    for h in range(16):
        hmask[h, h // 2] = 1.0
    w["c_hmask"] = hmask
    W3 = d["L0_attn_W_qkv"].reshape(D, 3, D)
    w["h_L0_WqT"] = f(W3[:, 0, :].T)
    w["h_L0_WkT"] = f(W3[:, 1, :].T)
    w["h_L0_WvT"] = f(W3[:, 2, :].T)
    w["h_L0_WoT"] = f(d["L0_attn_W_out"].T)
    w["h_L1f_qwT"] = f(d["L1_ffn_q_w"].T)
    w["h_L1f_kwT"] = f(d["L1_ffn_k_w"].T)
    w["h_L1f_vwT"] = f(d["L1_ffn_v_w"].T)
    w["h_L1f_owT"] = f(d["L1_ffn_o_w"].T)
    w["h_L2f_S0T"] = f(d["L2_ffn_slot_init"].T)
    w["h_L2f_WihT"] = f(d["L2_ffn_Wih"].T)
    w["h_L2f_WhhT"] = f(d["L2_ffn_Whh"].T)
    w["h_L2_WqT"] = f(d["L2_attn_Wq"].T)
    w["h_L2_WvT"] = f(d["L2_attn_Wv"].T)
    w["h_L2_WgT"] = f(d["L2_attn_Wg"].T)
    w["h_L2_WoT"] = f(d["L2_attn_Wo"].T)
    w["h_L3f_gwT"] = f(d["L3_ffn_gate_w"].T)
    w["h_L3f_wgT"] = f(d["L3_ffn_wg"].transpose(0, 2, 1))
    w["h_L3f_wuT"] = f(d["L3_ffn_wu"].transpose(0, 2, 1))
    w["h_L3f_wdT"] = f(d["L3_ffn_wd"].transpose(0, 2, 1))
    w["h_L3f_sgT"] = f(d["L3_ffn_sg"].T)
    w["h_L3f_suT"] = f(d["L3_ffn_su"].T)
    w["h_L3f_sdT"] = f(d["L3_ffn_sd"].T)
    w["h_L3_convw"] = f(d["L3_attn_conv_w"][:, 0, :].T)
    for name, shape, _dt in ALL_SPECS:
        if name.startswith("h_") or name.startswith("c_") or name == "xin":
            continue
        w[name] = np.ascontiguousarray(d[name]).astype(np.float32)
    xins = [f(d["L0_attn_x"][b].T) for b in range(d["L0_attn_x"].shape[0])]
    return w, xins


from concourse.bass_utils import run_bass_kernel_spmd

_CACHED = {}


def kernel(**inputs):
    assert inputs["L1_ffn_router_prob"].shape[-1] == 4
    w, xins = prep_inputs(inputs)
    m = build_full(12)
    in_maps = [dict(w) for _ in range(8)]
    for b in range(8):
        in_maps[b]["xin"] = xins[b]
    res = run_bass_kernel_spmd(m.nc, in_maps, core_ids=list(range(8)))
    out = np.stack([np.ascontiguousarray(res.results[b]["xout"].T) for b in range(8)])
    return out


# version 03
# speedup vs baseline: 1.1715x; 1.1715x over previous
import numpy as np
import concourse.bass as bass
import concourse.mybir as mybir

F32 = mybir.dt.float32
BF16 = mybir.dt.bfloat16
AF = mybir.ActivationFunctionType
ALU = mybir.AluOpType
AX = mybir.AxisListType

ENGINES = ("tensor", "vector", "scalar", "gpsimd", "sync")
P = 128


class KB:
    SEG_LIMIT = 1500

    def __init__(self, nc):
        self.nc = nc
        self.ops = {e: [] for e in ENGINES}
        self.dma_count = {e: 0 for e in ENGINES}
        self.buffers = {}
        self.watermark = {e: {} for e in ENGINES}
        self.n_pruned = 0
        self.segment = 0
        self._seg_ops = 0

    def seg(self):
        if self._seg_ops > self.SEG_LIMIT:
            self.segment += 1
            self._seg_ops = 0

    def emit(self, eng, fn, reads=(), writes=(), dma=False, tag=""):
        assert eng in ENGINES
        idx = len(self.ops[eng])
        if dma:
            tok = ("d", eng, self.dma_count[eng])
            self.dma_count[eng] += 1
        else:
            tok = ("c", eng, idx)
        waits = []

        def add_wait(t):
            kind, peng, pidx = t
            if kind == "c" and peng == eng and peng == "tensor":
                return
            if kind == "c" and peng == eng and pidx == idx:
                return
            waits.append(t)

        for b in reads:
            ent = self.buffers.get(b)
            if ent is not None and ent["writer"] is not None:
                add_wait(ent["writer"])
        for b in writes:
            ent = self.buffers.get(b)
            if ent is not None:
                if ent["writer"] is not None:
                    add_wait(ent["writer"])
                for rt in ent["readers"]:
                    kind, peng, pidx = rt
                    if kind == "c" and peng == eng:
                        continue
                    waits.append(rt)
        for b in reads:
            ent = self.buffers.setdefault(b, {"writer": None, "readers": []})
            ent["readers"].append(tok)
        for b in writes:
            ent = self.buffers.setdefault(b, {"writer": None, "readers": []})
            ent["writer"] = tok
            ent["readers"] = []
        wm = self.watermark[eng]
        final_waits = []
        seen = {}
        for (kind, peng, pidx) in waits:
            key = (kind, peng)
            if key in seen:
                seen[key] = max(seen[key], pidx)
            else:
                seen[key] = pidx
        for key, pidx in seen.items():
            if wm.get(key, -1) >= pidx:
                self.n_pruned += 1
                continue
            wm[key] = pidx
            final_waits.append((key[0], key[1], pidx))
        self.ops[eng].append(dict(fn=fn, waits=final_waits, dma=dma, tag=tag, tok=tok,
                                  seg=self.segment))
        self._seg_ops += 1
        return tok

    def dma(self, q, out, in_, reads=(), writes=(), tag="", **kw):
        rd = list(reads)
        return self.emit(q, lambda e: e.dma_start(out, in_, **kw),
                         reads=rd, writes=writes, dma=True, tag=tag)

    def finalize(self, es):
        nc = self.nc
        for e in ENGINES:
            for op in self.ops[e]:
                for (kind, peng, pidx) in op["waits"]:
                    if kind == "c":
                        self.ops[peng][pidx]["inc"] = True
        self.cval = {}
        for e in ENGINES:
            c = 0
            vals = []
            for op in self.ops[e]:
                if op.get("inc") and not op["dma"]:
                    c += 1
                vals.append(c)
            self.cval[e] = vals
        self.csem = {}
        self.dsem = {}
        for e in ENGINES:
            if any((not op["dma"]) and op.get("inc") for op in self.ops[e]):
                self.csem[e] = es.enter_context(nc.semaphore("sem_c_" + e))
            if self.dma_count[e]:
                self.dsem[e] = es.enter_context(nc.semaphore("sem_d_" + e))

        stats = {e: (len(self.ops[e]), sum(1 for o in self.ops[e] if o.get("inc")),
                     sum(len(o["waits"]) for o in self.ops[e]))
                 for e in ENGINES}
        self.stats = stats
        allsems = list(self.csem.values()) + list(self.dsem.values())
        for s in allsems:
            nc.gpsimd.sem_clear(s)
        nc.all_engine_barrier()
        wu = nc.alloc_sbuf_tensor_at("warmup_tile", [128, 512], mybir.dt.float32,
                                     offset=Mem.BASE + Mem.LIMIT - 2048)
        wua = wu.ap()
        nc.vector.memset(wua[:], 1.0)
        for i in range(24):
            nc.scalar.activation(wua[:, 0:256], wua[:, 0:256], AF.Exp, scale=0.0)
            nc.vector.tensor_copy(wua[:, 256:512], wua[:, 256:512])
            nc.gpsimd.memset(wua[:, 0:64], 1.0)
        nc.all_engine_barrier()

        kb = self
        nseg = self.segment + 1
        seg_ops = {e: [[] for _ in range(nseg)] for e in ENGINES}
        for e in ENGINES:
            for op in self.ops[e]:
                seg_ops[e][op["seg"]].append(op)

        def make_section(e, ops):
            def section(engobj):
                for op in ops:
                    for (kind, peng, pidx) in op["waits"]:
                        if kind == "c":
                            engobj.wait_ge(kb.csem[peng], kb.cval[peng][pidx])
                        else:
                            engobj.wait_ge(kb.dsem[peng], 16 * (pidx + 1))
                    inst = op["fn"](engobj)
                    if op["dma"]:
                        inst.then_inc(kb.dsem[e], 16)
                    elif op.get("inc"):
                        if e == "tensor":
                            inst.then_inc(kb.csem[e], 1)
                        else:
                            engobj.drain(fusable=True).then_inc(kb.csem[e], 1)
            return section

        for si in range(nseg):
            with nc.Block() as block:
                for e in ENGINES:
                    if seg_ops[e][si]:
                        getattr(block, e)(make_section(e, seg_ops[e][si]))
        return None


class Mem:
    BASE = 16512
    LIMIT = 229216 - 16512

    def __init__(self, nc):
        self.nc = nc
        self.tensors = {}

    def at(self, name, offset, shape, dtype):
        assert shape[0] <= 128
        nbytes = int(np.prod(shape[1:])) * mybir.dt.size(dtype)
        assert offset + nbytes <= self.LIMIT, (name, offset, nbytes)
        h = self.nc.alloc_sbuf_tensor_at(name, list(shape), dtype,
                                         offset=self.BASE + offset)
        ap = h.ap()
        self.tensors[name] = (offset, nbytes, ap)
        return ap


"""Megakernel builder for mk_macaron_d4 (per-core: T=4096 tokens, D=1024).

Layout conventions:
  - feature-major (fm): [feature chunk c (128 partitions), T free]
  - token-major (tm): [token (128 partitions), features free]
  - residual x lives in DRAM scratch (fp32, fm, [1024, 4096]), ping-ponged
    between d_xA / d_xB per sublayer; final sublayer writes ExternalOutput.
  - 8 token windows of N=512 per core.

Host prep (layout only): x transposed to fm; several weights transposed so
their contraction dim is first (see prep_inputs).
"""
import contextlib
import numpy as np

import concourse.bass as bass
import concourse.mybir as mybir

T = 4096
D = 1024
NW = 8
N = T // NW
NC = D // P

LN2 = 0.6931471805599453
PI = 3.141592653589793


def ts(i, n=128):
    return slice(i * n, (i + 1) * n)


class MK:
    def __init__(self, upto=99):
        self.nc = bass.Bass("TRN2", target_bir_lowering=False)
        self.es = contextlib.ExitStack()
        self.kb = KB(self.nc)
        self.mem = Mem(self.nc)
        self.upto = upto
        self.sub_idx = 0

    def sb(self, name, offset, shape, dtype):
        key = name if self.sub_idx == 0 else "%s_u%d" % (name, self.sub_idx)
        ap = self.mem.at(key, offset, shape, dtype)
        nbytes = int(np.prod(shape[1:])) * mybir.dt.size(dtype)
        self._pagemap[key] = (offset, nbytes)
        return ap

    def pages(self, name, lo=None, hi=None):
        key = "%s_u%d" % (name, self.sub_idx)
        if key not in self._pagemap:
            key = name
        off, nb = self._pagemap[key]
        a = off + (lo if lo is not None else 0)
        b = off + (hi if hi is not None else nb)
        return ["sb%d" % p for p in range(a // 1024, (b + 1023) // 1024)]

    def pg(self, *names):
        out = []
        for n in names:
            if isinstance(n, tuple):
                out += self.pages(*n)
            elif ("%s_u%d" % (n, self.sub_idx)) in self._pagemap or n in self._pagemap:
                out += self.pages(n)
            else:
                out.append(n)
        return out

    def setup(self, input_specs):
        nc = self.nc
        self._pagemap = {}
        self.din = {}
        for name, shape, dt in input_specs:
            self.din[name] = nc.dram_tensor(name, list(shape), dt, kind="ExternalInput").ap()
        self.xout = nc.dram_tensor("xout", [D, T], F32, kind="ExternalOutput").ap()
        self.d_xA = nc.dram_tensor("d_xA", [D, T], F32, kind="Internal").ap()
        self.d_xB = nc.dram_tensor("d_xB", [D, T], F32, kind="Internal").ap()
        self.d_q = nc.dram_tensor("d_q", [D, T], BF16, kind="Internal").ap()
        self.d_z = nc.dram_tensor("d_z", [D, T], BF16, kind="Internal").ap()
        self.d_v = nc.dram_tensor("d_v", [T, D], BF16, kind="Internal").ap()
        self.d_cw = nc.dram_tensor("d_cw", [8, T], F32, kind="Internal").ap()

        self.ps = []
        for i in range(8):
            h = nc.alloc_psum_tensor("psb%d" % i, [P, 512], F32)
            self.ps.append(h.ap())

        self.ones_bf = self.sb("ones_bf", 0, [P, 512], BF16)
        self.consts = self.sb("consts", 1024, [P, 16], F32)
        self.C_EPS5 = self.consts[:, 0:1]
        self.C_EPS6 = self.consts[:, 1:2]
        self.C_EPS3 = self.consts[:, 2:3]
        self.C_HALFPI = self.consts[:, 3:4]
        self.C_LNHALF = self.consts[:, 4:5]
        self.C_EPS8 = self.consts[:, 5:6]
        self.C_ZERO = self.consts[:, 6:7]
        kb = self.kb
        kb.emit("gpsimd", lambda g: g.memset(self.ones_bf[:], 1.0), writes=self.pg("ones_bf"))
        for col, val in ((0, 1e-5), (1, 1e-6), (2, 1e-3), (3, PI / 2), (4, -LN2), (5, 1e-8), (6, 0.0)):
            kb.emit("gpsimd", lambda g, c=col, v=val: g.memset(self.consts[:, c:c + 1], v),
                    writes=self.pg("consts"))

        self.ones_f32 = self.sb("ones_f32", 1088, [P, P], F32)
        kb.emit("gpsimd", lambda g: g.memset(self.ones_f32[:], 1.0), writes=self.pg("ones_f32"))
        self.ident_f32 = self.sb("ident_f32", 1600, [P, P], F32)
        kb.dma("sync", self.ident_f32[:], self.din["c_ident"][:], writes=self.pg("ident_f32"))
        self.ident_bf = self.sb("ident_bf", 4928, [P, P], BF16)
        kb.dma("gpsimd", self.ident_bf[:], self.din["c_ident"][:], writes=self.pg("ident_bf"))
        self.stsq = [self.sb("stsq0", 2112, [P, N], BF16), self.sb("stsq1", 3136, [P, N], BF16)]
        self.pptmp = self.sb("pptmp", 4160, [32, P], F32)
        self.PHASE = 5248

    def ln0_window(self, w, xsrc_dram, xw, zb, st, key_src):
        kb, pg = self.kb, self.pg
        wsl = slice(w * N, (w + 1) * N)
        xwk, zbk = st["xw_key"], st["zb_key"]
        kb.dma("sync", xw[:], xsrc_dram.rearrange("(c p) t -> p c t", p=P)[:, :, wsl],
               reads=[key_src + ":%d" % w], writes=pg(xwk), tag="ln_ldx")
        bS, bQ = st["bS"], st["bQ"]
        psS, psQ = self.ps[bS], self.ps[bQ]
        xb, rb = st["xb"], st["rb"]
        xbk, rbk = st["xb_key"], st["rb_key"]
        kb.emit("vector", lambda v: v.tensor_copy(xb.rearrange("p c t -> p (c t)"),
                                                  xw.rearrange("p c t -> p (c t)")),
                reads=pg(xwk), writes=pg(xbk), tag="ln_cast")
        kb.emit("vector", lambda v: v.tensor_sub(rb.rearrange("p c t -> p (c t)"),
                                                 xw.rearrange("p c t -> p (c t)"),
                                                 xb.rearrange("p c t -> p (c t)")),
                reads=pg(xwk, xbk), writes=pg(rbk), tag="ln_resid")
        sq = st["sq"]
        sqk = st["sq_key"]
        ones = self.ones_bf[:, 0:128]
        for c in range(NC):
            p2 = c % 2
            kb.emit("scalar", lambda s, c=c, p2=p2: s.activation(sq[p2][:], xb[:, c, :], AF.Square),
                    reads=pg(xbk), writes=pg(sqk[p2]), tag="ln_sq")
            kb.emit("tensor", lambda t, c=c: t.matmul(psS[:], ones, xb[:, c, :],
                                                      start=(c == 0), stop=False),
                    reads=pg(xbk, "ones_bf"), writes=["PS%d" % bS], tag="ln_mmS")
            kb.emit("tensor", lambda t, c=c: t.matmul(psS[:], ones, rb[:, c, :],
                                                      start=False, stop=(c == NC - 1)),
                    reads=pg(rbk, "ones_bf"), writes=["PS%d" % bS], tag="ln_mmSr")
            kb.emit("tensor", lambda t, c=c, p2=p2: t.matmul(psQ[:], ones, sq[p2][:],
                                                             start=(c == 0), stop=(c == NC - 1)),
                    reads=pg(sqk[p2], "ones_bf"), writes=["PS%d" % bQ], tag="ln_mmQ")
        mean, A, v1 = st["mean"], st["A"], st["v1"]
        meank, Ak, v1k = st["mean_key"], st["A_key"], st["v1_key"]
        kb.emit("vector", lambda v: v.tensor_scalar_mul(mean[:], psS[:], 1.0 / D),
                reads=["PS%d" % bS], writes=pg(meank), tag="ln_mean")
        kb.emit("vector", lambda v: v.tensor_mul(v1[:], mean[:], mean[:]),
                reads=pg(meank), writes=pg(v1k), tag="ln_msq")
        kb.emit("vector", lambda v: v.scalar_tensor_tensor(v1[:], psQ[:], 1.0 / D, v1[:],
                                                           ALU.mult, ALU.subtract),
                reads=["PS%d" % bQ] + pg(v1k), writes=pg(v1k), tag="ln_var")
        kb.emit("scalar", lambda s: s.activation(v1[:], v1[:], AF.Ln, bias=self.C_EPS5),
                reads=pg(v1k, "consts"), writes=pg(v1k), tag="ln_ln")
        kb.emit("scalar", lambda s: s.activation(A[:], v1[:], AF.Exp, scale=-0.5),
                reads=pg(v1k), writes=pg(Ak), tag="ln_exp")
        for c in range(NC):
            kb.emit("vector", lambda v, c=c: v.tensor_sub(st["d"][c % 2][:], xw[:, c, :], mean[:]),
                    reads=pg(xwk, meank), writes=pg(st["d_key"][c % 2]), tag="ln_sub")
            kb.emit("vector", lambda v, c=c: v.tensor_mul(zb[:, c, :], st["d"][c % 2][:], A[:]),
                    reads=pg(st["d_key"][c % 2], Ak), writes=pg(zbk, c * N * 2, (c + 1) * N * 2),
                    tag="ln_mul")

    def x_writeback(self, w, xw, xdst_dram, key_dst, xw_key):
        kb, pg = self.kb, self.pg
        wsl = slice(w * N, (w + 1) * N)
        kb.dma("sync", xdst_dram.rearrange("(c p) t -> p c t", p=P)[:, :, wsl], xw[:],
               reads=pg(xw_key), writes=[key_dst + ":%d" % w], tag="x_wb")

    def next_x(self):
        src = self.x_cur
        self.sub_idx += 1
        last = (self.sub_idx == min(self.upto, 12))
        if last:
            dst = ("xout", self.xout)
        else:
            dst = ("d_xB", self.d_xB) if src[0] == "d_xA" else ("d_xA", self.d_xA)
        self.x_cur = dst
        return src[1], src[0], dst[1], dst[0]

    def prep_trig(self, off):
        kb, pg = self.kb, self.pg
        C_sb = self.sb("trigC", off, [P, NC, D], BF16)
        S_sb = self.sb("trigS", off + 16 * 1024, [P, NC, D], BF16)
        t0 = self.sb("trig_t0", off + 32 * 1024, [P, D], F32)
        t1 = self.sb("trig_t1", off + 36 * 1024, [P, D], F32)
        th = self.din["L0_ffn_theta"].rearrange("(c p) o -> p c o", p=P)
        thn = self.din["L0_ffn_theta_noise"].rearrange("(c p) o -> p c o", p=P)
        ph = self.din["L0_ffn_phi"].rearrange("(c p) o -> p c o", p=P)
        phn = self.din["L0_ffn_phi_noise"].rearrange("(c p) o -> p c o", p=P)
        for c in range(NC):
            kb.dma("sync", t0[:], th[:, c, :], writes=pg("trig_t0"), tag="trig")
            kb.dma("gpsimd", t1[:], thn[:, c, :], writes=pg("trig_t1"), tag="trig")
            kb.emit("vector", lambda v: v.tensor_add(t0[:], t0[:], t1[:]),
                    reads=pg("trig_t0", "trig_t1"), writes=pg("trig_t0"))
            kb.emit("scalar", lambda s, c=c: s.activation(C_sb[:, c, :], t0[:], AF.Sin,
                                                          bias=self.C_HALFPI),
                    reads=pg("trig_t0", "consts"), writes=pg("trigC", c * 2 * D, (c + 1) * 2 * D))
            kb.dma("sync", t0[:], ph[:, c, :], writes=pg("trig_t0"), tag="trig")
            kb.dma("gpsimd", t1[:], phn[:, c, :], writes=pg("trig_t1"), tag="trig")
            kb.emit("vector", lambda v: v.tensor_add(t0[:], t0[:], t1[:]),
                    reads=pg("trig_t0", "trig_t1"), writes=pg("trig_t0"))
            kb.emit("scalar", lambda s, c=c: s.activation(S_sb[:, c, :], t0[:], AF.Sin),
                    reads=pg("trig_t0"), writes=pg("trigS", c * 2 * D, (c + 1) * 2 * D))
        return C_sb, S_sb

    def make_ln_state(self, base, par):
        s = {}
        name = lambda n: "%s_p%d_s%d" % (n, par, base)
        s["xw"] = self.sb(name("xw"), base, [P, NC, N], F32); s["xw_key"] = name("xw")
        s["zb"] = self.sb(name("zb"), base + 16384, [P, NC, N], BF16); s["zb_key"] = name("zb")
        s["xb"] = self.sb(name("xb"), base + 24576, [P, NC, N], BF16); s["xb_key"] = name("xb")
        s["rb"] = self.sb(name("rb"), base + 32768, [P, NC, N], BF16); s["rb_key"] = name("rb")
        sq0 = self.sb(name("sq0"), base + 40960, [P, N], BF16)
        sq1 = self.sb(name("sq1"), base + 41984, [P, N], BF16)
        s["sq"] = [sq0, sq1]; s["sq_key"] = [name("sq0"), name("sq1")]
        s["mean"] = self.sb(name("mean"), base + 43008, [P, N], F32); s["mean_key"] = name("mean")
        s["A"] = self.sb(name("A"), base + 45056, [P, N], F32); s["A_key"] = name("A")
        s["v1"] = self.sb(name("v1"), base + 47104, [P, N], F32); s["v1_key"] = name("v1")
        d0 = self.sb(name("d0"), base + 49152, [P, N], F32)
        d1 = self.sb(name("d1"), base + 51200, [P, N], F32)
        s["d"] = [d0, d1]; s["d_key"] = [name("d0"), name("d1")]
        return s, 53248

    def sub_l0ffn(self, C_sb, S_sb):
        kb, pg = self.kb, self.pg
        xsrc, ksrc, xdst, kdst = self.next_x()
        base = self.PHASE + 48 * 1024
        sts = []
        for par in range(2):
            st, sz = self.make_ln_state(base, par)
            st["bS"], st["bQ"] = 6, 7
            sts.append(st)
            base += sz
        ffn = []
        for par in range(2):
            nm = lambda n: "ff_%s_p%d" % (n, par)
            r2 = self.sb(nm("r2"), base, [P, N], F32)
            i2 = self.sb(nm("i2"), base + 2048, [P, N], F32)
            ffn.append(dict(r2=r2, i2=i2, r2k=nm("r2"), i2k=nm("i2")))
            base += 4096
        for w in range(NW):
            self.kb.seg()
            st = sts[w % 2]
            f = ffn[w % 2]
            xw, zb = st["xw"], st["zb"]
            self.ln0_window(w, xsrc, xw, zb, st, ksrc)
            for oc in range(NC):
                bR = (oc % 2) * 2
                bI = bR + 1
                psR, psI = self.ps[bR], self.ps[bI]
                for kc in range(NC):
                    kb.emit("tensor", lambda t, oc=oc, kc=kc, psR=psR, zb=zb:
                            t.matmul(psR[:], C_sb[:, kc, ts(oc)], zb[:, kc, :],
                                     start=(kc == 0), stop=(kc == NC - 1)),
                            reads=pg("trigC", st["zb_key"]),
                            writes=["PS%d" % bR], tag="f0_mmR")
                    kb.emit("tensor", lambda t, oc=oc, kc=kc, psI=psI, zb=zb:
                            t.matmul(psI[:], S_sb[:, kc, ts(oc)], zb[:, kc, :],
                                     start=(kc == 0), stop=(kc == NC - 1)),
                            reads=pg("trigS", st["zb_key"]),
                            writes=["PS%d" % bI], tag="f0_mmI")
                kb.emit("scalar", lambda s, psR=psR, f=f: s.activation(f["r2"][:], psR[:], AF.Square),
                        reads=["PS%d" % bR], writes=pg(f["r2k"]), tag="f0_sqR")
                kb.emit("scalar", lambda s, psI=psI, f=f: s.activation(f["i2"][:], psI[:], AF.Square),
                        reads=["PS%d" % bI], writes=pg(f["i2k"]), tag="f0_sqI")
                kb.emit("vector", lambda v, f=f: v.tensor_add(f["r2"][:], f["r2"][:], f["i2"][:]),
                        reads=pg(f["r2k"], f["i2k"]), writes=pg(f["r2k"]), tag="f0_add")
                kb.emit("scalar", lambda s, f=f: s.activation(f["i2"][:], f["r2"][:], AF.Ln),
                        reads=pg(f["r2k"]), writes=pg(f["i2k"]), tag="f0_ln")
                kb.emit("scalar", lambda s, f=f: s.activation(f["r2"][:], f["i2"][:], AF.Exp,
                                                              scale=0.5, bias=self.C_LNHALF),
                        reads=pg(f["i2k"], "consts"), writes=pg(f["r2k"]), tag="f0_exp")
                kb.emit("vector", lambda v, oc=oc, xw=xw, f=f:
                        v.tensor_add(xw[:, oc, :], xw[:, oc, :], f["r2"][:]),
                        reads=pg(f["r2k"]) + pg(st["xw_key"]), writes=pg(st["xw_key"]), tag="f0_res")
            self.x_writeback(w, xw, xdst, kdst, st["xw_key"])

    def ln_all_to_dram(self, xsrc, ksrc, dst_dram, kdst, also_resid=None):
        base = self.PHASE
        sts = []
        for par in range(2):
            st, sz = self.make_ln_state(base, par)
            st["bS"], st["bQ"] = 6, 7
            sts.append(st)
            base += sz
        kb, pg = self.kb, self.pg
        for w in range(NW):
            kb.seg()
            st = sts[w % 2]
            self.ln0_window(w, xsrc, st["xw"], st["zb"], st, ksrc)
            wsl = slice(w * N, (w + 1) * N)
            kb.dma("sync", dst_dram.rearrange("(c p) t -> p c t", p=P)[:, :, wsl],
                   st["zb"][:], reads=pg(st["zb_key"]), writes=[kdst + ":%d" % w],
                   tag="z_spill")
        return sts

    def load_weight_bf16(self, name_sb, off, src_ap, shape, q="gpsimd"):
        sb = self.sb(name_sb, off, [P, shape[0] // P, shape[1]], BF16)
        self.kb.dma(q, sb[:], src_ap.rearrange("(c p) m -> p c m", p=P),
                    writes=self.pg(name_sb), tag="wload")
        return sb

    def proj_fm(self, zwin, zkey, w_sb, wkey, out_sb, outkey, bank2, kcs=NC,
                ocs=NC, bias_row=None, bias_key=None, tag="proj"):
        kb, pg = self.kb, self.pg
        for oc in range(ocs):
            b = bank2[oc % 2]
            ps = self.ps[b]
            for kc in range(kcs):
                kb.emit("tensor", lambda t, oc=oc, kc=kc, ps=ps, w_sb=w_sb, zwin=zwin:
                        t.matmul(ps[:], w_sb[:, kc, ts(oc)], zwin[:, kc, :],
                                 start=(kc == 0), stop=(kc == kcs - 1 and bias_row is None)),
                        reads=pg(wkey, zkey), writes=["PS%d" % b], tag=tag)
            if bias_row is not None:
                kb.emit("tensor", lambda t, oc=oc, ps=ps, bias_row=bias_row:
                        t.matmul(ps[:], bias_row[:, ts(oc)], self.ones_bf[0:1, 0:N],
                                 start=False, stop=True),
                        reads=pg(bias_key, "ones_bf"), writes=["PS%d" % b], tag=tag + "_b")
            kb.emit("scalar", lambda s, oc=oc, ps=ps, out_sb=out_sb:
                    s.activation(out_sb[:, oc, :], ps[:], AF.Copy),
                    reads=["PS%d" % b], writes=pg((outkey, oc * N * 2, (oc + 1) * N * 2)),
                    tag=tag + "_ev")

    def proj_tm(self, zwin, zkey, w_sb, wkey, w, out_sb, outkey, bank2, kcs=NC,
                mfree=D, bias_row=None, bias_key=None, tag="projtm"):
        kb, pg = self.kb, self.pg
        nh = mfree // 512
        for tt in range(4):
            gt = w * 4 + tt
            for mh in range(nh):
                b = bank2[(tt * nh + mh) % 2]
                ps = self.ps[b]
                for kc in range(kcs):
                    kb.emit("tensor", lambda t, tt=tt, mh=mh, kc=kc, ps=ps, zwin=zwin, w_sb=w_sb:
                            t.matmul(ps[:], zwin[:, kc, ts(tt)], w_sb[:, kc, ts(mh, 512)],
                                     start=(kc == 0), stop=(kc == kcs - 1 and bias_row is None)),
                            reads=pg(wkey, zkey), writes=["PS%d" % b], tag=tag)
                if bias_row is not None:
                    kb.emit("tensor", lambda t, mh=mh, ps=ps, bias_row=bias_row:
                            t.matmul(ps[:], self.ones_bf[0:1, 0:128], bias_row[:, ts(mh, 512)],
                                     start=False, stop=True),
                            reads=pg(bias_key, "ones_bf"), writes=["PS%d" % b], tag=tag + "_b")
                kb.emit("scalar", lambda s, gt=gt, mh=mh, ps=ps, out_sb=out_sb:
                        s.activation(out_sb[:, gt, ts(mh, 512)], ps[:], AF.Copy),
                        reads=["PS%d" % b],
                        writes=pg((outkey, gt * mfree * 2, (gt + 1) * mfree * 2)),
                        tag=tag + "_ev")

    def load_row_bf16(self, name, off, src_ap, length, q="gpsimd"):
        sb = self.sb(name, off, [1, length], BF16)
        if src_ap.ndim == 1:
            src_ap = src_ap.unsqueeze(0)
        self.kb.dma(q, sb[:], src_ap, writes=self.pg(name), tag="rowload")
        return sb

    def load_pp_vec(self, name, off, src_ap, length, bank=5):
        ncs = length // P
        tmp = self.pptmp[0:ncs, :]
        sb = self.sb(name, off, [P, ncs], F32)
        kb, pg = self.kb, self.pg
        kb.dma("sync", tmp[:], src_ap.rearrange("(c p) -> c p", p=P),
               writes=pg("pptmp"), tag="ppload")
        ps = self.ps[bank]
        kb.emit("tensor", lambda t, tmp=tmp, ncs=ncs: t.transpose(ps[0:P, 0:ncs], tmp[:], self.ident_f32[0:ncs, 0:ncs]),
                reads=pg("pptmp", "ident_f32"), writes=["PS%d" % bank], tag="pptr")
        kb.emit("vector", lambda v, sb=sb: v.tensor_copy(sb[:], ps[0:P, 0:ncs]),
                reads=["PS%d" % bank], writes=pg(name), tag="ppev")
        return sb

    def attn_core(self, w, Qdram, K_sb, Kkey, negB, negBkey, kmax2,
                  kmax2key, scale, sgroups, Ow_sb, Owkey, a):
        kb, pg = self.kb, self.pg
        wsl = slice(w * N, (w + 1) * N)
        qw, qwkey = a["qw"], a["qwk"]
        kb.dma("sync", qw[:], Qdram.rearrange("(c p) t -> p c t", p=P)[:, :, wsl],
               reads=["d_q:%d" % w], writes=pg(qwkey), tag="qload")
        sqq, sqqk = a["sqq"], a["sqqk"]
        bq = a["bank_q"]
        psq = self.ps[bq]
        bgroups = a["bgroups"]
        for gi, chunks in enumerate(bgroups):
            for j, c in enumerate(chunks):
                p2 = j % 2
                kb.emit("scalar", lambda s, c=c, p2=p2, qw=qw: s.activation(sqq[p2][:], qw[:, c, :], AF.Square),
                        reads=pg(qwkey), writes=pg(sqqk[p2]), tag="qsq")
                kb.emit("tensor", lambda t, j=j, p2=p2, nchunks=len(chunks):
                        t.matmul(psq[:], self.ones_bf[:, 0:128], sqq[p2][:],
                                 start=(j == 0), stop=(j == nchunks - 1)),
                        reads=pg(sqqk[p2], "ones_bf"), writes=["PS%d" % bq], tag="qsqmm")
            kb.emit("vector", lambda v, gi=gi: v.reduce_max(negB[0:1, gi:gi + 1], psq[0:1, :], AX.X),
                    reads=["PS%d" % bq], writes=pg(negBkey), tag="negB1")
            kb.emit("vector", lambda v, gi=gi, scale=scale: v.tensor_scalar(
                        negB[0:1, gi:gi + 1], negB[0:1, gi:gi + 1], kmax2[0:1, gi:gi + 1],
                        -0.5 * scale, ALU.add, ALU.mult),
                    reads=pg(negBkey, kmax2key), writes=pg(negBkey), tag="negB2")
        ng = len(bgroups)
        kb.emit("vector", lambda v, ng=ng: v.tensor_copy(a["ebrow"][0:1, 0:ng], negB[0:1, 0:ng]),
                reads=pg(negBkey), writes=pg(a["ebrowk"]), tag="negB3")
        kb.emit("tensor", lambda t, ng=ng: t.matmul(psq[0:P, 0:ng], self.ones_bf[0:1, 0:128],
                                                    a["ebrow"][0:1, 0:ng], start=True, stop=True),
                reads=pg(a["ebrowk"], "ones_bf"), writes=["PS%d" % bq], tag="negB4")
        kb.emit("vector", lambda v: v.tensor_copy(a["expb"][:, 0:len(bgroups)], psq[0:P, 0:len(bgroups)]),
                reads=["PS%d" % bq], writes=pg(a["expbk"]), tag="negB5")
        rl, rlkey = a["rl"], a["rlk"]
        vsl, vslk = a["vsl"], a["vslk"]
        bS0, bS1, bL, bO0, bO1 = a["bank_S0"], a["bank_S1"], a["bank_l"], a["bank_O0"], a["bank_O1"]
        vseq = 0
        ptn = a["ptn"]
        nhalf = 32 // ptn
        def load_v(og):
            nonlocal vseq
            vlo, vhi = og["vcols"]
            m = vhi - vlo
            vb = vseq % 2
            vt = vsl[vb]
            kb.dma("gpsimd", vt[:, :, 0:m],
                   self.d_v.rearrange("(k p) d -> p k d", p=P)[:, :, vlo:vhi],
                   reads=["d_v:%d" % ww for ww in range(NW)],
                   writes=pg(vslk[vb]), tag="vload")
            vseq += 1
            return (vt, vslk[vb], m)

        for sgi, sg in enumerate(sgroups):
            PT, PTkey = a["PT2"][sgi % 2], a["PT2k"][sgi % 2]
            vts = [None] * len(sg["outs"])
            if len(sg["outs"]) == 1:
                vts[0] = load_v(sg["outs"][0])
            psL = self.ps[bL]
            for h in range(nhalf):
                for kt16 in range(ptn):
                    kt = h * ptn + kt16
                    bS = (bS0, bS1)[kt16 % 2]
                    psS = self.ps[bS]
                    if sg.get("kcs"):
                        for j, kc in enumerate(sg["kcs"]):
                            kb.emit("tensor", lambda t, kt=kt, kc=kc, j=j, psS=psS, K_sb=K_sb, nk=len(sg["kcs"]):
                                    t.matmul(psS[:], K_sb[:, kc, ts(kt)], qw[:, kc, :],
                                             start=(j == 0), stop=(j == nk - 1)),
                                    reads=pg(Kkey, qwkey), writes=["PS%d" % bS], tag="smm")
                    else:
                        lo, hi = sg["kpart"]
                        kc = sg["kc"]
                        kb.emit("tensor", lambda t, kt=kt, kc=kc, lo=lo, hi=hi, psS=psS, K_sb=K_sb, qw=qw:
                                t.matmul(psS[:], K_sb[lo:hi, kc, ts(kt)], qw[lo:hi, kc, :],
                                         start=True, stop=True),
                                reads=pg(Kkey, qwkey), writes=["PS%d" % bS], tag="smm")
                    gi = sg["bound_idx"]
                    kb.emit("scalar", lambda s, kt16=kt16, psS=psS, scale=scale, gi=gi:
                            s.activation(PT[:, kt16, :], psS[:], AF.Exp, scale=scale,
                                         bias=a["expb"][:, gi:gi + 1]),
                            reads=["PS%d" % bS] + pg(a["expbk"]),
                            writes=pg((PTkey, kt16 * N * 2, (kt16 + 1) * N * 2)),
                            tag="sexp")
                for kt16 in range(ptn):
                    kt = h * ptn + kt16
                    kb.emit("tensor", lambda t, kt16=kt16, kt=kt, psL=psL:
                            t.matmul(psL[:], self.ones_bf[:, 0:128], PT[:, kt16, :],
                                     start=(kt == 0), stop=(kt == 31)),
                            reads=pg(PTkey, "ones_bf"), writes=["PS%d" % bL], tag="lmm")
                last = (h == nhalf - 1)
                if last:
                    kb.emit("scalar", lambda s: s.activation(rl[:], psL[:], AF.Ln),
                            reads=["PS%d" % bL], writes=pg(rlkey), tag="lln")
                    kb.emit("scalar", lambda s: s.activation(rl[:], rl[:], AF.Exp, scale=-1.0),
                            reads=pg(rlkey), writes=pg(rlkey), tag="lexp")
                for ogi, og in enumerate(sg["outs"]):
                    if vts[ogi] is None:
                        vts[ogi] = load_v(og)
                    vt, vtk, m = vts[ogi]
                    bO = (bO0, bO1)[ogi % 2]
                    psO = self.ps[bO]
                    for kt16 in range(ptn):
                        kt = h * ptn + kt16
                        kb.emit("tensor", lambda t, kt16=kt16, kt=kt, psO=psO, vt=vt, m=m, vtk=vtk:
                                t.matmul(psO[0:m, :], vt[:, kt, 0:m], PT[:, kt16, :],
                                         start=(kt == 0), stop=(kt == 31)),
                                reads=pg(vtk, PTkey), writes=["PS%d" % bO], tag="pvmm")
                    if last:
                        dst_plo, dst_c = og["dst"]
                        kb.emit("vector", lambda v, m=m, dst_plo=dst_plo, dst_c=dst_c, psO=psO, Ow_sb=Ow_sb:
                                v.tensor_mul(Ow_sb[dst_plo:dst_plo + m, dst_c, :], psO[0:m, :],
                                             rl[0:m, :]),
                                reads=["PS%d" % bO] + pg(rlkey),
                                writes=pg((Owkey, dst_c * N * 2, (dst_c + 1) * N * 2)), tag="pvev")

    def knorm_window(self, w, src_sb, srckey, groups, kmax2, kmax2key, a, tag="kn"):
        kb, pg = self.kb, self.pg
        wsl = slice(w * N, (w + 1) * N)
        sqq, sqqk = a["sqq"], a["sqqk"]
        bq = a["bank_q"]
        psq = self.ps[bq]
        for gi, chunks in enumerate(groups):
            for j, c in enumerate(chunks):
                p2 = j % 2
                kb.emit("scalar", lambda s, c=c, p2=p2: s.activation(sqq[p2][:], src_sb[:, c, wsl], AF.Square),
                        reads=pg((srckey,)), writes=pg(sqqk[p2]), tag=tag + "sq")
                kb.emit("tensor", lambda t, j=j, p2=p2, nch=len(chunks):
                        t.matmul(psq[:], self.ones_bf[:, 0:128], sqq[p2][:],
                                 start=(j == 0), stop=(j == nch - 1)),
                        reads=pg(sqqk[p2], "ones_bf"), writes=["PS%d" % bq], tag=tag + "mm")
            kb.emit("vector", lambda v: v.reduce_max(a["kred"][:], psq[0:1, :], AX.X),
                    reads=["PS%d" % bq], writes=pg(a["kredk"]), tag=tag + "red")
            kb.emit("vector", lambda v, gi=gi: v.tensor_max(kmax2[0:1, gi:gi + 1],
                                                            kmax2[0:1, gi:gi + 1], a["kred"][:]),
                    reads=pg(a["kredk"], kmax2key), writes=pg(kmax2key), tag=tag + "max")

    def attn_scratch(self, base, ptn):
        nm = lambda n: "at_%s" % n
        a = {}
        a["qw"] = self.sb(nm("qw"), base, [P, NC, N], BF16); a["qwk"] = nm("qw")
        s0 = self.sb(nm("sqq0"), base + 8192, [P, N], BF16)
        s1 = self.sb(nm("sqq1"), base + 9216, [P, N], BF16)
        a["sqq"] = [s0, s1]; a["sqqk"] = [nm("sqq0"), nm("sqq1")]
        if ptn <= 16:
            a["PT2"] = [self.sb(nm("PT"), base + 10240, [P, ptn, N], BF16),
                        self.sb(nm("PTb"), base + 10240 + ptn * 1024, [P, ptn, N], BF16)]
            a["PT2k"] = [nm("PT"), nm("PTb")]
            o = base + 10240 + 2 * ptn * 1024
        else:
            pt = self.sb(nm("PT"), base + 10240, [P, ptn, N], BF16)
            a["PT2"] = [pt, pt]
            a["PT2k"] = [nm("PT"), nm("PT")]
            o = base + 10240 + ptn * 1024
        a["ptn"] = ptn
        a["rl"] = self.sb(nm("rl"), o, [P, N], F32); a["rlk"] = nm("rl")
        a["kred"] = self.sb(nm("kred"), o + 2048, [1, 1], F32); a["kredk"] = nm("kred")
        a["vsl"] = [self.sb(nm("vsl0"), o + 3072, [P, 32, P], BF16),
                    self.sb(nm("vsl1"), o + 11264, [P, 32, P], BF16)]
        a["vslk"] = [nm("vsl0"), nm("vsl1")]
        a["expb"] = self.sb(nm("expb"), o + 19456, [P, 8], F32); a["expbk"] = nm("expb")
        a["ebrow"] = self.sb(nm("ebrow"), o + 19488, [1, 8], BF16); a["ebrowk"] = nm("ebrow")
        a.update(bank_q=6, bank_S0=7, bank_S1=4, bank_l=5, bank_O0=0, bank_O1=1)
        return a, o + 20512

    def sub_attn_generic(self, kind):
        kb, pg = self.kb, self.pg
        xsrc, ksrc, xdst, kdst = self.next_x()
        self.ln_all_to_dram(xsrc, ksrc, self.d_z, "d_z")
        kb.seg()
        B0 = self.PHASE
        WQ_OFF, WK_OFF, WV_OFF = B0, B0 + 16384, B0 + 32768
        K_OFF = B0 + 49152
        ZS_OFF = K_OFF + 65536
        QS_OFF = ZS_OFF + 16384
        VS_OFF = QS_OFF + 8192
        NRM_OFF = VS_OFF + 8192
        if kind == 'A0':
            wq = self.load_weight_bf16("wqT", WQ_OFF, self.din["h_L0_WqT"], [D, D])
            wk = self.load_weight_bf16("wkT", WK_OFF, self.din["h_L0_WkT"], [D, D])
            wv = self.load_weight_bf16("wvT", WV_OFF, self.din["h_L0_WvT"], [D, D])
            qb = kbias = vbias = None
            scale = 1.0 / 32.0
        elif kind == 'A1':
            wq = self.load_weight_bf16("wqT", WQ_OFF, self.din["L1_attn_Wq"].rearrange("d h k -> d (h k)"), [D, D])
            wk = self.load_weight_bf16("wkT", WK_OFF, self.din["L1_attn_Wk"].rearrange("d h k -> d (h k)"), [D, D])
            wv = self.load_weight_bf16("wvT", WV_OFF, self.din["L1_attn_Wv"].rearrange("d h k -> d (h k)"), [D, D])
            qb = self.load_row_bf16("qb_row", NRM_OFF, self.din["L1_attn_bq"].rearrange("h k -> (h k)"), D)
            kbias = self.load_row_bf16("kb_row", NRM_OFF + 2048, self.din["L1_attn_bk"].rearrange("h k -> (h k)"), D)
            vbias = self.load_row_bf16("vb_row", NRM_OFF + 4096, self.din["L1_attn_bv"].rearrange("h k -> (h k)"), D)
            scale = 1.0 / 8.0
        else:
            wq = self.load_weight_bf16("wqT", WQ_OFF, self.din["h_L1f_qwT"], [D, D])
            wk = self.load_weight_bf16("wkT", WK_OFF, self.din["h_L1f_kwT"], [D, D])
            wv = self.load_weight_bf16("wvT", WV_OFF, self.din["h_L1f_vwT"], [D, D])
            qb = self.load_row_bf16("qb_row", NRM_OFF, self.din["L1_ffn_q_b"], D)
            kbias = self.load_row_bf16("kb_row", NRM_OFF + 2048, self.din["L1_ffn_k_b"], D)
            vbias = self.load_row_bf16("vb_row", NRM_OFF + 4096, self.din["L1_ffn_v_b"], D)
            scale = 1.0 / 32.0
        K_sb = self.sb("K_fm", K_OFF, [P, NC, T], BF16)
        zs = [self.sb("zs0", ZS_OFF, [P, NC, N], BF16), self.sb("zs1", ZS_OFF + 8192, [P, NC, N], BF16)]
        zsk = ["zs0", "zs1"]
        qstage = self.sb("qstage", QS_OFF, [P, NC, N], BF16)
        vstage = self.sb("vstage", VS_OFF, [P, 4, D], BF16)
        kmax2 = self.sb("kmax2", 209632, [1, 8], F32)
        kb.emit("gpsimd", lambda g: g.memset(kmax2[:], 0.0), writes=pg("kmax2"))
        if kind == 'F1':
            f1m = self.sb("f1m", NRM_OFF + 8192, [P, N], F32)
            f1a = self.sb("f1a", NRM_OFF + 10240, [P, N], F32)
            f1d = self.sb("f1d", NRM_OFF + 12288, [P, N], F32)
            f1nw = self.load_pp_vec("f1nw", NRM_OFF + 14336, self.din["L1_ffn_norm_w"], D)
            f1nb = self.load_pp_vec("f1nb", NRM_OFF + 14464, self.din["L1_ffn_norm_b"], D)
        sqq_a = {"sqq": self.stsq, "sqqk": ["stsq0", "stsq1"], "bank_q": 6,
                 "kred": self.sb("p1kred", NRM_OFF + 14720, [1, 1], F32), "kredk": "p1kred"}
        for w in range(NW):
            kb.seg()
            wsl = slice(w * N, (w + 1) * N)
            par = w % 2
            zw, zwk = zs[par], zsk[par]
            kb.dma("sync", zw[:], self.d_z.rearrange("(c p) t -> p c t", p=P)[:, :, wsl],
                   reads=["d_z:%d" % w], writes=pg(zwk), tag="zload")
            kcs = NC
            if kind == 'F1':
                self.stats_of(zw, zwk, f1m, "f1m", f1a, "f1a", bS=7, bQ=5)
                for c in range(NC):
                    kb.emit("vector", lambda v, c=c, zw=zw: v.tensor_sub(f1d[:], zw[:, c, :], f1m[:]),
                            reads=pg(zwk, "f1m"), writes=pg("f1d"), tag="f1sub")
                    kb.emit("vector", lambda v: v.tensor_mul(f1d[:], f1d[:], f1a[:]),
                            reads=pg("f1d", "f1a"), writes=pg("f1d"), tag="f1mul")
                    kb.emit("vector", lambda v, c=c, zw=zw: v.tensor_scalar(
                                zw[:, c, :], f1d[:], f1nw[:, c:c + 1], f1nb[:, c:c + 1],
                                ALU.mult, ALU.add),
                            reads=pg("f1d", "f1nw", "f1nb"), writes=pg(zwk), tag="f1ts")
                kcs = max(1, NC >> (w // 2))
            self.proj_fm(zw, zwk, wq, "wqT", qstage, "qstage", (0, 1), kcs=kcs,
                         bias_row=qb, bias_key=("qb_row" if qb is not None else None), tag="qp")
            kb.dma("sync", self.d_q.rearrange("(c p) t -> p c t", p=P)[:, :, wsl],
                   qstage[:], reads=pg("qstage"), writes=["d_q:%d" % w], tag="qspill")
            for oc in range(NC):
                b = (2, 3)[oc % 2]
                ps = self.ps[b]
                for kc in range(kcs):
                    kb.emit("tensor", lambda t, oc=oc, kc=kc, ps=ps, zw=zw, wk=wk, kcs=kcs:
                            t.matmul(ps[:], wk[:, kc, ts(oc)], zw[:, kc, :],
                                     start=(kc == 0), stop=(kbias is None and kc == kcs - 1)),
                            reads=pg("wkT", zwk), writes=["PS%d" % b], tag="kp")
                if kbias is not None:
                    kb.emit("tensor", lambda t, oc=oc, ps=ps, kbias=kbias:
                            t.matmul(ps[:], kbias[:, ts(oc)], self.ones_bf[0:1, 0:N],
                                     start=False, stop=True),
                            reads=pg("kb_row", "ones_bf"), writes=["PS%d" % b], tag="kpb")
                kb.emit("scalar", lambda s, oc=oc, ps=ps, wsl=wsl, K_sb=K_sb:
                        s.activation(K_sb[:, oc, wsl], ps[:], AF.Copy),
                        reads=["PS%d" % b], writes=pg(("K_fm", oc * T * 2 + w * N * 2,
                                                       oc * T * 2 + (w + 1) * N * 2)), tag="kev")
            self.proj_tm(zw, zwk, wv, "wvT", 0, vstage, "vstage", (2, 3), kcs=kcs,
                         bias_row=vbias, bias_key=("vb_row" if vbias is not None else None), tag="vp")
            kb.dma("sync", self.d_v.rearrange("(g p) m -> p g m", p=P)[:, ts(w, 4), :],
                   vstage[:], reads=pg("vstage"), writes=["d_v:%d" % w], tag="vspill")
            bgroups = [list(range(NC))] if kind == 'A0' else [[c] for c in range(NC)]
            self.knorm_window(w, K_sb, "K_fm", bgroups, kmax2, "kmax2", sqq_a)
        kb.seg()
        ptn = 32 if kind == 'A0' else 16
        a, aend = self.attn_scratch(K_OFF + 65536, ptn)
        a["bgroups"] = [list(range(NC))] if kind == 'A0' else [[c] for c in range(NC)]
        negB = self.sb("negB", 200416, [1, NC], F32)
        assert aend <= 200416, aend
        WO_OFF = B0
        if kind == 'A0':
            wo = self.load_weight_bf16("woT", WO_OFF, self.din["h_L0_WoT"], [D, D])
            ob = None
        elif kind == 'A1':
            wo = self.load_weight_bf16("woT", WO_OFF, self.din["L1_attn_Wo"].rearrange("h k d -> (h k) d"), [D, D])
            ob = self.load_row_bf16("ob_row", WO_OFF + 16384, self.din["L1_attn_bo"], D)
        else:
            wo = self.load_weight_bf16("woT", WO_OFF, self.din["h_L1f_owT"], [D, D])
            ob = self.load_row_bf16("ob_row", WO_OFF + 16384, self.din["L1_ffn_o_b"], D)
        OW_OFF = WO_OFF + 18432
        Ow = self.sb("Ow", OW_OFF, [P, NC, N], BF16)
        xw3 = self.sb("xw3", OW_OFF + 8192, [P, NC, N], F32)
        assert OW_OFF + 24576 <= K_OFF, OW_OFF
        if kind != 'A0':
            E3 = aend
            z3 = self.sb("z3", E3, [P, NC, N], BF16)
            u3 = self.sb("u3", E3 + 8192, [P, NC, N], BF16)
            assert E3 + 16384 + 512 <= 200416, E3
            t3 = self.sb("t3", OW_OFF + 24576, [P, N], F32)
            m3 = self.sb("m3", OW_OFF + 26624, [P, N], F32)
            a3 = self.sb("a3", OW_OFF + 28672, [P, N], F32)
            assert OW_OFF + 30720 <= K_OFF, OW_OFF
        if kind == 'A1':
            gam = self.load_pp_vec("gam", E3 + 16384, self.din["L1_attn_gamma"], D)
            bet = self.load_pp_vec("bet", E3 + 16384 + 128, self.din["L1_attn_beta"], D)
        if kind == 'A0':
            sgroups = [dict(kcs=list(range(NC)), bound_idx=0,
                            outs=[dict(vcols=(dc * P, (dc + 1) * P), dst=(0, dc)) for dc in range(NC)])]
        elif kind == 'A1':
            sgroups = []
            for c in range(NC):
                for j in range(2):
                    sgroups.append(dict(kpart=(j * 64, (j + 1) * 64), kc=c, bound_idx=c,
                                        outs=[dict(vcols=(c * P + j * 64, c * P + j * 64 + 64),
                                                   dst=(j * 64, c))]))
        else:
            sgroups = [dict(kpart=(0, P), kc=c, bound_idx=c,
                            outs=[dict(vcols=(c * P, (c + 1) * P), dst=(0, c))]) for c in range(NC)]
        for w in range(NW):
            kb.seg()
            wsl = slice(w * N, (w + 1) * N)
            self.attn_core(w, self.d_q, K_sb, "K_fm", negB, "negB",
                           kmax2, "kmax2", scale, sgroups, Ow, "Ow", a)
            if getattr(self, "dbg_ow", None) is not None:
                kb.dma("sync", self.dbg_ow.rearrange("(c p) t -> p c t", p=P)[:, :, wsl],
                       Ow[:], reads=pg("Ow"), writes=["dbgow:%d" % w], tag="dbgow")
            kb.seg()
            kb.dma("scalar", xw3[:], xsrc.rearrange("(c p) t -> p c t", p=P)[:, :, wsl],
                   reads=[ksrc + ":%d" % w], writes=pg("xw3"), tag="p3x")
            if kind != 'A0':
                kb.dma("scalar", z3[:], self.d_z.rearrange("(c p) t -> p c t", p=P)[:, :, wsl],
                       reads=["d_z:%d" % w], writes=pg("z3"), tag="p3z")
            mcs = NC if kind != 'F1' else max(1, NC >> (w // 2))
            for oc in range(NC):
                if kind == 'F1' and oc >= mcs:
                    kb.emit("vector", lambda v, oc=oc: v.scalar_tensor_tensor(
                                xw3[:, oc, :], z3[:, oc, :], 0.5, xw3[:, oc, :],
                                ALU.mult, ALU.add),
                            reads=pg("z3", "xw3"), writes=pg("xw3"), tag="f1pad")
                    continue
                b = (2, 3)[oc % 2]
                ps = self.ps[b]
                okcs = NC if kind != 'F1' else mcs
                for kc in range(okcs):
                    kb.emit("tensor", lambda t, oc=oc, kc=kc, ps=ps, wo=wo, okcs=okcs:
                            t.matmul(ps[:], wo[:, kc, ts(oc)], Ow[:, kc, :],
                                     start=(kc == 0), stop=(ob is None and kc == okcs - 1)),
                            reads=pg("woT", "Ow"), writes=["PS%d" % b], tag="op")
                if ob is not None:
                    kb.emit("tensor", lambda t, oc=oc, ps=ps, ob=ob:
                            t.matmul(ps[:], ob[:, ts(oc)], self.ones_bf[0:1, 0:N],
                                     start=False, stop=True),
                            reads=pg("ob_row", "ones_bf"), writes=["PS%d" % b], tag="opb")
                if kind == 'A0':
                    kb.emit("vector", lambda v, oc=oc, ps=ps: v.tensor_add(
                                xw3[:, oc, :], xw3[:, oc, :], ps[:]),
                            reads=["PS%d" % b] + pg("xw3"), writes=pg("xw3"), tag="a0res")
                elif kind == 'A1':
                    kb.emit("vector", lambda v, oc=oc, ps=ps: v.tensor_add(
                                u3[:, oc, :], z3[:, oc, :], ps[:]),
                            reads=["PS%d" % b] + pg("z3"),
                            writes=pg(("u3", oc * N * 2, (oc + 1) * N * 2)), tag="a1w")
                else:
                    kb.emit("vector", lambda v, oc=oc, ps=ps: v.scalar_tensor_tensor(
                                xw3[:, oc, :], ps[:], 0.5, xw3[:, oc, :], ALU.mult, ALU.add),
                            reads=["PS%d" % b] + pg("xw3"), writes=pg("xw3"), tag="f1r1")
                    kb.emit("vector", lambda v, oc=oc: v.scalar_tensor_tensor(
                                xw3[:, oc, :], z3[:, oc, :], 0.5, xw3[:, oc, :],
                                ALU.mult, ALU.add),
                            reads=pg("z3", "xw3"), writes=pg("xw3"), tag="f1r2")
            if kind == 'A1':
                self.stats_of(u3, "u3", m3, "m3", a3, "a3", bS=6, bQ=5, eps=1e-3)
                for oc in range(NC):
                    kb.emit("vector", lambda v, oc=oc: v.tensor_sub(t3[:], u3[:, oc, :], m3[:]),
                            reads=pg("u3", "m3"), writes=pg("t3"), tag="a1s")
                    kb.emit("vector", lambda v: v.tensor_mul(t3[:], t3[:], a3[:]),
                            reads=pg("t3", "a3"), writes=pg("t3"), tag="a1m")
                    kb.emit("vector", lambda v, oc=oc: v.tensor_scalar(
                                t3[:], t3[:], gam[:, oc:oc + 1], bet[:, oc:oc + 1],
                                ALU.mult, ALU.add),
                            reads=pg("t3", "gam", "bet"), writes=pg("t3"), tag="a1g")
                    kb.emit("vector", lambda v, oc=oc: v.tensor_add(
                                xw3[:, oc, :], xw3[:, oc, :], t3[:]),
                            reads=pg("t3", "xw3"), writes=pg("xw3"), tag="a1r")
            self.x_writeback(w, xw3, xdst, kdst, "xw3")

    def sub_l2attn(self):
        kb, pg = self.kb, self.pg
        xsrc, ksrc, xdst, kdst = self.next_x()
        self.ln_all_to_dram(xsrc, ksrc, self.d_z, "d_z")
        kb.seg()
        B0 = self.PHASE
        Z_OFF = B0
        W1_OFF = B0 + 65536
        W2_OFF = W1_OFF + 16384
        VC_OFF = W2_OFF + 16384
        ST_OFF = VC_OFF + 16384
        RET_OFF = ST_OFF + 16384
        SM_OFF = 201856
        assert RET_OFF + 65536 <= SM_OFF, RET_OFF
        z_sb = self.sb("z2r", Z_OFF, [P, NC, T], BF16)
        for w in range(NW):
            wsl = slice(w * N, (w + 1) * N)
            kb.dma("sync", z_sb[:, :, wsl], self.d_z.rearrange("(c p) t -> p c t", p=P)[:, :, wsl],
                   reads=["d_z:%d" % w], writes=pg(("z2r",)), tag="z2l")
        wqT = self.load_weight_bf16("r_wqT", W1_OFF, self.din["h_L2_WqT"], [D, D])
        wvT = self.load_weight_bf16("r_wvT", W2_OFF, self.din["h_L2_WvT"], [D, D])
        Vc = self.sb("r_Vc", VC_OFF, [P, T], F32)
        stt = self.sb("r_st", ST_OFF, [P, T], F32)
        ret = self.sb("r_ret", RET_OFF, [P, NC, T], BF16)
        lamc = self.sb("r_lamc", SM_OFF, [16, 8], F32)
        lam16 = self.sb("r_lam16", SM_OFF + 64, [16, 1], F32)
        lam_pp = self.sb("r_lampp", SM_OFF + 96, [P, 8], F32)
        hpar = self.sb("r_hpar", SM_OFF + 128, [16, P], F32)
        hmask = self.sb("r_hmask", SM_OFF + 640, [16, 8], F32)
        kb.dma("sync", lam16[:], self.din["L2_attn_beta"].unsqueeze(1), writes=pg("r_lam16"))
        kb.dma("sync", hpar[:], self.din["c_hpar"][:], writes=pg("r_hpar"))
        kb.dma("sync", hmask[:], self.din["c_hmask"][:], writes=pg("r_hmask"))
        kb.emit("scalar", lambda s: s.activation(lam16[:], lam16[:], AF.Exp, scale=-LN2),
                reads=pg("r_lam16"), writes=pg("r_lam16"), tag="lam")
        kb.emit("vector", lambda v: v.tensor_scalar(lam16[:], lam16[:], -1.0, 1.0,
                                                    ALU.mult, ALU.add),
                reads=pg("r_lam16"), writes=pg("r_lam16"), tag="lam")
        kb.emit("vector", lambda v: v.tensor_scalar_max(lam16[:], lam16[:], 1.1754944e-38),
                reads=pg("r_lam16"), writes=pg("r_lam16"), tag="lam")
        kb.emit("vector", lambda v: v.tensor_scalar_min(lam16[:], lam16[:], 1.0 - 1e-9),
                reads=pg("r_lam16"), writes=pg("r_lam16"), tag="lam")
        kb.emit("vector", lambda v: v.tensor_mul(lamc[:], lam16[:, 0:1].broadcast_to([16, 8]), hmask[:]),
                reads=pg("r_lam16", "r_hmask"), writes=pg("r_lamc"), tag="lam")
        kb.emit("tensor", lambda t: t.matmul(self.ps[5][0:P, 0:8], hpar[:], lamc[:],
                                             start=True, stop=True),
                reads=pg("r_hpar", "r_lamc"), writes=["PS5"], tag="lam")
        kb.emit("vector", lambda v: v.tensor_copy(lam_pp[:], self.ps[5][0:P, 0:8]),
                reads=["PS5"], writes=pg("r_lampp"), tag="lam")
        for c in range(NC):
            kb.seg()
            for w in range(NW):
                wsl = slice(w * N, (w + 1) * N)
                b = (0, 1)[w % 2]
                ps = self.ps[b]
                for kc in range(NC):
                    kb.emit("tensor", lambda t, c=c, kc=kc, ps=ps, wsl=wsl:
                            t.matmul(ps[:], wvT[:, kc, ts(c)], z_sb[:, kc, wsl],
                                     start=(kc == 0), stop=(kc == NC - 1)),
                            reads=pg("r_wvT", "z2r"), writes=["PS%d" % b], tag="rv")
                kb.emit("vector", lambda v, ps=ps, wsl=wsl: v.tensor_copy(Vc[:, wsl], ps[:]),
                        reads=["PS%d" % b], writes=pg(("r_Vc", wsl.start * 4, wsl.stop * 4)), tag="rvev")
            kb.emit("vector", lambda v, c=c: v.tensor_tensor_scan(
                        stt[:], lam_pp[:, c:c + 1].broadcast_to([P, T]), Vc[:],
                        0.0, ALU.mult, ALU.add),
                    reads=pg("r_Vc", "r_lampp"), writes=pg("r_st"), tag="rscan")
            for w in range(NW):
                wsl = slice(w * N, (w + 1) * N)
                b = (2, 3)[w % 2]
                ps = self.ps[b]
                for kc in range(NC):
                    kb.emit("tensor", lambda t, c=c, kc=kc, ps=ps, wsl=wsl:
                            t.matmul(ps[:], wqT[:, kc, ts(c)], z_sb[:, kc, wsl],
                                     start=(kc == 0), stop=(kc == NC - 1)),
                            reads=pg("r_wqT", "z2r"), writes=["PS%d" % b], tag="rq")
                kb.emit("vector", lambda v, c=c, ps=ps, wsl=wsl:
                        v.tensor_mul(ret[:, c, wsl], ps[:], stt[:, wsl]),
                        reads=["PS%d" % b] + pg("r_st"),
                        writes=pg(("r_ret", c * T * 2 + wsl.start * 2, c * T * 2 + wsl.stop * 2)),
                        tag="rmul")
        kb.seg()
        wgT = self.load_weight_bf16("r_wgT", W1_OFF, self.din["h_L2_WgT"], [D, D])
        woT = self.load_weight_bf16("r_woT", W2_OFF, self.din["h_L2_WoT"], [D, D])
        gw = self.sb("r_gw", VC_OFF, [P, NC, N], BF16)
        ym = self.sb("r_ym", VC_OFF + 8192, [P, NC, N], BF16)
        xw2 = self.sb("r_xw", ST_OFF, [P, NC, N], F32)
        m2 = self.sb("r_m2", SM_OFF + 704, [P, N], F32)
        a2 = self.sb("r_a2", SM_OFF + 2752, [P, N], F32)
        t2 = self.sb("r_t2", SM_OFF + 4800, [P, N], F32)
        lng = self.load_pp_vec("r_lng", SM_OFF + 6848, self.din["L2_attn_ln_gamma"], D)
        lnb = self.load_pp_vec("r_lnb", SM_OFF + 6976, self.din["L2_attn_ln_beta"], D)
        for w in range(NW):
            kb.seg()
            wsl = slice(w * N, (w + 1) * N)
            kb.dma("scalar", xw2[:], xsrc.rearrange("(c p) t -> p c t", p=P)[:, :, wsl],
                   reads=[ksrc + ":%d" % w], writes=pg("r_xw"), tag="rx")
            for oc in range(NC):
                b = (0, 1)[oc % 2]
                ps = self.ps[b]
                for kc in range(NC):
                    kb.emit("tensor", lambda t, oc=oc, kc=kc, ps=ps, wsl=wsl:
                            t.matmul(ps[:], wgT[:, kc, ts(oc)], z_sb[:, kc, wsl],
                                     start=(kc == 0), stop=(kc == NC - 1)),
                            reads=pg("r_wgT", "z2r"), writes=["PS%d" % b], tag="rg")
                kb.emit("scalar", lambda s, oc=oc, ps=ps: s.activation(gw[:, oc, :], ps[:], AF.Silu),
                        reads=["PS%d" % b], writes=pg(("r_gw", oc * N * 2, (oc + 1) * N * 2)),
                        tag="rgev")
            retw = ret[:, :, wsl]
            self.stats_of(retw, "r_ret", m2, "r_m2", a2, "r_a2", bS=6, bQ=5)
            for oc in range(NC):
                kb.emit("vector", lambda v, oc=oc, retw=retw: v.tensor_sub(t2[:], retw[:, oc, :], m2[:]),
                        reads=pg("r_ret", "r_m2"), writes=pg("r_t2"), tag="ry1")
                kb.emit("vector", lambda v: v.tensor_mul(t2[:], t2[:], a2[:]),
                        reads=pg("r_t2", "r_a2"), writes=pg("r_t2"), tag="ry2")
                kb.emit("vector", lambda v, oc=oc: v.tensor_scalar(
                            t2[:], t2[:], lng[:, oc:oc + 1], lnb[:, oc:oc + 1],
                            ALU.mult, ALU.add),
                        reads=pg("r_t2", "r_lng", "r_lnb"), writes=pg("r_t2"), tag="ry3")
                kb.emit("vector", lambda v, oc=oc: v.tensor_mul(ym[:, oc, :], t2[:], gw[:, oc, :]),
                        reads=pg("r_t2", "r_gw"),
                        writes=pg(("r_ym", oc * N * 2, (oc + 1) * N * 2)), tag="ry4")
            for oc in range(NC):
                b = (2, 3)[oc % 2]
                ps = self.ps[b]
                for kc in range(NC):
                    kb.emit("tensor", lambda t, oc=oc, kc=kc, ps=ps:
                            t.matmul(ps[:], woT[:, kc, ts(oc)], ym[:, kc, :],
                                     start=(kc == 0), stop=(kc == NC - 1)),
                            reads=pg("r_woT", "r_ym"), writes=["PS%d" % b], tag="ro")
                kb.emit("vector", lambda v, oc=oc, ps=ps: v.tensor_add(
                            xw2[:, oc, :], xw2[:, oc, :], ps[:]),
                        reads=["PS%d" % b] + pg("r_xw"), writes=pg("r_xw"), tag="rres")
            self.x_writeback(w, xw2, xdst, kdst, "r_xw")

    def sub_l3attn(self):
        kb, pg = self.kb, self.pg
        xsrc, ksrc, xdst, kdst = self.next_x()
        self.ln_all_to_dram(xsrc, ksrc, self.d_z, "d_z")
        kb.seg()
        B0 = self.PHASE
        M_OFF = B0
        A_OFF = B0 + 65536
        ZC_OFF = A_OFF + 16384
        YC_OFF = ZC_OFF + 16384
        CV_OFF = YC_OFF + 8192
        BY_OFF = CV_OFF + 16384
        SS_OFF = BY_OFF + 16384
        SM_OFF = SS_OFF + 16384
        m_sb = self.sb("l3_m", M_OFF, [P, NC, T], BF16)
        Abc = self.sb("l3_A", A_OFF, [P, T], F32)
        zc = [self.sb("l3_zc0", ZC_OFF, [P, T], BF16), self.sb("l3_zc1", ZC_OFF + 8192, [P, T], BF16)]
        zck = ["l3_zc0", "l3_zc1"]
        yc = self.sb("l3_yc", YC_OFF, [P, T], BF16)
        cv = self.sb("l3_cv", CV_OFF, [P, T], F32)
        by = self.sb("l3_by", BY_OFF, [P, T], F32)
        ssm = self.sb("l3_ss", SS_OFF, [P, T], F32)
        nw1 = self.load_pp_vec("l3_nw1", SM_OFF, self.din["L3_attn_norm_w"], D)
        cwk = [self.load_pp_vec("l3_cwk%d" % k, SM_OFF + 256 + k * 128, self.din["h_L3_convw"][k], D)
               for k in range(3)]
        cb_pp = self.load_pp_vec("l3_cb", SM_OFF + 1024, self.din["L3_attn_conv_b"], D)
        a_pp = self.load_pp_vec("l3_a", SM_OFF + 1280, self.din["L3_attn_a"], D)
        b_pp = self.load_pp_vec("l3_b", SM_OFF + 1536, self.din["L3_attn_b"], D)
        s_pp = self.sb("l3_s", SM_OFF + 1792, [P, NC], F32)
        v1 = self.sb("l3_v1", SM_OFF + 2048, [P, N], F32)
        kb.emit("scalar", lambda s: s.activation(s_pp[:], a_pp[:], AF.Tanh),
                reads=pg("l3_a"), writes=pg("l3_s"), tag="l3tanh")
        psQ = self.ps[7]
        for w in range(NW):
            wsl = slice(w * N, (w + 1) * N)
            zw = zc[w % 2]
            kb.dma("sync", zw[:, 0:NC * N].rearrange("p (c t) -> p c t", c=NC),
                   self.d_z.rearrange("(c p) t -> p c t", p=P)[:, :, wsl],
                   reads=["d_z:%d" % w], writes=pg(zck[w % 2]), tag="l3zl")
            for c in range(NC):
                p2 = c % 2
                kb.emit("scalar", lambda s, c=c, p2=p2, zw=zw: s.activation(
                            self.stsq[p2][:], zw[:, c * N:(c + 1) * N], AF.Square),
                        reads=pg(zck[w % 2]), writes=pg("stsq%d" % p2), tag="l3sq"),
                kb.emit("tensor", lambda t, c=c, p2=p2: t.matmul(
                            psQ[:], self.ones_bf[:, 0:128], self.stsq[p2][:],
                            start=(c == 0), stop=(c == NC - 1)),
                        reads=pg("stsq%d" % p2, "ones_bf"), writes=["PS7"], tag="l3mm")
            kb.emit("scalar", lambda s: s.activation(v1[:], psQ[:], AF.Ln, bias=self.C_EPS6,
                                                     scale=1.0 / D),
                    reads=["PS7", "consts"], writes=pg("l3_v1"), tag="l3ln")
            kb.emit("scalar", lambda s, wsl=wsl: s.activation(Abc[:, wsl], v1[:], AF.Exp, scale=-0.5),
                    reads=pg("l3_v1"), writes=pg(("l3_A", wsl.start * 4, wsl.stop * 4)), tag="l3ex")
        kb.seg()
        for c in range(NC):
            kb.seg()
            zcc = self.sb("l3_zcc%d" % c, ZC_OFF, [P, T], BF16) if False else zc[0]
            kb.dma("sync", zcc[:], self.d_z.rearrange("(c p) t -> p c t", p=P)[:, c, :],
                   reads=["d_z:%d" % w2 for w2 in range(NW)], writes=pg(zck[0]), tag="l3zc")
            kb.emit("vector", lambda v, c=c, zcc=zcc: v.scalar_tensor_tensor(
                        yc[:], zcc[:], nw1[:, c:c + 1], Abc[:], ALU.mult, ALU.mult),
                    reads=pg(zck[0], "l3_nw1", "l3_A"), writes=pg("l3_yc"), tag="l3y")
            kb.emit("vector", lambda v, c=c: v.tensor_scalar(
                        cv[:], yc[:], cwk[1][:, c:c + 1], cb_pp[:, c:c + 1],
                        ALU.mult, ALU.add),
                    reads=pg("l3_yc", "l3_cwk1", "l3_cb"), writes=pg("l3_cv"), tag="l3c1")
            kb.emit("vector", lambda v, c=c: v.scalar_tensor_tensor(
                        cv[:, 1:T], yc[:, 0:T - 1], cwk[0][:, c:c + 1], cv[:, 1:T],
                        ALU.mult, ALU.add),
                    reads=pg("l3_yc", "l3_cwk0", "l3_cv"), writes=pg("l3_cv"), tag="l3c0")
            kb.emit("vector", lambda v, c=c: v.scalar_tensor_tensor(
                        cv[:, 0:T - 1], yc[:, 1:T], cwk[2][:, c:c + 1], cv[:, 0:T - 1],
                        ALU.mult, ALU.add),
                    reads=pg("l3_yc", "l3_cwk2", "l3_cv"), writes=pg("l3_cv"), tag="l3c2")
            kb.emit("vector", lambda v, c=c: v.tensor_scalar_mul(by[:], yc[:], b_pp[:, c:c + 1]),
                    reads=pg("l3_yc", "l3_b"), writes=pg("l3_by"), tag="l3by")
            kb.emit("vector", lambda v, c=c: v.tensor_tensor_scan(
                        ssm[:], s_pp[:, c:c + 1].broadcast_to([P, T]), by[:],
                        0.0, ALU.mult, ALU.add),
                    reads=pg("l3_by", "l3_s"), writes=pg("l3_ss"), tag="l3scan")
            kb.emit("vector", lambda v, c=c, zcc=zcc: v.tensor_add(cv[:], cv[:], zcc[:]),
                    reads=pg("l3_cv", zck[0]), writes=pg("l3_cv"), tag="l3m1")
            kb.emit("vector", lambda v, c=c: v.tensor_add(
                        m_sb[:, c, :], cv[:], ssm[:]),
                    reads=pg("l3_cv", "l3_ss"),
                    writes=pg(("l3_m", c * T * 2, (c + 1) * T * 2)), tag="l3m2")
        kb.seg()
        nw2 = self.load_pp_vec("l3_nw2", SM_OFF + 2048 + 2048, self.din["L3_attn_mlp_norm_w"], D)
        bg_pp = self.load_pp_vec("l3_bg", SM_OFF + 4352, self.din["L3_attn_b_gate"], 4096)
        self.bu_row = self.load_row_bf16("bu_row", SM_OFF + 4608, self.din["L3_attn_b_up"], 4096)
        bd_pp = self.load_pp_vec("l3_bd", SM_OFF + 13056, self.din["L3_attn_b_down"], D)
        u2 = self.sb("l3_u2", ZC_OFF, [P, NC, N], BF16)
        hwin = self.sb("l3_hw", CV_OFF, [P, 32, N], BF16)
        wgt = [self.sb("l3_wg0", A_OFF, [P, NC, P], BF16),
               self.sb("l3_wg1", A_OFF + 2048, [P, NC, P], BF16)]
        wgtk = ["l3_wg0", "l3_wg1"]
        wut = [self.sb("l3_wu0", A_OFF + 4096, [P, NC, P], BF16),
               self.sb("l3_wu1", A_OFF + 6144, [P, NC, P], BF16)]
        wutk = ["l3_wu0", "l3_wu1"]
        wdt = [self.sb("l3_wd0", A_OFF + 8192, [P, D], BF16),
               self.sb("l3_wd1", A_OFF + 10240, [P, D], BF16)]
        wdtk = ["l3_wd0", "l3_wd1"]
        gt = self.sb("l3_gt", A_OFF + 12288, [P, N], BF16)
        xw3 = self.sb("l3_xw3", YC_OFF, [P, NC, N], F32) if False else self.sb("l3_xw3", SS_OFF, [P, NC, N], F32)
        m3 = self.sb("l3_m3", SM_OFF + 13568, [P, N], F32)
        a3 = self.sb("l3_a3", SM_OFF + 15616, [P, N], F32)
        for w in range(NW):
            kb.seg()
            wsl = slice(w * N, (w + 1) * N)
            mw = m_sb[:, :, wsl]
            for c in range(NC):
                p2 = c % 2
                kb.emit("scalar", lambda s, c=c, p2=p2, mw=mw: s.activation(
                            self.stsq[p2][:], mw[:, c, :], AF.Square),
                        reads=pg("l3_m"), writes=pg("stsq%d" % p2), tag="l3q2")
                kb.emit("tensor", lambda t, c=c, p2=p2: t.matmul(
                            psQ[:], self.ones_bf[:, 0:128], self.stsq[p2][:],
                            start=(c == 0), stop=(c == NC - 1)),
                        reads=pg("stsq%d" % p2, "ones_bf"), writes=["PS7"], tag="l3mm2")
            kb.emit("scalar", lambda s: s.activation(m3[:], psQ[:], AF.Ln, bias=self.C_EPS6,
                                                     scale=1.0 / D),
                    reads=["PS7", "consts"], writes=pg("l3_m3"), tag="l3l2")
            kb.emit("scalar", lambda s: s.activation(a3[:], m3[:], AF.Exp, scale=-0.5),
                    reads=pg("l3_m3"), writes=pg("l3_a3"), tag="l3e2")
            for c in range(NC):
                kb.emit("vector", lambda v, c=c, mw=mw: v.scalar_tensor_tensor(
                            u2[:, c, :], mw[:, c, :], nw2[:, c:c + 1], a3[:],
                            ALU.mult, ALU.mult),
                        reads=pg("l3_m", "l3_nw2", "l3_a3"),
                        writes=pg(("l3_u2", c * N * 2, (c + 1) * N * 2)), tag="l3u2")
            for fc in range(32):
                p2 = fc % 2
                kb.dma("gpsimd", wgt[p2][:], self.din["L3_attn_w_gate"].rearrange(
                            "(c p) h -> p c h", p=P)[:, :, ts(fc)],
                       writes=pg(wgtk[p2]), tag="l3wg")
                kb.dma("gpsimd", wut[p2][:], self.din["L3_attn_w_up"].rearrange(
                            "(c p) h -> p c h", p=P)[:, :, ts(fc)],
                       writes=pg(wutk[p2]), tag="l3wu")
                bA = (0, 1)[fc % 2]
                psA = self.ps[bA]
                for kc in range(NC):
                    kb.emit("tensor", lambda t, fc=fc, kc=kc, p2=p2, psA=psA:
                            t.matmul(psA[:], wgt[p2][:, kc, :], u2[:, kc, :],
                                     start=(kc == 0), stop=(kc == NC - 1)),
                            reads=pg(wgtk[p2], "l3_u2"), writes=["PS%d" % bA], tag="l3g")
                kb.emit("scalar", lambda s, fc=fc, psA=psA: s.activation(
                            gt[:], psA[:], AF.Silu, bias=bg_pp[:, fc:fc + 1]),
                        reads=["PS%d" % bA] + pg("l3_bg"), writes=pg("l3_gt"), tag="l3gs")
                bB = (2, 3)[fc % 2]
                psB = self.ps[bB]
                for kc in range(NC):
                    kb.emit("tensor", lambda t, fc=fc, kc=kc, p2=p2, psB=psB:
                            t.matmul(psB[:], wut[p2][:, kc, :], u2[:, kc, :],
                                     start=(kc == 0), stop=False),
                            reads=pg(wutk[p2], "l3_u2"), writes=["PS%d" % bB], tag="l3up")
                kb.emit("tensor", lambda t, fc=fc, psB=psB: t.matmul(
                            psB[:], self.bu_row[0:1, ts(fc)], self.ones_bf[0:1, 0:N],
                            start=False, stop=True),
                        reads=pg("bu_row", "ones_bf"), writes=["PS%d" % bB], tag="l3ub")
                kb.emit("vector", lambda v, fc=fc, psB=psB: v.tensor_mul(
                            hwin[:, fc, :], gt[:], psB[:]),
                        reads=pg("l3_gt") + ["PS%d" % bB],
                        writes=pg(("l3_hw", fc * N * 2, (fc + 1) * N * 2)), tag="l3h")
            for fc in range(32):
                p2 = fc % 2
                kb.dma("gpsimd", wdt[p2][:], self.din["L3_attn_w_down"].rearrange(
                            "(c p) d -> p c d", p=P)[:, fc, :],
                       writes=pg(wdtk[p2]), tag="l3wd")
                for dc in range(NC):
                    kb.emit("tensor", lambda t, fc=fc, dc=dc, p2=p2:
                            t.matmul(self.ps[dc][:], wdt[p2][:, ts(dc)], hwin[:, fc, :],
                                     start=(fc == 0), stop=(fc == 31)),
                            reads=pg(wdtk[p2], "l3_hw"), writes=["PS%d" % dc], tag="l3d")
            kb.dma("scalar", xw3[:], xsrc.rearrange("(c p) t -> p c t", p=P)[:, :, wsl],
                   reads=[ksrc + ":%d" % w], writes=pg("l3_xw3"), tag="l3x")
            for dc in range(NC):
                kb.emit("vector", lambda v, dc=dc, mw=mw: v.tensor_add(
                            xw3[:, dc, :], xw3[:, dc, :], mw[:, dc, :]),
                        reads=pg("l3_m", "l3_xw3"), writes=pg("l3_xw3"), tag="l3r1")
                kb.emit("vector", lambda v, dc=dc: v.scalar_tensor_tensor(
                            xw3[:, dc, :], self.ps[dc][:], bd_pp[:, dc:dc + 1], xw3[:, dc, :],
                            ALU.add, ALU.add),
                        reads=["PS%d" % dc] + pg("l3_bd", "l3_xw3"), writes=pg("l3_xw3"), tag="l3r2")
            self.x_writeback(w, xw3, xdst, kdst, "l3_xw3")

    def sub_l3ffn(self):
        kb, pg = self.kb, self.pg
        xsrc, ksrc, xdst, kdst = self.next_x()
        self.ln_all_to_dram(xsrc, ksrc, self.d_z, "d_z")
        kb.seg()
        B0 = self.PHASE
        ZS_OFF = B0
        GW_OFF = B0 + 16384
        HP_OFF = B0 + 20480
        WS_OFF = HP_OFF + 36864
        CW_OFF = WS_OFF + 12288
        XW_OFF = CW_OFF + 8192
        SM_OFF = XW_OFF + 16384
        zs = [self.sb("f3_z0", ZS_OFF, [P, NC, N], BF16), self.sb("f3_z1", ZS_OFF + 8192, [P, NC, N], BF16)]
        zsk = ["f3_z0", "f3_z1"]
        gwT = self.sb("f3_gwT", GW_OFF, [P, NC, 8], BF16)
        kb.dma("gpsimd", gwT[:], self.din["h_L3f_gwT"].rearrange("(c p) e -> p c e", p=P),
               writes=pg("f3_gwT"), tag="f3gw")
        hall = self.sb("f3_hall", HP_OFF, [P, 36, N], BF16)
        wg2 = [self.sb("f3_wg0", WS_OFF, [P, NC, P], BF16), self.sb("f3_wg1", WS_OFF + 2048, [P, NC, P], BF16)]
        wg2k = ["f3_wg0", "f3_wg1"]
        wu2 = [self.sb("f3_wu0", WS_OFF + 4096, [P, NC, P], BF16), self.sb("f3_wu1", WS_OFF + 6144, [P, NC, P], BF16)]
        wu2k = ["f3_wu0", "f3_wu1"]
        wd2 = [self.sb("f3_wd0", WS_OFF + 8192, [P, D], BF16), self.sb("f3_wd1", WS_OFF + 10240, [P, D], BF16)]
        wd2k = ["f3_wd0", "f3_wd1"]
        cwb = self.sb("f3_cwb", CW_OFF, [P, N], F32)
        cwrow = self.sb("f3_cwrow", CW_OFF + 2048, [1, N], BF16)
        xw = self.sb("f3_xw", XW_OFF, [P, NC, N], F32)
        gt8 = self.sb("f3_gt8", SM_OFF, [P, 8], F32)
        e8 = self.sb("f3_e8", SM_OFF + 64, [P, 8], F32)
        mx1 = self.sb("f3_mx", SM_OFF + 96, [P, 1], F32)
        s8 = self.sb("f3_s8", SM_OFF + 128, [P, 1], F32)
        t8 = self.sb("f3_t8", SM_OFF + 160, [P, 8], F32)
        cwt = self.sb("f3_cwt", SM_OFF + 192, [P, 8], F32)
        cwfm = self.sb("f3_cwfm", SM_OFF + 256, [8, T], F32)
        gsc = self.sb("f3_gsc", SM_OFF + 16640, [P, N], BF16)
        for w in range(NW):
            kb.seg()
            wsl = slice(w * N, (w + 1) * N)
            par = w % 2
            zw, zwk = zs[par], zsk[par]
            kb.dma("sync", zw[:], self.d_z.rearrange("(c p) t -> p c t", p=P)[:, :, wsl],
                   reads=["d_z:%d" % w], writes=pg(zwk), tag="f3zl")
            psg = self.ps[6]
            for kc in range(NC):
                kb.emit("tensor", lambda t, kc=kc, zw=zw: t.matmul(
                            psg[0:8, :], gwT[:, kc, :], zw[:, kc, :],
                            start=(kc == 0), stop=(kc == NC - 1)),
                        reads=pg("f3_gwT", zwk), writes=["PS6"], tag="f3g")
            kb.emit("scalar", lambda s: s.activation(gsc[0:8, :], psg[0:8, :], AF.Copy),
                    reads=["PS6"], writes=pg("f3_gsc"), tag="f3gev")
            for tt in range(4):
                pst = self.ps[7]
                pstb = pst.bitcast(BF16)
                kb.emit("tensor", lambda t, tt=tt, pstb=pstb: t.transpose(
                            pstb[0:P, 0:8], gsc[0:8, ts(tt)], self.ident_bf[0:8, 0:8]),
                        reads=pg("f3_gsc", "ident_bf"), writes=["PS7"], tag="f3tr")
                kb.emit("vector", lambda v, pstb=pstb: v.tensor_copy(gt8[:], pstb[0:P, 0:8]),
                        reads=["PS7"], writes=pg("f3_gt8"), tag="f3cp")
                kb.emit("vector", lambda v: v.reduce_max(mx1[:], gt8[:], AX.X),
                        reads=pg("f3_gt8"), writes=pg("f3_mx"), tag="f3mx")
                kb.emit("vector", lambda v: v.tensor_scalar_mul(mx1[:], mx1[:], -1.0),
                        reads=pg("f3_mx"), writes=pg("f3_mx"), tag="f3ng")
                kb.emit("scalar", lambda s: s.activation(e8[:], gt8[:], AF.Exp, bias=mx1[:]),
                        reads=pg("f3_gt8", "f3_mx"), writes=pg("f3_e8"), tag="f3ex")
                kb.emit("vector", lambda v: v.reduce_sum(s8[:], e8[:], AX.X),
                        reads=pg("f3_e8"), writes=pg("f3_s8"), tag="f3sm")
                kb.emit("vector", lambda v: v.reciprocal(s8[:], s8[:]),
                        reads=pg("f3_s8"), writes=pg("f3_s8"), tag="f3rc")
                kb.emit("vector", lambda v: v.tensor_scalar_mul(e8[:], e8[:], s8[:]),
                        reads=pg("f3_e8", "f3_s8"), writes=pg("f3_e8"), tag="f3pb")
                kb.emit("vector", lambda v: v.max(t8[:], e8[:]),
                        reads=pg("f3_e8"), writes=pg("f3_t8"), tag="f3t8")
                kb.emit("vector", lambda v: v.tensor_scalar(cwt[:], e8[:], t8[:, 1:2], None,
                                                            ALU.is_ge),
                        reads=pg("f3_e8", "f3_t8"), writes=pg("f3_cwt"), tag="f3ge")
                kb.emit("vector", lambda v: v.tensor_mul(cwt[:], cwt[:], e8[:]),
                        reads=pg("f3_cwt", "f3_e8"), writes=pg("f3_cwt"), tag="f3cw")
                kb.emit("tensor", lambda t: t.transpose(pst[0:8, 0:P], cwt[:],
                                                        self.ident_f32[:]),
                        reads=pg("f3_cwt", "ident_f32"), writes=["PS7"], tag="f3t2")
                kb.emit("vector", lambda v, w=w, tt=tt: v.tensor_copy(
                            cwfm[0:8, w * N + tt * P:w * N + (tt + 1) * P], pst[0:8, 0:P]),
                        reads=["PS7"], writes=pg("f3_cwfm"), tag="f3cf")
            kb.dma("sync", self.d_cw[:, wsl], cwfm[0:8, wsl], reads=pg("f3_cwfm"),
                   writes=["d_cw:%d" % w], tag="f3cs")
            for e in range(9):
                shared = (e == 8)
                p2 = e % 2
                if shared:
                    wsrc_g, wsrc_u = self.din["h_L3f_sgT"], self.din["h_L3f_suT"]
                else:
                    wsrc_g = self.din["h_L3f_wgT"][e]
                    wsrc_u = self.din["h_L3f_wuT"][e]
                for fc in range(4):
                    kb.dma("gpsimd", wg2[p2][:], wsrc_g.rearrange("(c p) f -> p c f", p=P)[:, :, ts(fc)],
                           writes=pg(wg2k[p2]), tag="f3wg")
                    kb.dma("gpsimd", wu2[p2][:], wsrc_u.rearrange("(c p) f -> p c f", p=P)[:, :, ts(fc)],
                           writes=pg(wu2k[p2]), tag="f3wu")
                    bA = (0, 1)[fc % 2]
                    psA = self.ps[bA]
                    for kc in range(NC):
                        kb.emit("tensor", lambda t, kc=kc, p2=p2, psA=psA, zw=zw:
                                t.matmul(psA[:], wg2[p2][:, kc, :], zw[:, kc, :],
                                         start=(kc == 0), stop=(kc == NC - 1)),
                                reads=pg(wg2k[p2], zwk), writes=["PS%d" % bA], tag="f3gm")
                    kb.emit("scalar", lambda s, psA=psA: s.activation(gsc[:], psA[:], AF.Silu),
                            reads=["PS%d" % bA], writes=pg("f3_gsc"), tag="f3si")
                    bB = (2, 3)[fc % 2]
                    psB = self.ps[bB]
                    for kc in range(NC):
                        kb.emit("tensor", lambda t, kc=kc, p2=p2, psB=psB, zw=zw:
                                t.matmul(psB[:], wu2[p2][:, kc, :], zw[:, kc, :],
                                         start=(kc == 0), stop=(kc == NC - 1)),
                                reads=pg(wu2k[p2], zwk), writes=["PS%d" % bB], tag="f3um")
                    hidx = e * 4 + fc
                    if shared:
                        kb.emit("vector", lambda v, hidx=hidx, psB=psB: v.tensor_mul(
                                    hall[:, hidx, :], gsc[:], psB[:]),
                                reads=pg("f3_gsc") + ["PS%d" % bB],
                                writes=pg(("f3_hall", hidx * N * 2, (hidx + 1) * N * 2)), tag="f3h")
                    else:
                        if fc == 0:
                            kb.dma("gpsimd", cwrow[:], self.d_cw[e:e + 1, wsl],
                                   reads=["d_cw:%d" % w], writes=pg("f3_cwrow"), tag="f3cr")
                            psC = self.ps[5]
                            kb.emit("tensor", lambda t: t.matmul(psC[:], self.ones_bf[0:1, 0:128],
                                                                 cwrow[:], start=True, stop=True),
                                    reads=pg("f3_cwrow", "ones_bf"), writes=["PS5"], tag="f3cb")
                            kb.emit("vector", lambda v: v.tensor_copy(cwb[:], psC[:]),
                                    reads=["PS5"], writes=pg("f3_cwb"), tag="f3ce")
                        kb.emit("vector", lambda v, psB=psB: v.tensor_mul(gsc[:], gsc[:], psB[:]),
                                reads=pg("f3_gsc") + ["PS%d" % bB], writes=pg("f3_gsc"), tag="f3h1")
                        kb.emit("vector", lambda v, hidx=hidx: v.tensor_mul(
                                    hall[:, hidx, :], gsc[:], cwb[:]),
                                reads=pg("f3_gsc", "f3_cwb"),
                                writes=pg(("f3_hall", hidx * N * 2, (hidx + 1) * N * 2)), tag="f3h2")
            for e in range(9):
                p2 = e % 2
                wsrc_d = self.din["h_L3f_sdT"] if e == 8 else self.din["h_L3f_wdT"][e]
                for fc in range(4):
                    kb.dma("gpsimd", wd2[p2][:], wsrc_d.rearrange("(c p) d -> p c d", p=P)[:, fc, :],
                           writes=pg(wd2k[p2]), tag="f3wd")
                    hidx = e * 4 + fc
                    for dc in range(NC):
                        kb.emit("tensor", lambda t, fc=fc, dc=dc, p2=p2, hidx=hidx, e=e:
                                t.matmul(self.ps[dc][:], wd2[p2][:, ts(dc)], hall[:, hidx, :],
                                         start=(e == 0 and fc == 0), stop=(e == 8 and fc == 3)),
                                reads=pg(wd2k[p2], "f3_hall"), writes=["PS%d" % dc], tag="f3d")
            kb.dma("scalar", xw[:], xsrc.rearrange("(c p) t -> p c t", p=P)[:, :, wsl],
                   reads=[ksrc + ":%d" % w], writes=pg("f3_xw"), tag="f3x")
            for dc in range(NC):
                kb.emit("vector", lambda v, dc=dc: v.scalar_tensor_tensor(
                            xw[:, dc, :], self.ps[dc][:], 0.5, xw[:, dc, :],
                            ALU.mult, ALU.add),
                        reads=["PS%d" % dc] + pg("f3_xw"), writes=pg("f3_xw"), tag="f3r")
            self.x_writeback(w, xw, xdst, kdst, "f3_xw")

    def sub_l2ffn(self):
        kb, pg = self.kb, self.pg
        xsrc, ksrc, xdst, kdst = self.next_x()
        self.ln_all_to_dram(xsrc, ksrc, self.d_z, "d_z")
        kb.seg()
        B0 = self.PHASE
        Z_OFF = B0
        W1_OFF = Z_OFF + 65536
        W2_OFF = W1_OFF + 16384
        S0F_OFF = W2_OFF + 512
        S0S_OFF = S0F_OFF + 1024
        H1_OFF = S0S_OFF + 2048
        ZT_OFF = H1_OFF + 4096
        AFM_OFF = 200416
        SM_OFF = ZT_OFF + 4096
        WIH_OFF = 139264
        z_sb = self.sb("f2_z", Z_OFF, [P, NC, T], BF16)
        for w in range(NW):
            wsl = slice(w * N, (w + 1) * N)
            kb.dma("sync", z_sb[:, :, wsl], self.d_z.rearrange("(c p) t -> p c t", p=P)[:, :, wsl],
                   reads=["d_z:%d" % w], writes=pg(("f2_z",)), tag="f2zl")
        w1 = self.sb("f2_w1", W1_OFF, [P, 16, 512], BF16)
        kb.dma("gpsimd", w1[:], self.din["L2_ffn_w1"].rearrange("(c p) f -> p c f", p=P),
               writes=pg("f2_w1"), tag="f2w1")
        w2 = self.sb("f2_w2", W2_OFF, [P, 4, 64], BF16)
        kb.dma("gpsimd", w2[:], self.din["L2_ffn_w2"].rearrange("(c p) s -> p c s", p=P),
               writes=pg("f2_w2"), tag="f2w2")
        S0f = self.sb("f2_S0f", S0F_OFF, [P, NC, 64], BF16)
        kb.dma("gpsimd", S0f[:], self.din["h_L2f_S0T"].rearrange("(c p) s -> p c s", p=P),
               writes=pg("f2_S0f"), tag="f2s0f")
        h1w = self.sb("f2_h1", H1_OFF, [P, 4, N], BF16)
        zt = [self.sb("f2_zt0", ZT_OFF, [P, D], BF16), self.sb("f2_zt1", ZT_OFF + 2048, [P, D], BF16)]
        ztk = ["f2_zt0", "f2_zt1"]
        afm = self.sb("f2_afm", AFM_OFF, [64, T], BF16)
        smb = self.sb("f2_smb", SM_OFF, [P, NC], BF16)
        smf = self.sb("f2_smf", SM_OFF + 32, [P, NC], F32)
        cfb = self.sb("f2_cfb", SM_OFF + 64, [P, 4], F32)
        b1pp = self.load_pp_vec("f2_b1", SM_OFF + 96, self.din["L2_ffn_b1"], 512)
        b2row = self.load_row_bf16("f2_b2", SM_OFF + 128, self.din["L2_ffn_b2"], 64)
        rtau = self.sb("f2_rtau", SM_OFF + 288, [1, 1], F32)
        rtaub = self.sb("f2_rtaub", SM_OFF + 320, [P, 1], F32)
        rlf = self.sb("f2_rlf", SM_OFF + 352, [64, N], F32)
        rltm = self.sb("f2_rltm", SM_OFF + 2400, [P, 64], F32)
        e64 = self.sb("f2_e64", SM_OFF + 2656, [P, 64], F32)
        mx1 = self.sb("f2_mx", SM_OFF + 2912, [P, 1], F32)
        t8 = self.sb("f2_t8", SM_OFF + 2944, [P, 8], F32)
        rd1 = self.sb("f2_rd", SM_OFF + 2976, [P, 1], F32)
        atm = self.sb("f2_atm", SM_OFF + 3008, [P, 64], BF16)
        for c in range(NC):
            kb.emit("vector", lambda v, c=c: v.reduce_sum(smf[:, c:c + 1], S0f[:, c, :], AX.X),
                    reads=pg("f2_S0f"), writes=pg("f2_smf"), tag="f2sm")
        kb.emit("vector", lambda v: v.tensor_scalar_mul(smb[:], smf[:], 1.0 / 64),
                reads=pg("f2_smf"), writes=pg("f2_smb"), tag="f2sm2")
        for fc in range(4):
            ps = self.ps[0]
            for kc in range(NC):
                kb.emit("tensor", lambda t, fc=fc, kc=kc, ps=ps: t.matmul(
                            ps[0:P, 0:1], w1[:, 8 + kc, ts(fc)], smb[:, kc:kc + 1],
                            start=(kc == 0), stop=(kc == NC - 1)),
                        reads=pg("f2_w1", "f2_smb"), writes=["PS0"], tag="f2cf")
            kb.emit("vector", lambda v, fc=fc: v.tensor_add(cfb[:, fc:fc + 1], ps[0:P, 0:1],
                                                            b1pp[:, fc:fc + 1]),
                    reads=["PS0"] + pg("f2_b1"), writes=pg("f2_cfb"), tag="f2cfe")
        kb.dma("sync", rtau[:], self.din["L2_ffn_tau"].unsqueeze(0), writes=pg("f2_rtau"))
        kb.emit("scalar", lambda s: s.activation(rtau[:], rtau[:], AF.Abs),
                reads=pg("f2_rtau"), writes=pg("f2_rtau"), tag="f2tau")
        kb.emit("vector", lambda v: v.tensor_scalar_add(rtau[:], rtau[:], 0.1),
                reads=pg("f2_rtau"), writes=pg("f2_rtau"), tag="f2tau")
        kb.emit("vector", lambda v: v.reciprocal(rtau[:], rtau[:]),
                reads=pg("f2_rtau"), writes=pg("f2_rtau"), tag="f2tau")
        kb.emit("vector", lambda v: v.tensor_copy(smb[0:1, 0:1].bitcast(BF16)[0:1, 0:1] if False else atm[0:1, 0:1], rtau[:]),
                reads=pg("f2_rtau"), writes=pg("f2_atm"), tag="f2tc")
        kb.emit("tensor", lambda t: t.matmul(self.ps[1][0:P, 0:1], self.ones_bf[0:1, 0:128],
                                             atm[0:1, 0:1], start=True, stop=True),
                reads=pg("f2_atm", "ones_bf"), writes=["PS1"], tag="f2tb")
        kb.emit("vector", lambda v: v.tensor_copy(rtaub[:], self.ps[1][0:P, 0:1]),
                reads=["PS1"], writes=pg("f2_rtaub"), tag="f2tb2")
        psSI = [self.ps[5], self.ps[6]]
        psDN = self.ps[7]
        for w in range(NW):
            kb.seg()
            wsl = slice(w * N, (w + 1) * N)
            for fc in range(4):
                b = (0, 1)[fc % 2]
                ps = self.ps[b]
                for kc in range(NC):
                    kb.emit("tensor", lambda t, fc=fc, kc=kc, ps=ps, wsl=wsl: t.matmul(
                                ps[:], w1[:, kc, ts(fc)], z_sb[:, kc, wsl],
                                start=(kc == 0), stop=(kc == NC - 1)),
                            reads=pg("f2_w1", "f2_z"), writes=["PS%d" % b], tag="f2h1")
                kb.emit("scalar", lambda s, fc=fc, ps=ps: s.activation(
                            h1w[:, fc, :], ps[:], AF.Gelu, bias=cfb[:, fc:fc + 1]),
                        reads=["PS%d" % b] + pg("f2_cfb"),
                        writes=pg(("f2_h1", fc * N * 2, (fc + 1) * N * 2)), tag="f2ge")
            b = 2
            ps = self.ps[b]
            for fc in range(4):
                kb.emit("tensor", lambda t, fc=fc, ps=ps: t.matmul(
                            ps[0:64, :], w2[:, fc, :], h1w[:, fc, :],
                            start=(fc == 0), stop=False),
                        reads=pg("f2_w2", "f2_h1"), writes=["PS2"], tag="f2rl")
            kb.emit("tensor", lambda t, ps=ps: t.matmul(
                        ps[0:64, :], b2row[:], self.ones_bf[0:1, 0:N],
                        start=False, stop=True),
                    reads=pg("f2_b2", "ones_bf"), writes=["PS2"], tag="f2rlb")
            kb.emit("scalar", lambda s, ps=ps: s.activation(rlf[:], ps[0:64, :], AF.Copy),
                    reads=["PS2"], writes=pg("f2_rlf"), tag="f2rle")
            for tt in range(4):
                gt = w * 4 + tt
                pst = self.ps[3]
                pstb = pst.bitcast(BF16)
                kb.emit("tensor", lambda t, tt=tt, pst=pst: t.transpose(
                            pst[0:P, 0:64], rlf[0:64, ts(tt)], self.ident_f32[0:64, 0:64]),
                        reads=pg("f2_rlf", "ident_f32"), writes=["PS3"], tag="f2tr")
                kb.emit("vector", lambda v, pst=pst: v.tensor_copy(rltm[:], pst[0:P, 0:64]),
                        reads=["PS3"], writes=pg("f2_rltm"), tag="f2te")
                kb.emit("vector", lambda v: v.tensor_scalar_mul(rltm[:], rltm[:], rtaub[:]),
                        reads=pg("f2_rltm", "f2_rtaub"), writes=pg("f2_rltm"), tag="f2sc")
                kb.emit("vector", lambda v: v.reduce_max(mx1[:], rltm[:], AX.X),
                        reads=pg("f2_rltm"), writes=pg("f2_mx"), tag="f2mx")
                kb.emit("vector", lambda v: v.tensor_scalar_mul(mx1[:], mx1[:], -1.0),
                        reads=pg("f2_mx"), writes=pg("f2_mx"), tag="f2ng")
                kb.emit("scalar", lambda s: s.activation(e64[:], rltm[:], AF.Exp, bias=mx1[:]),
                        reads=pg("f2_rltm", "f2_mx"), writes=pg("f2_e64"), tag="f2ex")
                kb.emit("vector", lambda v: v.max(t8[:], e64[:]),
                        reads=pg("f2_e64"), writes=pg("f2_t8"), tag="f2t8")
                kb.emit("vector", lambda v: v.reduce_sum(rd1[:], t8[:], AX.X),
                        reads=pg("f2_t8"), writes=pg("f2_rd"), tag="f2dn")
                kb.emit("vector", lambda v: v.reciprocal(rd1[:], rd1[:]),
                        reads=pg("f2_rd"), writes=pg("f2_rd"), tag="f2rc")
                kb.emit("vector", lambda v: v.tensor_scalar(rltm[:], e64[:], t8[:, 7:8], None,
                                                            ALU.is_ge),
                        reads=pg("f2_e64", "f2_t8"), writes=pg("f2_rltm"), tag="f2msk")
                kb.emit("vector", lambda v: v.tensor_mul(rltm[:], rltm[:], e64[:]),
                        reads=pg("f2_rltm", "f2_e64"), writes=pg("f2_rltm"), tag="f2am")
                kb.emit("vector", lambda v: v.tensor_scalar_mul(atm[:], rltm[:], rd1[:]),
                        reads=pg("f2_rltm", "f2_rd"), writes=pg("f2_atm"), tag="f2al")
                kb.emit("tensor", lambda t, pstb=pstb: t.transpose(
                            pstb[0:64, 0:P], atm[:], self.ident_bf[:]),
                        reads=pg("f2_atm", "ident_bf"), writes=["PS3"], tag="f2at")
                kb.emit("vector", lambda v, gt=gt, pstb=pstb: v.tensor_copy(
                            afm[0:64, gt * P:(gt + 1) * P], pstb[0:64, 0:P]),
                        reads=["PS3"], writes=pg("f2_afm"), tag="f2ae")
                p2 = tt % 2
                for c in range(NC):
                    kb.emit("tensor", lambda t, c=c, gt=gt, pstb=pstb: t.transpose(
                                pstb[0:P, 0:P], z_sb[:, c, gt * P:(gt + 1) * P],
                                self.ident_bf[:]),
                            reads=pg("f2_z", "ident_bf"), writes=["PS3"], tag="f2zt")
                    kb.emit("vector", lambda v, c=c, p2=p2, pstb=pstb: v.tensor_copy(
                                zt[p2][:, ts(c)], pstb[0:P, 0:P]),
                            reads=["PS3"], writes=pg(ztk[p2]), tag="f2ze")
                for half in range(2):
                    kb.emit("tensor", lambda t, half=half, p2=p2, gt=gt: t.matmul(
                                psSI[half][0:64, :], atm[:], zt[p2][:, ts(half, 512)],
                                start=(gt == 0), stop=(gt == 31)),
                            reads=pg("f2_atm", ztk[p2]), writes=["PS%d" % (5 + half)], tag="f2si")
                kb.emit("tensor", lambda t, gt=gt: t.matmul(
                            psDN[0:64, 0:1], atm[:], self.ones_bf[:, 0:1],
                            start=(gt == 0), stop=(gt == 31)),
                        reads=pg("f2_atm", "ones_bf"), writes=["PS7"], tag="f2dn2")
        kb.seg()
        SL = B0
        wih = self.sb("f2_wih", WIH_OFF, [P, NC, 3072], BF16)
        kb.dma("gpsimd", wih[:], self.din["h_L2f_WihT"].rearrange("(c p) f -> p c f", p=P),
               writes=pg("f2_wih"), tag="f2wih")
        sin_sm = self.sb("f2_sin", SL, [64, D], F32)
        den = self.sb("f2_den", SL + 4096, [64, 1], F32)
        sif = self.sb("f2_sif", SL + 4128, [P, NC, 64], BF16)
        gi = self.sb("f2_gi", SL + 5152, [P, 24, 64], F32)
        gh = self.sb("f2_gh", SL + 11296, [P, 24, 64], F32)
        rr = self.sb("f2_rr", SL + 17440, [P, NC, 64], F32)
        zzt = self.sb("f2_zz", SL + 19488, [P, NC, 64], F32)
        nn = self.sb("f2_nn", SL + 21536, [P, NC, 64], F32)
        snb = self.sb("f2_snb", SL + 23584, [P, NC, 64], BF16)
        qf = self.sb("f2_qf", SL + 24608, [P, 2, 64], BF16)
        kf = self.sb("f2_kf", SL + 24864, [P, 2, 64], BF16)
        vf = self.sb("f2_vf", SL + 25120, [P, NC, 64], BF16)
        vsm = self.sb("f2_vsm", SL + 26144, [64, D], BF16)
        snsm = self.sb("f2_snsm", SL + 28192, [64, D], BF16)
        ysm = self.sb("f2_ysm", SL + 30240, [64, D], F32)
        p2s = self.sb("f2_p2s", SL + 34336, [64, 64], BF16)
        p2t = self.sb("f2_p2t", SL + 34464, [64, 64], BF16)
        mu2 = self.sb("f2_mu2", SL + 34592, [64, 1], F32)
        l2t = self.sb("f2_l2", SL + 34624, [64, 1], F32)
        lngb = self.sb("f2_lngb", SL + 34656, [64, D], F32)
        lnbb = self.sb("f2_lnbb", SL + 38752, [64, D], F32)
        svs = self.sb("f2_svs", SL + 42848, [64, D], BF16)
        y2f = self.sb("f2_y2f", SL + 44896, [P, NC, 64], BF16)
        bihpp = self.load_pp_vec("f2_bih", SL + 45920, self.din["L2_ffn_bih"], 3072)
        bhhpp = self.load_pp_vec("f2_bhh", SL + 46016, self.din["L2_ffn_bhh"], 3072)
        bqpp = self.load_pp_vec("f2_bq", SL + 46112, self.din["L2_ffn_bq"], 256)
        bkpp = self.load_pp_vec("f2_bk", SL + 46144, self.din["L2_ffn_bk"], 256)
        bvpp = self.load_pp_vec("f2_bv", SL + 46176, self.din["L2_ffn_bv"], D)
        lngr = self.load_row_bf16("f2_lngr", SL + 46208, self.din["L2_ffn_ln_g"], D)
        lnbr = self.load_row_bf16("f2_lnbr", SL + 48256, self.din["L2_ffn_ln_b"], D)
        bvpr = self.load_row_bf16("f2_bvpr", SL + 50304, self.din["L2_ffn_bvp"], D)
        bor = self.load_row_bf16("f2_bor", SL + 52352, self.din["L2_ffn_bo"], D)
        wq2 = self.sb("f2_wq2", SL + 54400, [P, NC, 256], BF16)
        kb.dma("gpsimd", wq2[:], self.din["L2_ffn_wq"].rearrange("(c p) f -> p c f", p=P),
               writes=pg("f2_wq2"), tag="f2wq")
        wk2 = self.sb("f2_wk2", SL + 58496, [P, NC, 256], BF16)
        kb.dma("gpsimd", wk2[:], self.din["L2_ffn_wk"].rearrange("(c p) f -> p c f", p=P),
               writes=pg("f2_wk2"), tag="f2wk")
        wv2 = self.sb("f2_wv2", SL + 62592, [P, NC, D], BF16)
        kb.dma("gpsimd", wv2[:], self.din["L2_ffn_wv"].rearrange("(c p) f -> p c f", p=P),
               writes=pg("f2_wv2"), tag="f2wv")
        f2varscr = self.sb("f2_varscr", SL + 78976, [64, D], F32)
        f2ycast = self.sb("f2_ycast", SL + 83072, [64, D], BF16)
        f2wvp = self.sb("f2_wvp", SL + 85120, [P, NC, D], BF16)
        kb.dma("gpsimd", f2wvp[:], self.din["L2_ffn_wvp"].rearrange("(c p) f -> p c f", p=P),
               writes=pg("f2_wvp"), tag="f2wvp")
        for half in range(2):
            kb.emit("tensor", lambda t, half=half: t.matmul(
                        self.ps[0][0:64, :], self.ones_bf[0:1, 0:64], lngr[:, ts(half, 512)],
                        start=True, stop=True),
                    reads=pg("f2_lngr", "ones_bf"), writes=["PS0"], tag="f2lg")
            kb.emit("vector", lambda v, half=half: v.tensor_copy(lngb[:, ts(half, 512)],
                                                                 self.ps[0][0:64, :]),
                    reads=["PS0"], writes=pg("f2_lngb"), tag="f2lg2")
            kb.emit("tensor", lambda t, half=half: t.matmul(
                        self.ps[1][0:64, :], self.ones_bf[0:1, 0:64], lnbr[:, ts(half, 512)],
                        start=True, stop=True),
                    reads=pg("f2_lnbr", "ones_bf"), writes=["PS1"], tag="f2lb")
            kb.emit("vector", lambda v, half=half: v.tensor_copy(lnbb[:, ts(half, 512)],
                                                                 self.ps[1][0:64, :]),
                    reads=["PS1"], writes=pg("f2_lnbb"), tag="f2lb2")
        kb.emit("vector", lambda v: v.tensor_copy(den[:], psDN[0:64, 0:1]),
                reads=["PS7"], writes=pg("f2_den"), tag="f2de")
        kb.emit("vector", lambda v: v.tensor_scalar_add(den[:], den[:], 1e-8),
                reads=pg("f2_den"), writes=pg("f2_den"), tag="f2de2")
        kb.emit("vector", lambda v: v.reciprocal(den[:], den[:]),
                reads=pg("f2_den"), writes=pg("f2_den"), tag="f2de3")
        for half in range(2):
            kb.emit("vector", lambda v, half=half: v.tensor_scalar_mul(
                        sin_sm[:, ts(half, 512)], psSI[half][0:64, :], den[:]),
                    reads=["PS%d" % (5 + half)] + pg("f2_den"),
                    writes=pg("f2_sin"), tag="f2si2")
        for c in range(NC):
            pst = self.ps[2]
            kb.emit("tensor", lambda t, c=c, pst=pst: t.transpose(
                        pst[0:P, 0:64], sin_sm[0:64, ts(c)], self.ident_f32[0:64, 0:64]),
                    reads=pg("f2_sin", "ident_f32"), writes=["PS2"], tag="f2st")
            kb.emit("vector", lambda v, c=c, pst=pst: v.tensor_copy(sif[:, c, :], pst[0:P, 0:64]),
                    reads=["PS2"], writes=pg("f2_sif"), tag="f2se")
        for oc in range(24):
            b = (0, 1)[oc % 2]
            ps = self.ps[b]
            for kc in range(NC):
                kb.emit("tensor", lambda t, oc=oc, kc=kc, ps=ps: t.matmul(
                            ps[0:P, 0:64], wih[:, kc, ts(oc)], sif[:, kc, :],
                            start=(kc == 0), stop=(kc == NC - 1)),
                        reads=pg("f2_wih", "f2_sif"), writes=["PS%d" % b], tag="f2gi")
            kb.emit("vector", lambda v, oc=oc, ps=ps: v.tensor_scalar_add(
                        gi[:, oc, :], ps[0:P, 0:64], bihpp[:, oc:oc + 1]),
                    reads=["PS%d" % b] + pg("f2_bih"), writes=pg("f2_gi"), tag="f2gie")
        kb.dma("gpsimd", wih[:], self.din["h_L2f_WhhT"].rearrange("(c p) f -> p c f", p=P),
               reads=pg("f2_wih"), writes=pg("f2_wih"), tag="f2whh")
        for oc in range(24):
            b = (0, 1)[oc % 2]
            ps = self.ps[b]
            for kc in range(NC):
                kb.emit("tensor", lambda t, oc=oc, kc=kc, ps=ps: t.matmul(
                            ps[0:P, 0:64], wih[:, kc, ts(oc)], S0f[:, kc, :],
                            start=(kc == 0), stop=(kc == NC - 1)),
                        reads=pg("f2_wih", "f2_S0f"), writes=["PS%d" % b], tag="f2gh")
            kb.emit("vector", lambda v, oc=oc, ps=ps: v.tensor_scalar_add(
                        gh[:, oc, :], ps[0:P, 0:64], bhhpp[:, oc:oc + 1]),
                    reads=["PS%d" % b] + pg("f2_bhh"), writes=pg("f2_gh"), tag="f2ghe")
        for c in range(NC):
            kb.emit("vector", lambda v, c=c: v.tensor_add(rr[:, c, :], gi[:, c, :], gh[:, c, :]),
                    reads=pg("f2_gi", "f2_gh"), writes=pg("f2_rr"), tag="f2r1")
            kb.emit("scalar", lambda s, c=c: s.activation(rr[:, c, :], rr[:, c, :], AF.Sigmoid),
                    reads=pg("f2_rr"), writes=pg("f2_rr"), tag="f2r2")
            kb.emit("vector", lambda v, c=c: v.tensor_add(zzt[:, c, :], gi[:, 8 + c, :], gh[:, 8 + c, :]),
                    reads=pg("f2_gi", "f2_gh"), writes=pg("f2_zz"), tag="f2z1")
            kb.emit("scalar", lambda s, c=c: s.activation(zzt[:, c, :], zzt[:, c, :], AF.Sigmoid),
                    reads=pg("f2_zz"), writes=pg("f2_zz"), tag="f2z2")
            kb.emit("vector", lambda v, c=c: v.tensor_mul(nn[:, c, :], rr[:, c, :], gh[:, 16 + c, :]),
                    reads=pg("f2_rr", "f2_gh"), writes=pg("f2_nn"), tag="f2n1")
            kb.emit("vector", lambda v, c=c: v.tensor_add(nn[:, c, :], nn[:, c, :], gi[:, 16 + c, :]),
                    reads=pg("f2_nn", "f2_gi"), writes=pg("f2_nn"), tag="f2n2")
            kb.emit("scalar", lambda s, c=c: s.activation(nn[:, c, :], nn[:, c, :], AF.Tanh),
                    reads=pg("f2_nn"), writes=pg("f2_nn"), tag="f2n3")
            kb.emit("vector", lambda v, c=c: v.tensor_mul(rr[:, c, :], zzt[:, c, :], nn[:, c, :]),
                    reads=pg("f2_zz", "f2_nn"), writes=pg("f2_rr"), tag="f2s1")
            kb.emit("vector", lambda v, c=c: v.tensor_sub(nn[:, c, :], nn[:, c, :], rr[:, c, :]),
                    reads=pg("f2_nn", "f2_rr"), writes=pg("f2_nn"), tag="f2s2")
            kb.emit("vector", lambda v, c=c: v.tensor_mul(rr[:, c, :], zzt[:, c, :], S0f[:, c, :]),
                    reads=pg("f2_zz", "f2_S0f"), writes=pg("f2_rr"), tag="f2s3")
            kb.emit("vector", lambda v, c=c: v.tensor_add(snb[:, c, :], nn[:, c, :], rr[:, c, :]),
                    reads=pg("f2_nn", "f2_rr"), writes=pg("f2_snb"), tag="f2s4")
        for oc in range(2):
            ps = self.ps[2]
            for kc in range(NC):
                kb.emit("tensor", lambda t, oc=oc, kc=kc, ps=ps: t.matmul(
                            ps[0:P, 0:64], wq2[:, kc, ts(oc)], snb[:, kc, :],
                            start=(kc == 0), stop=(kc == NC - 1)),
                        reads=pg("f2_wq2", "f2_snb"), writes=["PS2"], tag="f2q")
            kb.emit("vector", lambda v, oc=oc, ps=ps: v.tensor_scalar_add(
                        qf[:, oc, :], ps[0:P, 0:64], bqpp[:, oc:oc + 1]),
                    reads=["PS2"] + pg("f2_bq"), writes=pg("f2_qf"), tag="f2qe")
            for kc in range(NC):
                kb.emit("tensor", lambda t, oc=oc, kc=kc, ps=ps: t.matmul(
                            ps[0:P, 0:64], wk2[:, kc, ts(oc)], snb[:, kc, :],
                            start=(kc == 0), stop=(kc == NC - 1)),
                        reads=pg("f2_wk2", "f2_snb"), writes=["PS2"], tag="f2k")
            kb.emit("vector", lambda v, oc=oc, ps=ps: v.tensor_scalar_add(
                        kf[:, oc, :], ps[0:P, 0:64], bkpp[:, oc:oc + 1]),
                    reads=["PS2"] + pg("f2_bk"), writes=pg("f2_kf"), tag="f2ke")
        for oc in range(NC):
            ps = self.ps[2]
            for kc in range(NC):
                kb.emit("tensor", lambda t, oc=oc, kc=kc, ps=ps: t.matmul(
                            ps[0:P, 0:64], wv2[:, kc, ts(oc)], snb[:, kc, :],
                            start=(kc == 0), stop=(kc == NC - 1)),
                        reads=pg("f2_wv2", "f2_snb"), writes=["PS2"], tag="f2v")
            kb.emit("vector", lambda v, oc=oc, ps=ps: v.tensor_scalar_add(
                        vf[:, oc, :], ps[0:P, 0:64], bvpp[:, oc:oc + 1]),
                    reads=["PS2"] + pg("f2_bv"), writes=pg("f2_vf"), tag="f2ve")
        ps = self.ps[3]
        for kc in range(2):
            kb.emit("tensor", lambda t, kc=kc, ps=ps: t.matmul(
                        ps[0:64, 0:64], qf[:, kc, :], kf[:, kc, :],
                        start=(kc == 0), stop=(kc == 1)),
                    reads=pg("f2_qf", "f2_kf"), writes=["PS3"], tag="f2s2m")
        kb.emit("vector", lambda v, ps=ps: v.reduce_max(mu2[:], ps[0:64, 0:64], AX.X),
                reads=["PS3"], writes=pg("f2_mu2"), tag="f2s2x")
        kb.emit("vector", lambda v: v.tensor_scalar_mul(mu2[:], mu2[:], -1.0 / 16.0),
                reads=pg("f2_mu2"), writes=pg("f2_mu2"), tag="f2s2n")
        kb.emit("scalar", lambda s, ps=ps: s.activation(p2s[0:64, :], ps[0:64, 0:64], AF.Exp,
                                                        bias=mu2[:], scale=1.0 / 16.0,
                                                        accum_out=l2t[:]),
                reads=["PS3"] + pg("f2_mu2"), writes=pg("f2_p2s", "f2_l2"), tag="f2s2e")
        kb.emit("vector", lambda v: v.reciprocal(l2t[:], l2t[:]),
                reads=pg("f2_l2"), writes=pg("f2_l2"), tag="f2s2r")
        kb.emit("vector", lambda v: v.tensor_scalar_mul(p2s[0:64, :], p2s[0:64, :], l2t[:]),
                reads=pg("f2_p2s", "f2_l2"), writes=pg("f2_p2s"), tag="f2s2d")
        pstb = self.ps[3].bitcast(BF16)
        kb.emit("tensor", lambda t, pstb=pstb: t.transpose(
                    pstb[0:64, 0:64], p2s[0:64, 0:64], self.ident_bf[0:64, 0:64]),
                reads=pg("f2_p2s", "ident_bf"), writes=["PS3"], tag="f2s2t")
        kb.emit("vector", lambda v, pstb=pstb: v.tensor_copy(p2t[0:64, :], pstb[0:64, 0:64]),
                reads=["PS3"], writes=pg("f2_p2t"), tag="f2s2te")
        for c in range(NC):
            kb.emit("tensor", lambda t, c=c, pstb=pstb: t.transpose(
                        pstb[0:64, 0:P], vf[:, c, :], self.ident_bf[:]),
                    reads=pg("f2_vf", "ident_bf"), writes=["PS3"], tag="f2vt")
            kb.emit("vector", lambda v, c=c, pstb=pstb: v.tensor_copy(vsm[:, ts(c)], pstb[0:64, 0:P]),
                    reads=["PS3"], writes=pg("f2_vsm"), tag="f2vte")
            kb.emit("tensor", lambda t, c=c, pstb=pstb: t.transpose(
                        pstb[0:64, 0:P], snb[:, c, :], self.ident_bf[:]),
                    reads=pg("f2_snb", "ident_bf"), writes=["PS3"], tag="f2nt")
            kb.emit("vector", lambda v, c=c, pstb=pstb: v.tensor_copy(snsm[:, ts(c)], pstb[0:64, 0:P]),
                    reads=["PS3"], writes=pg("f2_snsm"), tag="f2nte")
        for half in range(2):
            ps = self.ps[4]
            kb.emit("tensor", lambda t, half=half, ps=ps: t.matmul(
                        ps[0:64, :], p2t[0:64, :], vsm[:, ts(half, 512)],
                        start=True, stop=True),
                    reads=pg("f2_p2t", "f2_vsm"), writes=["PS4"], tag="f2cm")
            kb.emit("vector", lambda v, half=half, ps=ps: v.tensor_add(
                        ysm[:, ts(half, 512)], ps[0:64, :], snsm[0:64, ts(half, 512)]),
                    reads=["PS4"] + pg("f2_snsm"), writes=pg("f2_ysm"), tag="f2y")
        kb.emit("vector", lambda v: v.reduce_sum(mu2[:], ysm[0:64, :], AX.X),
                reads=pg("f2_ysm"), writes=pg("f2_mu2"), tag="f2l1")
        kb.emit("vector", lambda v: v.tensor_scalar_mul(mu2[:], mu2[:], 1.0 / D),
                reads=pg("f2_mu2"), writes=pg("f2_mu2"), tag="f2l2b")
        kb.emit("vector", lambda v: v.tensor_scalar(ysm[0:64, :], ysm[0:64, :], mu2[:], None,
                                                    ALU.subtract),
                reads=pg("f2_ysm", "f2_mu2"), writes=pg("f2_ysm"), tag="f2l3")
        kb.emit("vector", lambda v: v.tensor_mul(f2varscr[0:64, :], ysm[0:64, :], ysm[0:64, :]),
                reads=pg("f2_ysm"), writes=pg("f2_varscr"), tag="f2l4a")
        kb.emit("vector", lambda v: v.reduce_sum(l2t[:], f2varscr[0:64, :], AX.X),
                reads=pg("f2_varscr"), writes=pg("f2_l2"), tag="f2l4b")
        kb.emit("scalar", lambda s: s.activation(l2t[:], l2t[:], AF.Ln, bias=self.C_EPS5[0:64, :],
                                                 scale=1.0 / D),
                reads=pg("f2_l2", "consts"), writes=pg("f2_l2"), tag="f2l5")
        kb.emit("scalar", lambda s: s.activation(l2t[:], l2t[:], AF.Exp, scale=-0.5),
                reads=pg("f2_l2"), writes=pg("f2_l2"), tag="f2l6")
        kb.emit("vector", lambda v: v.tensor_scalar_mul(ysm[0:64, :], ysm[0:64, :], l2t[:]),
                reads=pg("f2_ysm", "f2_l2"), writes=pg("f2_ysm"), tag="f2l7")
        kb.emit("vector", lambda v: v.tensor_mul(ysm[0:64, :], ysm[0:64, :], lngb[0:64, :]),
                reads=pg("f2_ysm", "f2_lngb"), writes=pg("f2_ysm"), tag="f2l8")
        kb.emit("vector", lambda v: v.tensor_add(ysm[0:64, :], ysm[0:64, :], lnbb[0:64, :]),
                reads=pg("f2_ysm", "f2_lnbb"), writes=pg("f2_ysm"), tag="f2l9")
        kb.emit("vector", lambda v: v.tensor_copy(f2ycast[0:64, :], ysm[0:64, :]),
                reads=pg("f2_ysm"), writes=pg("f2_ycast"), tag="f2yc")
        for c in range(NC):
            pst = self.ps[2]
            kb.emit("tensor", lambda t, c=c, pst=pst: t.transpose(
                        pst.bitcast(BF16)[0:P, 0:64], f2ycast[0:64, ts(c)],
                        self.ident_bf[0:64, 0:64]),
                    reads=pg("f2_ycast", "ident_bf"), writes=["PS2"], tag="f2yt")
            kb.emit("vector", lambda v, c=c, pst=pst: v.tensor_copy(y2f[:, c, :],
                                                                    pst.bitcast(BF16)[0:P, 0:64]),
                    reads=["PS2"], writes=pg("f2_y2f"), tag="f2ye")
        for half in range(2):
            ps = self.ps[4]
            for kc in range(NC):
                kb.emit("tensor", lambda t, half=half, kc=kc, ps=ps: t.matmul(
                            ps[0:64, :], y2f[:, kc, :], f2wvp[:, kc, ts(half, 512)],
                            start=(kc == 0), stop=False),
                        reads=pg("f2_wvp", "f2_y2f"), writes=["PS4"], tag="f2sv")
            kb.emit("tensor", lambda t, half=half, ps=ps: t.matmul(
                        ps[0:64, :], self.ones_bf[0:1, 0:64], bvpr[:, ts(half, 512)],
                        start=False, stop=True),
                    reads=pg("f2_bvpr", "ones_bf"), writes=["PS4"], tag="f2svb")
            kb.emit("vector", lambda v, half=half, ps=ps: v.tensor_copy(
                        svs[:, ts(half, 512)], ps[0:64, :]),
                    reads=["PS4"], writes=pg("f2_svs"), tag="f2sve")
        kb.seg()
        OW2 = WIH_OFF
        wo2 = self.sb("f2_wo2", OW2, [P, NC, D], BF16)
        kb.dma("gpsimd", wo2[:], self.din["L2_ffn_wo"].rearrange("(c p) f -> p c f", p=P),
               reads=pg("f2_wih"), writes=pg("f2_wo2"), tag="f2wo")
        ow2 = self.sb("f2_ow", OW2 + 16384, [P, NC, N], BF16)
        xw2 = self.sb("f2_xw", OW2 + 24576, [P, NC, N], F32)
        for w in range(NW):
            kb.seg()
            wsl = slice(w * N, (w + 1) * N)
            for oc in range(NC):
                b = (0, 1)[oc % 2]
                ps = self.ps[b]
                kb.emit("tensor", lambda t, oc=oc, ps=ps, wsl=wsl: t.matmul(
                            ps[:], svs[0:64, ts(oc)], afm[0:64, wsl],
                            start=True, stop=True),
                        reads=pg("f2_svs", "f2_afm"), writes=["PS%d" % b], tag="f2o")
                kb.emit("scalar", lambda s, oc=oc, ps=ps: s.activation(ow2[:, oc, :], ps[:], AF.Copy),
                        reads=["PS%d" % b], writes=pg(("f2_ow", oc * N * 2, (oc + 1) * N * 2)),
                        tag="f2oe")
            kb.dma("scalar", xw2[:], xsrc.rearrange("(c p) t -> p c t", p=P)[:, :, wsl],
                   reads=[ksrc + ":%d" % w], writes=pg("f2_xw"), tag="f2x")
            for oc in range(NC):
                b = (2, 3)[oc % 2]
                ps = self.ps[b]
                for kc in range(NC):
                    kb.emit("tensor", lambda t, oc=oc, kc=kc, ps=ps: t.matmul(
                                ps[:], wo2[:, kc, ts(oc)], ow2[:, kc, :],
                                start=(kc == 0), stop=False),
                            reads=pg("f2_wo2", "f2_ow"), writes=["PS%d" % b], tag="f2f")
                kb.emit("tensor", lambda t, oc=oc, ps=ps: t.matmul(
                            ps[:], bor[:, ts(oc)], self.ones_bf[0:1, 0:N],
                            start=False, stop=True),
                        reads=pg("f2_bor", "ones_bf"), writes=["PS%d" % b], tag="f2fb")
                kb.emit("vector", lambda v, oc=oc, ps=ps: v.scalar_tensor_tensor(
                            xw2[:, oc, :], ps[:], 0.5, xw2[:, oc, :], ALU.mult, ALU.add),
                        reads=["PS%d" % b] + pg("f2_xw"), writes=pg("f2_xw"), tag="f2res")
            self.x_writeback(w, xw2, xdst, kdst, "f2_xw")

    def stats_of(self, src, srckey, mean, meank, A, Ak, bS, bQ, eps=1e-5, ncs=NC):
        kb, pg = self.kb, self.pg
        psS, psQ = self.ps[bS], self.ps[bQ]
        stsq = self.stsq
        for c in range(ncs):
            p2 = c % 2
            kb.emit("scalar", lambda s, c=c, p2=p2, src=src: s.activation(stsq[p2][:], src[:, c, :], AF.Square),
                    reads=pg(srckey), writes=pg("stsq%d" % p2), tag="st_sq")
            kb.emit("tensor", lambda t, c=c, src=src: t.matmul(psS[:], self.ones_bf[:, 0:128], src[:, c, :],
                                                               start=(c == 0), stop=(c == ncs - 1)),
                    reads=pg(srckey, "ones_bf"), writes=["PS%d" % bS], tag="st_mmS")
            kb.emit("tensor", lambda t, c=c, p2=p2: t.matmul(psQ[:], self.ones_bf[:, 0:128], stsq[p2][:],
                                                             start=(c == 0), stop=(c == ncs - 1)),
                    reads=pg("stsq%d" % p2, "ones_bf"), writes=["PS%d" % bQ], tag="st_mmQ")
        kb.emit("vector", lambda v: v.tensor_scalar_mul(mean[:], psS[:], 1.0 / (ncs * P)),
                reads=["PS%d" % bS], writes=pg(meank), tag="st_mean")
        kb.emit("vector", lambda v: v.tensor_mul(A[:], mean[:], mean[:]),
                reads=pg(meank), writes=pg(Ak), tag="st_msq")
        kb.emit("vector", lambda v: v.scalar_tensor_tensor(A[:], psQ[:], 1.0 / (ncs * P), A[:],
                                                           ALU.mult, ALU.subtract),
                reads=["PS%d" % bQ] + pg(Ak), writes=pg(Ak), tag="st_var")
        epst = {1e-5: self.C_EPS5, 1e-6: self.C_EPS6, 1e-3: self.C_EPS3}[eps]
        kb.emit("scalar", lambda s: s.activation(A[:], A[:], AF.Ln, bias=epst),
                reads=pg(Ak, "consts"), writes=pg(Ak), tag="st_ln")
        kb.emit("scalar", lambda s: s.activation(A[:], A[:], AF.Exp, scale=-0.5),
                reads=pg(Ak), writes=pg(Ak), tag="st_exp")

    def finalize(self):
        kb = self.kb
        if kb.dma_count["sync"]:
            nd = kb.dma_count["sync"]
            kb.emit("sync", lambda e: e.nop(), reads=[], writes=[], tag="fin")
            kb.ops["sync"][-1]["waits"].append(("d", "sync", nd - 1))
        if kb.dma_count["gpsimd"]:
            nd = kb.dma_count["gpsimd"]
            kb.emit("gpsimd", lambda e: e.nop(), tag="fin")
            kb.ops["gpsimd"][-1]["waits"].append(("d", "gpsimd", nd - 1))
        if kb.dma_count["scalar"]:
            nd = kb.dma_count["scalar"]
            kb.emit("scalar", lambda e: e.nop(), tag="fin")
            kb.ops["scalar"][-1]["waits"].append(("d", "scalar", nd - 1))
        return kb.finalize(self.es)


ALL_SPECS = [
    ("xin", [D, T], F32), ("c_ident", [128, 128], F32),
    ("c_hpar", [16, 128], F32), ("c_hmask", [16, 8], F32),
    ("L0_ffn_theta", [D, D], F32), ("L0_ffn_phi", [D, D], F32),
    ("L0_ffn_theta_noise", [D, D], F32), ("L0_ffn_phi_noise", [D, D], F32),
    ("h_L0_WqT", [D, D], F32), ("h_L0_WkT", [D, D], F32),
    ("h_L0_WvT", [D, D], F32), ("h_L0_WoT", [D, D], F32),
    ("h_L1f_qwT", [D, D], F32), ("L1_ffn_q_b", [D], F32),
    ("h_L1f_kwT", [D, D], F32), ("L1_ffn_k_b", [D], F32),
    ("h_L1f_vwT", [D, D], F32), ("L1_ffn_v_b", [D], F32),
    ("h_L1f_owT", [D, D], F32), ("L1_ffn_o_b", [D], F32),
    ("L1_ffn_norm_w", [D], F32), ("L1_ffn_norm_b", [D], F32),
    ("L1_attn_Wq", [D, 16, 64], F32), ("L1_attn_bq", [16, 64], F32),
    ("L1_attn_Wk", [D, 16, 64], F32), ("L1_attn_bk", [16, 64], F32),
    ("L1_attn_Wv", [D, 16, 64], F32), ("L1_attn_bv", [16, 64], F32),
    ("L1_attn_Wo", [16, 64, D], F32), ("L1_attn_bo", [D], F32),
    ("L1_attn_gamma", [D], F32), ("L1_attn_beta", [D], F32),
    ("L2_ffn_slot_init", [64, D], F32), ("h_L2f_S0T", [D, 64], F32),
    ("L2_ffn_w1", [2 * D, 512], F32), ("L2_ffn_b1", [512], F32),
    ("L2_ffn_w2", [512, 64], F32), ("L2_ffn_b2", [64], F32),
    ("L2_ffn_tau", [1], F32),
    ("h_L2f_WihT", [D, 3 * D], F32), ("h_L2f_WhhT", [D, 3 * D], F32),
    ("L2_ffn_bih", [3 * D], F32), ("L2_ffn_bhh", [3 * D], F32),
    ("L2_ffn_wq", [D, 256], F32), ("L2_ffn_bq", [256], F32),
    ("L2_ffn_wk", [D, 256], F32), ("L2_ffn_bk", [256], F32),
    ("L2_ffn_wv", [D, D], F32), ("L2_ffn_bv", [D], F32),
    ("L2_ffn_ln_g", [D], F32), ("L2_ffn_ln_b", [D], F32),
    ("L2_ffn_wvp", [D, D], F32), ("L2_ffn_bvp", [D], F32),
    ("L2_ffn_wo", [D, D], F32), ("L2_ffn_bo", [D], F32),
    ("h_L2_WqT", [D, D], F32), ("h_L2_WvT", [D, D], F32),
    ("h_L2_WgT", [D, D], F32), ("h_L2_WoT", [D, D], F32),
    ("L2_attn_beta", [16], F32),
    ("L2_attn_ln_gamma", [D], F32), ("L2_attn_ln_beta", [D], F32),
    ("h_L3f_gwT", [D, 8], F32),
    ("h_L3f_wgT", [8, D, 512], F32), ("h_L3f_wuT", [8, D, 512], F32),
    ("h_L3f_wdT", [8, 512, D], F32),
    ("h_L3f_sgT", [D, 512], F32), ("h_L3f_suT", [D, 512], F32),
    ("h_L3f_sdT", [512, D], F32),
    ("L3_attn_norm_w", [D], F32), ("h_L3_convw", [3, D], F32),
    ("L3_attn_conv_b", [D], F32), ("L3_attn_a", [D], F32),
    ("L3_attn_b", [D], F32), ("L3_attn_mlp_norm_w", [D], F32),
    ("L3_attn_w_gate", [D, 4096], F32), ("L3_attn_b_gate", [4096], F32),
    ("L3_attn_w_up", [D, 4096], F32), ("L3_attn_b_up", [4096], F32),
    ("L3_attn_w_down", [4096, D], F32), ("L3_attn_b_down", [D], F32),
]


def build_full(upto=12):
    m = MK(upto=upto)
    m.setup(ALL_SPECS)
    m.x_cur = ("xin", m.din["xin"])
    steps = []
    def f0():
        C_sb, S_sb = m.prep_trig(m.PHASE)
        m.sub_l0ffn(C_sb, S_sb)
    steps.append(f0)
    steps.append(lambda: m.sub_attn_generic('A0'))
    steps.append(f0)
    steps.append(lambda: m.sub_attn_generic('F1'))
    steps.append(lambda: m.sub_attn_generic('A1'))
    steps.append(lambda: m.sub_attn_generic('F1'))
    steps.append(lambda: m.sub_l2ffn())
    steps.append(lambda: m.sub_l2attn())
    steps.append(lambda: m.sub_l2ffn())
    steps.append(lambda: m.sub_l3ffn())
    steps.append(lambda: m.sub_l3attn())
    steps.append(lambda: m.sub_l3ffn())
    for i in range(min(upto, 12)):
        steps[i]()
        m.kb.seg()
    m.finalize()
    return m


def prep_inputs(inputs):
    f = np.ascontiguousarray
    d = dict(inputs)
    w = {}
    w["c_ident"] = np.eye(128, dtype=np.float32)
    hpar = np.zeros((16, 128), np.float32)
    for h in range(16):
        hpar[h, :] = [1.0 if h % 2 == p // 64 else 0.0 for p in range(128)]
    w["c_hpar"] = hpar
    hmask = np.zeros((16, 8), np.float32)
    for h in range(16):
        hmask[h, h // 2] = 1.0
    w["c_hmask"] = hmask
    W3 = d["L0_attn_W_qkv"].reshape(D, 3, D)
    w["h_L0_WqT"] = f(W3[:, 0, :].T)
    w["h_L0_WkT"] = f(W3[:, 1, :].T)
    w["h_L0_WvT"] = f(W3[:, 2, :].T)
    w["h_L0_WoT"] = f(d["L0_attn_W_out"].T)
    w["h_L1f_qwT"] = f(d["L1_ffn_q_w"].T)
    w["h_L1f_kwT"] = f(d["L1_ffn_k_w"].T)
    w["h_L1f_vwT"] = f(d["L1_ffn_v_w"].T)
    w["h_L1f_owT"] = f(d["L1_ffn_o_w"].T)
    w["h_L2f_S0T"] = f(d["L2_ffn_slot_init"].T)
    w["h_L2f_WihT"] = f(d["L2_ffn_Wih"].T)
    w["h_L2f_WhhT"] = f(d["L2_ffn_Whh"].T)
    w["h_L2_WqT"] = f(d["L2_attn_Wq"].T)
    w["h_L2_WvT"] = f(d["L2_attn_Wv"].T)
    w["h_L2_WgT"] = f(d["L2_attn_Wg"].T)
    w["h_L2_WoT"] = f(d["L2_attn_Wo"].T)
    w["h_L3f_gwT"] = f(d["L3_ffn_gate_w"].T)
    w["h_L3f_wgT"] = f(d["L3_ffn_wg"].transpose(0, 2, 1))
    w["h_L3f_wuT"] = f(d["L3_ffn_wu"].transpose(0, 2, 1))
    w["h_L3f_wdT"] = f(d["L3_ffn_wd"].transpose(0, 2, 1))
    w["h_L3f_sgT"] = f(d["L3_ffn_sg"].T)
    w["h_L3f_suT"] = f(d["L3_ffn_su"].T)
    w["h_L3f_sdT"] = f(d["L3_ffn_sd"].T)
    w["h_L3_convw"] = f(d["L3_attn_conv_w"][:, 0, :].T)
    for name, shape, _dt in ALL_SPECS:
        if name.startswith("h_") or name.startswith("c_") or name == "xin":
            continue
        w[name] = np.ascontiguousarray(d[name]).astype(np.float32)
    xins = [f(d["L0_attn_x"][b].T) for b in range(d["L0_attn_x"].shape[0])]
    return w, xins


from concourse.bass_utils import run_bass_kernel_spmd


def kernel(**inputs):
    assert inputs["L1_ffn_router_prob"].shape[-1] == 4
    w, xins = prep_inputs(inputs)
    m = build_full(12)
    in_maps = [dict(w) for _ in range(8)]
    for b in range(8):
        in_maps[b]["xin"] = xins[b]
    res = run_bass_kernel_spmd(m.nc, in_maps, core_ids=list(range(8)))
    out = np.stack([np.ascontiguousarray(res.results[b]["xout"].T) for b in range(8)])
    return out


# version 04
# speedup vs baseline: 1.2287x; 1.0489x over previous
import numpy as np
import concourse.bass as bass
import concourse.mybir as mybir

F32 = mybir.dt.float32
BF16 = mybir.dt.bfloat16
AF = mybir.ActivationFunctionType
ALU = mybir.AluOpType
AX = mybir.AxisListType

ENGINES = ("tensor", "vector", "scalar", "gpsimd", "sync")
P = 128


class KB:
    SEG_LIMIT = 1500

    def __init__(self, nc):
        self.nc = nc
        self.ops = {e: [] for e in ENGINES}
        self.dma_count = {e: 0 for e in ENGINES}
        self.buffers = {}
        self.watermark = {e: {} for e in ENGINES}
        self.n_pruned = 0
        self.segment = 0
        self._seg_ops = 0

    def seg(self):
        if self._seg_ops > self.SEG_LIMIT:
            self.segment += 1
            self._seg_ops = 0

    def emit(self, eng, fn, reads=(), writes=(), dma=False, tag=""):
        assert eng in ENGINES
        idx = len(self.ops[eng])
        if dma:
            tok = ("d", eng, self.dma_count[eng])
            self.dma_count[eng] += 1
        else:
            tok = ("c", eng, idx)
        waits = []

        def add_wait(t, raw):
            kind, peng, pidx = t
            if kind == "c" and peng == eng and peng == "tensor":
                return
            if kind == "c" and peng == eng and pidx == idx:
                return
            waits.append((t, raw))

        for b in reads:
            ent = self.buffers.get(b)
            if ent is not None and ent["writer"] is not None:
                add_wait(ent["writer"], True)
        for b in writes:
            ent = self.buffers.get(b)
            if ent is not None:
                if ent["writer"] is not None:
                    w = ent["writer"]
                    if not (w[0] == "c" and w[1] == eng):
                        add_wait(w, True)
                for rt in ent["readers"]:
                    kind, peng, pidx = rt
                    if kind == "c" and peng == eng:
                        continue
                    add_wait(rt, False)
        for b in reads:
            ent = self.buffers.setdefault(b, {"writer": None, "readers": []})
            ent["readers"].append(tok)
        for b in writes:
            ent = self.buffers.setdefault(b, {"writer": None, "readers": []})
            ent["writer"] = tok
            ent["readers"] = []
        wm = self.watermark[eng]
        final_waits = []
        seen = {}
        rawk = {}
        for ((kind, peng, pidx), raw) in waits:
            key = (kind, peng)
            if wm.get(key, -1) >= pidx:
                self.n_pruned += 1
                continue
            seen[key] = max(seen.get(key, -1), pidx)
            rawk[key] = rawk.get(key, False) or raw
        for key, pidx in seen.items():
            wm[key] = pidx
            final_waits.append((key[0], key[1], pidx))
            if rawk[key] and key[0] == "c" and key[1] != eng:
                self.ops[key[1]][pidx]["raw_waited"] = True
        self.ops[eng].append(dict(fn=fn, waits=final_waits, dma=dma, tag=tag, tok=tok,
                                  seg=self.segment))
        self._seg_ops += 1
        return tok

    def dma(self, q, out, in_, reads=(), writes=(), tag="", **kw):
        rd = list(reads)
        return self.emit(q, lambda e: e.dma_start(out, in_, **kw),
                         reads=rd, writes=writes, dma=True, tag=tag)

    def finalize(self, es):
        nc = self.nc
        for e in ENGINES:
            for op in self.ops[e]:
                for (kind, peng, pidx) in op["waits"]:
                    if kind == "c":
                        self.ops[peng][pidx]["inc"] = True
        self.cval = {}
        for e in ENGINES:
            c = 0
            vals = []
            for op in self.ops[e]:
                if op.get("inc") and not op["dma"]:
                    c += 1
                vals.append(c)
            self.cval[e] = vals
        self.csem = {}
        self.dsem = {}
        for e in ENGINES:
            if any((not op["dma"]) and op.get("inc") for op in self.ops[e]):
                self.csem[e] = es.enter_context(nc.semaphore("sem_c_" + e))
            if self.dma_count[e]:
                self.dsem[e] = es.enter_context(nc.semaphore("sem_d_" + e))

        stats = {e: (len(self.ops[e]), sum(1 for o in self.ops[e] if o.get("inc")),
                     sum(len(o["waits"]) for o in self.ops[e]))
                 for e in ENGINES}
        self.stats = stats
        allsems = list(self.csem.values()) + list(self.dsem.values())
        for s in allsems:
            nc.gpsimd.sem_clear(s)
        nc.all_engine_barrier()
        wu = nc.alloc_sbuf_tensor_at("warmup_tile", [128, 512], mybir.dt.float32,
                                     offset=Mem.BASE + Mem.LIMIT - 2048)
        wua = wu.ap()
        nc.vector.memset(wua[:], 1.0)
        for i in range(24):
            nc.scalar.activation(wua[:, 0:256], wua[:, 0:256], AF.Exp, scale=0.0)
            nc.vector.tensor_copy(wua[:, 256:512], wua[:, 256:512])
            nc.gpsimd.memset(wua[:, 0:64], 1.0)
        nc.all_engine_barrier()

        kb = self
        nseg = self.segment + 1
        seg_ops = {e: [[] for _ in range(nseg)] for e in ENGINES}
        for e in ENGINES:
            for op in self.ops[e]:
                seg_ops[e][op["seg"]].append(op)

        def make_section(e, ops):
            def section(engobj):
                for op in ops:
                    for (kind, peng, pidx) in op["waits"]:
                        if kind == "c":
                            engobj.wait_ge(kb.csem[peng], kb.cval[peng][pidx])
                        else:
                            engobj.wait_ge(kb.dsem[peng], 16 * (pidx + 1))
                    inst = op["fn"](engobj)
                    if op["dma"]:
                        inst.then_inc(kb.dsem[e], 16)
                    elif op.get("inc"):
                        if e == "tensor" or not op.get("raw_waited"):
                            inst.then_inc(kb.csem[e], 1)
                        else:
                            engobj.drain(fusable=True).then_inc(kb.csem[e], 1)
            return section

        for si in range(nseg):
            with nc.Block() as block:
                for e in ENGINES:
                    if seg_ops[e][si]:
                        getattr(block, e)(make_section(e, seg_ops[e][si]))
        return None


class Mem:
    BASE = 16512
    LIMIT = 229216 - 16512

    def __init__(self, nc):
        self.nc = nc
        self.tensors = {}

    def at(self, name, offset, shape, dtype):
        assert shape[0] <= 128
        nbytes = int(np.prod(shape[1:])) * mybir.dt.size(dtype)
        assert offset + nbytes <= self.LIMIT, (name, offset, nbytes)
        h = self.nc.alloc_sbuf_tensor_at(name, list(shape), dtype,
                                         offset=self.BASE + offset)
        ap = h.ap()
        self.tensors[name] = (offset, nbytes, ap)
        return ap


"""Megakernel builder for mk_macaron_d4 (per-core: T=4096 tokens, D=1024).

Layout conventions:
  - feature-major (fm): [feature chunk c (128 partitions), T free]
  - token-major (tm): [token (128 partitions), features free]
  - residual x lives in DRAM scratch (fp32, fm, [1024, 4096]), ping-ponged
    between d_xA / d_xB per sublayer; final sublayer writes ExternalOutput.
  - 8 token windows of N=512 per core.

Host prep (layout only): x transposed to fm; several weights transposed so
their contraction dim is first (see prep_inputs).
"""
import contextlib
import numpy as np

import concourse.bass as bass
import concourse.mybir as mybir

T = 4096
D = 1024
NW = 8
N = T // NW
NC = D // P

LN2 = 0.6931471805599453
PI = 3.141592653589793


def ts(i, n=128):
    return slice(i * n, (i + 1) * n)


class MK:
    def __init__(self, upto=99):
        self.nc = bass.Bass("TRN2", target_bir_lowering=False)
        self.es = contextlib.ExitStack()
        self.kb = KB(self.nc)
        self.mem = Mem(self.nc)
        self.upto = upto
        self.sub_idx = 0

    def sb(self, name, offset, shape, dtype):
        key = name if self.sub_idx == 0 else "%s_u%d" % (name, self.sub_idx)
        ap = self.mem.at(key, offset, shape, dtype)
        nbytes = int(np.prod(shape[1:])) * mybir.dt.size(dtype)
        self._pagemap[key] = (offset, nbytes)
        return ap

    def pages(self, name, lo=None, hi=None):
        key = "%s_u%d" % (name, self.sub_idx)
        if key not in self._pagemap:
            key = name
        off, nb = self._pagemap[key]
        a = off + (lo if lo is not None else 0)
        b = off + (hi if hi is not None else nb)
        return ["sb%d" % p for p in range(a // 1024, (b + 1023) // 1024)]

    def pg(self, *names):
        out = []
        for n in names:
            if isinstance(n, tuple):
                out += self.pages(*n)
            elif ("%s_u%d" % (n, self.sub_idx)) in self._pagemap or n in self._pagemap:
                out += self.pages(n)
            else:
                out.append(n)
        return out

    def setup(self, input_specs):
        nc = self.nc
        self._pagemap = {}
        self.din = {}
        for name, shape, dt in input_specs:
            self.din[name] = nc.dram_tensor(name, list(shape), dt, kind="ExternalInput").ap()
        self.xout = nc.dram_tensor("xout", [D, T], F32, kind="ExternalOutput").ap()
        self.d_xA = nc.dram_tensor("d_xA", [D, T], F32, kind="Internal").ap()
        self.d_xB = nc.dram_tensor("d_xB", [D, T], F32, kind="Internal").ap()
        self.d_q = nc.dram_tensor("d_q", [D, T], BF16, kind="Internal").ap()
        self.d_z = nc.dram_tensor("d_z", [D, T], BF16, kind="Internal").ap()
        self.d_v = nc.dram_tensor("d_v", [T, D], BF16, kind="Internal").ap()
        self.d_cw = nc.dram_tensor("d_cw", [8, T], F32, kind="Internal").ap()

        self.ps = []
        for i in range(8):
            h = nc.alloc_psum_tensor("psb%d" % i, [P, 512], F32)
            self.ps.append(h.ap())

        self.ones_bf = self.sb("ones_bf", 0, [P, 512], BF16)
        self.consts = self.sb("consts", 1024, [P, 16], F32)
        self.C_EPS5 = self.consts[:, 0:1]
        self.C_EPS6 = self.consts[:, 1:2]
        self.C_EPS3 = self.consts[:, 2:3]
        self.C_HALFPI = self.consts[:, 3:4]
        self.C_LNHALF = self.consts[:, 4:5]
        self.C_EPS8 = self.consts[:, 5:6]
        self.C_ZERO = self.consts[:, 6:7]
        kb = self.kb
        kb.emit("gpsimd", lambda g: g.memset(self.ones_bf[:], 1.0), writes=self.pg("ones_bf"))
        for col, val in ((0, 1e-5), (1, 1e-6), (2, 1e-3), (3, PI / 2), (4, -LN2), (5, 1e-8), (6, 0.0)):
            kb.emit("gpsimd", lambda g, c=col, v=val: g.memset(self.consts[:, c:c + 1], v),
                    writes=self.pg("consts"))

        self.ones_f32 = self.sb("ones_f32", 1088, [P, P], F32)
        kb.emit("gpsimd", lambda g: g.memset(self.ones_f32[:], 1.0), writes=self.pg("ones_f32"))
        self.ident_f32 = self.sb("ident_f32", 1600, [P, P], F32)
        kb.dma("sync", self.ident_f32[:], self.din["c_ident"][:], writes=self.pg("ident_f32"))
        self.ident_bf = self.sb("ident_bf", 4928, [P, P], BF16)
        kb.dma("gpsimd", self.ident_bf[:], self.din["c_ident"][:], writes=self.pg("ident_bf"))
        self.stsq = [self.sb("stsq0", 2112, [P, N], BF16), self.sb("stsq1", 3136, [P, N], BF16)]
        self.pptmp = self.sb("pptmp", 4160, [32, P], F32)
        self.PHASE = 5248

    def ln0_window(self, w, xsrc_dram, xw, zb, st, key_src):
        kb, pg = self.kb, self.pg
        wsl = slice(w * N, (w + 1) * N)
        xwk, zbk = st["xw_key"], st["zb_key"]
        kb.dma("sync", xw[:], xsrc_dram.rearrange("(c p) t -> p c t", p=P)[:, :, wsl],
               reads=[key_src + ":%d" % w], writes=pg(xwk), tag="ln_ldx")
        bS, bQ = st["bS"], st["bQ"]
        psS, psQ = self.ps[bS], self.ps[bQ]
        xb, rb = st["xb"], st["rb"]
        xbk, rbk = st["xb_key"], st["rb_key"]
        kb.emit("vector", lambda v: v.tensor_copy(xb.rearrange("p c t -> p (c t)"),
                                                  xw.rearrange("p c t -> p (c t)")),
                reads=pg(xwk), writes=pg(xbk), tag="ln_cast")
        kb.emit("vector", lambda v: v.tensor_sub(rb.rearrange("p c t -> p (c t)"),
                                                 xw.rearrange("p c t -> p (c t)"),
                                                 xb.rearrange("p c t -> p (c t)")),
                reads=pg(xwk, xbk), writes=pg(rbk), tag="ln_resid")
        sq = st["sq"]
        sqk = st["sq_key"]
        ones = self.ones_bf[:, 0:128]
        for c in range(NC):
            p2 = c % 2
            kb.emit("scalar", lambda s, c=c, p2=p2: s.activation(sq[p2][:], xb[:, c, :], AF.Square),
                    reads=pg(xbk), writes=pg(sqk[p2]), tag="ln_sq")
            kb.emit("tensor", lambda t, c=c: t.matmul(psS[:], ones, xb[:, c, :],
                                                      start=(c == 0), stop=False),
                    reads=pg(xbk, "ones_bf"), writes=["PS%d" % bS], tag="ln_mmS")
            kb.emit("tensor", lambda t, c=c: t.matmul(psS[:], ones, rb[:, c, :],
                                                      start=False, stop=(c == NC - 1)),
                    reads=pg(rbk, "ones_bf"), writes=["PS%d" % bS], tag="ln_mmSr")
            kb.emit("tensor", lambda t, c=c, p2=p2: t.matmul(psQ[:], ones, sq[p2][:],
                                                             start=(c == 0), stop=(c == NC - 1)),
                    reads=pg(sqk[p2], "ones_bf"), writes=["PS%d" % bQ], tag="ln_mmQ")
        mean, A, v1 = st["mean"], st["A"], st["v1"]
        meank, Ak, v1k = st["mean_key"], st["A_key"], st["v1_key"]
        kb.emit("vector", lambda v: v.tensor_scalar_mul(mean[:], psS[:], 1.0 / D),
                reads=["PS%d" % bS], writes=pg(meank), tag="ln_mean")
        kb.emit("vector", lambda v: v.tensor_mul(v1[:], mean[:], mean[:]),
                reads=pg(meank), writes=pg(v1k), tag="ln_msq")
        kb.emit("vector", lambda v: v.scalar_tensor_tensor(v1[:], psQ[:], 1.0 / D, v1[:],
                                                           ALU.mult, ALU.subtract),
                reads=["PS%d" % bQ] + pg(v1k), writes=pg(v1k), tag="ln_var")
        kb.emit("scalar", lambda s: s.activation(v1[:], v1[:], AF.Ln, bias=self.C_EPS5),
                reads=pg(v1k, "consts"), writes=pg(v1k), tag="ln_ln")
        kb.emit("scalar", lambda s: s.activation(A[:], v1[:], AF.Exp, scale=-0.5),
                reads=pg(v1k), writes=pg(Ak), tag="ln_exp")
        for c in range(NC):
            kb.emit("vector", lambda v, c=c: v.tensor_sub(st["d"][c % 2][:], xw[:, c, :], mean[:]),
                    reads=pg(xwk, meank), writes=pg(st["d_key"][c % 2]), tag="ln_sub")
            kb.emit("vector", lambda v, c=c: v.tensor_mul(zb[:, c, :], st["d"][c % 2][:], A[:]),
                    reads=pg(st["d_key"][c % 2], Ak), writes=pg(zbk, c * N * 2, (c + 1) * N * 2),
                    tag="ln_mul")

    def x_writeback(self, w, xw, xdst_dram, key_dst, xw_key):
        kb, pg = self.kb, self.pg
        wsl = slice(w * N, (w + 1) * N)
        kb.dma("sync", xdst_dram.rearrange("(c p) t -> p c t", p=P)[:, :, wsl], xw[:],
               reads=pg(xw_key), writes=[key_dst + ":%d" % w], tag="x_wb")

    def next_x(self):
        src = self.x_cur
        self.sub_idx += 1
        last = (self.sub_idx == min(self.upto, 12))
        if last:
            dst = ("xout", self.xout)
        else:
            dst = ("d_xB", self.d_xB) if src[0] == "d_xA" else ("d_xA", self.d_xA)
        self.x_cur = dst
        return src[1], src[0], dst[1], dst[0]

    def prep_trig(self, off):
        kb, pg = self.kb, self.pg
        C_sb = self.sb("trigC", off, [P, NC, D], BF16)
        S_sb = self.sb("trigS", off + 16 * 1024, [P, NC, D], BF16)
        t0 = self.sb("trig_t0", off + 32 * 1024, [P, D], F32)
        t1 = self.sb("trig_t1", off + 36 * 1024, [P, D], F32)
        th = self.din["L0_ffn_theta"].rearrange("(c p) o -> p c o", p=P)
        thn = self.din["L0_ffn_theta_noise"].rearrange("(c p) o -> p c o", p=P)
        ph = self.din["L0_ffn_phi"].rearrange("(c p) o -> p c o", p=P)
        phn = self.din["L0_ffn_phi_noise"].rearrange("(c p) o -> p c o", p=P)
        for c in range(NC):
            kb.dma("sync", t0[:], th[:, c, :], writes=pg("trig_t0"), tag="trig")
            kb.dma("gpsimd", t1[:], thn[:, c, :], writes=pg("trig_t1"), tag="trig")
            kb.emit("vector", lambda v: v.tensor_add(t0[:], t0[:], t1[:]),
                    reads=pg("trig_t0", "trig_t1"), writes=pg("trig_t0"))
            kb.emit("scalar", lambda s, c=c: s.activation(C_sb[:, c, :], t0[:], AF.Sin,
                                                          bias=self.C_HALFPI),
                    reads=pg("trig_t0", "consts"), writes=pg("trigC", c * 2 * D, (c + 1) * 2 * D))
            kb.dma("sync", t0[:], ph[:, c, :], writes=pg("trig_t0"), tag="trig")
            kb.dma("gpsimd", t1[:], phn[:, c, :], writes=pg("trig_t1"), tag="trig")
            kb.emit("vector", lambda v: v.tensor_add(t0[:], t0[:], t1[:]),
                    reads=pg("trig_t0", "trig_t1"), writes=pg("trig_t0"))
            kb.emit("scalar", lambda s, c=c: s.activation(S_sb[:, c, :], t0[:], AF.Sin),
                    reads=pg("trig_t0"), writes=pg("trigS", c * 2 * D, (c + 1) * 2 * D))
        return C_sb, S_sb

    def make_ln_state(self, base, par):
        s = {}
        name = lambda n: "%s_p%d_s%d" % (n, par, base)
        s["xw"] = self.sb(name("xw"), base, [P, NC, N], F32); s["xw_key"] = name("xw")
        s["zb"] = self.sb(name("zb"), base + 16384, [P, NC, N], BF16); s["zb_key"] = name("zb")
        s["xb"] = self.sb(name("xb"), base + 24576, [P, NC, N], BF16); s["xb_key"] = name("xb")
        s["rb"] = self.sb(name("rb"), base + 32768, [P, NC, N], BF16); s["rb_key"] = name("rb")
        sq0 = self.sb(name("sq0"), base + 40960, [P, N], BF16)
        sq1 = self.sb(name("sq1"), base + 41984, [P, N], BF16)
        s["sq"] = [sq0, sq1]; s["sq_key"] = [name("sq0"), name("sq1")]
        s["mean"] = self.sb(name("mean"), base + 43008, [P, N], F32); s["mean_key"] = name("mean")
        s["A"] = self.sb(name("A"), base + 45056, [P, N], F32); s["A_key"] = name("A")
        s["v1"] = self.sb(name("v1"), base + 47104, [P, N], F32); s["v1_key"] = name("v1")
        d0 = self.sb(name("d0"), base + 49152, [P, N], F32)
        d1 = self.sb(name("d1"), base + 51200, [P, N], F32)
        s["d"] = [d0, d1]; s["d_key"] = [name("d0"), name("d1")]
        return s, 53248

    def sub_l0ffn(self, C_sb, S_sb):
        kb, pg = self.kb, self.pg
        xsrc, ksrc, xdst, kdst = self.next_x()
        base = self.PHASE + 48 * 1024
        sts = []
        for par in range(2):
            st, sz = self.make_ln_state(base, par)
            st["bS"], st["bQ"] = 6, 7
            sts.append(st)
            base += sz
        ffn = []
        for par in range(2):
            nm = lambda n: "ff_%s_p%d" % (n, par)
            r2 = self.sb(nm("r2"), base, [P, N], F32)
            i2 = self.sb(nm("i2"), base + 2048, [P, N], F32)
            ffn.append(dict(r2=r2, i2=i2, r2k=nm("r2"), i2k=nm("i2")))
            base += 4096
        for w in range(NW):
            self.kb.seg()
            st = sts[w % 2]
            f = ffn[w % 2]
            xw, zb = st["xw"], st["zb"]
            self.ln0_window(w, xsrc, xw, zb, st, ksrc)
            for oc in range(NC):
                bR = (oc % 2) * 2
                bI = bR + 1
                psR, psI = self.ps[bR], self.ps[bI]
                for kc in range(NC):
                    kb.emit("tensor", lambda t, oc=oc, kc=kc, psR=psR, zb=zb:
                            t.matmul(psR[:], C_sb[:, kc, ts(oc)], zb[:, kc, :],
                                     start=(kc == 0), stop=(kc == NC - 1)),
                            reads=pg("trigC", st["zb_key"]),
                            writes=["PS%d" % bR], tag="f0_mmR")
                    kb.emit("tensor", lambda t, oc=oc, kc=kc, psI=psI, zb=zb:
                            t.matmul(psI[:], S_sb[:, kc, ts(oc)], zb[:, kc, :],
                                     start=(kc == 0), stop=(kc == NC - 1)),
                            reads=pg("trigS", st["zb_key"]),
                            writes=["PS%d" % bI], tag="f0_mmI")
                kb.emit("scalar", lambda s, psR=psR, f=f: s.activation(f["r2"][:], psR[:], AF.Square),
                        reads=["PS%d" % bR], writes=pg(f["r2k"]), tag="f0_sqR")
                kb.emit("scalar", lambda s, psI=psI, f=f: s.activation(f["i2"][:], psI[:], AF.Square),
                        reads=["PS%d" % bI], writes=pg(f["i2k"]), tag="f0_sqI")
                kb.emit("vector", lambda v, f=f: v.tensor_add(f["r2"][:], f["r2"][:], f["i2"][:]),
                        reads=pg(f["r2k"], f["i2k"]), writes=pg(f["r2k"]), tag="f0_add")
                kb.emit("scalar", lambda s, f=f: s.activation(f["i2"][:], f["r2"][:], AF.Ln),
                        reads=pg(f["r2k"]), writes=pg(f["i2k"]), tag="f0_ln")
                kb.emit("scalar", lambda s, f=f: s.activation(f["r2"][:], f["i2"][:], AF.Exp,
                                                              scale=0.5, bias=self.C_LNHALF),
                        reads=pg(f["i2k"], "consts"), writes=pg(f["r2k"]), tag="f0_exp")
                kb.emit("vector", lambda v, oc=oc, xw=xw, f=f:
                        v.tensor_add(xw[:, oc, :], xw[:, oc, :], f["r2"][:]),
                        reads=pg(f["r2k"]) + pg(st["xw_key"]), writes=pg(st["xw_key"]), tag="f0_res")
            self.x_writeback(w, xw, xdst, kdst, st["xw_key"])

    def ln_all_to_dram(self, xsrc, ksrc, dst_dram, kdst, also_resid=None):
        base = self.PHASE
        sts = []
        for par in range(2):
            st, sz = self.make_ln_state(base, par)
            st["bS"], st["bQ"] = 6, 7
            sts.append(st)
            base += sz
        kb, pg = self.kb, self.pg
        for w in range(NW):
            kb.seg()
            st = sts[w % 2]
            self.ln0_window(w, xsrc, st["xw"], st["zb"], st, ksrc)
            wsl = slice(w * N, (w + 1) * N)
            kb.dma("sync", dst_dram.rearrange("(c p) t -> p c t", p=P)[:, :, wsl],
                   st["zb"][:], reads=pg(st["zb_key"]), writes=[kdst + ":%d" % w],
                   tag="z_spill")
        return sts

    def load_weight_bf16(self, name_sb, off, src_ap, shape, q="gpsimd"):
        sb = self.sb(name_sb, off, [P, shape[0] // P, shape[1]], BF16)
        self.kb.dma(q, sb[:], src_ap.rearrange("(c p) m -> p c m", p=P),
                    writes=self.pg(name_sb), tag="wload")
        return sb

    def proj_fm(self, zwin, zkey, w_sb, wkey, out_sb, outkey, bank2, kcs=NC,
                ocs=NC, bias_row=None, bias_key=None, tag="proj"):
        kb, pg = self.kb, self.pg
        for oc in range(ocs):
            b = bank2[oc % 2]
            ps = self.ps[b]
            for kc in range(kcs):
                kb.emit("tensor", lambda t, oc=oc, kc=kc, ps=ps, w_sb=w_sb, zwin=zwin:
                        t.matmul(ps[:], w_sb[:, kc, ts(oc)], zwin[:, kc, :],
                                 start=(kc == 0), stop=(kc == kcs - 1 and bias_row is None)),
                        reads=pg(wkey, zkey), writes=["PS%d" % b], tag=tag)
            if bias_row is not None:
                kb.emit("tensor", lambda t, oc=oc, ps=ps, bias_row=bias_row:
                        t.matmul(ps[:], bias_row[:, ts(oc)], self.ones_bf[0:1, 0:N],
                                 start=False, stop=True),
                        reads=pg(bias_key, "ones_bf"), writes=["PS%d" % b], tag=tag + "_b")
            kb.emit("scalar", lambda s, oc=oc, ps=ps, out_sb=out_sb:
                    s.activation(out_sb[:, oc, :], ps[:], AF.Copy),
                    reads=["PS%d" % b], writes=pg((outkey, oc * N * 2, (oc + 1) * N * 2)),
                    tag=tag + "_ev")

    def proj_tm(self, zwin, zkey, w_sb, wkey, w, out_sb, outkey, bank2, kcs=NC,
                mfree=D, bias_row=None, bias_key=None, tag="projtm"):
        kb, pg = self.kb, self.pg
        nh = mfree // 512
        for tt in range(4):
            gt = w * 4 + tt
            for mh in range(nh):
                b = bank2[(tt * nh + mh) % 2]
                ps = self.ps[b]
                for kc in range(kcs):
                    kb.emit("tensor", lambda t, tt=tt, mh=mh, kc=kc, ps=ps, zwin=zwin, w_sb=w_sb:
                            t.matmul(ps[:], zwin[:, kc, ts(tt)], w_sb[:, kc, ts(mh, 512)],
                                     start=(kc == 0), stop=(kc == kcs - 1 and bias_row is None)),
                            reads=pg(wkey, zkey), writes=["PS%d" % b], tag=tag)
                if bias_row is not None:
                    kb.emit("tensor", lambda t, mh=mh, ps=ps, bias_row=bias_row:
                            t.matmul(ps[:], self.ones_bf[0:1, 0:128], bias_row[:, ts(mh, 512)],
                                     start=False, stop=True),
                            reads=pg(bias_key, "ones_bf"), writes=["PS%d" % b], tag=tag + "_b")
                kb.emit("scalar", lambda s, gt=gt, mh=mh, ps=ps, out_sb=out_sb:
                        s.activation(out_sb[:, gt, ts(mh, 512)], ps[:], AF.Copy),
                        reads=["PS%d" % b],
                        writes=pg((outkey, gt * mfree * 2, (gt + 1) * mfree * 2)),
                        tag=tag + "_ev")

    def load_row_bf16(self, name, off, src_ap, length, q="gpsimd"):
        sb = self.sb(name, off, [1, length], BF16)
        if src_ap.ndim == 1:
            src_ap = src_ap.unsqueeze(0)
        self.kb.dma(q, sb[:], src_ap, writes=self.pg(name), tag="rowload")
        return sb

    def load_pp_vec(self, name, off, src_ap, length, bank=5):
        ncs = length // P
        tmp = self.pptmp[0:ncs, :]
        sb = self.sb(name, off, [P, ncs], F32)
        kb, pg = self.kb, self.pg
        kb.dma("sync", tmp[:], src_ap.rearrange("(c p) -> c p", p=P),
               writes=pg("pptmp"), tag="ppload")
        ps = self.ps[bank]
        kb.emit("tensor", lambda t, tmp=tmp, ncs=ncs: t.transpose(ps[0:P, 0:ncs], tmp[:], self.ident_f32[0:ncs, 0:ncs]),
                reads=pg("pptmp", "ident_f32"), writes=["PS%d" % bank], tag="pptr")
        kb.emit("vector", lambda v, sb=sb: v.tensor_copy(sb[:], ps[0:P, 0:ncs]),
                reads=["PS%d" % bank], writes=pg(name), tag="ppev")
        return sb

    def attn_core(self, w, Qdram, K_sb, Kkey, negB, negBkey, kmax2,
                  kmax2key, scale, sgroups, Ow_sb, Owkey, a):
        kb, pg = self.kb, self.pg
        wsl = slice(w * N, (w + 1) * N)
        qw, qwkey = a["qw"], a["qwk"]
        kb.dma("sync", qw[:], Qdram.rearrange("(c p) t -> p c t", p=P)[:, :, wsl],
               reads=["d_q:%d" % w], writes=pg(qwkey), tag="qload")
        sqq, sqqk = a["sqq"], a["sqqk"]
        bq = a["bank_q"]
        psq = self.ps[bq]
        bgroups = a["bgroups"]
        for gi, chunks in enumerate(bgroups):
            for j, c in enumerate(chunks):
                p2 = j % 2
                kb.emit("scalar", lambda s, c=c, p2=p2, qw=qw: s.activation(sqq[p2][:], qw[:, c, :], AF.Square),
                        reads=pg(qwkey), writes=pg(sqqk[p2]), tag="qsq")
                kb.emit("tensor", lambda t, j=j, p2=p2, nchunks=len(chunks):
                        t.matmul(psq[:], self.ones_bf[:, 0:128], sqq[p2][:],
                                 start=(j == 0), stop=(j == nchunks - 1)),
                        reads=pg(sqqk[p2], "ones_bf"), writes=["PS%d" % bq], tag="qsqmm")
            kb.emit("vector", lambda v, gi=gi: v.reduce_max(negB[0:1, gi:gi + 1], psq[0:1, :], AX.X),
                    reads=["PS%d" % bq], writes=pg(negBkey), tag="negB1")
            kb.emit("vector", lambda v, gi=gi, scale=scale: v.tensor_scalar(
                        negB[0:1, gi:gi + 1], negB[0:1, gi:gi + 1], kmax2[0:1, gi:gi + 1],
                        -0.5 * scale, ALU.add, ALU.mult),
                    reads=pg(negBkey, kmax2key), writes=pg(negBkey), tag="negB2")
        ng = len(bgroups)
        kb.emit("vector", lambda v, ng=ng: v.tensor_copy(a["ebrow"][0:1, 0:ng], negB[0:1, 0:ng]),
                reads=pg(negBkey), writes=pg(a["ebrowk"]), tag="negB3")
        kb.emit("tensor", lambda t, ng=ng: t.matmul(psq[0:P, 0:ng], self.ones_bf[0:1, 0:128],
                                                    a["ebrow"][0:1, 0:ng], start=True, stop=True),
                reads=pg(a["ebrowk"], "ones_bf"), writes=["PS%d" % bq], tag="negB4")
        kb.emit("vector", lambda v: v.tensor_copy(a["expb"][:, 0:len(bgroups)], psq[0:P, 0:len(bgroups)]),
                reads=["PS%d" % bq], writes=pg(a["expbk"]), tag="negB5")
        rl, rlkey = a["rl"], a["rlk"]
        vsl, vslk = a["vsl"], a["vslk"]
        bS0, bS1, bL, bO0, bO1 = a["bank_S0"], a["bank_S1"], a["bank_l"], a["bank_O0"], a["bank_O1"]
        vseq = 0
        ptn = a["ptn"]
        nhalf = 32 // ptn
        def load_v(og):
            nonlocal vseq
            vlo, vhi = og["vcols"]
            m = vhi - vlo
            vb = vseq % 2
            vt = vsl[vb]
            kb.dma("gpsimd", vt[:, :, 0:m],
                   self.d_v.rearrange("(k p) d -> p k d", p=P)[:, :, vlo:vhi],
                   reads=["d_v:%d" % ww for ww in range(NW)],
                   writes=pg(vslk[vb]), tag="vload")
            vseq += 1
            return (vt, vslk[vb], m)

        for sgi, sg in enumerate(sgroups):
            PT, PTkey = a["PT2"][sgi % 2], a["PT2k"][sgi % 2]
            vts = [None] * len(sg["outs"])
            if len(sg["outs"]) == 1:
                vts[0] = load_v(sg["outs"][0])
            psL = self.ps[bL]
            for h in range(nhalf):
                for kt16 in range(ptn):
                    kt = h * ptn + kt16
                    bS = (bS0, bS1)[kt16 % 2]
                    psS = self.ps[bS]
                    if sg.get("kcs"):
                        for j, kc in enumerate(sg["kcs"]):
                            kb.emit("tensor", lambda t, kt=kt, kc=kc, j=j, psS=psS, K_sb=K_sb, nk=len(sg["kcs"]):
                                    t.matmul(psS[:], K_sb[:, kc, ts(kt)], qw[:, kc, :],
                                             start=(j == 0), stop=(j == nk - 1)),
                                    reads=pg(Kkey, qwkey), writes=["PS%d" % bS], tag="smm")
                    else:
                        lo, hi = sg["kpart"]
                        kc = sg["kc"]
                        kb.emit("tensor", lambda t, kt=kt, kc=kc, lo=lo, hi=hi, psS=psS, K_sb=K_sb, qw=qw:
                                t.matmul(psS[:], K_sb[lo:hi, kc, ts(kt)], qw[lo:hi, kc, :],
                                         start=True, stop=True),
                                reads=pg(Kkey, qwkey), writes=["PS%d" % bS], tag="smm")
                    gi = sg["bound_idx"]
                    kb.emit("scalar", lambda s, kt16=kt16, psS=psS, scale=scale, gi=gi:
                            s.activation(PT[:, kt16, :], psS[:], AF.Exp, scale=scale,
                                         bias=a["expb"][:, gi:gi + 1]),
                            reads=["PS%d" % bS] + pg(a["expbk"]),
                            writes=pg((PTkey, kt16 * N * 2, (kt16 + 1) * N * 2)),
                            tag="sexp")
                for kt16 in range(ptn):
                    kt = h * ptn + kt16
                    kb.emit("tensor", lambda t, kt16=kt16, kt=kt, psL=psL:
                            t.matmul(psL[:], self.ones_bf[:, 0:128], PT[:, kt16, :],
                                     start=(kt == 0), stop=(kt == 31)),
                            reads=pg(PTkey, "ones_bf"), writes=["PS%d" % bL], tag="lmm")
                last = (h == nhalf - 1)
                if last:
                    kb.emit("scalar", lambda s: s.activation(rl[:], psL[:], AF.Ln),
                            reads=["PS%d" % bL], writes=pg(rlkey), tag="lln")
                    kb.emit("scalar", lambda s: s.activation(rl[:], rl[:], AF.Exp, scale=-1.0),
                            reads=pg(rlkey), writes=pg(rlkey), tag="lexp")
                for ogi, og in enumerate(sg["outs"]):
                    if vts[ogi] is None:
                        vts[ogi] = load_v(og)
                    vt, vtk, m = vts[ogi]
                    bO = (bO0, bO1)[ogi % 2]
                    psO = self.ps[bO]
                    for kt16 in range(ptn):
                        kt = h * ptn + kt16
                        kb.emit("tensor", lambda t, kt16=kt16, kt=kt, psO=psO, vt=vt, m=m, vtk=vtk:
                                t.matmul(psO[0:m, :], vt[:, kt, 0:m], PT[:, kt16, :],
                                         start=(kt == 0), stop=(kt == 31)),
                                reads=pg(vtk, PTkey), writes=["PS%d" % bO], tag="pvmm")
                    if last:
                        dst_plo, dst_c = og["dst"]
                        kb.emit("vector", lambda v, m=m, dst_plo=dst_plo, dst_c=dst_c, psO=psO, Ow_sb=Ow_sb:
                                v.tensor_mul(Ow_sb[dst_plo:dst_plo + m, dst_c, :], psO[0:m, :],
                                             rl[0:m, :]),
                                reads=["PS%d" % bO] + pg(rlkey),
                                writes=pg((Owkey, dst_c * N * 2, (dst_c + 1) * N * 2)), tag="pvev")

    def knorm_window(self, w, src_sb, srckey, groups, kmax2, kmax2key, a, tag="kn"):
        kb, pg = self.kb, self.pg
        wsl = slice(w * N, (w + 1) * N)
        sqq, sqqk = a["sqq"], a["sqqk"]
        bq = a["bank_q"]
        psq = self.ps[bq]
        for gi, chunks in enumerate(groups):
            for j, c in enumerate(chunks):
                p2 = j % 2
                kb.emit("scalar", lambda s, c=c, p2=p2: s.activation(sqq[p2][:], src_sb[:, c, wsl], AF.Square),
                        reads=pg((srckey,)), writes=pg(sqqk[p2]), tag=tag + "sq")
                kb.emit("tensor", lambda t, j=j, p2=p2, nch=len(chunks):
                        t.matmul(psq[:], self.ones_bf[:, 0:128], sqq[p2][:],
                                 start=(j == 0), stop=(j == nch - 1)),
                        reads=pg(sqqk[p2], "ones_bf"), writes=["PS%d" % bq], tag=tag + "mm")
            kb.emit("vector", lambda v: v.reduce_max(a["kred"][:], psq[0:1, :], AX.X),
                    reads=["PS%d" % bq], writes=pg(a["kredk"]), tag=tag + "red")
            kb.emit("vector", lambda v, gi=gi: v.tensor_max(kmax2[0:1, gi:gi + 1],
                                                            kmax2[0:1, gi:gi + 1], a["kred"][:]),
                    reads=pg(a["kredk"], kmax2key), writes=pg(kmax2key), tag=tag + "max")

    def attn_scratch(self, base, ptn):
        nm = lambda n: "at_%s" % n
        a = {}
        a["qw"] = self.sb(nm("qw"), base, [P, NC, N], BF16); a["qwk"] = nm("qw")
        s0 = self.sb(nm("sqq0"), base + 8192, [P, N], BF16)
        s1 = self.sb(nm("sqq1"), base + 9216, [P, N], BF16)
        a["sqq"] = [s0, s1]; a["sqqk"] = [nm("sqq0"), nm("sqq1")]
        if ptn <= 16:
            a["PT2"] = [self.sb(nm("PT"), base + 10240, [P, ptn, N], BF16),
                        self.sb(nm("PTb"), base + 10240 + ptn * 1024, [P, ptn, N], BF16)]
            a["PT2k"] = [nm("PT"), nm("PTb")]
            o = base + 10240 + 2 * ptn * 1024
        else:
            pt = self.sb(nm("PT"), base + 10240, [P, ptn, N], BF16)
            a["PT2"] = [pt, pt]
            a["PT2k"] = [nm("PT"), nm("PT")]
            o = base + 10240 + ptn * 1024
        a["ptn"] = ptn
        a["rl"] = self.sb(nm("rl"), o, [P, N], F32); a["rlk"] = nm("rl")
        a["kred"] = self.sb(nm("kred"), o + 2048, [1, 1], F32); a["kredk"] = nm("kred")
        a["vsl"] = [self.sb(nm("vsl0"), o + 3072, [P, 32, P], BF16),
                    self.sb(nm("vsl1"), o + 11264, [P, 32, P], BF16)]
        a["vslk"] = [nm("vsl0"), nm("vsl1")]
        a["expb"] = self.sb(nm("expb"), o + 19456, [P, 8], F32); a["expbk"] = nm("expb")
        a["ebrow"] = self.sb(nm("ebrow"), o + 19488, [1, 8], BF16); a["ebrowk"] = nm("ebrow")
        a.update(bank_q=6, bank_S0=7, bank_S1=4, bank_l=5, bank_O0=0, bank_O1=1)
        return a, o + 20512

    def sub_attn_generic(self, kind):
        kb, pg = self.kb, self.pg
        xsrc, ksrc, xdst, kdst = self.next_x()
        self.ln_all_to_dram(xsrc, ksrc, self.d_z, "d_z")
        kb.seg()
        B0 = self.PHASE
        WQ_OFF, WK_OFF, WV_OFF = B0, B0 + 16384, B0 + 32768
        K_OFF = B0 + 49152
        ZS_OFF = K_OFF + 65536
        QS_OFF = ZS_OFF + 16384
        VS_OFF = QS_OFF + 8192
        NRM_OFF = VS_OFF + 8192
        if kind == 'A0':
            wq = self.load_weight_bf16("wqT", WQ_OFF, self.din["h_L0_WqT"], [D, D])
            wk = self.load_weight_bf16("wkT", WK_OFF, self.din["h_L0_WkT"], [D, D])
            wv = self.load_weight_bf16("wvT", WV_OFF, self.din["h_L0_WvT"], [D, D])
            qb = kbias = vbias = None
            scale = 1.0 / 32.0
        elif kind == 'A1':
            wq = self.load_weight_bf16("wqT", WQ_OFF, self.din["L1_attn_Wq"].rearrange("d h k -> d (h k)"), [D, D])
            wk = self.load_weight_bf16("wkT", WK_OFF, self.din["L1_attn_Wk"].rearrange("d h k -> d (h k)"), [D, D])
            wv = self.load_weight_bf16("wvT", WV_OFF, self.din["L1_attn_Wv"].rearrange("d h k -> d (h k)"), [D, D])
            qb = self.load_row_bf16("qb_row", NRM_OFF, self.din["L1_attn_bq"].rearrange("h k -> (h k)"), D)
            kbias = self.load_row_bf16("kb_row", NRM_OFF + 2048, self.din["L1_attn_bk"].rearrange("h k -> (h k)"), D)
            vbias = self.load_row_bf16("vb_row", NRM_OFF + 4096, self.din["L1_attn_bv"].rearrange("h k -> (h k)"), D)
            scale = 1.0 / 8.0
        else:
            wq = self.load_weight_bf16("wqT", WQ_OFF, self.din["h_L1f_qwT"], [D, D])
            wk = self.load_weight_bf16("wkT", WK_OFF, self.din["h_L1f_kwT"], [D, D])
            wv = self.load_weight_bf16("wvT", WV_OFF, self.din["h_L1f_vwT"], [D, D])
            qb = self.load_row_bf16("qb_row", NRM_OFF, self.din["L1_ffn_q_b"], D)
            kbias = self.load_row_bf16("kb_row", NRM_OFF + 2048, self.din["L1_ffn_k_b"], D)
            vbias = self.load_row_bf16("vb_row", NRM_OFF + 4096, self.din["L1_ffn_v_b"], D)
            scale = 1.0 / 32.0
        K_sb = self.sb("K_fm", K_OFF, [P, NC, T], BF16)
        zs = [self.sb("zs0", ZS_OFF, [P, NC, N], BF16), self.sb("zs1", ZS_OFF + 8192, [P, NC, N], BF16)]
        zsk = ["zs0", "zs1"]
        qstage = self.sb("qstage", QS_OFF, [P, NC, N], BF16)
        vstage = self.sb("vstage", VS_OFF, [P, 4, D], BF16)
        kmax2 = self.sb("kmax2", 209632, [1, 8], F32)
        kb.emit("gpsimd", lambda g: g.memset(kmax2[:], 0.0), writes=pg("kmax2"))
        if kind == 'F1':
            f1m = self.sb("f1m", NRM_OFF + 8192, [P, N], F32)
            f1a = self.sb("f1a", NRM_OFF + 10240, [P, N], F32)
            f1d = self.sb("f1d", NRM_OFF + 12288, [P, N], F32)
            f1nw = self.load_pp_vec("f1nw", NRM_OFF + 14336, self.din["L1_ffn_norm_w"], D)
            f1nb = self.load_pp_vec("f1nb", NRM_OFF + 14464, self.din["L1_ffn_norm_b"], D)
        sqq_a = {"sqq": self.stsq, "sqqk": ["stsq0", "stsq1"], "bank_q": 6,
                 "kred": self.sb("p1kred", NRM_OFF + 14720, [1, 1], F32), "kredk": "p1kred"}
        for w in range(NW):
            kb.seg()
            wsl = slice(w * N, (w + 1) * N)
            par = w % 2
            zw, zwk = zs[par], zsk[par]
            kb.dma("sync", zw[:], self.d_z.rearrange("(c p) t -> p c t", p=P)[:, :, wsl],
                   reads=["d_z:%d" % w], writes=pg(zwk), tag="zload")
            kcs = NC
            if kind == 'F1':
                self.stats_of(zw, zwk, f1m, "f1m", f1a, "f1a", bS=7, bQ=5)
                for c in range(NC):
                    kb.emit("vector", lambda v, c=c, zw=zw: v.tensor_sub(f1d[:], zw[:, c, :], f1m[:]),
                            reads=pg(zwk, "f1m"), writes=pg("f1d"), tag="f1sub")
                    kb.emit("vector", lambda v: v.tensor_mul(f1d[:], f1d[:], f1a[:]),
                            reads=pg("f1d", "f1a"), writes=pg("f1d"), tag="f1mul")
                    kb.emit("vector", lambda v, c=c, zw=zw: v.tensor_scalar(
                                zw[:, c, :], f1d[:], f1nw[:, c:c + 1], f1nb[:, c:c + 1],
                                ALU.mult, ALU.add),
                            reads=pg("f1d", "f1nw", "f1nb"), writes=pg(zwk), tag="f1ts")
                kcs = max(1, NC >> (w // 2))
            self.proj_fm(zw, zwk, wq, "wqT", qstage, "qstage", (0, 1), kcs=kcs,
                         bias_row=qb, bias_key=("qb_row" if qb is not None else None), tag="qp")
            kb.dma("sync", self.d_q.rearrange("(c p) t -> p c t", p=P)[:, :, wsl],
                   qstage[:], reads=pg("qstage"), writes=["d_q:%d" % w], tag="qspill")
            for oc in range(NC):
                b = (2, 3)[oc % 2]
                ps = self.ps[b]
                for kc in range(kcs):
                    kb.emit("tensor", lambda t, oc=oc, kc=kc, ps=ps, zw=zw, wk=wk, kcs=kcs:
                            t.matmul(ps[:], wk[:, kc, ts(oc)], zw[:, kc, :],
                                     start=(kc == 0), stop=(kbias is None and kc == kcs - 1)),
                            reads=pg("wkT", zwk), writes=["PS%d" % b], tag="kp")
                if kbias is not None:
                    kb.emit("tensor", lambda t, oc=oc, ps=ps, kbias=kbias:
                            t.matmul(ps[:], kbias[:, ts(oc)], self.ones_bf[0:1, 0:N],
                                     start=False, stop=True),
                            reads=pg("kb_row", "ones_bf"), writes=["PS%d" % b], tag="kpb")
                kb.emit("scalar", lambda s, oc=oc, ps=ps, wsl=wsl, K_sb=K_sb:
                        s.activation(K_sb[:, oc, wsl], ps[:], AF.Copy),
                        reads=["PS%d" % b], writes=pg(("K_fm", oc * T * 2 + w * N * 2,
                                                       oc * T * 2 + (w + 1) * N * 2)), tag="kev")
            self.proj_tm(zw, zwk, wv, "wvT", 0, vstage, "vstage", (2, 3), kcs=kcs,
                         bias_row=vbias, bias_key=("vb_row" if vbias is not None else None), tag="vp")
            kb.dma("sync", self.d_v.rearrange("(g p) m -> p g m", p=P)[:, ts(w, 4), :],
                   vstage[:], reads=pg("vstage"), writes=["d_v:%d" % w], tag="vspill")
            bgroups = [list(range(NC))] if kind == 'A0' else [[c] for c in range(NC)]
            self.knorm_window(w, K_sb, "K_fm", bgroups, kmax2, "kmax2", sqq_a)
        kb.seg()
        ptn = 32 if kind == 'A0' else 16
        a, aend = self.attn_scratch(K_OFF + 65536, ptn)
        a["bgroups"] = [list(range(NC))] if kind == 'A0' else [[c] for c in range(NC)]
        negB = self.sb("negB", 200416, [1, NC], F32)
        assert aend <= 200416, aend
        WO_OFF = B0
        if kind == 'A0':
            wo = self.load_weight_bf16("woT", WO_OFF, self.din["h_L0_WoT"], [D, D])
            ob = None
        elif kind == 'A1':
            wo = self.load_weight_bf16("woT", WO_OFF, self.din["L1_attn_Wo"].rearrange("h k d -> (h k) d"), [D, D])
            ob = self.load_row_bf16("ob_row", WO_OFF + 16384, self.din["L1_attn_bo"], D)
        else:
            wo = self.load_weight_bf16("woT", WO_OFF, self.din["h_L1f_owT"], [D, D])
            ob = self.load_row_bf16("ob_row", WO_OFF + 16384, self.din["L1_ffn_o_b"], D)
        OW_OFF = WO_OFF + 18432
        Ow = self.sb("Ow", OW_OFF, [P, NC, N], BF16)
        xw3 = self.sb("xw3", OW_OFF + 8192, [P, NC, N], F32)
        assert OW_OFF + 24576 <= K_OFF, OW_OFF
        if kind != 'A0':
            E3 = aend
            z3 = self.sb("z3", E3, [P, NC, N], BF16)
            u3 = self.sb("u3", E3 + 8192, [P, NC, N], BF16)
            assert E3 + 16384 + 512 <= 200416, E3
            t3 = self.sb("t3", OW_OFF + 24576, [P, N], F32)
            m3 = self.sb("m3", OW_OFF + 26624, [P, N], F32)
            a3 = self.sb("a3", OW_OFF + 28672, [P, N], F32)
            assert OW_OFF + 30720 <= K_OFF, OW_OFF
        if kind == 'A1':
            gam = self.load_pp_vec("gam", E3 + 16384, self.din["L1_attn_gamma"], D)
            bet = self.load_pp_vec("bet", E3 + 16384 + 128, self.din["L1_attn_beta"], D)
        if kind == 'A0':
            sgroups = [dict(kcs=list(range(NC)), bound_idx=0,
                            outs=[dict(vcols=(dc * P, (dc + 1) * P), dst=(0, dc)) for dc in range(NC)])]
        elif kind == 'A1':
            sgroups = []
            for c in range(NC):
                for j in range(2):
                    sgroups.append(dict(kpart=(j * 64, (j + 1) * 64), kc=c, bound_idx=c,
                                        outs=[dict(vcols=(c * P + j * 64, c * P + j * 64 + 64),
                                                   dst=(j * 64, c))]))
        else:
            sgroups = [dict(kpart=(0, P), kc=c, bound_idx=c,
                            outs=[dict(vcols=(c * P, (c + 1) * P), dst=(0, c))]) for c in range(NC)]
        for w in range(NW):
            kb.seg()
            wsl = slice(w * N, (w + 1) * N)
            self.attn_core(w, self.d_q, K_sb, "K_fm", negB, "negB",
                           kmax2, "kmax2", scale, sgroups, Ow, "Ow", a)
            if getattr(self, "dbg_ow", None) is not None:
                kb.dma("sync", self.dbg_ow.rearrange("(c p) t -> p c t", p=P)[:, :, wsl],
                       Ow[:], reads=pg("Ow"), writes=["dbgow:%d" % w], tag="dbgow")
            kb.seg()
            kb.dma("scalar", xw3[:], xsrc.rearrange("(c p) t -> p c t", p=P)[:, :, wsl],
                   reads=[ksrc + ":%d" % w], writes=pg("xw3"), tag="p3x")
            if kind != 'A0':
                kb.dma("scalar", z3[:], self.d_z.rearrange("(c p) t -> p c t", p=P)[:, :, wsl],
                       reads=["d_z:%d" % w], writes=pg("z3"), tag="p3z")
            mcs = NC if kind != 'F1' else max(1, NC >> (w // 2))
            for oc in range(NC):
                if kind == 'F1' and oc >= mcs:
                    kb.emit("vector", lambda v, oc=oc: v.scalar_tensor_tensor(
                                xw3[:, oc, :], z3[:, oc, :], 0.5, xw3[:, oc, :],
                                ALU.mult, ALU.add),
                            reads=pg("z3", "xw3"), writes=pg("xw3"), tag="f1pad")
                    continue
                b = (2, 3)[oc % 2]
                ps = self.ps[b]
                okcs = NC if kind != 'F1' else mcs
                for kc in range(okcs):
                    kb.emit("tensor", lambda t, oc=oc, kc=kc, ps=ps, wo=wo, okcs=okcs:
                            t.matmul(ps[:], wo[:, kc, ts(oc)], Ow[:, kc, :],
                                     start=(kc == 0), stop=(ob is None and kc == okcs - 1)),
                            reads=pg("woT", "Ow"), writes=["PS%d" % b], tag="op")
                if ob is not None:
                    kb.emit("tensor", lambda t, oc=oc, ps=ps, ob=ob:
                            t.matmul(ps[:], ob[:, ts(oc)], self.ones_bf[0:1, 0:N],
                                     start=False, stop=True),
                            reads=pg("ob_row", "ones_bf"), writes=["PS%d" % b], tag="opb")
                if kind == 'A0':
                    kb.emit("vector", lambda v, oc=oc, ps=ps: v.tensor_add(
                                xw3[:, oc, :], xw3[:, oc, :], ps[:]),
                            reads=["PS%d" % b] + pg("xw3"), writes=pg("xw3"), tag="a0res")
                elif kind == 'A1':
                    kb.emit("vector", lambda v, oc=oc, ps=ps: v.tensor_add(
                                u3[:, oc, :], z3[:, oc, :], ps[:]),
                            reads=["PS%d" % b] + pg("z3"),
                            writes=pg(("u3", oc * N * 2, (oc + 1) * N * 2)), tag="a1w")
                else:
                    kb.emit("vector", lambda v, oc=oc, ps=ps: v.scalar_tensor_tensor(
                                xw3[:, oc, :], ps[:], 0.5, xw3[:, oc, :], ALU.mult, ALU.add),
                            reads=["PS%d" % b] + pg("xw3"), writes=pg("xw3"), tag="f1r1")
                    kb.emit("vector", lambda v, oc=oc: v.scalar_tensor_tensor(
                                xw3[:, oc, :], z3[:, oc, :], 0.5, xw3[:, oc, :],
                                ALU.mult, ALU.add),
                            reads=pg("z3", "xw3"), writes=pg("xw3"), tag="f1r2")
            if kind == 'A1':
                self.stats_of(u3, "u3", m3, "m3", a3, "a3", bS=6, bQ=5, eps=1e-3)
                for oc in range(NC):
                    kb.emit("vector", lambda v, oc=oc: v.tensor_sub(t3[:], u3[:, oc, :], m3[:]),
                            reads=pg("u3", "m3"), writes=pg("t3"), tag="a1s")
                    kb.emit("vector", lambda v: v.tensor_mul(t3[:], t3[:], a3[:]),
                            reads=pg("t3", "a3"), writes=pg("t3"), tag="a1m")
                    kb.emit("vector", lambda v, oc=oc: v.tensor_scalar(
                                t3[:], t3[:], gam[:, oc:oc + 1], bet[:, oc:oc + 1],
                                ALU.mult, ALU.add),
                            reads=pg("t3", "gam", "bet"), writes=pg("t3"), tag="a1g")
                    kb.emit("vector", lambda v, oc=oc: v.tensor_add(
                                xw3[:, oc, :], xw3[:, oc, :], t3[:]),
                            reads=pg("t3", "xw3"), writes=pg("xw3"), tag="a1r")
            self.x_writeback(w, xw3, xdst, kdst, "xw3")

    def sub_l2attn(self):
        kb, pg = self.kb, self.pg
        xsrc, ksrc, xdst, kdst = self.next_x()
        self.ln_all_to_dram(xsrc, ksrc, self.d_z, "d_z")
        kb.seg()
        B0 = self.PHASE
        Z_OFF = B0
        W1_OFF = B0 + 65536
        W2_OFF = W1_OFF + 16384
        VC_OFF = W2_OFF + 16384
        ST_OFF = VC_OFF + 16384
        RET_OFF = ST_OFF + 16384
        SM_OFF = 201856
        assert RET_OFF + 65536 <= SM_OFF, RET_OFF
        z_sb = self.sb("z2r", Z_OFF, [P, NC, T], BF16)
        for w in range(NW):
            wsl = slice(w * N, (w + 1) * N)
            kb.dma("sync", z_sb[:, :, wsl], self.d_z.rearrange("(c p) t -> p c t", p=P)[:, :, wsl],
                   reads=["d_z:%d" % w], writes=pg(("z2r",)), tag="z2l")
        wqT = self.load_weight_bf16("r_wqT", W1_OFF, self.din["h_L2_WqT"], [D, D])
        wvT = self.load_weight_bf16("r_wvT", W2_OFF, self.din["h_L2_WvT"], [D, D])
        Vc = self.sb("r_Vc", VC_OFF, [P, T], F32)
        stt = self.sb("r_st", ST_OFF, [P, T], F32)
        ret = self.sb("r_ret", RET_OFF, [P, NC, T], BF16)
        lamc = self.sb("r_lamc", SM_OFF, [16, 8], F32)
        lam16 = self.sb("r_lam16", SM_OFF + 64, [16, 1], F32)
        lam_pp = self.sb("r_lampp", SM_OFF + 96, [P, 8], F32)
        hpar = self.sb("r_hpar", SM_OFF + 128, [16, P], F32)
        hmask = self.sb("r_hmask", SM_OFF + 640, [16, 8], F32)
        kb.dma("sync", lam16[:], self.din["L2_attn_beta"].unsqueeze(1), writes=pg("r_lam16"))
        kb.dma("sync", hpar[:], self.din["c_hpar"][:], writes=pg("r_hpar"))
        kb.dma("sync", hmask[:], self.din["c_hmask"][:], writes=pg("r_hmask"))
        kb.emit("scalar", lambda s: s.activation(lam16[:], lam16[:], AF.Exp, scale=-LN2),
                reads=pg("r_lam16"), writes=pg("r_lam16"), tag="lam")
        kb.emit("vector", lambda v: v.tensor_scalar(lam16[:], lam16[:], -1.0, 1.0,
                                                    ALU.mult, ALU.add),
                reads=pg("r_lam16"), writes=pg("r_lam16"), tag="lam")
        kb.emit("vector", lambda v: v.tensor_scalar_max(lam16[:], lam16[:], 1.1754944e-38),
                reads=pg("r_lam16"), writes=pg("r_lam16"), tag="lam")
        kb.emit("vector", lambda v: v.tensor_scalar_min(lam16[:], lam16[:], 1.0 - 1e-9),
                reads=pg("r_lam16"), writes=pg("r_lam16"), tag="lam")
        kb.emit("vector", lambda v: v.tensor_mul(lamc[:], lam16[:, 0:1].broadcast_to([16, 8]), hmask[:]),
                reads=pg("r_lam16", "r_hmask"), writes=pg("r_lamc"), tag="lam")
        kb.emit("tensor", lambda t: t.matmul(self.ps[5][0:P, 0:8], hpar[:], lamc[:],
                                             start=True, stop=True),
                reads=pg("r_hpar", "r_lamc"), writes=["PS5"], tag="lam")
        kb.emit("vector", lambda v: v.tensor_copy(lam_pp[:], self.ps[5][0:P, 0:8]),
                reads=["PS5"], writes=pg("r_lampp"), tag="lam")
        for c in range(NC):
            kb.seg()
            for w in range(NW):
                wsl = slice(w * N, (w + 1) * N)
                b = (0, 1)[w % 2]
                ps = self.ps[b]
                for kc in range(NC):
                    kb.emit("tensor", lambda t, c=c, kc=kc, ps=ps, wsl=wsl:
                            t.matmul(ps[:], wvT[:, kc, ts(c)], z_sb[:, kc, wsl],
                                     start=(kc == 0), stop=(kc == NC - 1)),
                            reads=pg("r_wvT", "z2r"), writes=["PS%d" % b], tag="rv")
                kb.emit("vector", lambda v, ps=ps, wsl=wsl: v.tensor_copy(Vc[:, wsl], ps[:]),
                        reads=["PS%d" % b], writes=pg(("r_Vc", wsl.start * 4, wsl.stop * 4)), tag="rvev")
            kb.emit("vector", lambda v, c=c: v.tensor_tensor_scan(
                        stt[:], lam_pp[:, c:c + 1].broadcast_to([P, T]), Vc[:],
                        0.0, ALU.mult, ALU.add),
                    reads=pg("r_Vc", "r_lampp"), writes=pg("r_st"), tag="rscan")
            for w in range(NW):
                wsl = slice(w * N, (w + 1) * N)
                b = (2, 3)[w % 2]
                ps = self.ps[b]
                for kc in range(NC):
                    kb.emit("tensor", lambda t, c=c, kc=kc, ps=ps, wsl=wsl:
                            t.matmul(ps[:], wqT[:, kc, ts(c)], z_sb[:, kc, wsl],
                                     start=(kc == 0), stop=(kc == NC - 1)),
                            reads=pg("r_wqT", "z2r"), writes=["PS%d" % b], tag="rq")
                kb.emit("vector", lambda v, c=c, ps=ps, wsl=wsl:
                        v.tensor_mul(ret[:, c, wsl], ps[:], stt[:, wsl]),
                        reads=["PS%d" % b] + pg("r_st"),
                        writes=pg(("r_ret", c * T * 2 + wsl.start * 2, c * T * 2 + wsl.stop * 2)),
                        tag="rmul")
        kb.seg()
        wgT = self.load_weight_bf16("r_wgT", W1_OFF, self.din["h_L2_WgT"], [D, D])
        woT = self.load_weight_bf16("r_woT", W2_OFF, self.din["h_L2_WoT"], [D, D])
        gw = self.sb("r_gw", VC_OFF, [P, NC, N], BF16)
        ym = self.sb("r_ym", VC_OFF + 8192, [P, NC, N], BF16)
        xw2 = self.sb("r_xw", ST_OFF, [P, NC, N], F32)
        m2 = self.sb("r_m2", SM_OFF + 704, [P, N], F32)
        a2 = self.sb("r_a2", SM_OFF + 2752, [P, N], F32)
        t2 = self.sb("r_t2", SM_OFF + 4800, [P, N], F32)
        lng = self.load_pp_vec("r_lng", SM_OFF + 6848, self.din["L2_attn_ln_gamma"], D)
        lnb = self.load_pp_vec("r_lnb", SM_OFF + 6976, self.din["L2_attn_ln_beta"], D)
        for w in range(NW):
            kb.seg()
            wsl = slice(w * N, (w + 1) * N)
            kb.dma("scalar", xw2[:], xsrc.rearrange("(c p) t -> p c t", p=P)[:, :, wsl],
                   reads=[ksrc + ":%d" % w], writes=pg("r_xw"), tag="rx")
            for oc in range(NC):
                b = (0, 1)[oc % 2]
                ps = self.ps[b]
                for kc in range(NC):
                    kb.emit("tensor", lambda t, oc=oc, kc=kc, ps=ps, wsl=wsl:
                            t.matmul(ps[:], wgT[:, kc, ts(oc)], z_sb[:, kc, wsl],
                                     start=(kc == 0), stop=(kc == NC - 1)),
                            reads=pg("r_wgT", "z2r"), writes=["PS%d" % b], tag="rg")
                kb.emit("scalar", lambda s, oc=oc, ps=ps: s.activation(gw[:, oc, :], ps[:], AF.Silu),
                        reads=["PS%d" % b], writes=pg(("r_gw", oc * N * 2, (oc + 1) * N * 2)),
                        tag="rgev")
            retw = ret[:, :, wsl]
            self.stats_of(retw, "r_ret", m2, "r_m2", a2, "r_a2", bS=6, bQ=5)
            for oc in range(NC):
                kb.emit("vector", lambda v, oc=oc, retw=retw: v.tensor_sub(t2[:], retw[:, oc, :], m2[:]),
                        reads=pg("r_ret", "r_m2"), writes=pg("r_t2"), tag="ry1")
                kb.emit("vector", lambda v: v.tensor_mul(t2[:], t2[:], a2[:]),
                        reads=pg("r_t2", "r_a2"), writes=pg("r_t2"), tag="ry2")
                kb.emit("vector", lambda v, oc=oc: v.tensor_scalar(
                            t2[:], t2[:], lng[:, oc:oc + 1], lnb[:, oc:oc + 1],
                            ALU.mult, ALU.add),
                        reads=pg("r_t2", "r_lng", "r_lnb"), writes=pg("r_t2"), tag="ry3")
                kb.emit("vector", lambda v, oc=oc: v.tensor_mul(ym[:, oc, :], t2[:], gw[:, oc, :]),
                        reads=pg("r_t2", "r_gw"),
                        writes=pg(("r_ym", oc * N * 2, (oc + 1) * N * 2)), tag="ry4")
            for oc in range(NC):
                b = (2, 3)[oc % 2]
                ps = self.ps[b]
                for kc in range(NC):
                    kb.emit("tensor", lambda t, oc=oc, kc=kc, ps=ps:
                            t.matmul(ps[:], woT[:, kc, ts(oc)], ym[:, kc, :],
                                     start=(kc == 0), stop=(kc == NC - 1)),
                            reads=pg("r_woT", "r_ym"), writes=["PS%d" % b], tag="ro")
                kb.emit("vector", lambda v, oc=oc, ps=ps: v.tensor_add(
                            xw2[:, oc, :], xw2[:, oc, :], ps[:]),
                        reads=["PS%d" % b] + pg("r_xw"), writes=pg("r_xw"), tag="rres")
            self.x_writeback(w, xw2, xdst, kdst, "r_xw")

    def sub_l3attn(self):
        kb, pg = self.kb, self.pg
        xsrc, ksrc, xdst, kdst = self.next_x()
        self.ln_all_to_dram(xsrc, ksrc, self.d_z, "d_z")
        kb.seg()
        B0 = self.PHASE
        M_OFF = B0
        A_OFF = B0 + 65536
        ZC_OFF = A_OFF + 16384
        YC_OFF = ZC_OFF + 16384
        CV_OFF = YC_OFF + 8192
        BY_OFF = CV_OFF + 16384
        SS_OFF = BY_OFF + 16384
        SM_OFF = SS_OFF + 16384
        m_sb = self.sb("l3_m", M_OFF, [P, NC, T], BF16)
        Abc = self.sb("l3_A", A_OFF, [P, T], F32)
        zc = [self.sb("l3_zc0", ZC_OFF, [P, T], BF16), self.sb("l3_zc1", ZC_OFF + 8192, [P, T], BF16)]
        zck = ["l3_zc0", "l3_zc1"]
        yc = self.sb("l3_yc", YC_OFF, [P, T], BF16)
        cv = self.sb("l3_cv", CV_OFF, [P, T], F32)
        by = self.sb("l3_by", BY_OFF, [P, T], F32)
        ssm = self.sb("l3_ss", SS_OFF, [P, T], F32)
        nw1 = self.load_pp_vec("l3_nw1", SM_OFF, self.din["L3_attn_norm_w"], D)
        cwk = [self.load_pp_vec("l3_cwk%d" % k, SM_OFF + 256 + k * 128, self.din["h_L3_convw"][k], D)
               for k in range(3)]
        cb_pp = self.load_pp_vec("l3_cb", SM_OFF + 1024, self.din["L3_attn_conv_b"], D)
        a_pp = self.load_pp_vec("l3_a", SM_OFF + 1280, self.din["L3_attn_a"], D)
        b_pp = self.load_pp_vec("l3_b", SM_OFF + 1536, self.din["L3_attn_b"], D)
        s_pp = self.sb("l3_s", SM_OFF + 1792, [P, NC], F32)
        v1 = self.sb("l3_v1", SM_OFF + 2048, [P, N], F32)
        kb.emit("scalar", lambda s: s.activation(s_pp[:], a_pp[:], AF.Tanh),
                reads=pg("l3_a"), writes=pg("l3_s"), tag="l3tanh")
        psQ = self.ps[7]
        for w in range(NW):
            wsl = slice(w * N, (w + 1) * N)
            zw = zc[w % 2]
            kb.dma("sync", zw[:, 0:NC * N].rearrange("p (c t) -> p c t", c=NC),
                   self.d_z.rearrange("(c p) t -> p c t", p=P)[:, :, wsl],
                   reads=["d_z:%d" % w], writes=pg(zck[w % 2]), tag="l3zl")
            for c in range(NC):
                p2 = c % 2
                kb.emit("scalar", lambda s, c=c, p2=p2, zw=zw: s.activation(
                            self.stsq[p2][:], zw[:, c * N:(c + 1) * N], AF.Square),
                        reads=pg(zck[w % 2]), writes=pg("stsq%d" % p2), tag="l3sq"),
                kb.emit("tensor", lambda t, c=c, p2=p2: t.matmul(
                            psQ[:], self.ones_bf[:, 0:128], self.stsq[p2][:],
                            start=(c == 0), stop=(c == NC - 1)),
                        reads=pg("stsq%d" % p2, "ones_bf"), writes=["PS7"], tag="l3mm")
            kb.emit("scalar", lambda s: s.activation(v1[:], psQ[:], AF.Ln, bias=self.C_EPS6,
                                                     scale=1.0 / D),
                    reads=["PS7", "consts"], writes=pg("l3_v1"), tag="l3ln")
            kb.emit("scalar", lambda s, wsl=wsl: s.activation(Abc[:, wsl], v1[:], AF.Exp, scale=-0.5),
                    reads=pg("l3_v1"), writes=pg(("l3_A", wsl.start * 4, wsl.stop * 4)), tag="l3ex")
        kb.seg()
        for c in range(NC):
            kb.seg()
            zcc = self.sb("l3_zcc%d" % c, ZC_OFF, [P, T], BF16) if False else zc[0]
            kb.dma("sync", zcc[:], self.d_z.rearrange("(c p) t -> p c t", p=P)[:, c, :],
                   reads=["d_z:%d" % w2 for w2 in range(NW)], writes=pg(zck[0]), tag="l3zc")
            kb.emit("vector", lambda v, c=c, zcc=zcc: v.scalar_tensor_tensor(
                        yc[:], zcc[:], nw1[:, c:c + 1], Abc[:], ALU.mult, ALU.mult),
                    reads=pg(zck[0], "l3_nw1", "l3_A"), writes=pg("l3_yc"), tag="l3y")
            kb.emit("vector", lambda v, c=c: v.tensor_scalar(
                        cv[:], yc[:], cwk[1][:, c:c + 1], cb_pp[:, c:c + 1],
                        ALU.mult, ALU.add),
                    reads=pg("l3_yc", "l3_cwk1", "l3_cb"), writes=pg("l3_cv"), tag="l3c1")
            kb.emit("vector", lambda v, c=c: v.scalar_tensor_tensor(
                        cv[:, 1:T], yc[:, 0:T - 1], cwk[0][:, c:c + 1], cv[:, 1:T],
                        ALU.mult, ALU.add),
                    reads=pg("l3_yc", "l3_cwk0", "l3_cv"), writes=pg("l3_cv"), tag="l3c0")
            kb.emit("vector", lambda v, c=c: v.scalar_tensor_tensor(
                        cv[:, 0:T - 1], yc[:, 1:T], cwk[2][:, c:c + 1], cv[:, 0:T - 1],
                        ALU.mult, ALU.add),
                    reads=pg("l3_yc", "l3_cwk2", "l3_cv"), writes=pg("l3_cv"), tag="l3c2")
            kb.emit("vector", lambda v, c=c: v.tensor_scalar_mul(by[:], yc[:], b_pp[:, c:c + 1]),
                    reads=pg("l3_yc", "l3_b"), writes=pg("l3_by"), tag="l3by")
            kb.emit("vector", lambda v, c=c: v.tensor_tensor_scan(
                        ssm[:], s_pp[:, c:c + 1].broadcast_to([P, T]), by[:],
                        0.0, ALU.mult, ALU.add),
                    reads=pg("l3_by", "l3_s"), writes=pg("l3_ss"), tag="l3scan")
            kb.emit("vector", lambda v, c=c, zcc=zcc: v.tensor_add(cv[:], cv[:], zcc[:]),
                    reads=pg("l3_cv", zck[0]), writes=pg("l3_cv"), tag="l3m1")
            kb.emit("vector", lambda v, c=c: v.tensor_add(
                        m_sb[:, c, :], cv[:], ssm[:]),
                    reads=pg("l3_cv", "l3_ss"),
                    writes=pg(("l3_m", c * T * 2, (c + 1) * T * 2)), tag="l3m2")
        kb.seg()
        nw2 = self.load_pp_vec("l3_nw2", SM_OFF + 2048 + 2048, self.din["L3_attn_mlp_norm_w"], D)
        bg_pp = self.load_pp_vec("l3_bg", SM_OFF + 4352, self.din["L3_attn_b_gate"], 4096)
        self.bu_row = self.load_row_bf16("bu_row", SM_OFF + 4608, self.din["L3_attn_b_up"], 4096)
        bd_pp = self.load_pp_vec("l3_bd", SM_OFF + 13056, self.din["L3_attn_b_down"], D)
        u2 = self.sb("l3_u2", ZC_OFF, [P, NC, N], BF16)
        hwin = self.sb("l3_hw", CV_OFF, [P, 32, N], BF16)
        wgt = [self.sb("l3_wg0", A_OFF, [P, NC, P], BF16),
               self.sb("l3_wg1", A_OFF + 2048, [P, NC, P], BF16)]
        wgtk = ["l3_wg0", "l3_wg1"]
        wut = [self.sb("l3_wu0", A_OFF + 4096, [P, NC, P], BF16),
               self.sb("l3_wu1", A_OFF + 6144, [P, NC, P], BF16)]
        wutk = ["l3_wu0", "l3_wu1"]
        wdt = [self.sb("l3_wd0", A_OFF + 8192, [P, D], BF16),
               self.sb("l3_wd1", A_OFF + 10240, [P, D], BF16)]
        wdtk = ["l3_wd0", "l3_wd1"]
        gt = self.sb("l3_gt", A_OFF + 12288, [P, N], BF16)
        xw3 = self.sb("l3_xw3", YC_OFF, [P, NC, N], F32) if False else self.sb("l3_xw3", SS_OFF, [P, NC, N], F32)
        m3 = self.sb("l3_m3", SM_OFF + 13568, [P, N], F32)
        a3 = self.sb("l3_a3", SM_OFF + 15616, [P, N], F32)
        for w in range(NW):
            kb.seg()
            wsl = slice(w * N, (w + 1) * N)
            mw = m_sb[:, :, wsl]
            for c in range(NC):
                p2 = c % 2
                kb.emit("scalar", lambda s, c=c, p2=p2, mw=mw: s.activation(
                            self.stsq[p2][:], mw[:, c, :], AF.Square),
                        reads=pg("l3_m"), writes=pg("stsq%d" % p2), tag="l3q2")
                kb.emit("tensor", lambda t, c=c, p2=p2: t.matmul(
                            psQ[:], self.ones_bf[:, 0:128], self.stsq[p2][:],
                            start=(c == 0), stop=(c == NC - 1)),
                        reads=pg("stsq%d" % p2, "ones_bf"), writes=["PS7"], tag="l3mm2")
            kb.emit("scalar", lambda s: s.activation(m3[:], psQ[:], AF.Ln, bias=self.C_EPS6,
                                                     scale=1.0 / D),
                    reads=["PS7", "consts"], writes=pg("l3_m3"), tag="l3l2")
            kb.emit("scalar", lambda s: s.activation(a3[:], m3[:], AF.Exp, scale=-0.5),
                    reads=pg("l3_m3"), writes=pg("l3_a3"), tag="l3e2")
            for c in range(NC):
                kb.emit("vector", lambda v, c=c, mw=mw: v.scalar_tensor_tensor(
                            u2[:, c, :], mw[:, c, :], nw2[:, c:c + 1], a3[:],
                            ALU.mult, ALU.mult),
                        reads=pg("l3_m", "l3_nw2", "l3_a3"),
                        writes=pg(("l3_u2", c * N * 2, (c + 1) * N * 2)), tag="l3u2")
            for fc in range(32):
                p2 = fc % 2
                kb.dma("gpsimd", wgt[p2][:], self.din["L3_attn_w_gate"].rearrange(
                            "(c p) h -> p c h", p=P)[:, :, ts(fc)],
                       writes=pg(wgtk[p2]), tag="l3wg")
                kb.dma("gpsimd", wut[p2][:], self.din["L3_attn_w_up"].rearrange(
                            "(c p) h -> p c h", p=P)[:, :, ts(fc)],
                       writes=pg(wutk[p2]), tag="l3wu")
                bA = (0, 1)[fc % 2]
                psA = self.ps[bA]
                for kc in range(NC):
                    kb.emit("tensor", lambda t, fc=fc, kc=kc, p2=p2, psA=psA:
                            t.matmul(psA[:], wgt[p2][:, kc, :], u2[:, kc, :],
                                     start=(kc == 0), stop=(kc == NC - 1)),
                            reads=pg(wgtk[p2], "l3_u2"), writes=["PS%d" % bA], tag="l3g")
                kb.emit("scalar", lambda s, fc=fc, psA=psA: s.activation(
                            gt[:], psA[:], AF.Silu, bias=bg_pp[:, fc:fc + 1]),
                        reads=["PS%d" % bA] + pg("l3_bg"), writes=pg("l3_gt"), tag="l3gs")
                bB = (2, 3)[fc % 2]
                psB = self.ps[bB]
                for kc in range(NC):
                    kb.emit("tensor", lambda t, fc=fc, kc=kc, p2=p2, psB=psB:
                            t.matmul(psB[:], wut[p2][:, kc, :], u2[:, kc, :],
                                     start=(kc == 0), stop=False),
                            reads=pg(wutk[p2], "l3_u2"), writes=["PS%d" % bB], tag="l3up")
                kb.emit("tensor", lambda t, fc=fc, psB=psB: t.matmul(
                            psB[:], self.bu_row[0:1, ts(fc)], self.ones_bf[0:1, 0:N],
                            start=False, stop=True),
                        reads=pg("bu_row", "ones_bf"), writes=["PS%d" % bB], tag="l3ub")
                kb.emit("vector", lambda v, fc=fc, psB=psB: v.tensor_mul(
                            hwin[:, fc, :], gt[:], psB[:]),
                        reads=pg("l3_gt") + ["PS%d" % bB],
                        writes=pg(("l3_hw", fc * N * 2, (fc + 1) * N * 2)), tag="l3h")
            for fc in range(32):
                p2 = fc % 2
                kb.dma("gpsimd", wdt[p2][:], self.din["L3_attn_w_down"].rearrange(
                            "(c p) d -> p c d", p=P)[:, fc, :],
                       writes=pg(wdtk[p2]), tag="l3wd")
                for dc in range(NC):
                    kb.emit("tensor", lambda t, fc=fc, dc=dc, p2=p2:
                            t.matmul(self.ps[dc][:], wdt[p2][:, ts(dc)], hwin[:, fc, :],
                                     start=(fc == 0), stop=(fc == 31)),
                            reads=pg(wdtk[p2], "l3_hw"), writes=["PS%d" % dc], tag="l3d")
            kb.dma("scalar", xw3[:], xsrc.rearrange("(c p) t -> p c t", p=P)[:, :, wsl],
                   reads=[ksrc + ":%d" % w], writes=pg("l3_xw3"), tag="l3x")
            for dc in range(NC):
                kb.emit("vector", lambda v, dc=dc, mw=mw: v.tensor_add(
                            xw3[:, dc, :], xw3[:, dc, :], mw[:, dc, :]),
                        reads=pg("l3_m", "l3_xw3"), writes=pg("l3_xw3"), tag="l3r1")
                kb.emit("vector", lambda v, dc=dc: v.scalar_tensor_tensor(
                            xw3[:, dc, :], self.ps[dc][:], bd_pp[:, dc:dc + 1], xw3[:, dc, :],
                            ALU.add, ALU.add),
                        reads=["PS%d" % dc] + pg("l3_bd", "l3_xw3"), writes=pg("l3_xw3"), tag="l3r2")
            self.x_writeback(w, xw3, xdst, kdst, "l3_xw3")

    def sub_l3ffn(self):
        kb, pg = self.kb, self.pg
        xsrc, ksrc, xdst, kdst = self.next_x()
        self.ln_all_to_dram(xsrc, ksrc, self.d_z, "d_z")
        kb.seg()
        B0 = self.PHASE
        ZS_OFF = B0
        GW_OFF = B0 + 16384
        HP_OFF = B0 + 20480
        WS_OFF = HP_OFF + 36864
        CW_OFF = WS_OFF + 12288
        XW_OFF = CW_OFF + 8192
        SM_OFF = XW_OFF + 16384
        zs = [self.sb("f3_z0", ZS_OFF, [P, NC, N], BF16), self.sb("f3_z1", ZS_OFF + 8192, [P, NC, N], BF16)]
        zsk = ["f3_z0", "f3_z1"]
        gwT = self.sb("f3_gwT", GW_OFF, [P, NC, 8], BF16)
        kb.dma("gpsimd", gwT[:], self.din["h_L3f_gwT"].rearrange("(c p) e -> p c e", p=P),
               writes=pg("f3_gwT"), tag="f3gw")
        hall = self.sb("f3_hall", HP_OFF, [P, 36, N], BF16)
        wg2 = [self.sb("f3_wg0", WS_OFF, [P, NC, P], BF16), self.sb("f3_wg1", WS_OFF + 2048, [P, NC, P], BF16)]
        wg2k = ["f3_wg0", "f3_wg1"]
        wu2 = [self.sb("f3_wu0", WS_OFF + 4096, [P, NC, P], BF16), self.sb("f3_wu1", WS_OFF + 6144, [P, NC, P], BF16)]
        wu2k = ["f3_wu0", "f3_wu1"]
        wd2 = [self.sb("f3_wd0", WS_OFF + 8192, [P, D], BF16), self.sb("f3_wd1", WS_OFF + 10240, [P, D], BF16)]
        wd2k = ["f3_wd0", "f3_wd1"]
        cwb = self.sb("f3_cwb", CW_OFF, [P, N], F32)
        cwrow = self.sb("f3_cwrow", CW_OFF + 2048, [1, N], BF16)
        xw = self.sb("f3_xw", XW_OFF, [P, NC, N], F32)
        gt8 = self.sb("f3_gt8", SM_OFF, [P, 8], F32)
        e8 = self.sb("f3_e8", SM_OFF + 64, [P, 8], F32)
        mx1 = self.sb("f3_mx", SM_OFF + 96, [P, 1], F32)
        s8 = self.sb("f3_s8", SM_OFF + 128, [P, 1], F32)
        t8 = self.sb("f3_t8", SM_OFF + 160, [P, 8], F32)
        cwt = self.sb("f3_cwt", SM_OFF + 192, [P, 8], F32)
        cwfm = self.sb("f3_cwfm", SM_OFF + 256, [8, T], F32)
        gsc = self.sb("f3_gsc", SM_OFF + 16640, [P, N], BF16)
        for w in range(NW):
            kb.seg()
            wsl = slice(w * N, (w + 1) * N)
            par = w % 2
            zw, zwk = zs[par], zsk[par]
            kb.dma("sync", zw[:], self.d_z.rearrange("(c p) t -> p c t", p=P)[:, :, wsl],
                   reads=["d_z:%d" % w], writes=pg(zwk), tag="f3zl")
            psg = self.ps[6]
            for kc in range(NC):
                kb.emit("tensor", lambda t, kc=kc, zw=zw: t.matmul(
                            psg[0:8, :], gwT[:, kc, :], zw[:, kc, :],
                            start=(kc == 0), stop=(kc == NC - 1)),
                        reads=pg("f3_gwT", zwk), writes=["PS6"], tag="f3g")
            kb.emit("scalar", lambda s: s.activation(gsc[0:8, :], psg[0:8, :], AF.Copy),
                    reads=["PS6"], writes=pg("f3_gsc"), tag="f3gev")
            for tt in range(4):
                pst = self.ps[7]
                pstb = pst.bitcast(BF16)
                kb.emit("tensor", lambda t, tt=tt, pstb=pstb: t.transpose(
                            pstb[0:P, 0:8], gsc[0:8, ts(tt)], self.ident_bf[0:8, 0:8]),
                        reads=pg("f3_gsc", "ident_bf"), writes=["PS7"], tag="f3tr")
                kb.emit("vector", lambda v, pstb=pstb: v.tensor_copy(gt8[:], pstb[0:P, 0:8]),
                        reads=["PS7"], writes=pg("f3_gt8"), tag="f3cp")
                kb.emit("vector", lambda v: v.reduce_max(mx1[:], gt8[:], AX.X),
                        reads=pg("f3_gt8"), writes=pg("f3_mx"), tag="f3mx")
                kb.emit("vector", lambda v: v.tensor_scalar_mul(mx1[:], mx1[:], -1.0),
                        reads=pg("f3_mx"), writes=pg("f3_mx"), tag="f3ng")
                kb.emit("scalar", lambda s: s.activation(e8[:], gt8[:], AF.Exp, bias=mx1[:]),
                        reads=pg("f3_gt8", "f3_mx"), writes=pg("f3_e8"), tag="f3ex")
                kb.emit("vector", lambda v: v.reduce_sum(s8[:], e8[:], AX.X),
                        reads=pg("f3_e8"), writes=pg("f3_s8"), tag="f3sm")
                kb.emit("vector", lambda v: v.reciprocal(s8[:], s8[:]),
                        reads=pg("f3_s8"), writes=pg("f3_s8"), tag="f3rc")
                kb.emit("vector", lambda v: v.tensor_scalar_mul(e8[:], e8[:], s8[:]),
                        reads=pg("f3_e8", "f3_s8"), writes=pg("f3_e8"), tag="f3pb")
                kb.emit("vector", lambda v: v.max(t8[:], e8[:]),
                        reads=pg("f3_e8"), writes=pg("f3_t8"), tag="f3t8")
                kb.emit("vector", lambda v: v.tensor_scalar(cwt[:], e8[:], t8[:, 1:2], None,
                                                            ALU.is_ge),
                        reads=pg("f3_e8", "f3_t8"), writes=pg("f3_cwt"), tag="f3ge")
                kb.emit("vector", lambda v: v.tensor_mul(cwt[:], cwt[:], e8[:]),
                        reads=pg("f3_cwt", "f3_e8"), writes=pg("f3_cwt"), tag="f3cw")
                kb.emit("tensor", lambda t: t.transpose(pst[0:8, 0:P], cwt[:],
                                                        self.ident_f32[:]),
                        reads=pg("f3_cwt", "ident_f32"), writes=["PS7"], tag="f3t2")
                kb.emit("vector", lambda v, w=w, tt=tt: v.tensor_copy(
                            cwfm[0:8, w * N + tt * P:w * N + (tt + 1) * P], pst[0:8, 0:P]),
                        reads=["PS7"], writes=pg("f3_cwfm"), tag="f3cf")
            kb.dma("sync", self.d_cw[:, wsl], cwfm[0:8, wsl], reads=pg("f3_cwfm"),
                   writes=["d_cw:%d" % w], tag="f3cs")
            for e in range(9):
                shared = (e == 8)
                p2 = e % 2
                if shared:
                    wsrc_g, wsrc_u = self.din["h_L3f_sgT"], self.din["h_L3f_suT"]
                else:
                    wsrc_g = self.din["h_L3f_wgT"][e]
                    wsrc_u = self.din["h_L3f_wuT"][e]
                for fc in range(4):
                    kb.dma("gpsimd", wg2[p2][:], wsrc_g.rearrange("(c p) f -> p c f", p=P)[:, :, ts(fc)],
                           writes=pg(wg2k[p2]), tag="f3wg")
                    kb.dma("gpsimd", wu2[p2][:], wsrc_u.rearrange("(c p) f -> p c f", p=P)[:, :, ts(fc)],
                           writes=pg(wu2k[p2]), tag="f3wu")
                    bA = (0, 1)[fc % 2]
                    psA = self.ps[bA]
                    for kc in range(NC):
                        kb.emit("tensor", lambda t, kc=kc, p2=p2, psA=psA, zw=zw:
                                t.matmul(psA[:], wg2[p2][:, kc, :], zw[:, kc, :],
                                         start=(kc == 0), stop=(kc == NC - 1)),
                                reads=pg(wg2k[p2], zwk), writes=["PS%d" % bA], tag="f3gm")
                    kb.emit("scalar", lambda s, psA=psA: s.activation(gsc[:], psA[:], AF.Silu),
                            reads=["PS%d" % bA], writes=pg("f3_gsc"), tag="f3si")
                    bB = (2, 3)[fc % 2]
                    psB = self.ps[bB]
                    for kc in range(NC):
                        kb.emit("tensor", lambda t, kc=kc, p2=p2, psB=psB, zw=zw:
                                t.matmul(psB[:], wu2[p2][:, kc, :], zw[:, kc, :],
                                         start=(kc == 0), stop=(kc == NC - 1)),
                                reads=pg(wu2k[p2], zwk), writes=["PS%d" % bB], tag="f3um")
                    hidx = e * 4 + fc
                    if shared:
                        kb.emit("vector", lambda v, hidx=hidx, psB=psB: v.tensor_mul(
                                    hall[:, hidx, :], gsc[:], psB[:]),
                                reads=pg("f3_gsc") + ["PS%d" % bB],
                                writes=pg(("f3_hall", hidx * N * 2, (hidx + 1) * N * 2)), tag="f3h")
                    else:
                        if fc == 0:
                            kb.dma("gpsimd", cwrow[:], self.d_cw[e:e + 1, wsl],
                                   reads=["d_cw:%d" % w], writes=pg("f3_cwrow"), tag="f3cr")
                            psC = self.ps[5]
                            kb.emit("tensor", lambda t: t.matmul(psC[:], self.ones_bf[0:1, 0:128],
                                                                 cwrow[:], start=True, stop=True),
                                    reads=pg("f3_cwrow", "ones_bf"), writes=["PS5"], tag="f3cb")
                            kb.emit("vector", lambda v: v.tensor_copy(cwb[:], psC[:]),
                                    reads=["PS5"], writes=pg("f3_cwb"), tag="f3ce")
                        kb.emit("vector", lambda v, psB=psB: v.tensor_mul(gsc[:], gsc[:], psB[:]),
                                reads=pg("f3_gsc") + ["PS%d" % bB], writes=pg("f3_gsc"), tag="f3h1")
                        kb.emit("vector", lambda v, hidx=hidx: v.tensor_mul(
                                    hall[:, hidx, :], gsc[:], cwb[:]),
                                reads=pg("f3_gsc", "f3_cwb"),
                                writes=pg(("f3_hall", hidx * N * 2, (hidx + 1) * N * 2)), tag="f3h2")
            for e in range(9):
                p2 = e % 2
                wsrc_d = self.din["h_L3f_sdT"] if e == 8 else self.din["h_L3f_wdT"][e]
                for fc in range(4):
                    kb.dma("gpsimd", wd2[p2][:], wsrc_d.rearrange("(c p) d -> p c d", p=P)[:, fc, :],
                           writes=pg(wd2k[p2]), tag="f3wd")
                    hidx = e * 4 + fc
                    for dc in range(NC):
                        kb.emit("tensor", lambda t, fc=fc, dc=dc, p2=p2, hidx=hidx, e=e:
                                t.matmul(self.ps[dc][:], wd2[p2][:, ts(dc)], hall[:, hidx, :],
                                         start=(e == 0 and fc == 0), stop=(e == 8 and fc == 3)),
                                reads=pg(wd2k[p2], "f3_hall"), writes=["PS%d" % dc], tag="f3d")
            kb.dma("scalar", xw[:], xsrc.rearrange("(c p) t -> p c t", p=P)[:, :, wsl],
                   reads=[ksrc + ":%d" % w], writes=pg("f3_xw"), tag="f3x")
            for dc in range(NC):
                kb.emit("vector", lambda v, dc=dc: v.scalar_tensor_tensor(
                            xw[:, dc, :], self.ps[dc][:], 0.5, xw[:, dc, :],
                            ALU.mult, ALU.add),
                        reads=["PS%d" % dc] + pg("f3_xw"), writes=pg("f3_xw"), tag="f3r")
            self.x_writeback(w, xw, xdst, kdst, "f3_xw")

    def sub_l2ffn(self):
        kb, pg = self.kb, self.pg
        xsrc, ksrc, xdst, kdst = self.next_x()
        self.ln_all_to_dram(xsrc, ksrc, self.d_z, "d_z")
        kb.seg()
        B0 = self.PHASE
        Z_OFF = B0
        W1_OFF = Z_OFF + 65536
        W2_OFF = W1_OFF + 16384
        S0F_OFF = W2_OFF + 512
        S0S_OFF = S0F_OFF + 1024
        H1_OFF = S0S_OFF + 2048
        ZT_OFF = H1_OFF + 4096
        AFM_OFF = 200416
        SM_OFF = ZT_OFF + 4096
        WIH_OFF = 139264
        z_sb = self.sb("f2_z", Z_OFF, [P, NC, T], BF16)
        for w in range(NW):
            wsl = slice(w * N, (w + 1) * N)
            kb.dma("sync", z_sb[:, :, wsl], self.d_z.rearrange("(c p) t -> p c t", p=P)[:, :, wsl],
                   reads=["d_z:%d" % w], writes=pg(("f2_z",)), tag="f2zl")
        w1 = self.sb("f2_w1", W1_OFF, [P, 16, 512], BF16)
        kb.dma("gpsimd", w1[:], self.din["L2_ffn_w1"].rearrange("(c p) f -> p c f", p=P),
               writes=pg("f2_w1"), tag="f2w1")
        w2 = self.sb("f2_w2", W2_OFF, [P, 4, 64], BF16)
        kb.dma("gpsimd", w2[:], self.din["L2_ffn_w2"].rearrange("(c p) s -> p c s", p=P),
               writes=pg("f2_w2"), tag="f2w2")
        S0f = self.sb("f2_S0f", S0F_OFF, [P, NC, 64], BF16)
        kb.dma("gpsimd", S0f[:], self.din["h_L2f_S0T"].rearrange("(c p) s -> p c s", p=P),
               writes=pg("f2_S0f"), tag="f2s0f")
        h1w = self.sb("f2_h1", H1_OFF, [P, 4, N], BF16)
        zt = [self.sb("f2_zt0", ZT_OFF, [P, D], BF16), self.sb("f2_zt1", ZT_OFF + 2048, [P, D], BF16)]
        ztk = ["f2_zt0", "f2_zt1"]
        afm = self.sb("f2_afm", AFM_OFF, [64, T], BF16)
        smb = self.sb("f2_smb", SM_OFF, [P, NC], BF16)
        smf = self.sb("f2_smf", SM_OFF + 32, [P, NC], F32)
        cfb = self.sb("f2_cfb", SM_OFF + 64, [P, 4], F32)
        b1pp = self.load_pp_vec("f2_b1", SM_OFF + 96, self.din["L2_ffn_b1"], 512)
        b2row = self.load_row_bf16("f2_b2", SM_OFF + 128, self.din["L2_ffn_b2"], 64)
        rtau = self.sb("f2_rtau", SM_OFF + 288, [1, 1], F32)
        rtaub = self.sb("f2_rtaub", SM_OFF + 320, [P, 1], F32)
        rlf = self.sb("f2_rlf", SM_OFF + 352, [64, N], F32)
        rltm = self.sb("f2_rltm", SM_OFF + 2400, [P, 64], F32)
        e64 = self.sb("f2_e64", SM_OFF + 2656, [P, 64], F32)
        mx1 = self.sb("f2_mx", SM_OFF + 2912, [P, 1], F32)
        t8 = self.sb("f2_t8", SM_OFF + 2944, [P, 8], F32)
        rd1 = self.sb("f2_rd", SM_OFF + 2976, [P, 1], F32)
        atm = self.sb("f2_atm", SM_OFF + 3008, [P, 64], BF16)
        for c in range(NC):
            kb.emit("vector", lambda v, c=c: v.reduce_sum(smf[:, c:c + 1], S0f[:, c, :], AX.X),
                    reads=pg("f2_S0f"), writes=pg("f2_smf"), tag="f2sm")
        kb.emit("vector", lambda v: v.tensor_scalar_mul(smb[:], smf[:], 1.0 / 64),
                reads=pg("f2_smf"), writes=pg("f2_smb"), tag="f2sm2")
        for fc in range(4):
            ps = self.ps[0]
            for kc in range(NC):
                kb.emit("tensor", lambda t, fc=fc, kc=kc, ps=ps: t.matmul(
                            ps[0:P, 0:1], w1[:, 8 + kc, ts(fc)], smb[:, kc:kc + 1],
                            start=(kc == 0), stop=(kc == NC - 1)),
                        reads=pg("f2_w1", "f2_smb"), writes=["PS0"], tag="f2cf")
            kb.emit("vector", lambda v, fc=fc: v.tensor_add(cfb[:, fc:fc + 1], ps[0:P, 0:1],
                                                            b1pp[:, fc:fc + 1]),
                    reads=["PS0"] + pg("f2_b1"), writes=pg("f2_cfb"), tag="f2cfe")
        kb.dma("sync", rtau[:], self.din["L2_ffn_tau"].unsqueeze(0), writes=pg("f2_rtau"))
        kb.emit("scalar", lambda s: s.activation(rtau[:], rtau[:], AF.Abs),
                reads=pg("f2_rtau"), writes=pg("f2_rtau"), tag="f2tau")
        kb.emit("vector", lambda v: v.tensor_scalar_add(rtau[:], rtau[:], 0.1),
                reads=pg("f2_rtau"), writes=pg("f2_rtau"), tag="f2tau")
        kb.emit("vector", lambda v: v.reciprocal(rtau[:], rtau[:]),
                reads=pg("f2_rtau"), writes=pg("f2_rtau"), tag="f2tau")
        kb.emit("vector", lambda v: v.tensor_copy(smb[0:1, 0:1].bitcast(BF16)[0:1, 0:1] if False else atm[0:1, 0:1], rtau[:]),
                reads=pg("f2_rtau"), writes=pg("f2_atm"), tag="f2tc")
        kb.emit("tensor", lambda t: t.matmul(self.ps[1][0:P, 0:1], self.ones_bf[0:1, 0:128],
                                             atm[0:1, 0:1], start=True, stop=True),
                reads=pg("f2_atm", "ones_bf"), writes=["PS1"], tag="f2tb")
        kb.emit("vector", lambda v: v.tensor_copy(rtaub[:], self.ps[1][0:P, 0:1]),
                reads=["PS1"], writes=pg("f2_rtaub"), tag="f2tb2")
        psSI = [self.ps[5], self.ps[6]]
        psDN = self.ps[7]
        for w in range(NW):
            kb.seg()
            wsl = slice(w * N, (w + 1) * N)
            for fc in range(4):
                b = (0, 1)[fc % 2]
                ps = self.ps[b]
                for kc in range(NC):
                    kb.emit("tensor", lambda t, fc=fc, kc=kc, ps=ps, wsl=wsl: t.matmul(
                                ps[:], w1[:, kc, ts(fc)], z_sb[:, kc, wsl],
                                start=(kc == 0), stop=(kc == NC - 1)),
                            reads=pg("f2_w1", "f2_z"), writes=["PS%d" % b], tag="f2h1")
                kb.emit("scalar", lambda s, fc=fc, ps=ps: s.activation(
                            h1w[:, fc, :], ps[:], AF.Gelu, bias=cfb[:, fc:fc + 1]),
                        reads=["PS%d" % b] + pg("f2_cfb"),
                        writes=pg(("f2_h1", fc * N * 2, (fc + 1) * N * 2)), tag="f2ge")
            b = 2
            ps = self.ps[b]
            for fc in range(4):
                kb.emit("tensor", lambda t, fc=fc, ps=ps: t.matmul(
                            ps[0:64, :], w2[:, fc, :], h1w[:, fc, :],
                            start=(fc == 0), stop=False),
                        reads=pg("f2_w2", "f2_h1"), writes=["PS2"], tag="f2rl")
            kb.emit("tensor", lambda t, ps=ps: t.matmul(
                        ps[0:64, :], b2row[:], self.ones_bf[0:1, 0:N],
                        start=False, stop=True),
                    reads=pg("f2_b2", "ones_bf"), writes=["PS2"], tag="f2rlb")
            kb.emit("scalar", lambda s, ps=ps: s.activation(rlf[:], ps[0:64, :], AF.Copy),
                    reads=["PS2"], writes=pg("f2_rlf"), tag="f2rle")
            for tt in range(4):
                gt = w * 4 + tt
                pst = self.ps[3]
                pstb = pst.bitcast(BF16)
                kb.emit("tensor", lambda t, tt=tt, pst=pst: t.transpose(
                            pst[0:P, 0:64], rlf[0:64, ts(tt)], self.ident_f32[0:64, 0:64]),
                        reads=pg("f2_rlf", "ident_f32"), writes=["PS3"], tag="f2tr")
                kb.emit("vector", lambda v, pst=pst: v.tensor_copy(rltm[:], pst[0:P, 0:64]),
                        reads=["PS3"], writes=pg("f2_rltm"), tag="f2te")
                kb.emit("vector", lambda v: v.tensor_scalar_mul(rltm[:], rltm[:], rtaub[:]),
                        reads=pg("f2_rltm", "f2_rtaub"), writes=pg("f2_rltm"), tag="f2sc")
                kb.emit("vector", lambda v: v.reduce_max(mx1[:], rltm[:], AX.X),
                        reads=pg("f2_rltm"), writes=pg("f2_mx"), tag="f2mx")
                kb.emit("vector", lambda v: v.tensor_scalar_mul(mx1[:], mx1[:], -1.0),
                        reads=pg("f2_mx"), writes=pg("f2_mx"), tag="f2ng")
                kb.emit("scalar", lambda s: s.activation(e64[:], rltm[:], AF.Exp, bias=mx1[:]),
                        reads=pg("f2_rltm", "f2_mx"), writes=pg("f2_e64"), tag="f2ex")
                kb.emit("vector", lambda v: v.max(t8[:], e64[:]),
                        reads=pg("f2_e64"), writes=pg("f2_t8"), tag="f2t8")
                kb.emit("vector", lambda v: v.reduce_sum(rd1[:], t8[:], AX.X),
                        reads=pg("f2_t8"), writes=pg("f2_rd"), tag="f2dn")
                kb.emit("vector", lambda v: v.reciprocal(rd1[:], rd1[:]),
                        reads=pg("f2_rd"), writes=pg("f2_rd"), tag="f2rc")
                kb.emit("vector", lambda v: v.tensor_scalar(rltm[:], e64[:], t8[:, 7:8], None,
                                                            ALU.is_ge),
                        reads=pg("f2_e64", "f2_t8"), writes=pg("f2_rltm"), tag="f2msk")
                kb.emit("vector", lambda v: v.tensor_mul(rltm[:], rltm[:], e64[:]),
                        reads=pg("f2_rltm", "f2_e64"), writes=pg("f2_rltm"), tag="f2am")
                kb.emit("vector", lambda v: v.tensor_scalar_mul(atm[:], rltm[:], rd1[:]),
                        reads=pg("f2_rltm", "f2_rd"), writes=pg("f2_atm"), tag="f2al")
                kb.emit("tensor", lambda t, pstb=pstb: t.transpose(
                            pstb[0:64, 0:P], atm[:], self.ident_bf[:]),
                        reads=pg("f2_atm", "ident_bf"), writes=["PS3"], tag="f2at")
                kb.emit("vector", lambda v, gt=gt, pstb=pstb: v.tensor_copy(
                            afm[0:64, gt * P:(gt + 1) * P], pstb[0:64, 0:P]),
                        reads=["PS3"], writes=pg("f2_afm"), tag="f2ae")
                p2 = tt % 2
                for c in range(NC):
                    kb.emit("tensor", lambda t, c=c, gt=gt, pstb=pstb: t.transpose(
                                pstb[0:P, 0:P], z_sb[:, c, gt * P:(gt + 1) * P],
                                self.ident_bf[:]),
                            reads=pg("f2_z", "ident_bf"), writes=["PS3"], tag="f2zt")
                    kb.emit("vector", lambda v, c=c, p2=p2, pstb=pstb: v.tensor_copy(
                                zt[p2][:, ts(c)], pstb[0:P, 0:P]),
                            reads=["PS3"], writes=pg(ztk[p2]), tag="f2ze")
                for half in range(2):
                    kb.emit("tensor", lambda t, half=half, p2=p2, gt=gt: t.matmul(
                                psSI[half][0:64, :], atm[:], zt[p2][:, ts(half, 512)],
                                start=(gt == 0), stop=(gt == 31)),
                            reads=pg("f2_atm", ztk[p2]), writes=["PS%d" % (5 + half)], tag="f2si")
                kb.emit("tensor", lambda t, gt=gt: t.matmul(
                            psDN[0:64, 0:1], atm[:], self.ones_bf[:, 0:1],
                            start=(gt == 0), stop=(gt == 31)),
                        reads=pg("f2_atm", "ones_bf"), writes=["PS7"], tag="f2dn2")
        kb.seg()
        SL = B0
        wih = self.sb("f2_wih", WIH_OFF, [P, NC, 3072], BF16)
        kb.dma("gpsimd", wih[:], self.din["h_L2f_WihT"].rearrange("(c p) f -> p c f", p=P),
               writes=pg("f2_wih"), tag="f2wih")
        sin_sm = self.sb("f2_sin", SL, [64, D], F32)
        den = self.sb("f2_den", SL + 4096, [64, 1], F32)
        sif = self.sb("f2_sif", SL + 4128, [P, NC, 64], BF16)
        gi = self.sb("f2_gi", SL + 5152, [P, 24, 64], F32)
        gh = self.sb("f2_gh", SL + 11296, [P, 24, 64], F32)
        rr = self.sb("f2_rr", SL + 17440, [P, NC, 64], F32)
        zzt = self.sb("f2_zz", SL + 19488, [P, NC, 64], F32)
        nn = self.sb("f2_nn", SL + 21536, [P, NC, 64], F32)
        snb = self.sb("f2_snb", SL + 23584, [P, NC, 64], BF16)
        qf = self.sb("f2_qf", SL + 24608, [P, 2, 64], BF16)
        kf = self.sb("f2_kf", SL + 24864, [P, 2, 64], BF16)
        vf = self.sb("f2_vf", SL + 25120, [P, NC, 64], BF16)
        vsm = self.sb("f2_vsm", SL + 26144, [64, D], BF16)
        snsm = self.sb("f2_snsm", SL + 28192, [64, D], BF16)
        ysm = self.sb("f2_ysm", SL + 30240, [64, D], F32)
        p2s = self.sb("f2_p2s", SL + 34336, [64, 64], BF16)
        p2t = self.sb("f2_p2t", SL + 34464, [64, 64], BF16)
        mu2 = self.sb("f2_mu2", SL + 34592, [64, 1], F32)
        l2t = self.sb("f2_l2", SL + 34624, [64, 1], F32)
        lngb = self.sb("f2_lngb", SL + 34656, [64, D], F32)
        lnbb = self.sb("f2_lnbb", SL + 38752, [64, D], F32)
        svs = self.sb("f2_svs", SL + 42848, [64, D], BF16)
        y2f = self.sb("f2_y2f", SL + 44896, [P, NC, 64], BF16)
        bihpp = self.load_pp_vec("f2_bih", SL + 45920, self.din["L2_ffn_bih"], 3072)
        bhhpp = self.load_pp_vec("f2_bhh", SL + 46016, self.din["L2_ffn_bhh"], 3072)
        bqpp = self.load_pp_vec("f2_bq", SL + 46112, self.din["L2_ffn_bq"], 256)
        bkpp = self.load_pp_vec("f2_bk", SL + 46144, self.din["L2_ffn_bk"], 256)
        bvpp = self.load_pp_vec("f2_bv", SL + 46176, self.din["L2_ffn_bv"], D)
        lngr = self.load_row_bf16("f2_lngr", SL + 46208, self.din["L2_ffn_ln_g"], D)
        lnbr = self.load_row_bf16("f2_lnbr", SL + 48256, self.din["L2_ffn_ln_b"], D)
        bvpr = self.load_row_bf16("f2_bvpr", SL + 50304, self.din["L2_ffn_bvp"], D)
        bor = self.load_row_bf16("f2_bor", SL + 52352, self.din["L2_ffn_bo"], D)
        wq2 = self.sb("f2_wq2", SL + 54400, [P, NC, 256], BF16)
        kb.dma("gpsimd", wq2[:], self.din["L2_ffn_wq"].rearrange("(c p) f -> p c f", p=P),
               writes=pg("f2_wq2"), tag="f2wq")
        wk2 = self.sb("f2_wk2", SL + 58496, [P, NC, 256], BF16)
        kb.dma("gpsimd", wk2[:], self.din["L2_ffn_wk"].rearrange("(c p) f -> p c f", p=P),
               writes=pg("f2_wk2"), tag="f2wk")
        wv2 = self.sb("f2_wv2", SL + 62592, [P, NC, D], BF16)
        kb.dma("gpsimd", wv2[:], self.din["L2_ffn_wv"].rearrange("(c p) f -> p c f", p=P),
               writes=pg("f2_wv2"), tag="f2wv")
        f2varscr = self.sb("f2_varscr", SL + 78976, [64, D], F32)
        f2ycast = self.sb("f2_ycast", SL + 83072, [64, D], BF16)
        f2wvp = self.sb("f2_wvp", SL + 85120, [P, NC, D], BF16)
        kb.dma("gpsimd", f2wvp[:], self.din["L2_ffn_wvp"].rearrange("(c p) f -> p c f", p=P),
               writes=pg("f2_wvp"), tag="f2wvp")
        for half in range(2):
            kb.emit("tensor", lambda t, half=half: t.matmul(
                        self.ps[0][0:64, :], self.ones_bf[0:1, 0:64], lngr[:, ts(half, 512)],
                        start=True, stop=True),
                    reads=pg("f2_lngr", "ones_bf"), writes=["PS0"], tag="f2lg")
            kb.emit("vector", lambda v, half=half: v.tensor_copy(lngb[:, ts(half, 512)],
                                                                 self.ps[0][0:64, :]),
                    reads=["PS0"], writes=pg("f2_lngb"), tag="f2lg2")
            kb.emit("tensor", lambda t, half=half: t.matmul(
                        self.ps[1][0:64, :], self.ones_bf[0:1, 0:64], lnbr[:, ts(half, 512)],
                        start=True, stop=True),
                    reads=pg("f2_lnbr", "ones_bf"), writes=["PS1"], tag="f2lb")
            kb.emit("vector", lambda v, half=half: v.tensor_copy(lnbb[:, ts(half, 512)],
                                                                 self.ps[1][0:64, :]),
                    reads=["PS1"], writes=pg("f2_lnbb"), tag="f2lb2")
        kb.emit("vector", lambda v: v.tensor_copy(den[:], psDN[0:64, 0:1]),
                reads=["PS7"], writes=pg("f2_den"), tag="f2de")
        kb.emit("vector", lambda v: v.tensor_scalar_add(den[:], den[:], 1e-8),
                reads=pg("f2_den"), writes=pg("f2_den"), tag="f2de2")
        kb.emit("vector", lambda v: v.reciprocal(den[:], den[:]),
                reads=pg("f2_den"), writes=pg("f2_den"), tag="f2de3")
        for half in range(2):
            kb.emit("vector", lambda v, half=half: v.tensor_scalar_mul(
                        sin_sm[:, ts(half, 512)], psSI[half][0:64, :], den[:]),
                    reads=["PS%d" % (5 + half)] + pg("f2_den"),
                    writes=pg("f2_sin"), tag="f2si2")
        for c in range(NC):
            pst = self.ps[2]
            kb.emit("tensor", lambda t, c=c, pst=pst: t.transpose(
                        pst[0:P, 0:64], sin_sm[0:64, ts(c)], self.ident_f32[0:64, 0:64]),
                    reads=pg("f2_sin", "ident_f32"), writes=["PS2"], tag="f2st")
            kb.emit("vector", lambda v, c=c, pst=pst: v.tensor_copy(sif[:, c, :], pst[0:P, 0:64]),
                    reads=["PS2"], writes=pg("f2_sif"), tag="f2se")
        for oc in range(24):
            b = (0, 1)[oc % 2]
            ps = self.ps[b]
            for kc in range(NC):
                kb.emit("tensor", lambda t, oc=oc, kc=kc, ps=ps: t.matmul(
                            ps[0:P, 0:64], wih[:, kc, ts(oc)], sif[:, kc, :],
                            start=(kc == 0), stop=(kc == NC - 1)),
                        reads=pg("f2_wih", "f2_sif"), writes=["PS%d" % b], tag="f2gi")
            kb.emit("vector", lambda v, oc=oc, ps=ps: v.tensor_scalar_add(
                        gi[:, oc, :], ps[0:P, 0:64], bihpp[:, oc:oc + 1]),
                    reads=["PS%d" % b] + pg("f2_bih"), writes=pg("f2_gi"), tag="f2gie")
        kb.dma("gpsimd", wih[:], self.din["h_L2f_WhhT"].rearrange("(c p) f -> p c f", p=P),
               reads=pg("f2_wih"), writes=pg("f2_wih"), tag="f2whh")
        for oc in range(24):
            b = (0, 1)[oc % 2]
            ps = self.ps[b]
            for kc in range(NC):
                kb.emit("tensor", lambda t, oc=oc, kc=kc, ps=ps: t.matmul(
                            ps[0:P, 0:64], wih[:, kc, ts(oc)], S0f[:, kc, :],
                            start=(kc == 0), stop=(kc == NC - 1)),
                        reads=pg("f2_wih", "f2_S0f"), writes=["PS%d" % b], tag="f2gh")
            kb.emit("vector", lambda v, oc=oc, ps=ps: v.tensor_scalar_add(
                        gh[:, oc, :], ps[0:P, 0:64], bhhpp[:, oc:oc + 1]),
                    reads=["PS%d" % b] + pg("f2_bhh"), writes=pg("f2_gh"), tag="f2ghe")
        for c in range(NC):
            kb.emit("vector", lambda v, c=c: v.tensor_add(rr[:, c, :], gi[:, c, :], gh[:, c, :]),
                    reads=pg("f2_gi", "f2_gh"), writes=pg("f2_rr"), tag="f2r1")
            kb.emit("scalar", lambda s, c=c: s.activation(rr[:, c, :], rr[:, c, :], AF.Sigmoid),
                    reads=pg("f2_rr"), writes=pg("f2_rr"), tag="f2r2")
            kb.emit("vector", lambda v, c=c: v.tensor_add(zzt[:, c, :], gi[:, 8 + c, :], gh[:, 8 + c, :]),
                    reads=pg("f2_gi", "f2_gh"), writes=pg("f2_zz"), tag="f2z1")
            kb.emit("scalar", lambda s, c=c: s.activation(zzt[:, c, :], zzt[:, c, :], AF.Sigmoid),
                    reads=pg("f2_zz"), writes=pg("f2_zz"), tag="f2z2")
            kb.emit("vector", lambda v, c=c: v.tensor_mul(nn[:, c, :], rr[:, c, :], gh[:, 16 + c, :]),
                    reads=pg("f2_rr", "f2_gh"), writes=pg("f2_nn"), tag="f2n1")
            kb.emit("vector", lambda v, c=c: v.tensor_add(nn[:, c, :], nn[:, c, :], gi[:, 16 + c, :]),
                    reads=pg("f2_nn", "f2_gi"), writes=pg("f2_nn"), tag="f2n2")
            kb.emit("scalar", lambda s, c=c: s.activation(nn[:, c, :], nn[:, c, :], AF.Tanh),
                    reads=pg("f2_nn"), writes=pg("f2_nn"), tag="f2n3")
            kb.emit("vector", lambda v, c=c: v.tensor_mul(rr[:, c, :], zzt[:, c, :], nn[:, c, :]),
                    reads=pg("f2_zz", "f2_nn"), writes=pg("f2_rr"), tag="f2s1")
            kb.emit("vector", lambda v, c=c: v.tensor_sub(nn[:, c, :], nn[:, c, :], rr[:, c, :]),
                    reads=pg("f2_nn", "f2_rr"), writes=pg("f2_nn"), tag="f2s2")
            kb.emit("vector", lambda v, c=c: v.tensor_mul(rr[:, c, :], zzt[:, c, :], S0f[:, c, :]),
                    reads=pg("f2_zz", "f2_S0f"), writes=pg("f2_rr"), tag="f2s3")
            kb.emit("vector", lambda v, c=c: v.tensor_add(snb[:, c, :], nn[:, c, :], rr[:, c, :]),
                    reads=pg("f2_nn", "f2_rr"), writes=pg("f2_snb"), tag="f2s4")
        for oc in range(2):
            ps = self.ps[2]
            for kc in range(NC):
                kb.emit("tensor", lambda t, oc=oc, kc=kc, ps=ps: t.matmul(
                            ps[0:P, 0:64], wq2[:, kc, ts(oc)], snb[:, kc, :],
                            start=(kc == 0), stop=(kc == NC - 1)),
                        reads=pg("f2_wq2", "f2_snb"), writes=["PS2"], tag="f2q")
            kb.emit("vector", lambda v, oc=oc, ps=ps: v.tensor_scalar_add(
                        qf[:, oc, :], ps[0:P, 0:64], bqpp[:, oc:oc + 1]),
                    reads=["PS2"] + pg("f2_bq"), writes=pg("f2_qf"), tag="f2qe")
            for kc in range(NC):
                kb.emit("tensor", lambda t, oc=oc, kc=kc, ps=ps: t.matmul(
                            ps[0:P, 0:64], wk2[:, kc, ts(oc)], snb[:, kc, :],
                            start=(kc == 0), stop=(kc == NC - 1)),
                        reads=pg("f2_wk2", "f2_snb"), writes=["PS2"], tag="f2k")
            kb.emit("vector", lambda v, oc=oc, ps=ps: v.tensor_scalar_add(
                        kf[:, oc, :], ps[0:P, 0:64], bkpp[:, oc:oc + 1]),
                    reads=["PS2"] + pg("f2_bk"), writes=pg("f2_kf"), tag="f2ke")
        for oc in range(NC):
            ps = self.ps[2]
            for kc in range(NC):
                kb.emit("tensor", lambda t, oc=oc, kc=kc, ps=ps: t.matmul(
                            ps[0:P, 0:64], wv2[:, kc, ts(oc)], snb[:, kc, :],
                            start=(kc == 0), stop=(kc == NC - 1)),
                        reads=pg("f2_wv2", "f2_snb"), writes=["PS2"], tag="f2v")
            kb.emit("vector", lambda v, oc=oc, ps=ps: v.tensor_scalar_add(
                        vf[:, oc, :], ps[0:P, 0:64], bvpp[:, oc:oc + 1]),
                    reads=["PS2"] + pg("f2_bv"), writes=pg("f2_vf"), tag="f2ve")
        ps = self.ps[3]
        for kc in range(2):
            kb.emit("tensor", lambda t, kc=kc, ps=ps: t.matmul(
                        ps[0:64, 0:64], qf[:, kc, :], kf[:, kc, :],
                        start=(kc == 0), stop=(kc == 1)),
                    reads=pg("f2_qf", "f2_kf"), writes=["PS3"], tag="f2s2m")
        kb.emit("vector", lambda v, ps=ps: v.reduce_max(mu2[:], ps[0:64, 0:64], AX.X),
                reads=["PS3"], writes=pg("f2_mu2"), tag="f2s2x")
        kb.emit("vector", lambda v: v.tensor_scalar_mul(mu2[:], mu2[:], -1.0 / 16.0),
                reads=pg("f2_mu2"), writes=pg("f2_mu2"), tag="f2s2n")
        kb.emit("scalar", lambda s, ps=ps: s.activation(p2s[0:64, :], ps[0:64, 0:64], AF.Exp,
                                                        bias=mu2[:], scale=1.0 / 16.0,
                                                        accum_out=l2t[:]),
                reads=["PS3"] + pg("f2_mu2"), writes=pg("f2_p2s", "f2_l2"), tag="f2s2e")
        kb.emit("vector", lambda v: v.reciprocal(l2t[:], l2t[:]),
                reads=pg("f2_l2"), writes=pg("f2_l2"), tag="f2s2r")
        kb.emit("vector", lambda v: v.tensor_scalar_mul(p2s[0:64, :], p2s[0:64, :], l2t[:]),
                reads=pg("f2_p2s", "f2_l2"), writes=pg("f2_p2s"), tag="f2s2d")
        pstb = self.ps[3].bitcast(BF16)
        kb.emit("tensor", lambda t, pstb=pstb: t.transpose(
                    pstb[0:64, 0:64], p2s[0:64, 0:64], self.ident_bf[0:64, 0:64]),
                reads=pg("f2_p2s", "ident_bf"), writes=["PS3"], tag="f2s2t")
        kb.emit("vector", lambda v, pstb=pstb: v.tensor_copy(p2t[0:64, :], pstb[0:64, 0:64]),
                reads=["PS3"], writes=pg("f2_p2t"), tag="f2s2te")
        for c in range(NC):
            kb.emit("tensor", lambda t, c=c, pstb=pstb: t.transpose(
                        pstb[0:64, 0:P], vf[:, c, :], self.ident_bf[:]),
                    reads=pg("f2_vf", "ident_bf"), writes=["PS3"], tag="f2vt")
            kb.emit("vector", lambda v, c=c, pstb=pstb: v.tensor_copy(vsm[:, ts(c)], pstb[0:64, 0:P]),
                    reads=["PS3"], writes=pg("f2_vsm"), tag="f2vte")
            kb.emit("tensor", lambda t, c=c, pstb=pstb: t.transpose(
                        pstb[0:64, 0:P], snb[:, c, :], self.ident_bf[:]),
                    reads=pg("f2_snb", "ident_bf"), writes=["PS3"], tag="f2nt")
            kb.emit("vector", lambda v, c=c, pstb=pstb: v.tensor_copy(snsm[:, ts(c)], pstb[0:64, 0:P]),
                    reads=["PS3"], writes=pg("f2_snsm"), tag="f2nte")
        for half in range(2):
            ps = self.ps[4]
            kb.emit("tensor", lambda t, half=half, ps=ps: t.matmul(
                        ps[0:64, :], p2t[0:64, :], vsm[:, ts(half, 512)],
                        start=True, stop=True),
                    reads=pg("f2_p2t", "f2_vsm"), writes=["PS4"], tag="f2cm")
            kb.emit("vector", lambda v, half=half, ps=ps: v.tensor_add(
                        ysm[:, ts(half, 512)], ps[0:64, :], snsm[0:64, ts(half, 512)]),
                    reads=["PS4"] + pg("f2_snsm"), writes=pg("f2_ysm"), tag="f2y")
        kb.emit("vector", lambda v: v.reduce_sum(mu2[:], ysm[0:64, :], AX.X),
                reads=pg("f2_ysm"), writes=pg("f2_mu2"), tag="f2l1")
        kb.emit("vector", lambda v: v.tensor_scalar_mul(mu2[:], mu2[:], 1.0 / D),
                reads=pg("f2_mu2"), writes=pg("f2_mu2"), tag="f2l2b")
        kb.emit("vector", lambda v: v.tensor_scalar(ysm[0:64, :], ysm[0:64, :], mu2[:], None,
                                                    ALU.subtract),
                reads=pg("f2_ysm", "f2_mu2"), writes=pg("f2_ysm"), tag="f2l3")
        kb.emit("vector", lambda v: v.tensor_mul(f2varscr[0:64, :], ysm[0:64, :], ysm[0:64, :]),
                reads=pg("f2_ysm"), writes=pg("f2_varscr"), tag="f2l4a")
        kb.emit("vector", lambda v: v.reduce_sum(l2t[:], f2varscr[0:64, :], AX.X),
                reads=pg("f2_varscr"), writes=pg("f2_l2"), tag="f2l4b")
        kb.emit("scalar", lambda s: s.activation(l2t[:], l2t[:], AF.Ln, bias=self.C_EPS5[0:64, :],
                                                 scale=1.0 / D),
                reads=pg("f2_l2", "consts"), writes=pg("f2_l2"), tag="f2l5")
        kb.emit("scalar", lambda s: s.activation(l2t[:], l2t[:], AF.Exp, scale=-0.5),
                reads=pg("f2_l2"), writes=pg("f2_l2"), tag="f2l6")
        kb.emit("vector", lambda v: v.tensor_scalar_mul(ysm[0:64, :], ysm[0:64, :], l2t[:]),
                reads=pg("f2_ysm", "f2_l2"), writes=pg("f2_ysm"), tag="f2l7")
        kb.emit("vector", lambda v: v.tensor_mul(ysm[0:64, :], ysm[0:64, :], lngb[0:64, :]),
                reads=pg("f2_ysm", "f2_lngb"), writes=pg("f2_ysm"), tag="f2l8")
        kb.emit("vector", lambda v: v.tensor_add(ysm[0:64, :], ysm[0:64, :], lnbb[0:64, :]),
                reads=pg("f2_ysm", "f2_lnbb"), writes=pg("f2_ysm"), tag="f2l9")
        kb.emit("vector", lambda v: v.tensor_copy(f2ycast[0:64, :], ysm[0:64, :]),
                reads=pg("f2_ysm"), writes=pg("f2_ycast"), tag="f2yc")
        for c in range(NC):
            pst = self.ps[2]
            kb.emit("tensor", lambda t, c=c, pst=pst: t.transpose(
                        pst.bitcast(BF16)[0:P, 0:64], f2ycast[0:64, ts(c)],
                        self.ident_bf[0:64, 0:64]),
                    reads=pg("f2_ycast", "ident_bf"), writes=["PS2"], tag="f2yt")
            kb.emit("vector", lambda v, c=c, pst=pst: v.tensor_copy(y2f[:, c, :],
                                                                    pst.bitcast(BF16)[0:P, 0:64]),
                    reads=["PS2"], writes=pg("f2_y2f"), tag="f2ye")
        for half in range(2):
            ps = self.ps[4]
            for kc in range(NC):
                kb.emit("tensor", lambda t, half=half, kc=kc, ps=ps: t.matmul(
                            ps[0:64, :], y2f[:, kc, :], f2wvp[:, kc, ts(half, 512)],
                            start=(kc == 0), stop=False),
                        reads=pg("f2_wvp", "f2_y2f"), writes=["PS4"], tag="f2sv")
            kb.emit("tensor", lambda t, half=half, ps=ps: t.matmul(
                        ps[0:64, :], self.ones_bf[0:1, 0:64], bvpr[:, ts(half, 512)],
                        start=False, stop=True),
                    reads=pg("f2_bvpr", "ones_bf"), writes=["PS4"], tag="f2svb")
            kb.emit("vector", lambda v, half=half, ps=ps: v.tensor_copy(
                        svs[:, ts(half, 512)], ps[0:64, :]),
                    reads=["PS4"], writes=pg("f2_svs"), tag="f2sve")
        kb.seg()
        OW2 = WIH_OFF
        wo2 = self.sb("f2_wo2", OW2, [P, NC, D], BF16)
        kb.dma("gpsimd", wo2[:], self.din["L2_ffn_wo"].rearrange("(c p) f -> p c f", p=P),
               reads=pg("f2_wih"), writes=pg("f2_wo2"), tag="f2wo")
        ow2 = self.sb("f2_ow", OW2 + 16384, [P, NC, N], BF16)
        xw2 = self.sb("f2_xw", OW2 + 24576, [P, NC, N], F32)
        for w in range(NW):
            kb.seg()
            wsl = slice(w * N, (w + 1) * N)
            for oc in range(NC):
                b = (0, 1)[oc % 2]
                ps = self.ps[b]
                kb.emit("tensor", lambda t, oc=oc, ps=ps, wsl=wsl: t.matmul(
                            ps[:], svs[0:64, ts(oc)], afm[0:64, wsl],
                            start=True, stop=True),
                        reads=pg("f2_svs", "f2_afm"), writes=["PS%d" % b], tag="f2o")
                kb.emit("scalar", lambda s, oc=oc, ps=ps: s.activation(ow2[:, oc, :], ps[:], AF.Copy),
                        reads=["PS%d" % b], writes=pg(("f2_ow", oc * N * 2, (oc + 1) * N * 2)),
                        tag="f2oe")
            kb.dma("scalar", xw2[:], xsrc.rearrange("(c p) t -> p c t", p=P)[:, :, wsl],
                   reads=[ksrc + ":%d" % w], writes=pg("f2_xw"), tag="f2x")
            for oc in range(NC):
                b = (2, 3)[oc % 2]
                ps = self.ps[b]
                for kc in range(NC):
                    kb.emit("tensor", lambda t, oc=oc, kc=kc, ps=ps: t.matmul(
                                ps[:], wo2[:, kc, ts(oc)], ow2[:, kc, :],
                                start=(kc == 0), stop=False),
                            reads=pg("f2_wo2", "f2_ow"), writes=["PS%d" % b], tag="f2f")
                kb.emit("tensor", lambda t, oc=oc, ps=ps: t.matmul(
                            ps[:], bor[:, ts(oc)], self.ones_bf[0:1, 0:N],
                            start=False, stop=True),
                        reads=pg("f2_bor", "ones_bf"), writes=["PS%d" % b], tag="f2fb")
                kb.emit("vector", lambda v, oc=oc, ps=ps: v.scalar_tensor_tensor(
                            xw2[:, oc, :], ps[:], 0.5, xw2[:, oc, :], ALU.mult, ALU.add),
                        reads=["PS%d" % b] + pg("f2_xw"), writes=pg("f2_xw"), tag="f2res")
            self.x_writeback(w, xw2, xdst, kdst, "f2_xw")

    def stats_of(self, src, srckey, mean, meank, A, Ak, bS, bQ, eps=1e-5, ncs=NC):
        kb, pg = self.kb, self.pg
        psS, psQ = self.ps[bS], self.ps[bQ]
        stsq = self.stsq
        for c in range(ncs):
            p2 = c % 2
            kb.emit("scalar", lambda s, c=c, p2=p2, src=src: s.activation(stsq[p2][:], src[:, c, :], AF.Square),
                    reads=pg(srckey), writes=pg("stsq%d" % p2), tag="st_sq")
            kb.emit("tensor", lambda t, c=c, src=src: t.matmul(psS[:], self.ones_bf[:, 0:128], src[:, c, :],
                                                               start=(c == 0), stop=(c == ncs - 1)),
                    reads=pg(srckey, "ones_bf"), writes=["PS%d" % bS], tag="st_mmS")
            kb.emit("tensor", lambda t, c=c, p2=p2: t.matmul(psQ[:], self.ones_bf[:, 0:128], stsq[p2][:],
                                                             start=(c == 0), stop=(c == ncs - 1)),
                    reads=pg("stsq%d" % p2, "ones_bf"), writes=["PS%d" % bQ], tag="st_mmQ")
        kb.emit("vector", lambda v: v.tensor_scalar_mul(mean[:], psS[:], 1.0 / (ncs * P)),
                reads=["PS%d" % bS], writes=pg(meank), tag="st_mean")
        kb.emit("vector", lambda v: v.tensor_mul(A[:], mean[:], mean[:]),
                reads=pg(meank), writes=pg(Ak), tag="st_msq")
        kb.emit("vector", lambda v: v.scalar_tensor_tensor(A[:], psQ[:], 1.0 / (ncs * P), A[:],
                                                           ALU.mult, ALU.subtract),
                reads=["PS%d" % bQ] + pg(Ak), writes=pg(Ak), tag="st_var")
        epst = {1e-5: self.C_EPS5, 1e-6: self.C_EPS6, 1e-3: self.C_EPS3}[eps]
        kb.emit("scalar", lambda s: s.activation(A[:], A[:], AF.Ln, bias=epst),
                reads=pg(Ak, "consts"), writes=pg(Ak), tag="st_ln")
        kb.emit("scalar", lambda s: s.activation(A[:], A[:], AF.Exp, scale=-0.5),
                reads=pg(Ak), writes=pg(Ak), tag="st_exp")

    def finalize(self):
        kb = self.kb
        if kb.dma_count["sync"]:
            nd = kb.dma_count["sync"]
            kb.emit("sync", lambda e: e.nop(), reads=[], writes=[], tag="fin")
            kb.ops["sync"][-1]["waits"].append(("d", "sync", nd - 1))
        if kb.dma_count["gpsimd"]:
            nd = kb.dma_count["gpsimd"]
            kb.emit("gpsimd", lambda e: e.nop(), tag="fin")
            kb.ops["gpsimd"][-1]["waits"].append(("d", "gpsimd", nd - 1))
        if kb.dma_count["scalar"]:
            nd = kb.dma_count["scalar"]
            kb.emit("scalar", lambda e: e.nop(), tag="fin")
            kb.ops["scalar"][-1]["waits"].append(("d", "scalar", nd - 1))
        return kb.finalize(self.es)


ALL_SPECS = [
    ("xin", [D, T], F32), ("c_ident", [128, 128], F32),
    ("c_hpar", [16, 128], F32), ("c_hmask", [16, 8], F32),
    ("L0_ffn_theta", [D, D], F32), ("L0_ffn_phi", [D, D], F32),
    ("L0_ffn_theta_noise", [D, D], F32), ("L0_ffn_phi_noise", [D, D], F32),
    ("h_L0_WqT", [D, D], F32), ("h_L0_WkT", [D, D], F32),
    ("h_L0_WvT", [D, D], F32), ("h_L0_WoT", [D, D], F32),
    ("h_L1f_qwT", [D, D], F32), ("L1_ffn_q_b", [D], F32),
    ("h_L1f_kwT", [D, D], F32), ("L1_ffn_k_b", [D], F32),
    ("h_L1f_vwT", [D, D], F32), ("L1_ffn_v_b", [D], F32),
    ("h_L1f_owT", [D, D], F32), ("L1_ffn_o_b", [D], F32),
    ("L1_ffn_norm_w", [D], F32), ("L1_ffn_norm_b", [D], F32),
    ("L1_attn_Wq", [D, 16, 64], F32), ("L1_attn_bq", [16, 64], F32),
    ("L1_attn_Wk", [D, 16, 64], F32), ("L1_attn_bk", [16, 64], F32),
    ("L1_attn_Wv", [D, 16, 64], F32), ("L1_attn_bv", [16, 64], F32),
    ("L1_attn_Wo", [16, 64, D], F32), ("L1_attn_bo", [D], F32),
    ("L1_attn_gamma", [D], F32), ("L1_attn_beta", [D], F32),
    ("L2_ffn_slot_init", [64, D], F32), ("h_L2f_S0T", [D, 64], F32),
    ("L2_ffn_w1", [2 * D, 512], F32), ("L2_ffn_b1", [512], F32),
    ("L2_ffn_w2", [512, 64], F32), ("L2_ffn_b2", [64], F32),
    ("L2_ffn_tau", [1], F32),
    ("h_L2f_WihT", [D, 3 * D], F32), ("h_L2f_WhhT", [D, 3 * D], F32),
    ("L2_ffn_bih", [3 * D], F32), ("L2_ffn_bhh", [3 * D], F32),
    ("L2_ffn_wq", [D, 256], F32), ("L2_ffn_bq", [256], F32),
    ("L2_ffn_wk", [D, 256], F32), ("L2_ffn_bk", [256], F32),
    ("L2_ffn_wv", [D, D], F32), ("L2_ffn_bv", [D], F32),
    ("L2_ffn_ln_g", [D], F32), ("L2_ffn_ln_b", [D], F32),
    ("L2_ffn_wvp", [D, D], F32), ("L2_ffn_bvp", [D], F32),
    ("L2_ffn_wo", [D, D], F32), ("L2_ffn_bo", [D], F32),
    ("h_L2_WqT", [D, D], F32), ("h_L2_WvT", [D, D], F32),
    ("h_L2_WgT", [D, D], F32), ("h_L2_WoT", [D, D], F32),
    ("L2_attn_beta", [16], F32),
    ("L2_attn_ln_gamma", [D], F32), ("L2_attn_ln_beta", [D], F32),
    ("h_L3f_gwT", [D, 8], F32),
    ("h_L3f_wgT", [8, D, 512], F32), ("h_L3f_wuT", [8, D, 512], F32),
    ("h_L3f_wdT", [8, 512, D], F32),
    ("h_L3f_sgT", [D, 512], F32), ("h_L3f_suT", [D, 512], F32),
    ("h_L3f_sdT", [512, D], F32),
    ("L3_attn_norm_w", [D], F32), ("h_L3_convw", [3, D], F32),
    ("L3_attn_conv_b", [D], F32), ("L3_attn_a", [D], F32),
    ("L3_attn_b", [D], F32), ("L3_attn_mlp_norm_w", [D], F32),
    ("L3_attn_w_gate", [D, 4096], F32), ("L3_attn_b_gate", [4096], F32),
    ("L3_attn_w_up", [D, 4096], F32), ("L3_attn_b_up", [4096], F32),
    ("L3_attn_w_down", [4096, D], F32), ("L3_attn_b_down", [D], F32),
]


def build_full(upto=12):
    m = MK(upto=upto)
    m.setup(ALL_SPECS)
    m.x_cur = ("xin", m.din["xin"])
    steps = []
    def f0():
        C_sb, S_sb = m.prep_trig(m.PHASE)
        m.sub_l0ffn(C_sb, S_sb)
    steps.append(f0)
    steps.append(lambda: m.sub_attn_generic('A0'))
    steps.append(f0)
    steps.append(lambda: m.sub_attn_generic('F1'))
    steps.append(lambda: m.sub_attn_generic('A1'))
    steps.append(lambda: m.sub_attn_generic('F1'))
    steps.append(lambda: m.sub_l2ffn())
    steps.append(lambda: m.sub_l2attn())
    steps.append(lambda: m.sub_l2ffn())
    steps.append(lambda: m.sub_l3ffn())
    steps.append(lambda: m.sub_l3attn())
    steps.append(lambda: m.sub_l3ffn())
    for i in range(min(upto, 12)):
        steps[i]()
        m.kb.seg()
    m.finalize()
    return m


def prep_inputs(inputs):
    f = np.ascontiguousarray
    d = dict(inputs)
    w = {}
    w["c_ident"] = np.eye(128, dtype=np.float32)
    hpar = np.zeros((16, 128), np.float32)
    for h in range(16):
        hpar[h, :] = [1.0 if h % 2 == p // 64 else 0.0 for p in range(128)]
    w["c_hpar"] = hpar
    hmask = np.zeros((16, 8), np.float32)
    for h in range(16):
        hmask[h, h // 2] = 1.0
    w["c_hmask"] = hmask
    W3 = d["L0_attn_W_qkv"].reshape(D, 3, D)
    w["h_L0_WqT"] = f(W3[:, 0, :].T)
    w["h_L0_WkT"] = f(W3[:, 1, :].T)
    w["h_L0_WvT"] = f(W3[:, 2, :].T)
    w["h_L0_WoT"] = f(d["L0_attn_W_out"].T)
    w["h_L1f_qwT"] = f(d["L1_ffn_q_w"].T)
    w["h_L1f_kwT"] = f(d["L1_ffn_k_w"].T)
    w["h_L1f_vwT"] = f(d["L1_ffn_v_w"].T)
    w["h_L1f_owT"] = f(d["L1_ffn_o_w"].T)
    w["h_L2f_S0T"] = f(d["L2_ffn_slot_init"].T)
    w["h_L2f_WihT"] = f(d["L2_ffn_Wih"].T)
    w["h_L2f_WhhT"] = f(d["L2_ffn_Whh"].T)
    w["h_L2_WqT"] = f(d["L2_attn_Wq"].T)
    w["h_L2_WvT"] = f(d["L2_attn_Wv"].T)
    w["h_L2_WgT"] = f(d["L2_attn_Wg"].T)
    w["h_L2_WoT"] = f(d["L2_attn_Wo"].T)
    w["h_L3f_gwT"] = f(d["L3_ffn_gate_w"].T)
    w["h_L3f_wgT"] = f(d["L3_ffn_wg"].transpose(0, 2, 1))
    w["h_L3f_wuT"] = f(d["L3_ffn_wu"].transpose(0, 2, 1))
    w["h_L3f_wdT"] = f(d["L3_ffn_wd"].transpose(0, 2, 1))
    w["h_L3f_sgT"] = f(d["L3_ffn_sg"].T)
    w["h_L3f_suT"] = f(d["L3_ffn_su"].T)
    w["h_L3f_sdT"] = f(d["L3_ffn_sd"].T)
    w["h_L3_convw"] = f(d["L3_attn_conv_w"][:, 0, :].T)
    for name, shape, _dt in ALL_SPECS:
        if name.startswith("h_") or name.startswith("c_") or name == "xin":
            continue
        w[name] = np.ascontiguousarray(d[name]).astype(np.float32)
    xins = [f(d["L0_attn_x"][b].T) for b in range(d["L0_attn_x"].shape[0])]
    return w, xins


from concourse.bass_utils import run_bass_kernel_spmd


def kernel(**inputs):
    assert inputs["L1_ffn_router_prob"].shape[-1] == 4
    w, xins = prep_inputs(inputs)
    m = build_full(12)
    in_maps = [dict(w) for _ in range(8)]
    for b in range(8):
        in_maps[b]["xin"] = xins[b]
    res = run_bass_kernel_spmd(m.nc, in_maps, core_ids=list(range(8)))
    out = np.stack([np.ascontiguousarray(res.results[b]["xout"].T) for b in range(8)])
    return out
